# version 02
# baseline (speedup 1.0000x reference)
import numpy as np
import ml_dtypes
import concourse.bass as bass
import concourse.mybir as mybir
from concourse.bass_utils import run_bass_kernel_spmd

F32 = mybir.dt.float32
BF16 = mybir.dt.bfloat16
I8 = mybir.dt.int8
AF = mybir.ActivationFunctionType
ALU = mybir.AluOpType
AX = mybir.AxisListType
BF = ml_dtypes.bfloat16

D = 4096
T = 1024
NT = 8
CTX = 2048
NCC = 4
NCT = 16
KT = 32
QL = 1536
QLT = 12
KVL = 512
KVLT = 4
NH = 32
NPAIR = 16
II = 11008
IT = 86
NNC = 22
NDC = 8
SCALE = float(1.0 / np.sqrt(192.0))
LN_EPS = 1e-5
RMS_EPS = 1e-6


def _ncw(j):
    return min(512, II - 512 * j)


def build_nc(dbg_phase=None):
    nc = bass.Bass("TRN2", target_bir_lowering=False)

    def din(name, shape, dt):
        return nc.dram_tensor(name, list(shape), dt, kind="ExternalInput").ap()

    x_q = din("x_q", (T, D), F32)
    x_b = din("x_b", (CTX, D), F32)
    wq_aT = din("wq_aT", (D, QL), F32)
    q_nw = din("q_nw", (QL, 1), F32)
    wq_bT = din("wq_bT", (QL, NH * 192), F32)
    wkv_aT = din("wkv_aT", (D, KVL + 64), F32)
    kv_nw = din("kv_nw", (KVL, 1), F32)
    wkv_bT = din("wkv_bT", (KVL, NH * 256), F32)
    woT = din("woT", (D, D), F32)
    gate_q8 = din("gate_q8", (D, II), I8)
    up_q8 = din("up_q8", (D, II), I8)
    down_q8 = din("down_q8", (II, D), I8)
    gate_s = din("gate_s", (KT, II), F32)
    gate_z = din("gate_z", (KT, II), F32)
    up_s = din("up_s", (KT, II), F32)
    up_z = din("up_z", (KT, II), F32)
    down_s = din("down_s", (IT, D), F32)
    down_z = din("down_z", (IT, D), F32)
    rope_q = din("rope_q", (64, T), BF16)
    rope_k = din("rope_k", (64, CTX), BF16)
    mask_in = din("mask_in", (NT, 128, CTX), BF16)

    out_t = nc.dram_tensor("out", [T, D], F32, kind="ExternalOutput").ap()
    dbg_t = nc.dram_tensor("dbg_arena", [128, 192 * 1024], mybir.dt.uint8,
                           kind="ExternalOutput").ap()

    sbf = {}
    for nm, tot in [("gs", KT * II), ("gz", KT * II), ("us", KT * II),
                    ("uz", KT * II), ("ds", IT * D), ("dz", IT * D)]:
        sbf[nm] = nc.dram_tensor("sc_" + nm, [1, tot], BF16, kind="Internal").ap()
    hT_sc = nc.dram_tensor("hT_sc", [II, T], BF16, kind="Internal").ap()
    x1_sc = nc.dram_tensor("x1_sc", [T, D], BF16, kind="Internal").ap()
    rq_sc = nc.dram_tensor("rq_sc", [1, T], F32, kind="Internal").ap()
    rkv_sc = nc.dram_tensor("rkv_sc", [1, CTX], F32, kind="Internal").ap()

    ident = nc.alloc_sbuf_tensor("ident", [128, 128], BF16).ap()
    ones_col = nc.alloc_sbuf_tensor("ones_col", [128, 1], BF16).ap()
    qnw_sb = nc.alloc_sbuf_tensor("qnw_sb", [128, QLT], F32).ap()
    kvnw_sb = nc.alloc_sbuf_tensor("kvnw_sb", [128, KVLT], F32).ap()
    ropq_c = nc.alloc_sbuf_tensor("ropq_c", [32, T], BF16).ap()
    ropq_s = nc.alloc_sbuf_tensor("ropq_s", [32, T], BF16).ap()
    ropk_c = nc.alloc_sbuf_tensor("ropk_c", [32, CTX], BF16).ap()
    ropk_s = nc.alloc_sbuf_tensor("ropk_s", [32, CTX], BF16).ap()
    stat = nc.alloc_sbuf_tensor("stat", [128, 64], F32).ap()
    mv = nc.alloc_sbuf_tensor("mv", [128, 8], F32).ap()
    srt = nc.alloc_sbuf_tensor("srt", [128, NT], F32).ap()
    sm = nc.alloc_sbuf_tensor("sm", [128, 16], F32).ap()
    cst = nc.alloc_sbuf_tensor("cst", [128, 8], F32).ap()
    CONSTS = [float(LN_EPS), float(RMS_EPS), 1.0, 0.0,
              float(1.0 / QL), float(1.0 / KVL)]
    for ci, cv in enumerate(CONSTS):
        nc.const_aps.register(F32, cv, cst[:, ci:ci + 1])

    AR_BYTES = 192 * 1024
    arena_t = nc.alloc_sbuf_tensor("arena", [128, AR_BYTES], mybir.dt.uint8)
    ar_bf = arena_t.ap().bitcast(BF16)
    ar_f32 = arena_t.ap().bitcast(F32)

    def abf(off, n):
        assert off % 2 == 0 and off + 2 * n <= AR_BYTES, (off, n)
        return ar_bf[:, off // 2: off // 2 + n]

    def af32(off, n):
        assert off % 4 == 0 and off + 4 * n <= AR_BYTES, (off, n)
        return ar_f32[:, off // 4: off // 4 + n]

    def ar1(tile, t0=0, n=1024):
        return abf(2048 * tile + 2 * t0, n)

    def qlra(j, c0=0, n=1024):
        return abf(65536 + 2048 * j + 2 * c0, n)

    psA = nc.alloc_psum_tensor("psA", [128, 2048], F32).ap()
    psB = nc.alloc_psum_tensor("psB", [128, 512], F32).ap()
    psC = nc.alloc_psum_tensor("psC", [128, 512], F32).ap()
    psT = nc.alloc_psum_tensor("psT", [128, 1024], BF16).ap()
    psD = nc.alloc_psum_tensor("psD", [128, 512], F32).ap()

    class Sem:
        __slots__ = ("h", "n")

        def __init__(self, name):
            self.h = nc.alloc_semaphore(name)
            self.n = 0

    ldS = Sem("ldS")
    ldG = Sem("ldG")
    ldX = [Sem("ldX0"), Sem("ldX1")]
    ldX3 = Sem("ldX3")
    ldM = Sem("ldM")
    ldXS = Sem("ldXS")
    ldW = [Sem("ldW0"), Sem("ldW1")]
    ldH = [Sem("ldH0"), Sem("ldH1")]
    ldQp = [Sem("ldQ0"), Sem("ldQ1")]
    ldBp = [Sem("ldB0"), Sem("ldB1")]
    ldZp = [Sem("ldZ0"), Sem("ldZ1")]
    stS = Sem("stS")
    oSp = [Sem("oS0"), Sem("oS1")]
    smm = Sem("smm")
    sve = Sem("sve")
    sac = Sem("sac")
    spo = Sem("spo")

    ops = {"tensor": [], "vector": [], "scalar": [], "gpsimd": [], "sync": []}
    stopped = {"v": False}

    def emit(eng, build, waits=(), incs=()):
        if stopped["v"]:
            for s, a in incs:
                s.n += a
            return
        wl = tuple((s.h, (s.n if v is None else v)) for (s, v) in waits)
        il = []
        for s, a in incs:
            s.n += a
            il.append((s.h, a))
        il = tuple(il)

        def f(e, build=build, wl=wl, il=il):
            for h, v in wl:
                e.wait_ge(h, v)
            ins = build(e)
            for h, a in il:
                ins = ins.then_inc(h, a)
        ops[eng].append(f)

    def W(sem, val=None):
        return (sem, val)

    def ewait(eng, sem, val=None):
        if stopped["v"]:
            return
        v = sem.n if val is None else val
        ops[eng].append(lambda e, h=sem.h, v=v: e.wait_ge(h, v))

    def dmaS(dst, src, sem=ldS, waits=(), slow=False):
        if slow:
            emit("sync", lambda e, d=dst, s=src: e.dma_start(
                out=d, in_=s, allow_slow_non_contiguous=True),
                waits=waits, incs=((sem, 16),))
        else:
            emit("sync", lambda e, d=dst, s=src: e.dma_start(out=d, in_=s),
                 waits=waits, incs=((sem, 16),))

    def dmaG(dst, src, accum=None, sem=ldG, waits=()):
        if accum is None:
            emit("gpsimd", lambda e, d=dst, s=src: e.dma_start(out=d, in_=s),
                 waits=waits, incs=((sem, 16),))
        else:
            emit("gpsimd",
                 lambda e, d=dst, s=src, a=accum:
                     e.dma_start(out=d, in_=s, accum_op=a),
                 waits=waits, incs=((sem, 16),))

    def mm(out, lhsT, rhs, start, stop, waits=(), incs=()):
        emit("tensor",
             lambda e, o=out, l=lhsT, r=rhs, s1=start, s2=stop:
                 e.matmul(o, l, r, start=s1, stop=s2),
             waits=waits, incs=incs)

    def tr(out, in_, waits=(), incs=()):
        emit("tensor", lambda e, o=out, i=in_: e.transpose(o, i, ident),
             waits=waits, incs=incs)

    for ci, cv in enumerate(CONSTS):
        emit("vector", lambda e, ci=ci, cv=cv: e.memset(cst[:, ci:ci + 1], cv),
             incs=((sve, 1),))
    if dbg_phase is not None:
        emit("vector", lambda e: e.memset(ar_f32[:, :], 0.0),
             incs=((sve, 1),))
    emit("gpsimd", lambda e: e.memset(ident, 0.0), incs=((spo, 1),))
    emit("gpsimd", lambda e: e.memset(ones_col, 1.0), incs=((spo, 1),))
    ewait("gpsimd", spo)
    emit("gpsimd", lambda e: e.affine_select(
        out=ident, in_=ident, compare_op=ALU.not_equal, fill=1.0,
        base=0, pattern=[[-1, 128]], channel_multiplier=1), incs=((spo, 1),))

    dmaS(qnw_sb[:, 0:QLT], q_nw.rearrange("(j p) o -> p (j o)", p=128),
         slow=True)
    dmaS(kvnw_sb[:, 0:KVLT], kv_nw.rearrange("(j p) o -> p (j o)", p=128),
         slow=True)
    dmaS(ropq_c[:, :], rope_q[0:32, :])
    dmaS(ropq_s[:, :], rope_q[32:64, :])
    dmaS(ropk_c[:, :], rope_k[0:32, :])
    dmaS(ropk_s[:, :], rope_k[32:64, :])

    cvt_jobs = [(gate_s, sbf["gs"], KT * II), (gate_z, sbf["gz"], KT * II),
                (up_s, sbf["us"], KT * II), (up_z, sbf["uz"], KT * II),
                (down_s, sbf["ds"], IT * D), (down_z, sbf["dz"], IT * D)]
    for src, dst, tot in cvt_jobs:
        per = tot // 128
        cv_f = af32(0, per)
        cv_b = abf(4 * per, per)
        flat = src.rearrange("g n -> (g n)").rearrange("(p m) -> p m", p=128)
        dmaS(cv_f, flat, waits=(W(sve),))
        emit("vector", lambda e, a=cv_b, b=cv_f: e.tensor_copy(a, b),
             waits=(W(ldS),), incs=((sve, 1),))
        emit("sync", lambda e, d=dst, s=cv_b:
             e.dma_start(out=d.rearrange("o (p m) -> p (o m)", p=128), in_=s),
             waits=(W(sve),), incs=((stS, 16),))
        ewait("vector", stS)
    cvt_done_st = stS.n

    def layer_norm_apply(src, dst_bf, width, pre_waits):
        nch = width // 512
        for j in range(nch):
            emit("vector",
                 lambda e, s=src, j=j: e.bn_stats(
                     stat[:, 6 * j:6 * j + 6], s[:, 512 * j:512 * (j + 1)]),
                 waits=pre_waits if j == 0 else (), incs=((sve, 1),))
        emit("vector", lambda e, n=nch: e.bn_aggr(mv[:, 0:2], stat[:, 0:6 * n]),
             waits=(W(sve),), incs=((sve, 1),))
        emit("scalar", lambda e: e.activation(
            out=mv[:, 2:3], in_=mv[:, 1:2], func=AF.Sqrt,
            bias=float(LN_EPS), scale=1.0),
            waits=(W(sve),), incs=((sac, 1),))
        emit("vector", lambda e: e.reciprocal(mv[:, 3:4], mv[:, 2:3]),
             waits=(W(sac),), incs=((sve, 1),))
        emit("vector", lambda e, s=src, d=dst_bf: e.tensor_scalar(
            out=d, in0=s, scalar1=mv[:, 0:1], scalar2=mv[:, 3:4],
            op0=ALU.subtract, op1=ALU.mult),
            waits=(W(sve),), incs=((sve, 1),))
        return sve.n

    def chmajor_from_tok(src_bf, tt, dst_fn, src_done_sve, nk=KT):
        for k4 in range((nk + 3) // 4):
            ks = list(range(4 * k4, min(4 * k4 + 4, nk)))
            for idx, k in enumerate(ks):
                wl = ()
                if idx == 0:
                    wl = (W(sve, src_done_sve), W(sve)) if k4 == 0 else (W(sve),)
                tr(psT[:, 128 * idx:128 * (idx + 1)],
                   src_bf[:, 128 * k:128 * (k + 1)],
                   waits=wl,
                   incs=((smm, 1),) if idx == len(ks) - 1 else ())
            for idx, k in enumerate(ks):
                emit("vector",
                     lambda e, k=k, idx=idx, d=dst_fn, tt=tt: e.tensor_copy(
                         d(k)[:, 128 * tt:128 * (tt + 1)],
                         psT[:, 128 * idx:128 * (idx + 1)]),
                     waits=(W(smm),) if idx == 0 else (),
                     incs=((sve, 1),) if idx == len(ks) - 1 else ())


    def _dbg_stop(tag):
        if dbg_phase != tag or stopped["v"]:
            return False
        ewait("sync", sve)
        ewait("sync", smm)
        ewait("sync", sac)
        emit("sync", lambda e: e.dma_start(out=dbg_t, in_=arena_t.ap()),
             waits=(), incs=((stS, 16),))
        ewait("sync", stS)
        stopped["v"] = True
        return True

    ewait("tensor", spo)
    O_XA = 90112
    O_XB2 = 106496
    O_XT = 122880
    apply_done = {}
    for tt in range(NT):
        xa = af32(O_XA if tt % 2 == 0 else O_XB2, 4096)
        wl = ()
        if tt >= 2:
            wl = (W(sve, apply_done[tt - 2]),)
        if tt == 0:
            wl = (W(stS, cvt_done_st),)
        dmaS(xa, x_q[128 * tt:128 * (tt + 1), :], sem=ldX[tt % 2], waits=wl)
        xt = abf(O_XT, 4096)
        a_done = layer_norm_apply(
            xa, xt, 4096,
            (W(ldX[tt % 2]),) + ((W(smm),) if tt > 0 else ()))
        apply_done[tt] = a_done
        chmajor_from_tok(xt, tt, lambda k: ar1(k), a_done)

    if _dbg_stop('p1'):
        pass
    O_WQA = 90112
    p1_v = sve.n
    for k in range(KT):
        dmaG(abf(O_WQA + 3072 * k, 1536), wq_aT[128 * k:128 * (k + 1), :],
             waits=(W(sve, p1_v),) if k == 0 else ())
    wqa_ld = ldG.n
    for j in range(QLT):
        for tc in range(2):
            ps = psB if (2 * j + tc) % 2 == 0 else psC
            for k in range(KT):
                mm(ps[:, 0:512],
                   abf(O_WQA + 3072 * k + 256 * j, 128),
                   ar1(k, 512 * tc, 512),
                   start=(k == 0), stop=(k == KT - 1),
                   waits=((W(ldG, wqa_ld), W(sve)) if k == 0 else ()),
                   incs=((smm, 1),) if k == KT - 1 else ())
            emit("vector",
                 lambda e, j=j, tc=tc, p=ps: e.tensor_copy(
                     qlra(j, 512 * tc, 512), p[:, 0:512]),
                 waits=(W(smm),), incs=((sve, 1),))
    O_SQ = 188416
    rowv = af32(189440, 1024)
    for tc in range(2):
        psq = psD if tc == 0 else psB
        for j in range(QLT):
            sq = abf(O_SQ, 512)
            emit("vector",
                 lambda e, j=j, tc=tc, s=sq: e.tensor_mul(
                     s, qlra(j, 512 * tc, 512), qlra(j, 512 * tc, 512)),
                 waits=(W(sve), W(smm)), incs=((sve, 1),))
            mm(psq[0:1, 0:512], ones_col, sq,
               start=(j == 0), stop=(j == QLT - 1),
               waits=(W(sve),), incs=((smm, 1),))
        emit("scalar", lambda e, tc=tc, p=psq: e.activation(
            out=rowv[0:1, 512 * tc:512 * (tc + 1)], in_=p[0:1, 0:512],
            func=AF.Sqrt, bias=float(RMS_EPS), scale=float(1.0 / QL)),
            waits=(W(smm),), incs=((sac, 1),))
    emit("vector", lambda e: e.reciprocal(rowv[0:1, 0:1024],
                                          rowv[0:1, 0:1024]),
         waits=(W(sac),), incs=((sve, 1),))
    emit("sync", lambda e: e.dma_start(out=rq_sc, in_=rowv[0:1, 0:1024]),
         waits=(W(sve),), incs=((stS, 16),))
    dmaS(srt[:, 0:NT], rq_sc.rearrange("o (a p) -> p (o a)", p=128),
         waits=(W(stS),), slow=True)
    emit("vector", lambda e: e.tensor_scalar(
        out=srt[:, 0:NT], in0=srt[:, 0:NT], scalar1=float(SCALE), scalar2=None,
        op0=ALU.mult), waits=(W(ldS),), incs=((sve, 1),))

    if _dbg_stop('p2'):
        pass
    O_WKA = 90112
    O_XLC = 126976
    O_CKV = 159744
    O_KPE = 176128
    O_KPO = 180224
    O_KPB = 184320
    O_RSB = 188416
    O_TMP1 = 135168
    O_TMP2 = 139264
    rowv = af32(143360, 2048)
    p2_v, p2_m = sve.n, smm.n
    for k in range(KT):
        dmaG(abf(O_WKA + 1152 * k, 576), wkv_aT[128 * k:128 * (k + 1), :],
             waits=(W(sve, p2_v), W(smm, p2_m)) if k == 0 else ())
    wka_ld = ldG.n
    O_X3 = 0
    O_XT3 = 16384
    for cc in range(NCC):
        for st4 in range(4):
            ct = 4 * cc + st4
            xa = af32(O_X3, 4096)
            wl = (W(sve),) if ct > 0 else (W(sve, p2_v),)
            dmaS(xa, x_b[128 * ct:128 * (ct + 1), :], sem=ldX3, waits=wl)
            xt = abf(O_XT3, 4096)
            a_done = layer_norm_apply(xa, xt, 4096, (W(ldX3), W(smm)))
            chmajor_from_tok(xt, st4, lambda k: abf(O_XLC + 1024 * k, 512),
                             a_done)
        ewait("tensor", sve)
        for c in range(6):
            ps = [psA[:, 0:512], psA[:, 512:1024], psA[:, 1024:1536],
                  psA[:, 1536:2048], psB[:, 0:512], psD[:, 0:512]][c]
            mcols = 128 if c < 4 else 32
            coff = 256 * c if c < 4 else 1024 + 64 * (c - 4)
            for k in range(KT):
                wl = ()
                if k == 0:
                    wl = (W(sve),)
                    if cc == 0 and c == 0:
                        wl = (W(ldG, wka_ld), W(sve))
                mm(ps[0:mcols, 0:512],
                   abf(O_WKA + 1152 * k + coff, mcols),
                   abf(O_XLC + 1024 * k, 512),
                   start=(k == 0), stop=(k == KT - 1),
                   waits=wl,
                   incs=((smm, 1),) if k == KT - 1 else ())
            if c < 4:
                emit("vector", lambda e, c=c, cc=cc, p=ps: e.tensor_copy(
                    abf(O_CKV + 4096 * c + 1024 * cc, 512), p[:, 0:512]),
                    waits=(W(smm),), incs=((sve, 1),))
            else:
                emit("vector", lambda e, c=c, cc=cc, p=ps: e.tensor_copy(
                    abf((O_KPE if c == 4 else O_KPO) + 1024 * cc, 512)
                    [0:32, :],
                    p[0:32, 0:512]),
                    waits=(W(smm),), incs=((sve, 1),))
    kpe = abf(O_KPE, 2048)
    kpo = abf(O_KPO, 2048)
    tmp1 = abf(O_TMP1, 2048)
    tmp2 = abf(O_TMP2, 2048)
    kpb = abf(O_KPB, 2048)
    emit("vector", lambda e: e.tensor_mul(tmp1[0:32, :], kpe[0:32, :],
                                          ropk_c[:, :]),
         waits=(W(sve),), incs=((sve, 1),))
    emit("vector", lambda e: e.tensor_mul(tmp2[0:32, :], kpo[0:32, :],
                                          ropk_s[:, :]), incs=((sve, 1),))
    emit("vector", lambda e: e.tensor_sub(kpb[0:32, :], tmp1[0:32, :],
                                          tmp2[0:32, :]),
         waits=(W(sve),), incs=((sve, 1),))
    emit("vector", lambda e: e.tensor_mul(tmp1[0:32, :], kpe[0:32, :],
                                          ropk_s[:, :]),
         waits=(W(sve),), incs=((sve, 1),))
    emit("vector", lambda e: e.tensor_mul(tmp2[0:32, :], kpo[0:32, :],
                                          ropk_c[:, :]), incs=((sve, 1),))
    emit("vector", lambda e: e.tensor_add(kpb[32:64, :], tmp1[0:32, :],
                                          tmp2[0:32, :]),
         waits=(W(sve),), incs=((sve, 1),))
    for c in range(KVLT):
        sq = abf(O_XLC, 2048)
        emit("vector", lambda e, c=c, s=sq: e.tensor_mul(
            s, abf(O_CKV + 4096 * c, 2048), abf(O_CKV + 4096 * c, 2048)),
            waits=(W(sve), W(smm)), incs=((sve, 1),))
        for cc in range(NCC):
            mm(psA[0:1, 512 * cc:512 * (cc + 1)], ones_col,
               sq[:, 512 * cc:512 * (cc + 1)],
               start=(c == 0), stop=(c == KVLT - 1),
               waits=(W(sve),) if cc == 0 else (), incs=((smm, 1),))
    for cc in range(NCC):
        emit("scalar", lambda e, cc=cc: e.activation(
            out=rowv[0:1, 512 * cc:512 * (cc + 1)],
            in_=psA[0:1, 512 * cc:512 * (cc + 1)],
            func=AF.Sqrt, bias=float(RMS_EPS), scale=float(1.0 / KVL)),
            waits=(W(smm),), incs=((sac, 1),))
    emit("vector", lambda e: e.reciprocal(rowv[0:1, 0:2048],
                                          rowv[0:1, 0:2048]),
         waits=(W(sac),), incs=((sve, 1),))
    emit("sync", lambda e: e.dma_start(out=rkv_sc, in_=rowv[0:1, 0:2048]),
         waits=(W(sve),), incs=((stS, 16),))
    rsb = af32(O_RSB, 2048)
    emit("gpsimd", lambda e: e.dma_start(
        out=rsb, in_=rkv_sc.broadcast_to([128, CTX])),
        waits=(W(stS), W(sve)), incs=((ldG, 16),))
    for c in range(KVLT):
        emit("vector", lambda e, c=c: e.tensor_mul(
            abf(O_CKV + 4096 * c, 2048), abf(O_CKV + 4096 * c, 2048), rsb),
            waits=(W(ldG),) if c == 0 else (), incs=((sve, 1),))
    ckv_ready_v = sve.n

    if _dbg_stop('p3'):
        pass
    O_WKVB = 90112
    O_WQB = 94208
    O_KN = 103424
    O_VT = 111616
    O_QN = 119808
    O_QPF = 123904
    O_QPB = 128000
    O_PRB = 132096
    O_PRT = 136192
    O_MSK = 140288

    def arK(h2, n0, n):
        return abf(O_KN + 4096 * h2 + 2 * n0, n)

    def arV(ct, n0=0, n=256):
        return abf(O_VT + 512 * ct + 2 * n0, n)

    def arQN(h2, t0, n):
        return abf(O_QN + 2048 * h2 + 2 * t0, n)

    probs = abf(O_PRB, 2048)
    probsT = abf(O_PRT, 2048)
    maskb = abf(O_MSK, 2048)

    for pr in range(NPAIR):
        pm0 = smm.n
        for c in range(KVLT):
            wl = ()
            if c == 0:
                wl = (W(sve, ckv_ready_v),) if pr == 0 else (W(smm, pm0),)
            dmaG(abf(O_WKVB + 1024 * c, 512),
                 wkv_bT[128 * c:128 * (c + 1), 512 * pr:512 * (pr + 1)],
                 waits=wl)
        for c in range(QLT):
            dmaG(abf(O_WQB + 768 * c, 384),
                 wq_bT[128 * c:128 * (c + 1), 384 * pr:384 * (pr + 1)])
        wld = ldG.n
        for c in range(KVLT):
            emit("vector", lambda e, c=c: e.tensor_scalar(
                out=abf(O_WKVB + 1024 * c, 512), in0=abf(O_WKVB + 1024 * c, 512),
                scalar1=kvnw_sb[:, c:c + 1], scalar2=None, op0=ALU.mult),
                waits=(W(ldG, wld),) if c == 0 else (), incs=((sve, 1),))
        for c in range(QLT):
            emit("vector", lambda e, c=c: e.tensor_scalar(
                out=abf(O_WQB + 768 * c, 384), in0=abf(O_WQB + 768 * c, 384),
                scalar1=qnw_sb[:, c:c + 1], scalar2=None, op0=ALU.mult),
                incs=((sve, 1),))
        wfold = sve.n

        for h2 in range(2):
            for tc in range(2):
                ps = psB if tc == 0 else psC
                for c in range(QLT):
                    mm(ps[:, 0:512],
                       abf(O_WQB + 768 * c + 384 * h2, 128),
                       qlra(c, 512 * tc, 512),
                       start=(c == 0), stop=(c == QLT - 1),
                       waits=(W(sve, wfold), W(sve)) if c == 0 else (),
                       incs=((smm, 1),) if c == QLT - 1 else ())
                emit("vector", lambda e, h2=h2, tc=tc, p=ps: e.tensor_copy(
                    arQN(h2, 512 * tc, 512), p[:, 0:512]),
                    waits=(W(smm),), incs=((sve, 1),))
            for pe_half in range(2):
                ps = psB if pe_half == 0 else psC
                for tc in range(2):
                    for c in range(QLT):
                        mm(ps[0:32, 512 * 0:512] if False else
                           ps[0:32, 0:512],
                           abf(O_WQB + 768 * c + 384 * h2 + 256
                               + 64 * pe_half, 32),
                           qlra(c, 512 * tc, 512),
                           start=(c == 0), stop=(c == QLT - 1),
                           waits=(W(sve),) if c == 0 else (),
                           incs=((smm, 1),) if c == QLT - 1 else ())
                    emit("vector", lambda e, tc=tc, p=ps, ph=pe_half:
                         e.tensor_copy(
                             abf(O_QPF + 2048 * ph + 1024 * tc, 512)
                             [0:32, :],
                             p[0:32, 0:512]),
                         waits=(W(smm),), incs=((sve, 1),))
            qpe = abf(O_QPF, 1024)
            qpo = abf(O_QPF + 2048, 1024)
            qt1 = abf(O_PRB, 1024)
            qt2 = abf(O_PRB + 2048, 1024)
            qpb = abf(O_QPB + 2048 * h2, 1024)
            emit("vector", lambda e, qpe=qpe, qt1=qt1: e.tensor_mul(
                qt1[0:32, :], qpe[0:32, :], ropq_c[:, :]),
                waits=(W(sve), W(smm)), incs=((sve, 1),))
            emit("vector", lambda e, qpo=qpo, qt2=qt2: e.tensor_mul(
                qt2[0:32, :], qpo[0:32, :], ropq_s[:, :]),
                incs=((sve, 1),))
            emit("vector", lambda e, qpb=qpb, qt1=qt1, qt2=qt2: e.tensor_sub(
                qpb[0:32, :], qt1[0:32, :], qt2[0:32, :]),
                waits=(W(sve),), incs=((sve, 1),))
            emit("vector", lambda e, qpe=qpe, qt1=qt1: e.tensor_mul(
                qt1[0:32, :], qpe[0:32, :], ropq_s[:, :]),
                waits=(W(sve),), incs=((sve, 1),))
            emit("vector", lambda e, qpo=qpo, qt2=qt2: e.tensor_mul(
                qt2[0:32, :], qpo[0:32, :], ropq_c[:, :]),
                incs=((sve, 1),))
            emit("vector", lambda e, qpb=qpb, qt1=qt1, qt2=qt2: e.tensor_add(
                qpb[32:64, :], qt1[0:32, :], qt2[0:32, :]),
                waits=(W(sve),), incs=((sve, 1),))

        for h2 in range(2):
            for cc in range(NCC):
                ps = psB if cc % 2 == 0 else psC
                for c in range(KVLT):
                    mm(ps[:, 0:512],
                       abf(O_WKVB + 1024 * c + 256 * h2, 128),
                       abf(O_CKV + 4096 * c + 1024 * cc, 512),
                       start=(c == 0), stop=(c == KVLT - 1),
                       waits=(W(sve),) if c == 0 else (),
                       incs=((smm, 1),) if c == KVLT - 1 else ())
                emit("vector", lambda e, h2=h2, cc=cc, p=ps: e.tensor_copy(
                    arK(h2, 512 * cc, 512), p[:, 0:512]),
                    waits=(W(smm),), incs=((sve, 1),))
        for ct in range(NCT):
            ps = psB if ct % 2 == 0 else psC
            for c in range(KVLT):
                mm(ps[:, 0:256],
                   abf(O_CKV + 4096 * c + 256 * ct, 128),
                   abf(O_WKVB + 1024 * c + 512, 256),
                   start=(c == 0), stop=(c == KVLT - 1),
                   waits=(W(sve),) if c == 0 else (),
                   incs=((smm, 1),) if c == KVLT - 1 else ())
            emit("vector", lambda e, ct=ct, p=ps: e.tensor_copy(
                arV(ct), p[:, 0:256]),
                waits=(W(smm),), incs=((sve, 1),))
        pair_ready_v = sve.n

        for tt in range(NT):
            dmaS(maskb, mask_in[tt, :, :], sem=ldM, waits=(W(sve),))
            mask_ld = ldM.n
            for h2 in range(2):
                hh = 2 * pr + h2
                for cc in range(NCC):
                    wl = ()
                    if cc == 0:
                        wl = (W(sve, pair_ready_v), W(sac), W(sve))
                    mm(psA[:, 512 * cc:512 * (cc + 1)],
                       arQN(h2, 128 * tt, 128),
                       arK(h2, 512 * cc, 512),
                       start=True, stop=False, waits=wl)
                    mm(psA[:, 512 * cc:512 * (cc + 1)],
                       abf(O_QPB + 2048 * h2 + 256 * tt, 128)[0:64, :],
                       kpb[0:64, 512 * cc:512 * (cc + 1)],
                       start=False, stop=True,
                       incs=((smm, 1),) if cc == NCC - 1 else ())
                emit("vector", lambda e: e.tensor_add(
                    psA[:, 0:2048], psA[:, 0:2048], maskb),
                    waits=(W(smm), W(ldM, mask_ld)), incs=((sve, 1),))
                emit("vector", lambda e: e.reduce_max(
                    sm[:, 0:1], psA[:, 0:2048], axis=AX.X),
                    waits=(W(sve),), incs=((sve, 1),))
                emit("vector", lambda e, tt=tt: e.tensor_scalar(
                    out=sm[:, 1:2], in0=sm[:, 0:1],
                    scalar1=srt[:, tt:tt + 1], scalar2=-1.0,
                    op0=ALU.mult, op1=ALU.mult),
                    waits=(W(sve),), incs=((sve, 1),))
                for cc in range(NCC):
                    emit("scalar", lambda e, cc=cc, tt=tt: e.activation(
                        out=probs[:, 512 * cc:512 * (cc + 1)],
                        in_=psA[:, 512 * cc:512 * (cc + 1)],
                        func=AF.Exp, bias=sm[:, 1:2],
                        scale=srt[:, tt:tt + 1],
                        accum_out=sm[:, 4 + cc:5 + cc]),
                        waits=(W(sve),) if cc == 0 else (),
                        incs=((sac, 1),))
                emit("vector", lambda e: e.reduce_sum(
                    sm[:, 2:3], sm[:, 4:8], axis=AX.X),
                    waits=(W(sac),), incs=((sve, 1),))
                emit("vector", lambda e: e.reciprocal(sm[:, 3:4], sm[:, 2:3]),
                     waits=(W(sve),), incs=((sve, 1),))
                emit("vector", lambda e: e.tensor_scalar(
                    out=probs, in0=probs, scalar1=sm[:, 3:4], scalar2=None,
                    op0=ALU.mult),
                    waits=(W(sve),), incs=((sve, 1),))
                pnorm_v = sve.n
                for b4 in range(4):
                    for j in range(4):
                        ct = 4 * b4 + j
                        wl = ()
                        if j == 0:
                            wl = (W(sve, pnorm_v), W(sve)) if b4 == 0 \
                                else (W(sve),)
                        tr(psT[:, 128 * j:128 * (j + 1)],
                           probs[:, 128 * ct:128 * (ct + 1)],
                           waits=wl,
                           incs=((smm, 1),) if j == 3 else ())
                    for j in range(4):
                        ct = 4 * b4 + j
                        emit("vector", lambda e, ct=ct, j=j: e.tensor_copy(
                            probsT[:, 128 * ct:128 * (ct + 1)],
                            psT[:, 128 * j:128 * (j + 1)]),
                            waits=(W(smm),) if j == 0 else (),
                            incs=((sve, 1),) if j == 3 else ())
                for ct in range(NCT):
                    mm(psD[:, 0:128],
                       arV(ct, 128 * h2, 128),
                       probsT[:, 128 * ct:128 * (ct + 1)],
                       start=(ct == 0), stop=(ct == NCT - 1),
                       waits=(W(sve),) if ct == 0 else (),
                       incs=((smm, 1),) if ct == NCT - 1 else ())
                emit("vector", lambda e, hh=hh, tt=tt: e.tensor_copy(
                    ar1(hh, 128 * tt, 128), psD[:, 0:128]),
                    waits=(W(smm),), incs=((sve, 1),))

    if _dbg_stop('p4'):
        pass
    O_OO = 65536
    O_WO = 131072
    O_XF0 = 132096
    O_XF1 = 148480
    O_X1T = 164864
    O_XT5 = 173056

    def oot(ht, t0=0, n=1024):
        return abf(O_OO + 2048 * ht + 2 * t0, n)

    p4_v = sve.n
    wo_mm_done = {}
    for ht in range(KT):
        pse, pso = (psB, psC) if ht % 2 == 0 else \
            (psA[:, 0:512], psA[:, 512:1024])
        for vc in range(KT):
            wob = abf(O_WO + 512 * (vc % 2), 128)
            wl = ()
            if vc >= 2:
                wl = (W(smm, wo_mm_done[(ht, vc - 2)]),)
            elif ht > 0:
                wl = (W(smm, wo_mm_done[(ht - 1, KT - 2 + vc)]),)
            elif vc == 0:
                wl = (W(sve, p4_v),)
            dmaG(wob, woT[128 * vc:128 * (vc + 1), 128 * ht:128 * (ht + 1)],
                 sem=ldW[vc % 2], waits=wl)
            wo_ld = ldW[vc % 2].n
            for tc in range(2):
                ps = pse if tc == 0 else pso
                mm(ps[:, 0:512] if ht % 2 == 0 else ps,
                   wob, ar1(vc, 512 * tc, 512),
                   start=(vc == 0), stop=(vc == KT - 1),
                   waits=(W(ldW[vc % 2], wo_ld),) + ((W(sve),)
                         if vc == 0 else ()) if tc == 0 else (),
                   incs=((smm, 1),) if tc == 1 else ())
            wo_mm_done[(ht, vc)] = smm.n
        for tc in range(2):
            ps = pse if tc == 0 else pso
            emit("vector", lambda e, ht=ht, tc=tc,
                 p=(pse if tc == 0 else pso), ish=(ht % 2 == 0):
                 e.tensor_copy(oot(ht, 512 * tc, 512),
                               p[:, 0:512] if ish else p),
                 waits=(W(smm),) if tc == 0 else (), incs=((sve, 1),))
    spill_prev = stS.n
    for tt in range(NT):
        xf = af32(O_XF0 if tt % 2 == 0 else O_XF1, 4096)
        dmaS(xf, x_q[128 * tt:128 * (tt + 1), :], sem=ldX[tt % 2],
             waits=(W(sve),))
        xf_ld = ldX[tt % 2].n
        x1tile = abf(O_X1T, 4096)
        for k4 in range(8):
            for j in range(4):
                ht = 4 * k4 + j
                wl = ()
                if j == 0:
                    wl = (W(sve), W(stS, spill_prev)) if k4 == 0 else (W(sve),)
                tr(psT[:, 128 * j:128 * (j + 1)],
                   oot(ht, 128 * tt, 128), waits=wl,
                   incs=((smm, 1),) if j == 3 else ())
            emit("vector", lambda e, k4=k4, x1tile=x1tile, xf=xf:
                 e.tensor_add(
                     x1tile[:, 512 * k4:512 * (k4 + 1)], psT[:, 0:512],
                     xf[:, 512 * k4:512 * (k4 + 1)]),
                 waits=(W(smm), W(ldX[tt % 2], xf_ld)), incs=((sve, 1),))
        xt5 = abf(O_XT5, 4096)
        a_done = layer_norm_apply(x1tile, xt5, 4096, (W(sve),))
        chmajor_from_tok(xt5, tt, lambda k: ar1(k), a_done)
        emit("sync", lambda e, tt=tt, x1tile=x1tile: e.dma_start(
            out=x1_sc[128 * tt:128 * (tt + 1), :], in_=x1tile),
            waits=(W(sve, a_done),), incs=((stS, 16),))
        spill_prev = stS.n
    x1_spilled = stS.n

    if _dbg_stop('p5'):
        pass
    slot_state = {"off": 0}

    def dequant_unit(tiles, par, w_memset=()):
        lq, lb, lz = ldQp[par], ldBp[par], ldZp[par]
        ms_done = []
        for i, (dst, q8s, sf, zf, off, n) in enumerate(tiles):
            emit("vector", lambda e, d=dst: e.memset(d, -8.0),
                 waits=w_memset if i == 0 else (), incs=((sve, 1),))
            ms_done.append(sve.n)
        for i, (dst, q8s, sf, zf, off, n) in enumerate(tiles):
            emit("gpsimd", lambda e, d=dst, s=q8s: e.dma_start(
                out=d, in_=s, accum_op=ALU.add),
                waits=(W(sve, ms_done[i]),), incs=((lq, 16),))
        q_tot = lq.n
        mu_done = [0] * len(tiles)
        for w0 in range(0, len(tiles), 8):
            wave = list(range(w0, min(w0 + 8, len(tiles))))
            prev_mu = mu_done[w0 - 1] if w0 > 0 else ms_done[-1]
            sslots = {}
            for wi, i in enumerate(wave):
                dst, q8s, sf, zf, off, n = tiles[i]
                sb = abf(slot_state["off"] + 1024 * (i % 8), n)
                sslots[i] = sb
                emit("gpsimd",
                     lambda e, sb=sb, sf=sf, o=off, n=n: e.dma_start(
                         out=sb, in_=sf[0:1, o:o + n].broadcast_to([128, n])),
                     waits=(W(sve, prev_mu),) if wi == 0 else (),
                     incs=((lb, 16),))
            s_tot = lb.n
            for wi, i in enumerate(wave):
                dst = tiles[i][0]
                emit("vector",
                     lambda e, d=dst, sb=sslots[i]: e.tensor_mul(d, d, sb),
                     waits=(W(lq, q_tot), W(lb, s_tot)) if wi == 0 else (),
                     incs=((sve, 1),))
                mu_done[i] = sve.n
        for i, (dst, q8s, sf, zf, off, n) in enumerate(tiles):
            emit("gpsimd",
                 lambda e, d=dst, zf=zf, o=off, n=n: e.dma_start(
                     out=d, in_=zf[0:1, o:o + n].broadcast_to([128, n]),
                     accum_op=ALU.add),
                 waits=(W(sve, mu_done[i]),), incs=((lz, 16),))
        return lz.n

    O_WFF = 65536

    def wff(half, k, n=512):
        return abf(O_WFF + 32768 * half + 1024 * k, n)
    O_SG = 131072
    O_HB = 139264
    O_HT = 140288
    O_SS6 = 148480
    slot_state["off"] = O_SS6

    p5_v, p5_m = sve.n, smm.n
    units = []
    for j in range(NNC):
        units.append(("g", j))
        units.append(("u", j))
    unit_mm_done = {}
    spill_cnt = {"n": stS.n}
    for ui, (mat, j) in enumerate(units):
        half = ui % 2
        n0, n = 512 * j, _ncw(j)
        q8 = gate_q8 if mat == "g" else up_q8
        s_fl = sbf["gs"] if mat == "g" else sbf["us"]
        z_fl = sbf["gz"] if mat == "g" else sbf["uz"]
        if ui >= 2:
            wm = (W(smm, unit_mm_done[ui - 2]),)
        else:
            wm = (W(sve, p5_v), W(smm, p5_m), W(stS, x1_spilled))
        tiles = [(wff(half, k, n), q8[128 * k:128 * (k + 1), n0:n0 + n],
                  s_fl, z_fl, II * k + n0, n) for k in range(KT)]
        zl = dequant_unit(tiles, half, w_memset=wm)
        ps_pool = [psA[:, 0:512], psA[:, 512:1024], psA[:, 1024:1536],
                   psA[:, 1536:2048]]
        for tt in range(NT):
            ps = ps_pool[tt % 4]
            for k in range(KT):
                wl = ()
                if k == 0:
                    wl = (W(sac), W(sve))
                    if tt == 0:
                        wl = (W(ldZp[half], zl), W(sac), W(sve))
                mm(ps[:, 0:n], ar1(k, 128 * tt, 128), wff(half, k, n),
                   start=(k == 0), stop=(k == KT - 1),
                   waits=wl, incs=((smm, 1),) if k == KT - 1 else ())
            if mat == "g":
                emit("scalar", lambda e, tt=tt, p=ps, n=n: e.activation(
                    out=abf(O_SG + 1024 * tt, n), in_=p[:, 0:n],
                    func=AF.Silu, bias=0.0, scale=1.0),
                    waits=(W(smm), W(sve)), incs=((sac, 1),))
            else:
                hb = abf(O_HB, 512)
                emit("vector", lambda e, tt=tt, p=ps, n=n, hb=hb: e.tensor_mul(
                    hb[:, 0:n], abf(O_SG + 1024 * tt, n), p[:, 0:n]),
                    waits=(W(smm), W(sac)), incs=((sve, 1),))
                hmul_v = sve.n
                nk4 = (n + 127) // 128
                for jj in range(nk4):
                    tr(psT[:, 128 * jj:128 * (jj + 1)],
                       hb[:, 128 * jj:128 * (jj + 1)],
                       waits=(W(sve, hmul_v), W(sve)) if jj == 0 else (),
                       incs=((smm, 1),) if jj == nk4 - 1 else ())
                for jj in range(nk4):
                    emit("vector", lambda e, jj=jj, tt=tt: e.tensor_copy(
                        abf(O_HT + 2048 * jj + 256 * tt, 128),
                        psT[:, 128 * jj:128 * (jj + 1)]),
                        waits=(W(smm), W(stS, spill_cnt["n"]))
                        if jj == 0 else (),
                        incs=((sve, 1),) if jj == nk4 - 1 else ())
        unit_mm_done[ui] = smm.n
        if mat == "u":
            nk4 = (n + 127) // 128
            for jj in range(nk4):
                it_g = 4 * j + jj
                emit("sync", lambda e, jj=jj, it_g=it_g: e.dma_start(
                    out=hT_sc[128 * it_g:128 * (it_g + 1), :],
                    in_=abf(O_HT + 2048 * jj, 1024)),
                    waits=(W(sve),), incs=((stS, 16),))
            spill_cnt["n"] = stS.n
    h_spilled = stS.n
    p6_m = smm.n

    O_HT7 = 0

    def ht7(buf, i, n=128):
        return abf(O_HT7 + 22528 * buf + 256 * i, n)
    O_X1S = 45056
    O_OUTB = 49152
    O_WD = 65536

    def wdd(i, n=512):
        return abf(O_WD + 1024 * i, n)
    O_SS7 = 155648
    slot_state["off"] = O_SS7

    p6_v = sve.n
    dc_mm_done = {}
    ht_segs = [(0, 16), (16, 16), (32, 16), (48, 16), (64, 16), (80, 6)]
    ob_store = {0: 0, 1: 0}
    for dc in range(NDC):
        if dc > 0:
            wm = (W(smm, dc_mm_done[dc - 1]),)
        else:
            wm = (W(sve, p6_v), W(smm, p6_m), W(stS, h_spilled))
        tiles = [(wdd(i),
                  down_q8[128 * i:128 * (i + 1), 512 * dc:512 * (dc + 1)],
                  sbf["ds"], sbf["dz"], D * i + 512 * dc, 512)
                 for i in range(IT)]
        deq_done = dequant_unit(tiles, dc % 2, w_memset=wm)
        tt_mm = {}
        for tt in range(NT):
            buf = tt % 2
            if tt < 2:
                hw = (W(smm, dc_mm_done[dc - 1]),) if dc > 0 else \
                    (W(smm, p6_m), W(stS, h_spilled))
            else:
                hw = (W(smm, tt_mm[tt - 2]),)
            for si, (i0, ni) in enumerate(ht_segs):
                hsrc = hT_sc[128 * i0:128 * (i0 + ni),
                             128 * tt:128 * (tt + 1)]
                hsrc = hsrc.rearrange("(i p) c -> p i c", p=128)
                dst = abf(O_HT7 + 22528 * buf + 256 * i0, ni * 128)
                dst = dst.rearrange("p (i c) -> p i c", c=128)
                dmaS(dst, hsrc, sem=ldH[buf], waits=hw if si == 0 else ())
            dmaS(abf(O_X1S + 1024 * buf, 512),
                 x1_sc[128 * tt:128 * (tt + 1), 512 * dc:512 * (dc + 1)],
                 sem=ldH[buf],
                 waits=(W(stS, x1_spilled),) if (dc == 0 and tt < 2)
                 else (W(sve),))
            h_ld = ldH[buf].n
            ps = psB if tt % 2 == 0 else psC
            for i in range(IT):
                mm(ps[:, 0:512], ht7(buf, i), wdd(i),
                   start=(i == 0), stop=(i == IT - 1),
                   waits=(W(ldH[buf], h_ld), W(ldZp[dc % 2], deq_done),
                          W(sve))
                   if i == 0 else (),
                   incs=((smm, 1),) if i == IT - 1 else ())
            tt_mm[tt] = smm.n
            ob = af32(O_OUTB + 2048 * buf, 512)
            emit("vector", lambda e, p=ps, ob=ob, buf=buf: e.tensor_add(
                ob, p[:, 0:512], abf(O_X1S + 1024 * buf, 512)),
                waits=(W(smm), W(oSp[buf], ob_store[buf])), incs=((sve, 1),))
            emit("sync", lambda e, ob=ob, tt=tt, dc=dc: e.dma_start(
                out=out_t[128 * tt:128 * (tt + 1), 512 * dc:512 * (dc + 1)],
                in_=ob),
                waits=(W(sve),), incs=((oSp[buf], 16),))
            ob_store[buf] = oSp[buf].n
        dc_mm_done[dc] = smm.n

    ewait("sync", oSp[0])
    ewait("sync", oSp[1])

    with nc.Block() as block:
        @block.tensor
        def _(eng):
            for f in ops["tensor"]:
                f(eng)

        @block.vector
        def _(eng):
            for f in ops["vector"]:
                f(eng)

        @block.scalar
        def _(eng):
            for f in ops["scalar"]:
                f(eng)

        @block.gpsimd
        def _(eng):
            for f in ops["gpsimd"]:
                f(eng)

        @block.sync
        def _(eng):
            for f in ops["sync"]:
                f(eng)

    return nc



_CACHE = {}


def _rope_perm_cols():
    return np.concatenate([np.arange(0, 64, 2), np.arange(1, 64, 2)])


def _prep_weights(inp):
    w = {}
    w["wq_aT"] = np.ascontiguousarray(inp["L0_attn_wq_a"].T)
    w["q_nw"] = np.ascontiguousarray(inp["L0_attn_q_norm_w"].reshape(QL, 1))
    wqbT = np.ascontiguousarray(inp["L0_attn_wq_b"].T)
    perm = np.zeros(NH * 192, dtype=np.int64)
    rp = _rope_perm_cols()
    for h in range(NH):
        b = 192 * h
        perm[b:b + 128] = b + np.arange(128)
        perm[b + 128:b + 192] = b + 128 + rp
    w["wq_bT"] = np.ascontiguousarray(wqbT[:, perm])
    wkvaT = np.ascontiguousarray(inp["L0_attn_wkv_a"].T)
    perm2 = np.concatenate([np.arange(KVL), KVL + rp])
    w["wkv_aT"] = np.ascontiguousarray(wkvaT[:, perm2])
    w["kv_nw"] = np.ascontiguousarray(inp["L0_attn_kv_norm_w"].reshape(KVL, 1))
    wkvbT = np.ascontiguousarray(inp["L0_attn_wkv_b"].T)
    perm3 = np.zeros(NH * 256, dtype=np.int64)
    for p in range(NPAIR):
        b = 512 * p
        h0, h1 = 2 * p, 2 * p + 1
        perm3[b:b + 128] = 256 * h0 + np.arange(128)
        perm3[b + 128:b + 256] = 256 * h1 + np.arange(128)
        perm3[b + 256:b + 384] = 256 * h0 + 128 + np.arange(128)
        perm3[b + 384:b + 512] = 256 * h1 + 128 + np.arange(128)
    w["wkv_bT"] = np.ascontiguousarray(wkvbT[:, perm3])
    w["woT"] = np.ascontiguousarray(inp["L0_attn_wo"].T)
    for nm, key in [("gate_q8", "L0_ffn_gate_q"), ("up_q8", "L0_ffn_up_q"),
                    ("down_q8", "L0_ffn_down_q")]:
        q = np.ascontiguousarray(inp[key])
        w[nm] = np.ascontiguousarray(q.view(np.int8)[:, ::4])
    for nm, key in [("gate_s", "L0_ffn_gate_s"), ("gate_z", "L0_ffn_gate_z"),
                    ("up_s", "L0_ffn_up_s"), ("up_z", "L0_ffn_up_z"),
                    ("down_s", "L0_ffn_down_s"), ("down_z", "L0_ffn_down_z")]:
        w[nm] = np.ascontiguousarray(inp[key])
    return w


def _rope_tables(p0, n):
    freqs = (1.0 / 10000.0 ** (np.arange(0, 64, 2, dtype=np.float64) / 64.0))
    t = np.arange(p0, p0 + n, dtype=np.float64)
    ang = freqs[:, None] * t[None, :]
    return np.concatenate([np.cos(ang), np.sin(ang)], axis=0).astype(BF)


def _mask(p0):
    s = np.arange(CTX)
    m = np.zeros((NT, 128, CTX), dtype=np.float32)
    for tt in range(NT):
        qpos = p0 + 128 * tt + np.arange(128)
        m[tt] = np.where(s[None, :] <= qpos[:, None], 0.0, -1e9)
    return m.astype(BF)


def kernel(L0_attn_x, L0_attn_wq_a, L0_attn_q_norm_w, L0_attn_wq_b,
           L0_attn_wkv_a, L0_attn_kv_norm_w, L0_attn_wkv_b, L0_attn_wo,
           L0_ffn_gate_q, L0_ffn_gate_s, L0_ffn_gate_z, L0_ffn_up_q,
           L0_ffn_up_s, L0_ffn_up_z, L0_ffn_down_q, L0_ffn_down_s,
           L0_ffn_down_z):
    inp = dict(L0_attn_wq_a=np.asarray(L0_attn_wq_a),
               L0_attn_q_norm_w=np.asarray(L0_attn_q_norm_w),
               L0_attn_wq_b=np.asarray(L0_attn_wq_b),
               L0_attn_wkv_a=np.asarray(L0_attn_wkv_a),
               L0_attn_kv_norm_w=np.asarray(L0_attn_kv_norm_w),
               L0_attn_wkv_b=np.asarray(L0_attn_wkv_b),
               L0_attn_wo=np.asarray(L0_attn_wo),
               L0_ffn_gate_q=np.asarray(L0_ffn_gate_q),
               L0_ffn_gate_s=np.asarray(L0_ffn_gate_s),
               L0_ffn_gate_z=np.asarray(L0_ffn_gate_z),
               L0_ffn_up_q=np.asarray(L0_ffn_up_q),
               L0_ffn_up_s=np.asarray(L0_ffn_up_s),
               L0_ffn_up_z=np.asarray(L0_ffn_up_z),
               L0_ffn_down_q=np.asarray(L0_ffn_down_q),
               L0_ffn_down_s=np.asarray(L0_ffn_down_s),
               L0_ffn_down_z=np.asarray(L0_ffn_down_z))
    x = np.asarray(L0_attn_x)
    w = _prep_weights(inp)

    if "nc" not in _CACHE:
        _CACHE["nc"] = build_nc()
    nc = _CACHE["nc"]

    in_maps = []
    for c in range(8):
        b, half = c // 2, c % 2
        p0 = 1024 * half
        m = dict(w)
        m["x_q"] = np.ascontiguousarray(x[b, p0:p0 + T, :])
        m["x_b"] = np.ascontiguousarray(x[b])
        m["rope_q"] = _rope_tables(p0, T)
        m["rope_k"] = _rope_tables(0, CTX)
        m["mask_in"] = _mask(p0)
        in_maps.append(m)

    res = run_bass_kernel_spmd(nc, in_maps, core_ids=list(range(8)))
    out = np.empty((4, CTX, D), dtype=np.float32)
    for c in range(8):
        b, half = c // 2, c % 2
        out[b, 1024 * half:1024 * half + T, :] = res.results[c]["out"]
    return out


# version 03
# speedup vs baseline: 1.0181x; 1.0181x over previous
import numpy as np
import ml_dtypes
import concourse.bass as bass
import concourse.mybir as mybir
from concourse.bass_utils import run_bass_kernel_spmd

F32 = mybir.dt.float32
BF16 = mybir.dt.bfloat16
I8 = mybir.dt.int8
AF = mybir.ActivationFunctionType
ALU = mybir.AluOpType
AX = mybir.AxisListType
BF = ml_dtypes.bfloat16

D = 4096
T = 1024
NT = 8
CTX = 2048
NCC = 4
NCT = 16
KT = 32
QL = 1536
QLT = 12
KVL = 512
KVLT = 4
NH = 32
NPAIR = 16
II = 11008
IT = 86
NNC = 22
NDC = 8
SCALE = float(1.0 / np.sqrt(192.0))
LN_EPS = 1e-5
RMS_EPS = 1e-6


def _ncw(j):
    return min(512, II - 512 * j)


def build_nc(dbg_phase=None):
    nc = bass.Bass("TRN2", target_bir_lowering=False)

    def din(name, shape, dt):
        return nc.dram_tensor(name, list(shape), dt, kind="ExternalInput").ap()

    x_q = din("x_q", (T, D), F32)
    x_b = din("x_b", (CTX, D), F32)
    wq_aT = din("wq_aT", (D, QL), F32)
    q_nw = din("q_nw", (QL, 1), F32)
    wq_bT = din("wq_bT", (QL, NH * 192), F32)
    wkv_aT = din("wkv_aT", (D, KVL + 64), F32)
    kv_nw = din("kv_nw", (KVL, 1), F32)
    wkv_bT = din("wkv_bT", (KVL, NH * 256), F32)
    woT = din("woT", (D, D), F32)
    gate_q8 = din("gate_q8", (D, II), I8)
    up_q8 = din("up_q8", (D, II), I8)
    down_q8 = din("down_q8", (II, D), I8)
    gate_s = din("gate_s", (KT, II), F32)
    gate_z = din("gate_z", (KT, II), F32)
    up_s = din("up_s", (KT, II), F32)
    up_z = din("up_z", (KT, II), F32)
    down_s = din("down_s", (IT, D), F32)
    down_z = din("down_z", (IT, D), F32)
    rope_q = din("rope_q", (64, T), BF16)
    rope_k = din("rope_k", (64, CTX), BF16)
    mask_in = din("mask_in", (NT, 128, CTX), BF16)

    out_t = nc.dram_tensor("out", [T, D], F32, kind="ExternalOutput").ap()
    dbg_t = nc.dram_tensor("dbg_arena", [128, 192 * 1024], mybir.dt.uint8,
                           kind="ExternalOutput").ap()

    sbf = {}
    for nm, tot in [("gs", KT * II), ("gz", KT * II), ("us", KT * II),
                    ("uz", KT * II), ("ds", IT * D), ("dz", IT * D)]:
        sbf[nm] = nc.dram_tensor("sc_" + nm, [1, tot], BF16, kind="Internal").ap()
    hT_sc = nc.dram_tensor("hT_sc", [II, T], BF16, kind="Internal").ap()
    x1_sc = nc.dram_tensor("x1_sc", [T, D], BF16, kind="Internal").ap()
    rq_sc = nc.dram_tensor("rq_sc", [1, T], F32, kind="Internal").ap()
    rkv_sc = nc.dram_tensor("rkv_sc", [1, CTX], F32, kind="Internal").ap()

    ident = nc.alloc_sbuf_tensor("ident", [128, 128], BF16).ap()
    ones_col = nc.alloc_sbuf_tensor("ones_col", [128, 1], BF16).ap()
    qnw_sb = nc.alloc_sbuf_tensor("qnw_sb", [128, QLT], F32).ap()
    kvnw_sb = nc.alloc_sbuf_tensor("kvnw_sb", [128, KVLT], F32).ap()
    ropq_c = nc.alloc_sbuf_tensor("ropq_c", [32, T], BF16).ap()
    ropq_s = nc.alloc_sbuf_tensor("ropq_s", [32, T], BF16).ap()
    ropk_c = nc.alloc_sbuf_tensor("ropk_c", [32, CTX], BF16).ap()
    ropk_s = nc.alloc_sbuf_tensor("ropk_s", [32, CTX], BF16).ap()
    stat = nc.alloc_sbuf_tensor("stat", [128, 64], F32).ap()
    mv = nc.alloc_sbuf_tensor("mv", [128, 8], F32).ap()
    srt = nc.alloc_sbuf_tensor("srt", [128, NT], F32).ap()
    sm = nc.alloc_sbuf_tensor("sm", [128, 16], F32).ap()
    cst = nc.alloc_sbuf_tensor("cst", [128, 8], F32).ap()
    CONSTS = [float(LN_EPS), float(RMS_EPS), 1.0, 0.0,
              float(1.0 / QL), float(1.0 / KVL)]
    for ci, cv in enumerate(CONSTS):
        nc.const_aps.register(F32, cv, cst[:, ci:ci + 1])

    AR_BYTES = 192 * 1024
    arena_t = nc.alloc_sbuf_tensor("arena", [128, AR_BYTES], mybir.dt.uint8)
    ar_bf = arena_t.ap().bitcast(BF16)
    ar_f32 = arena_t.ap().bitcast(F32)

    def abf(off, n):
        assert off % 2 == 0 and off + 2 * n <= AR_BYTES, (off, n)
        return ar_bf[:, off // 2: off // 2 + n]

    def af32(off, n):
        assert off % 4 == 0 and off + 4 * n <= AR_BYTES, (off, n)
        return ar_f32[:, off // 4: off // 4 + n]

    def ar1(tile, t0=0, n=1024):
        return abf(2048 * tile + 2 * t0, n)

    def qlra(j, c0=0, n=1024):
        return abf(65536 + 2048 * j + 2 * c0, n)

    psA = nc.alloc_psum_tensor("psA", [128, 2048], F32).ap()
    psB = nc.alloc_psum_tensor("psB", [128, 512], F32).ap()
    psC = nc.alloc_psum_tensor("psC", [128, 512], F32).ap()
    psT = nc.alloc_psum_tensor("psT", [128, 1024], BF16).ap()
    psD = nc.alloc_psum_tensor("psD", [128, 512], F32).ap()

    class Sem:
        __slots__ = ("h", "n")

        def __init__(self, name):
            self.h = nc.alloc_semaphore(name)
            self.n = 0

    ldS = Sem("ldS")
    ldG = Sem("ldG")
    ldX = [Sem("ldX0"), Sem("ldX1")]
    ldX3 = Sem("ldX3")
    ldM = Sem("ldM")
    ldXS = Sem("ldXS")
    ldW = [Sem("ldW0"), Sem("ldW1")]
    ldH = [Sem("ldH0"), Sem("ldH1")]
    ldQp = [Sem("ldQ0"), Sem("ldQ1")]
    ldBp = [Sem("ldB0"), Sem("ldB1"), Sem("ldB2"), Sem("ldB3")]
    ldZp = [Sem("ldZ0"), Sem("ldZ1")]
    stS = Sem("stS")
    oSp = [Sem("oS0"), Sem("oS1")]
    smm = Sem("smm")
    sve = Sem("sve")
    sac = Sem("sac")
    spo = Sem("spo")

    ops = {"tensor": [], "vector": [], "scalar": [], "gpsimd": [], "sync": []}
    stopped = {"v": False}

    def emit(eng, build, waits=(), incs=()):
        if stopped["v"]:
            for s, a in incs:
                s.n += a
            return
        wl = tuple((s.h, (s.n if v is None else v)) for (s, v) in waits)
        il = []
        for s, a in incs:
            s.n += a
            il.append((s.h, a))
        il = tuple(il)

        def f(e, build=build, wl=wl, il=il):
            for h, v in wl:
                e.wait_ge(h, v)
            ins = build(e)
            for h, a in il:
                ins = ins.then_inc(h, a)
        ops[eng].append(f)

    def W(sem, val=None):
        return (sem, val)

    def ewait(eng, sem, val=None):
        if stopped["v"]:
            return
        v = sem.n if val is None else val
        ops[eng].append(lambda e, h=sem.h, v=v: e.wait_ge(h, v))

    def dmaS(dst, src, sem=ldS, waits=(), slow=False):
        if slow:
            emit("sync", lambda e, d=dst, s=src: e.dma_start(
                out=d, in_=s, allow_slow_non_contiguous=True),
                waits=waits, incs=((sem, 16),))
        else:
            emit("sync", lambda e, d=dst, s=src: e.dma_start(out=d, in_=s),
                 waits=waits, incs=((sem, 16),))

    def dmaG(dst, src, accum=None, sem=ldG, waits=()):
        if accum is None:
            emit("gpsimd", lambda e, d=dst, s=src: e.dma_start(out=d, in_=s),
                 waits=waits, incs=((sem, 16),))
        else:
            emit("gpsimd",
                 lambda e, d=dst, s=src, a=accum:
                     e.dma_start(out=d, in_=s, accum_op=a),
                 waits=waits, incs=((sem, 16),))

    def mm(out, lhsT, rhs, start, stop, waits=(), incs=()):
        emit("tensor",
             lambda e, o=out, l=lhsT, r=rhs, s1=start, s2=stop:
                 e.matmul(o, l, r, start=s1, stop=s2),
             waits=waits, incs=incs)

    def tr(out, in_, waits=(), incs=()):
        emit("tensor", lambda e, o=out, i=in_: e.transpose(o, i, ident),
             waits=waits, incs=incs)

    for ci, cv in enumerate(CONSTS):
        emit("vector", lambda e, ci=ci, cv=cv: e.memset(cst[:, ci:ci + 1], cv),
             incs=((sve, 1),))
    if dbg_phase is not None:
        emit("vector", lambda e: e.memset(ar_f32[:, :], 0.0),
             incs=((sve, 1),))
    emit("gpsimd", lambda e: e.memset(ident, 0.0), incs=((spo, 1),))
    emit("gpsimd", lambda e: e.memset(ones_col, 1.0), incs=((spo, 1),))
    ewait("gpsimd", spo)
    emit("gpsimd", lambda e: e.affine_select(
        out=ident, in_=ident, compare_op=ALU.not_equal, fill=1.0,
        base=0, pattern=[[-1, 128]], channel_multiplier=1), incs=((spo, 1),))

    dmaS(qnw_sb[:, 0:QLT], q_nw.rearrange("(j p) o -> p (j o)", p=128),
         slow=True)
    dmaS(kvnw_sb[:, 0:KVLT], kv_nw.rearrange("(j p) o -> p (j o)", p=128),
         slow=True)
    dmaS(ropq_c[:, :], rope_q[0:32, :])
    dmaS(ropq_s[:, :], rope_q[32:64, :])
    dmaS(ropk_c[:, :], rope_k[0:32, :])
    dmaS(ropk_s[:, :], rope_k[32:64, :])

    cvt_jobs = [(gate_s, sbf["gs"], KT * II), (gate_z, sbf["gz"], KT * II),
                (up_s, sbf["us"], KT * II), (up_z, sbf["uz"], KT * II),
                (down_s, sbf["ds"], IT * D), (down_z, sbf["dz"], IT * D)]
    for src, dst, tot in cvt_jobs:
        per = tot // 128
        cv_f = af32(0, per)
        cv_b = abf(4 * per, per)
        flat = src.rearrange("g n -> (g n)").rearrange("(p m) -> p m", p=128)
        dmaS(cv_f, flat, waits=(W(sve),))
        emit("vector", lambda e, a=cv_b, b=cv_f: e.tensor_copy(a, b),
             waits=(W(ldS),), incs=((sve, 1),))
        emit("sync", lambda e, d=dst, s=cv_b:
             e.dma_start(out=d.rearrange("o (p m) -> p (o m)", p=128), in_=s),
             waits=(W(sve),), incs=((stS, 16),))
        ewait("vector", stS)
    cvt_done_st = stS.n

    def layer_norm_apply(src, dst_bf, width, pre_waits):
        nch = width // 512
        for j in range(nch):
            emit("vector",
                 lambda e, s=src, j=j: e.bn_stats(
                     stat[:, 6 * j:6 * j + 6], s[:, 512 * j:512 * (j + 1)]),
                 waits=pre_waits if j == 0 else (), incs=((sve, 1),))
        emit("vector", lambda e, n=nch: e.bn_aggr(mv[:, 0:2], stat[:, 0:6 * n]),
             waits=(W(sve),), incs=((sve, 1),))
        emit("scalar", lambda e: e.activation(
            out=mv[:, 2:3], in_=mv[:, 1:2], func=AF.Sqrt,
            bias=float(LN_EPS), scale=1.0),
            waits=(W(sve),), incs=((sac, 1),))
        emit("vector", lambda e: e.reciprocal(mv[:, 3:4], mv[:, 2:3]),
             waits=(W(sac),), incs=((sve, 1),))
        emit("vector", lambda e, s=src, d=dst_bf: e.tensor_scalar(
            out=d, in0=s, scalar1=mv[:, 0:1], scalar2=mv[:, 3:4],
            op0=ALU.subtract, op1=ALU.mult),
            waits=(W(sve),), incs=((sve, 1),))
        return sve.n

    def chmajor_from_tok(src_bf, tt, dst_fn, src_done_sve, nk=KT):
        for k4 in range((nk + 3) // 4):
            ks = list(range(4 * k4, min(4 * k4 + 4, nk)))
            for idx, k in enumerate(ks):
                wl = ()
                if idx == 0:
                    wl = (W(sve, src_done_sve), W(sve)) if k4 == 0 else (W(sve),)
                tr(psT[:, 128 * idx:128 * (idx + 1)],
                   src_bf[:, 128 * k:128 * (k + 1)],
                   waits=wl,
                   incs=((smm, 1),) if idx == len(ks) - 1 else ())
            for idx, k in enumerate(ks):
                emit("vector",
                     lambda e, k=k, idx=idx, d=dst_fn, tt=tt: e.tensor_copy(
                         d(k)[:, 128 * tt:128 * (tt + 1)],
                         psT[:, 128 * idx:128 * (idx + 1)]),
                     waits=(W(smm),) if idx == 0 else (),
                     incs=((sve, 1),) if idx == len(ks) - 1 else ())


    def _dbg_stop(tag):
        if dbg_phase != tag or stopped["v"]:
            return False
        ewait("sync", sve)
        ewait("sync", smm)
        ewait("sync", sac)
        emit("sync", lambda e: e.dma_start(out=dbg_t, in_=arena_t.ap()),
             waits=(), incs=((stS, 16),))
        ewait("sync", stS)
        stopped["v"] = True
        return True

    ewait("tensor", spo)
    O_XA = 90112
    O_XB2 = 106496
    O_XT = 122880
    apply_done = {}
    for tt in range(NT):
        xa = af32(O_XA if tt % 2 == 0 else O_XB2, 4096)
        wl = ()
        if tt >= 2:
            wl = (W(sve, apply_done[tt - 2]),)
        if tt == 0:
            wl = (W(stS, cvt_done_st),)
        dmaS(xa, x_q[128 * tt:128 * (tt + 1), :], sem=ldX[tt % 2], waits=wl)
        xt = abf(O_XT, 4096)
        a_done = layer_norm_apply(
            xa, xt, 4096,
            (W(ldX[tt % 2]),) + ((W(smm),) if tt > 0 else ()))
        apply_done[tt] = a_done
        chmajor_from_tok(xt, tt, lambda k: ar1(k), a_done)

    if _dbg_stop('p1'):
        pass
    O_WQA = 90112
    p1_v = sve.n
    for k in range(KT):
        dmaG(abf(O_WQA + 3072 * k, 1536), wq_aT[128 * k:128 * (k + 1), :],
             waits=(W(sve, p1_v),) if k == 0 else ())
    wqa_ld = ldG.n
    for j in range(QLT):
        for tc in range(2):
            ps = psB if (2 * j + tc) % 2 == 0 else psC
            for k in range(KT):
                mm(ps[:, 0:512],
                   abf(O_WQA + 3072 * k + 256 * j, 128),
                   ar1(k, 512 * tc, 512),
                   start=(k == 0), stop=(k == KT - 1),
                   waits=((W(ldG, wqa_ld), W(sve)) if k == 0 else ()),
                   incs=((smm, 1),) if k == KT - 1 else ())
            emit("vector",
                 lambda e, j=j, tc=tc, p=ps: e.tensor_copy(
                     qlra(j, 512 * tc, 512), p[:, 0:512]),
                 waits=(W(smm),), incs=((sve, 1),))
    O_SQ = 188416
    rowv = af32(189440, 1024)
    for tc in range(2):
        psq = psD if tc == 0 else psB
        for j in range(QLT):
            sq = abf(O_SQ, 512)
            emit("vector",
                 lambda e, j=j, tc=tc, s=sq: e.tensor_mul(
                     s, qlra(j, 512 * tc, 512), qlra(j, 512 * tc, 512)),
                 waits=(W(sve), W(smm)), incs=((sve, 1),))
            mm(psq[0:1, 0:512], ones_col, sq,
               start=(j == 0), stop=(j == QLT - 1),
               waits=(W(sve),), incs=((smm, 1),))
        emit("scalar", lambda e, tc=tc, p=psq: e.activation(
            out=rowv[0:1, 512 * tc:512 * (tc + 1)], in_=p[0:1, 0:512],
            func=AF.Sqrt, bias=float(RMS_EPS), scale=float(1.0 / QL)),
            waits=(W(smm),), incs=((sac, 1),))
    emit("vector", lambda e: e.reciprocal(rowv[0:1, 0:1024],
                                          rowv[0:1, 0:1024]),
         waits=(W(sac),), incs=((sve, 1),))
    emit("sync", lambda e: e.dma_start(out=rq_sc, in_=rowv[0:1, 0:1024]),
         waits=(W(sve),), incs=((stS, 16),))
    dmaS(srt[:, 0:NT], rq_sc.rearrange("o (a p) -> p (o a)", p=128),
         waits=(W(stS),), slow=True)
    emit("vector", lambda e: e.tensor_scalar(
        out=srt[:, 0:NT], in0=srt[:, 0:NT], scalar1=float(SCALE), scalar2=None,
        op0=ALU.mult), waits=(W(ldS),), incs=((sve, 1),))

    if _dbg_stop('p2'):
        pass
    O_WKA = 90112
    O_XLC = 126976
    O_CKV = 159744
    O_KPE = 176128
    O_KPO = 180224
    O_KPB = 184320
    O_RSB = 188416
    O_TMP1 = 135168
    O_TMP2 = 139264
    rowv = af32(143360, 2048)
    p2_v, p2_m = sve.n, smm.n
    for k in range(KT):
        dmaG(abf(O_WKA + 1152 * k, 576), wkv_aT[128 * k:128 * (k + 1), :],
             waits=(W(sve, p2_v), W(smm, p2_m)) if k == 0 else ())
    wka_ld = ldG.n
    O_X3 = 0
    O_XT3 = 16384
    for cc in range(NCC):
        for st4 in range(4):
            ct = 4 * cc + st4
            xa = af32(O_X3, 4096)
            wl = (W(sve),) if ct > 0 else (W(sve, p2_v),)
            dmaS(xa, x_b[128 * ct:128 * (ct + 1), :], sem=ldX3, waits=wl)
            xt = abf(O_XT3, 4096)
            a_done = layer_norm_apply(xa, xt, 4096, (W(ldX3), W(smm)))
            chmajor_from_tok(xt, st4, lambda k: abf(O_XLC + 1024 * k, 512),
                             a_done)
        ewait("tensor", sve)
        for c in range(6):
            ps = [psA[:, 0:512], psA[:, 512:1024], psA[:, 1024:1536],
                  psA[:, 1536:2048], psB[:, 0:512], psD[:, 0:512]][c]
            mcols = 128 if c < 4 else 32
            coff = 256 * c if c < 4 else 1024 + 64 * (c - 4)
            for k in range(KT):
                wl = ()
                if k == 0:
                    wl = (W(sve),)
                    if cc == 0 and c == 0:
                        wl = (W(ldG, wka_ld), W(sve))
                mm(ps[0:mcols, 0:512],
                   abf(O_WKA + 1152 * k + coff, mcols),
                   abf(O_XLC + 1024 * k, 512),
                   start=(k == 0), stop=(k == KT - 1),
                   waits=wl,
                   incs=((smm, 1),) if k == KT - 1 else ())
            if c < 4:
                emit("vector", lambda e, c=c, cc=cc, p=ps: e.tensor_copy(
                    abf(O_CKV + 4096 * c + 1024 * cc, 512), p[:, 0:512]),
                    waits=(W(smm),), incs=((sve, 1),))
            else:
                emit("vector", lambda e, c=c, cc=cc, p=ps: e.tensor_copy(
                    abf((O_KPE if c == 4 else O_KPO) + 1024 * cc, 512)
                    [0:32, :],
                    p[0:32, 0:512]),
                    waits=(W(smm),), incs=((sve, 1),))
    kpe = abf(O_KPE, 2048)
    kpo = abf(O_KPO, 2048)
    tmp1 = abf(O_TMP1, 2048)
    tmp2 = abf(O_TMP2, 2048)
    kpb = abf(O_KPB, 2048)
    emit("vector", lambda e: e.tensor_mul(tmp1[0:32, :], kpe[0:32, :],
                                          ropk_c[:, :]),
         waits=(W(sve),), incs=((sve, 1),))
    emit("vector", lambda e: e.tensor_mul(tmp2[0:32, :], kpo[0:32, :],
                                          ropk_s[:, :]), incs=((sve, 1),))
    emit("vector", lambda e: e.tensor_sub(kpb[0:32, :], tmp1[0:32, :],
                                          tmp2[0:32, :]),
         waits=(W(sve),), incs=((sve, 1),))
    emit("vector", lambda e: e.tensor_mul(tmp1[0:32, :], kpe[0:32, :],
                                          ropk_s[:, :]),
         waits=(W(sve),), incs=((sve, 1),))
    emit("vector", lambda e: e.tensor_mul(tmp2[0:32, :], kpo[0:32, :],
                                          ropk_c[:, :]), incs=((sve, 1),))
    emit("vector", lambda e: e.tensor_add(kpb[32:64, :], tmp1[0:32, :],
                                          tmp2[0:32, :]),
         waits=(W(sve),), incs=((sve, 1),))
    for c in range(KVLT):
        sq = abf(O_XLC, 2048)
        emit("vector", lambda e, c=c, s=sq: e.tensor_mul(
            s, abf(O_CKV + 4096 * c, 2048), abf(O_CKV + 4096 * c, 2048)),
            waits=(W(sve), W(smm)), incs=((sve, 1),))
        for cc in range(NCC):
            mm(psA[0:1, 512 * cc:512 * (cc + 1)], ones_col,
               sq[:, 512 * cc:512 * (cc + 1)],
               start=(c == 0), stop=(c == KVLT - 1),
               waits=(W(sve),) if cc == 0 else (), incs=((smm, 1),))
    for cc in range(NCC):
        emit("scalar", lambda e, cc=cc: e.activation(
            out=rowv[0:1, 512 * cc:512 * (cc + 1)],
            in_=psA[0:1, 512 * cc:512 * (cc + 1)],
            func=AF.Sqrt, bias=float(RMS_EPS), scale=float(1.0 / KVL)),
            waits=(W(smm),), incs=((sac, 1),))
    emit("vector", lambda e: e.reciprocal(rowv[0:1, 0:2048],
                                          rowv[0:1, 0:2048]),
         waits=(W(sac),), incs=((sve, 1),))
    emit("sync", lambda e: e.dma_start(out=rkv_sc, in_=rowv[0:1, 0:2048]),
         waits=(W(sve),), incs=((stS, 16),))
    rsb = af32(O_RSB, 2048)
    emit("gpsimd", lambda e: e.dma_start(
        out=rsb, in_=rkv_sc.broadcast_to([128, CTX])),
        waits=(W(stS), W(sve)), incs=((ldG, 16),))
    for c in range(KVLT):
        emit("vector", lambda e, c=c: e.tensor_mul(
            abf(O_CKV + 4096 * c, 2048), abf(O_CKV + 4096 * c, 2048), rsb),
            waits=(W(ldG),) if c == 0 else (), incs=((sve, 1),))
    ckv_ready_v = sve.n

    if _dbg_stop('p3'):
        pass
    O_WKVB = 90112
    O_WQB = 94208
    O_KN = 103424
    O_VT = 111616
    O_QN = 119808
    O_QPF = 123904
    O_QPB = 128000
    O_PRB = 132096
    O_PRT = 136192
    O_MSK = 140288

    def arK(h2, n0, n):
        return abf(O_KN + 4096 * h2 + 2 * n0, n)

    def arV(ct, n0=0, n=256):
        return abf(O_VT + 512 * ct + 2 * n0, n)

    def arQN(h2, t0, n):
        return abf(O_QN + 2048 * h2 + 2 * t0, n)

    probs = abf(O_PRB, 2048)
    probsT = abf(O_PRT, 2048)
    maskb = abf(O_MSK, 2048)
    O_SC = [144384, 188416]
    p4it = {"i": 0, "evac": {}, "exp": {}}

    for pr in range(NPAIR):
        pm0 = smm.n
        for c in range(KVLT):
            wl = ()
            if c == 0:
                wl = (W(sve, ckv_ready_v),) if pr == 0 else (W(smm, pm0),)
            dmaG(abf(O_WKVB + 1024 * c, 512),
                 wkv_bT[128 * c:128 * (c + 1), 512 * pr:512 * (pr + 1)],
                 waits=wl)
        for c in range(QLT):
            dmaG(abf(O_WQB + 768 * c, 384),
                 wq_bT[128 * c:128 * (c + 1), 384 * pr:384 * (pr + 1)])
        wld = ldG.n
        for c in range(KVLT):
            emit("vector", lambda e, c=c: e.tensor_scalar(
                out=abf(O_WKVB + 1024 * c, 512), in0=abf(O_WKVB + 1024 * c, 512),
                scalar1=kvnw_sb[:, c:c + 1], scalar2=None, op0=ALU.mult),
                waits=(W(ldG, wld),) if c == 0 else (), incs=((sve, 1),))
        for c in range(QLT):
            emit("vector", lambda e, c=c: e.tensor_scalar(
                out=abf(O_WQB + 768 * c, 384), in0=abf(O_WQB + 768 * c, 384),
                scalar1=qnw_sb[:, c:c + 1], scalar2=None, op0=ALU.mult),
                incs=((sve, 1),))
        wfold = sve.n

        for h2 in range(2):
            for tc in range(2):
                ps = psB if tc == 0 else psC
                for c in range(QLT):
                    mm(ps[:, 0:512],
                       abf(O_WQB + 768 * c + 384 * h2, 128),
                       qlra(c, 512 * tc, 512),
                       start=(c == 0), stop=(c == QLT - 1),
                       waits=(W(sve, wfold), W(sve)) if c == 0 else (),
                       incs=((smm, 1),) if c == QLT - 1 else ())
                emit("vector", lambda e, h2=h2, tc=tc, p=ps: e.tensor_copy(
                    arQN(h2, 512 * tc, 512), p[:, 0:512]),
                    waits=(W(smm),), incs=((sve, 1),))
            for pe_half in range(2):
                ps = psB if pe_half == 0 else psC
                for tc in range(2):
                    for c in range(QLT):
                        mm(ps[0:32, 512 * 0:512] if False else
                           ps[0:32, 0:512],
                           abf(O_WQB + 768 * c + 384 * h2 + 256
                               + 64 * pe_half, 32),
                           qlra(c, 512 * tc, 512),
                           start=(c == 0), stop=(c == QLT - 1),
                           waits=(W(sve),) if c == 0 else (),
                           incs=((smm, 1),) if c == QLT - 1 else ())
                    emit("vector", lambda e, tc=tc, p=ps, ph=pe_half:
                         e.tensor_copy(
                             abf(O_QPF + 2048 * ph + 1024 * tc, 512)
                             [0:32, :],
                             p[0:32, 0:512]),
                         waits=(W(smm),), incs=((sve, 1),))
            qpe = abf(O_QPF, 1024)
            qpo = abf(O_QPF + 2048, 1024)
            qt1 = abf(O_PRB, 1024)
            qt2 = abf(O_PRB + 2048, 1024)
            qpb = abf(O_QPB + 2048 * h2, 1024)
            emit("vector", lambda e, qpe=qpe, qt1=qt1: e.tensor_mul(
                qt1[0:32, :], qpe[0:32, :], ropq_c[:, :]),
                waits=(W(sve), W(smm)), incs=((sve, 1),))
            emit("vector", lambda e, qpo=qpo, qt2=qt2: e.tensor_mul(
                qt2[0:32, :], qpo[0:32, :], ropq_s[:, :]),
                incs=((sve, 1),))
            emit("vector", lambda e, qpb=qpb, qt1=qt1, qt2=qt2: e.tensor_sub(
                qpb[0:32, :], qt1[0:32, :], qt2[0:32, :]),
                waits=(W(sve),), incs=((sve, 1),))
            emit("vector", lambda e, qpe=qpe, qt1=qt1: e.tensor_mul(
                qt1[0:32, :], qpe[0:32, :], ropq_s[:, :]),
                waits=(W(sve),), incs=((sve, 1),))
            emit("vector", lambda e, qpo=qpo, qt2=qt2: e.tensor_mul(
                qt2[0:32, :], qpo[0:32, :], ropq_c[:, :]),
                incs=((sve, 1),))
            emit("vector", lambda e, qpb=qpb, qt1=qt1, qt2=qt2: e.tensor_add(
                qpb[32:64, :], qt1[0:32, :], qt2[0:32, :]),
                waits=(W(sve),), incs=((sve, 1),))

        for h2 in range(2):
            for cc in range(NCC):
                ps = psB if cc % 2 == 0 else psC
                for c in range(KVLT):
                    mm(ps[:, 0:512],
                       abf(O_WKVB + 1024 * c + 256 * h2, 128),
                       abf(O_CKV + 4096 * c + 1024 * cc, 512),
                       start=(c == 0), stop=(c == KVLT - 1),
                       waits=(W(sve),) if c == 0 else (),
                       incs=((smm, 1),) if c == KVLT - 1 else ())
                emit("vector", lambda e, h2=h2, cc=cc, p=ps: e.tensor_copy(
                    arK(h2, 512 * cc, 512), p[:, 0:512]),
                    waits=(W(smm),), incs=((sve, 1),))
        for ct in range(NCT):
            ps = psB if ct % 2 == 0 else psC
            for c in range(KVLT):
                mm(ps[:, 0:256],
                   abf(O_CKV + 4096 * c + 256 * ct, 128),
                   abf(O_WKVB + 1024 * c + 512, 256),
                   start=(c == 0), stop=(c == KVLT - 1),
                   waits=(W(sve),) if c == 0 else (),
                   incs=((smm, 1),) if c == KVLT - 1 else ())
            emit("vector", lambda e, ct=ct, p=ps: e.tensor_copy(
                arV(ct), p[:, 0:256]),
                waits=(W(smm),), incs=((sve, 1),))
        pair_ready_v = sve.n

        for tt in range(NT):
            dmaS(maskb, mask_in[tt, :, :], sem=ldM, waits=(W(sve),))
            mask_ld = ldM.n
            for h2 in range(2):
                hh = 2 * pr + h2
                it = p4it["i"]
                p4it["i"] = it + 1
                sc = af32(O_SC[it % 2], 2048)
                for cc in range(NCC):
                    wl = ()
                    if cc == 0:
                        wl = (W(sve, p4it["evac"].get(it - 1, 0)),
                              W(sve, pair_ready_v) if (tt == 0 and h2 == 0)
                              else (sve, 0))
                    mm(psA[:, 512 * cc:512 * (cc + 1)],
                       arQN(h2, 128 * tt, 128),
                       arK(h2, 512 * cc, 512),
                       start=True, stop=False, waits=wl)
                    mm(psA[:, 512 * cc:512 * (cc + 1)],
                       abf(O_QPB + 2048 * h2 + 256 * tt, 128)[0:64, :],
                       kpb[0:64, 512 * cc:512 * (cc + 1)],
                       start=False, stop=True,
                       incs=((smm, 1),) if cc == NCC - 1 else ())
                emit("vector", lambda e, sc=sc: e.tensor_add(
                    sc, psA[:, 0:2048], maskb),
                    waits=(W(smm), W(ldM, mask_ld),
                           W(sac, p4it["exp"].get(it - 2, 0))),
                    incs=((sve, 1),))
                p4it["evac"][it] = sve.n
                emit("vector", lambda e, sc=sc: e.reduce_max(
                    sm[:, 0:1], sc, axis=AX.X),
                    waits=(W(sve),), incs=((sve, 1),))
                emit("vector", lambda e, tt=tt: e.tensor_scalar(
                    out=sm[:, 1:2], in0=sm[:, 0:1],
                    scalar1=srt[:, tt:tt + 1], scalar2=-1.0,
                    op0=ALU.mult, op1=ALU.mult),
                    waits=(W(sve),), incs=((sve, 1),))
                for cc in range(NCC):
                    emit("scalar", lambda e, cc=cc, tt=tt, sc=sc: e.activation(
                        out=probs[:, 512 * cc:512 * (cc + 1)],
                        in_=sc[:, 512 * cc:512 * (cc + 1)],
                        func=AF.Exp, bias=sm[:, 1:2],
                        scale=srt[:, tt:tt + 1],
                        accum_out=sm[:, 4 + cc:5 + cc]),
                        waits=(W(sve),) if cc == 0 else (),
                        incs=((sac, 1),))
                p4it["exp"][it] = sac.n
                emit("vector", lambda e: e.reduce_sum(
                    sm[:, 2:3], sm[:, 4:8], axis=AX.X),
                    waits=(W(sac),), incs=((sve, 1),))
                emit("vector", lambda e: e.reciprocal(sm[:, 3:4], sm[:, 2:3]),
                     waits=(W(sve),), incs=((sve, 1),))
                emit("vector", lambda e: e.tensor_scalar(
                    out=probs, in0=probs, scalar1=sm[:, 3:4], scalar2=None,
                    op0=ALU.mult),
                    waits=(W(sve),), incs=((sve, 1),))
                pnorm_v = sve.n
                for b4 in range(4):
                    for j in range(4):
                        ct = 4 * b4 + j
                        wl = ()
                        if j == 0:
                            wl = (W(sve, pnorm_v), W(sve)) if b4 == 0 \
                                else (W(sve),)
                        tr(psT[:, 128 * j:128 * (j + 1)],
                           probs[:, 128 * ct:128 * (ct + 1)],
                           waits=wl,
                           incs=((smm, 1),) if j == 3 else ())
                    for j in range(4):
                        ct = 4 * b4 + j
                        emit("vector", lambda e, ct=ct, j=j: e.tensor_copy(
                            probsT[:, 128 * ct:128 * (ct + 1)],
                            psT[:, 128 * j:128 * (j + 1)]),
                            waits=(W(smm),) if j == 0 else (),
                            incs=((sve, 1),) if j == 3 else ())
                for ct in range(NCT):
                    mm(psD[:, 0:128],
                       arV(ct, 128 * h2, 128),
                       probsT[:, 128 * ct:128 * (ct + 1)],
                       start=(ct == 0), stop=(ct == NCT - 1),
                       waits=(W(sve),) if ct == 0 else (),
                       incs=((smm, 1),) if ct == NCT - 1 else ())
                emit("vector", lambda e, hh=hh, tt=tt: e.tensor_copy(
                    ar1(hh, 128 * tt, 128), psD[:, 0:128]),
                    waits=(W(smm),), incs=((sve, 1),))

    if _dbg_stop('p4'):
        pass
    O_OO = 65536
    O_WO = 131072
    O_XF0 = 147456
    O_XF1 = 163840
    O_X1T = 180224
    O_XT5 = 188416

    def oot(ht, t0=0, n=1024):
        return abf(O_OO + 2048 * ht + 2 * t0, n)

    p4_v = sve.n
    wo_mm_done = {}
    wosrc = woT.rearrange("(v p) h -> p v h", p=128)
    for ht in range(KT):
        pse, pso = (psB, psC) if ht % 2 == 0 else \
            (psA[:, 0:512], psA[:, 512:1024])
        par = ht % 2
        woblk = abf(O_WO + 8192 * par, 4096)
        woblk3 = woblk.rearrange("p (v h) -> p v h", h=128)
        for si in range(4):
            wl = ()
            if si == 0:
                if ht >= 2:
                    wl = (W(smm, wo_mm_done[ht - 2]),)
                elif ht == 0:
                    wl = (W(sve, p4_v),)
            emit("gpsimd", lambda e, si=si, ht=ht, woblk3=woblk3:
                 e.dma_start(
                     out=woblk3[:, 8 * si:8 * (si + 1), :],
                     in_=wosrc[:, 8 * si:8 * (si + 1),
                               128 * ht:128 * (ht + 1)]),
                 waits=wl, incs=((ldW[par], 16),))
        wo_ld = ldW[par].n
        for vc in range(KT):
            wob = abf(O_WO + 8192 * par + 256 * vc, 128)
            for tc in range(2):
                ps = pse if tc == 0 else pso
                mm(ps[:, 0:512] if ht % 2 == 0 else ps,
                   wob, ar1(vc, 512 * tc, 512),
                   start=(vc == 0), stop=(vc == KT - 1),
                   waits=((W(ldW[par], wo_ld), W(sve))
                          if vc == 0 else ()) if tc == 0 else (),
                   incs=((smm, 1),) if tc == 1 else ())
        wo_mm_done[ht] = smm.n
        for tc in range(2):
            ps = pse if tc == 0 else pso
            emit("vector", lambda e, ht=ht, tc=tc,
                 p=(pse if tc == 0 else pso), ish=(ht % 2 == 0):
                 e.tensor_copy(oot(ht, 512 * tc, 512),
                               p[:, 0:512] if ish else p),
                 waits=(W(smm),) if tc == 0 else (), incs=((sve, 1),))
    spill_prev = stS.n
    for tt in range(NT):
        xf = af32(O_XF0 if tt % 2 == 0 else O_XF1, 4096)
        dmaS(xf, x_q[128 * tt:128 * (tt + 1), :], sem=ldX[tt % 2],
             waits=(W(sve),))
        xf_ld = ldX[tt % 2].n
        x1tile = abf(O_X1T, 4096)
        for k4 in range(8):
            for j in range(4):
                ht = 4 * k4 + j
                wl = ()
                if j == 0:
                    wl = (W(sve), W(stS, spill_prev)) if k4 == 0 else (W(sve),)
                tr(psT[:, 128 * j:128 * (j + 1)],
                   oot(ht, 128 * tt, 128), waits=wl,
                   incs=((smm, 1),) if j == 3 else ())
            emit("vector", lambda e, k4=k4, x1tile=x1tile, xf=xf:
                 e.tensor_add(
                     x1tile[:, 512 * k4:512 * (k4 + 1)], psT[:, 0:512],
                     xf[:, 512 * k4:512 * (k4 + 1)]),
                 waits=(W(smm), W(ldX[tt % 2], xf_ld)), incs=((sve, 1),))
        xt5 = abf(O_XT5, 4096)
        a_done = layer_norm_apply(x1tile, xt5, 4096, (W(sve),))
        chmajor_from_tok(xt5, tt, lambda k: ar1(k), a_done)
        emit("sync", lambda e, tt=tt, x1tile=x1tile: e.dma_start(
            out=x1_sc[128 * tt:128 * (tt + 1), :], in_=x1tile),
            waits=(W(sve, a_done),), incs=((stS, 16),))
        spill_prev = stS.n
    x1_spilled = stS.n

    if _dbg_stop('p5'):
        pass
    slot_state = {"off": 0, "slot_bytes": 0}

    def dequant_unit(region_off, q8_col, s_mat, z_mat, n0, n, nk, par,
                     w_memset=(), kseg=8):
        lq, lz = ldQp[par], ldZp[par]
        reg = abf(region_off, nk * n)
        reg3 = reg.rearrange("p (k n) -> p k n", n=n)
        q3 = q8_col.rearrange("(k p) n -> p k n", p=128)
        emit("vector", lambda e, reg=reg: e.memset(reg, -8.0),
             waits=w_memset, incs=((sve, 1),))
        ms_v = sve.n
        segs = [(k0, min(kseg, nk - k0)) for k0 in range(0, nk, kseg)]
        for si, (k0, kn) in enumerate(segs):
            emit("gpsimd", lambda e, q3=q3, reg3=reg3, k0=k0, kn=kn:
                 e.dma_start(out=reg3[:, k0:k0 + kn, :],
                             in_=q3[:, k0:k0 + kn, :], accum_op=ALU.add),
                 waits=(W(sve, ms_v),) if si == 0 else (), incs=((lq, 16),))
        q_tot = lq.n
        mu_last = 0
        for si, (k0, kn) in enumerate(segs):
            lb = ldBp[2 * par + si % 2]
            slot = abf(slot_state["off"]
                       + slot_state["slot_bytes"] * (si % 2), kn * n)
            ssrc = s_mat[k0:k0 + kn, n0:n0 + n].unsqueeze(0) \
                .broadcast_to([128, kn, n])
            emit("gpsimd",
                 lambda e, slot=slot, ssrc=ssrc, kn=kn, n=n: e.dma_start(
                     out=slot.rearrange("p (k n) -> p k n", n=n), in_=ssrc),
                 waits=(W(sve, mu_last),) if si >= 2 else (),
                 incs=((lb, 16),))
            s_tot = lb.n
            emit("vector",
                 lambda e, reg=reg, k0=k0, kn=kn, n=n, slot=slot:
                     e.tensor_mul(abf(region_off + 2 * k0 * n, kn * n),
                                  abf(region_off + 2 * k0 * n, kn * n), slot),
                 waits=(W(lq, q_tot), W(lb, s_tot)), incs=((sve, 1),))
            if si % 2 == 1:
                mu_last = sve.n
        mu_all = sve.n
        for si, (k0, kn) in enumerate(segs):
            zsrc = z_mat[k0:k0 + kn, n0:n0 + n].unsqueeze(0) \
                .broadcast_to([128, kn, n])
            emit("gpsimd",
                 lambda e, reg3=reg3, zsrc=zsrc, k0=k0, kn=kn: e.dma_start(
                     out=reg3[:, k0:k0 + kn, :], in_=zsrc, accum_op=ALU.add),
                 waits=(W(sve, mu_all),) if si == 0 else (), incs=((lz, 16),))
        return lz.n

    s_gate_m = sbf["gs"].rearrange("o (g n) -> (o g) n", n=II)
    z_gate_m = sbf["gz"].rearrange("o (g n) -> (o g) n", n=II)
    s_up_m = sbf["us"].rearrange("o (g n) -> (o g) n", n=II)
    z_up_m = sbf["uz"].rearrange("o (g n) -> (o g) n", n=II)
    s_down_m = sbf["ds"].rearrange("o (g n) -> (o g) n", n=D)
    z_down_m = sbf["dz"].rearrange("o (g n) -> (o g) n", n=D)

    O_WFF = 65536

    def wff(half, k, n=512):
        return abf(O_WFF + 32768 * half + 2 * n * k, n)
    O_SG = 131072
    O_HB = 139264
    O_HT = 140288
    O_SS6 = 148480
    slot_state["off"] = O_SS6
    slot_state["slot_bytes"] = 8192

    p5_v, p5_m = sve.n, smm.n
    units = []
    for j in range(NNC):
        units.append(("g", j))
        units.append(("u", j))
    unit_mm_done = {}
    spill_cnt = {"n": stS.n}
    for ui, (mat, j) in enumerate(units):
        half = ui % 2
        n0, n = 512 * j, _ncw(j)
        q8 = gate_q8 if mat == "g" else up_q8
        s_fl = sbf["gs"] if mat == "g" else sbf["us"]
        z_fl = sbf["gz"] if mat == "g" else sbf["uz"]
        if ui >= 2:
            wm = (W(smm, unit_mm_done[ui - 2]),)
        else:
            wm = (W(sve, p5_v), W(smm, p5_m), W(stS, x1_spilled))
        s_m = s_gate_m if mat == "g" else s_up_m
        z_m = z_gate_m if mat == "g" else z_up_m
        zl = dequant_unit(O_WFF + 32768 * half, q8[:, n0:n0 + n],
                          s_m, z_m, n0, n, KT, half, w_memset=wm)
        ps_pool = [psA[:, 0:512], psA[:, 512:1024], psA[:, 1024:1536],
                   psA[:, 1536:2048]]
        for tt in range(NT):
            ps = ps_pool[tt % 4]
            for k in range(KT):
                wl = ()
                if k == 0:
                    wl = (W(sac), W(sve))
                    if tt == 0:
                        wl = (W(ldZp[half], zl), W(sac), W(sve))
                mm(ps[:, 0:n], ar1(k, 128 * tt, 128), wff(half, k, n),
                   start=(k == 0), stop=(k == KT - 1),
                   waits=wl, incs=((smm, 1),) if k == KT - 1 else ())
            if mat == "g":
                emit("scalar", lambda e, tt=tt, p=ps, n=n: e.activation(
                    out=abf(O_SG + 1024 * tt, n), in_=p[:, 0:n],
                    func=AF.Silu, bias=0.0, scale=1.0),
                    waits=(W(smm), W(sve)), incs=((sac, 1),))
            else:
                hb = abf(O_HB, 512)
                emit("vector", lambda e, tt=tt, p=ps, n=n, hb=hb: e.tensor_mul(
                    hb[:, 0:n], abf(O_SG + 1024 * tt, n), p[:, 0:n]),
                    waits=(W(smm), W(sac)), incs=((sve, 1),))
                hmul_v = sve.n
                nk4 = (n + 127) // 128
                for jj in range(nk4):
                    tr(psT[:, 128 * jj:128 * (jj + 1)],
                       hb[:, 128 * jj:128 * (jj + 1)],
                       waits=(W(sve, hmul_v), W(sve)) if jj == 0 else (),
                       incs=((smm, 1),) if jj == nk4 - 1 else ())
                for jj in range(nk4):
                    emit("vector", lambda e, jj=jj, tt=tt: e.tensor_copy(
                        abf(O_HT + 2048 * jj + 256 * tt, 128),
                        psT[:, 128 * jj:128 * (jj + 1)]),
                        waits=(W(smm), W(stS, spill_cnt["n"]))
                        if jj == 0 else (),
                        incs=((sve, 1),) if jj == nk4 - 1 else ())
        unit_mm_done[ui] = smm.n
        if mat == "u":
            nk4 = (n + 127) // 128
            for jj in range(nk4):
                it_g = 4 * j + jj
                emit("sync", lambda e, jj=jj, it_g=it_g: e.dma_start(
                    out=hT_sc[128 * it_g:128 * (it_g + 1), :],
                    in_=abf(O_HT + 2048 * jj, 1024)),
                    waits=(W(sve),), incs=((stS, 16),))
            spill_cnt["n"] = stS.n
    h_spilled = stS.n
    p6_m = smm.n

    O_HT7 = 0

    def ht7(buf, i, n=128):
        return abf(O_HT7 + 22528 * buf + 256 * i, n)
    O_X1S = 45056
    O_OUTB = 49152
    O_WD = 65536

    def wdd(buf, i, n=256):
        return abf(O_WD + 44032 * buf + 512 * i, n)
    O_SS7 = 155648
    slot_state["off"] = O_SS7
    slot_state["slot_bytes"] = 8192

    p6_v = sve.n
    NDCQ = 16
    dcq_mm = {}
    ht_segs = [(0, 16), (16, 16), (32, 16), (48, 16), (64, 16), (80, 6)]
    ob_store = {0: 0, 1: 0}
    tt_mm = {}
    for dq in range(NDCQ):
        wbuf = dq % 2
        if dq >= 2:
            wm = (W(smm, dcq_mm[dq - 2]),)
        elif dq == 0:
            wm = (W(sve, p6_v), W(smm, p6_m), W(stS, h_spilled))
        else:
            wm = ()
        deq_done = dequant_unit(O_WD + 44032 * wbuf,
                                down_q8[:, 256 * dq:256 * (dq + 1)],
                                s_down_m, z_down_m, 256 * dq, 256, IT, wbuf,
                                w_memset=wm, kseg=16)
        for tt in range(NT):
            buf = tt % 2
            key = (dq, tt)
            pkey = (dq, tt - 2) if tt >= 2 else (dq - 1, tt + 6)
            if pkey in tt_mm:
                hw = (W(smm, tt_mm[pkey]),)
            else:
                hw = (W(smm, p6_m), W(stS, h_spilled))
            for si, (i0, ni) in enumerate(ht_segs):
                hsrc = hT_sc[128 * i0:128 * (i0 + ni),
                             128 * tt:128 * (tt + 1)]
                hsrc = hsrc.rearrange("(i p) c -> p i c", p=128)
                dst = abf(O_HT7 + 22528 * buf + 256 * i0, ni * 128)
                dst = dst.rearrange("p (i c) -> p i c", c=128)
                dmaS(dst, hsrc, sem=ldH[buf], waits=hw if si == 0 else ())
            dmaS(abf(O_X1S + 1024 * buf, 256),
                 x1_sc[128 * tt:128 * (tt + 1), 256 * dq:256 * (dq + 1)],
                 sem=ldH[buf],
                 waits=(W(stS, x1_spilled),) if (dq == 0 and tt < 2)
                 else (W(sve),))
            h_ld = ldH[buf].n
            ps = psB if tt % 2 == 0 else psC
            for i in range(IT):
                mm(ps[:, 0:256], ht7(buf, i), wdd(wbuf, i),
                   start=(i == 0), stop=(i == IT - 1),
                   waits=(W(ldH[buf], h_ld), W(ldZp[wbuf], deq_done),
                          W(sve))
                   if i == 0 else (),
                   incs=((smm, 1),) if i == IT - 1 else ())
            tt_mm[key] = smm.n
            ob = af32(O_OUTB + 2048 * buf, 256)
            emit("vector", lambda e, p=ps, ob=ob, buf=buf: e.tensor_add(
                ob, p[:, 0:256], abf(O_X1S + 1024 * buf, 256)),
                waits=(W(smm), W(oSp[buf], ob_store[buf])), incs=((sve, 1),))
            emit("sync", lambda e, ob=ob, tt=tt, dq=dq: e.dma_start(
                out=out_t[128 * tt:128 * (tt + 1),
                          256 * dq:256 * (dq + 1)],
                in_=ob),
                waits=(W(sve),), incs=((oSp[buf], 16),))
            ob_store[buf] = oSp[buf].n
        dcq_mm[dq] = smm.n

    ewait("sync", oSp[0])
    ewait("sync", oSp[1])

    with nc.Block() as block:
        @block.tensor
        def _(eng):
            for f in ops["tensor"]:
                f(eng)

        @block.vector
        def _(eng):
            for f in ops["vector"]:
                f(eng)

        @block.scalar
        def _(eng):
            for f in ops["scalar"]:
                f(eng)

        @block.gpsimd
        def _(eng):
            for f in ops["gpsimd"]:
                f(eng)

        @block.sync
        def _(eng):
            for f in ops["sync"]:
                f(eng)

    return nc



_CACHE = {}


def _rope_perm_cols():
    return np.concatenate([np.arange(0, 64, 2), np.arange(1, 64, 2)])


def _prep_weights(inp):
    w = {}
    w["wq_aT"] = np.ascontiguousarray(inp["L0_attn_wq_a"].T)
    w["q_nw"] = np.ascontiguousarray(inp["L0_attn_q_norm_w"].reshape(QL, 1))
    wqbT = np.ascontiguousarray(inp["L0_attn_wq_b"].T)
    perm = np.zeros(NH * 192, dtype=np.int64)
    rp = _rope_perm_cols()
    for h in range(NH):
        b = 192 * h
        perm[b:b + 128] = b + np.arange(128)
        perm[b + 128:b + 192] = b + 128 + rp
    w["wq_bT"] = np.ascontiguousarray(wqbT[:, perm])
    wkvaT = np.ascontiguousarray(inp["L0_attn_wkv_a"].T)
    perm2 = np.concatenate([np.arange(KVL), KVL + rp])
    w["wkv_aT"] = np.ascontiguousarray(wkvaT[:, perm2])
    w["kv_nw"] = np.ascontiguousarray(inp["L0_attn_kv_norm_w"].reshape(KVL, 1))
    wkvbT = np.ascontiguousarray(inp["L0_attn_wkv_b"].T)
    perm3 = np.zeros(NH * 256, dtype=np.int64)
    for p in range(NPAIR):
        b = 512 * p
        h0, h1 = 2 * p, 2 * p + 1
        perm3[b:b + 128] = 256 * h0 + np.arange(128)
        perm3[b + 128:b + 256] = 256 * h1 + np.arange(128)
        perm3[b + 256:b + 384] = 256 * h0 + 128 + np.arange(128)
        perm3[b + 384:b + 512] = 256 * h1 + 128 + np.arange(128)
    w["wkv_bT"] = np.ascontiguousarray(wkvbT[:, perm3])
    w["woT"] = np.ascontiguousarray(inp["L0_attn_wo"].T)
    for nm, key in [("gate_q8", "L0_ffn_gate_q"), ("up_q8", "L0_ffn_up_q"),
                    ("down_q8", "L0_ffn_down_q")]:
        q = np.ascontiguousarray(inp[key])
        w[nm] = np.ascontiguousarray(q.view(np.int8)[:, ::4])
    for nm, key in [("gate_s", "L0_ffn_gate_s"), ("gate_z", "L0_ffn_gate_z"),
                    ("up_s", "L0_ffn_up_s"), ("up_z", "L0_ffn_up_z"),
                    ("down_s", "L0_ffn_down_s"), ("down_z", "L0_ffn_down_z")]:
        w[nm] = np.ascontiguousarray(inp[key])
    return w


def _rope_tables(p0, n):
    freqs = (1.0 / 10000.0 ** (np.arange(0, 64, 2, dtype=np.float64) / 64.0))
    t = np.arange(p0, p0 + n, dtype=np.float64)
    ang = freqs[:, None] * t[None, :]
    return np.concatenate([np.cos(ang), np.sin(ang)], axis=0).astype(BF)


def _mask(p0):
    s = np.arange(CTX)
    m = np.zeros((NT, 128, CTX), dtype=np.float32)
    for tt in range(NT):
        qpos = p0 + 128 * tt + np.arange(128)
        m[tt] = np.where(s[None, :] <= qpos[:, None], 0.0, -1e9)
    return m.astype(BF)


def kernel(L0_attn_x, L0_attn_wq_a, L0_attn_q_norm_w, L0_attn_wq_b,
           L0_attn_wkv_a, L0_attn_kv_norm_w, L0_attn_wkv_b, L0_attn_wo,
           L0_ffn_gate_q, L0_ffn_gate_s, L0_ffn_gate_z, L0_ffn_up_q,
           L0_ffn_up_s, L0_ffn_up_z, L0_ffn_down_q, L0_ffn_down_s,
           L0_ffn_down_z):
    inp = dict(L0_attn_wq_a=np.asarray(L0_attn_wq_a),
               L0_attn_q_norm_w=np.asarray(L0_attn_q_norm_w),
               L0_attn_wq_b=np.asarray(L0_attn_wq_b),
               L0_attn_wkv_a=np.asarray(L0_attn_wkv_a),
               L0_attn_kv_norm_w=np.asarray(L0_attn_kv_norm_w),
               L0_attn_wkv_b=np.asarray(L0_attn_wkv_b),
               L0_attn_wo=np.asarray(L0_attn_wo),
               L0_ffn_gate_q=np.asarray(L0_ffn_gate_q),
               L0_ffn_gate_s=np.asarray(L0_ffn_gate_s),
               L0_ffn_gate_z=np.asarray(L0_ffn_gate_z),
               L0_ffn_up_q=np.asarray(L0_ffn_up_q),
               L0_ffn_up_s=np.asarray(L0_ffn_up_s),
               L0_ffn_up_z=np.asarray(L0_ffn_up_z),
               L0_ffn_down_q=np.asarray(L0_ffn_down_q),
               L0_ffn_down_s=np.asarray(L0_ffn_down_s),
               L0_ffn_down_z=np.asarray(L0_ffn_down_z))
    x = np.asarray(L0_attn_x)
    w = _prep_weights(inp)

    if "nc" not in _CACHE:
        _CACHE["nc"] = build_nc()
    nc = _CACHE["nc"]

    in_maps = []
    for c in range(8):
        b, half = c // 2, c % 2
        p0 = 1024 * half
        m = dict(w)
        m["x_q"] = np.ascontiguousarray(x[b, p0:p0 + T, :])
        m["x_b"] = np.ascontiguousarray(x[b])
        m["rope_q"] = _rope_tables(p0, T)
        m["rope_k"] = _rope_tables(0, CTX)
        m["mask_in"] = _mask(p0)
        in_maps.append(m)

    res = run_bass_kernel_spmd(nc, in_maps, core_ids=list(range(8)))
    out = np.empty((4, CTX, D), dtype=np.float32)
    for c in range(8):
        b, half = c // 2, c % 2
        out[b, 1024 * half:1024 * half + T, :] = res.results[c]["out"]
    return out


# version 05
# speedup vs baseline: 1.0634x; 1.0445x over previous
import numpy as np
import ml_dtypes
import concourse.bass as bass
import concourse.mybir as mybir
from concourse.bass_utils import run_bass_kernel_spmd

F32 = mybir.dt.float32
BF16 = mybir.dt.bfloat16
I8 = mybir.dt.int8
AF = mybir.ActivationFunctionType
ALU = mybir.AluOpType
AX = mybir.AxisListType
BF = ml_dtypes.bfloat16

D = 4096
T = 1024
NT = 8
CTX = 2048
NCC = 4
NCT = 16
KT = 32
QL = 1536
QLT = 12
KVL = 512
KVLT = 4
NH = 32
NPAIR = 16
II = 11008
IT = 86
NNC = 22
NDC = 8
SCALE = float(1.0 / np.sqrt(192.0))
LN_EPS = 1e-5
RMS_EPS = 1e-6


def _ncw(j):
    return min(512, II - 512 * j)


def build_nc(dbg_phase=None):
    nc = bass.Bass("TRN2", target_bir_lowering=False)

    def din(name, shape, dt):
        return nc.dram_tensor(name, list(shape), dt, kind="ExternalInput").ap()

    x_q = din("x_q", (T, D), F32)
    x_b = din("x_b", (CTX, D), F32)
    wq_aT = din("wq_aT", (D, QL), F32)
    q_nw = din("q_nw", (QL, 1), F32)
    wq_bT = din("wq_bT", (QL, NH * 192), F32)
    wkv_aT = din("wkv_aT", (D, KVL + 64), F32)
    kv_nw = din("kv_nw", (KVL, 1), F32)
    wkv_bT = din("wkv_bT", (KVL, NH * 256), F32)
    woT = din("woT", (D, D), F32)
    gate_q8 = din("gate_q8", (128, KT * II), I8)
    up_q8 = din("up_q8", (128, KT * II), I8)
    down_q8 = din("down_q8", (128, IT * D), I8)
    gate_s = din("gate_s", (1, KT * II), F32)
    gate_z = din("gate_z", (1, KT * II), F32)
    up_s = din("up_s", (1, KT * II), F32)
    up_z = din("up_z", (1, KT * II), F32)
    down_s = din("down_s", (1, IT * D), F32)
    down_z = din("down_z", (1, IT * D), F32)
    rope_q = din("rope_q", (64, T), BF16)
    rope_k = din("rope_k", (64, CTX), BF16)
    mask_in = din("mask_in", (NT, 128, CTX), BF16)

    out_t = nc.dram_tensor("out", [T, D], F32, kind="ExternalOutput").ap()
    dbg_t = nc.dram_tensor("dbg_arena", [128, 192 * 1024], mybir.dt.uint8,
                           kind="ExternalOutput").ap()

    sbf = {}
    for nm, tot in [("gs", KT * II), ("gz", KT * II), ("us", KT * II),
                    ("uz", KT * II), ("ds", IT * D), ("dz", IT * D)]:
        sbf[nm] = nc.dram_tensor("sc_" + nm, [1, tot], BF16, kind="Internal").ap()
    hT_sc = nc.dram_tensor("hT_sc", [II, T], BF16, kind="Internal").ap()
    x1_sc = nc.dram_tensor("x1_sc", [T, D], BF16, kind="Internal").ap()
    rq_sc = nc.dram_tensor("rq_sc", [1, T], F32, kind="Internal").ap()
    rkv_sc = nc.dram_tensor("rkv_sc", [1, CTX], F32, kind="Internal").ap()

    ident = nc.alloc_sbuf_tensor("ident", [128, 128], BF16).ap()
    ones_col = nc.alloc_sbuf_tensor("ones_col", [128, 1], BF16).ap()
    qnw_sb = nc.alloc_sbuf_tensor("qnw_sb", [128, QLT], F32).ap()
    kvnw_sb = nc.alloc_sbuf_tensor("kvnw_sb", [128, KVLT], F32).ap()
    ropq_c = nc.alloc_sbuf_tensor("ropq_c", [32, T], BF16).ap()
    ropq_s = nc.alloc_sbuf_tensor("ropq_s", [32, T], BF16).ap()
    ropk_c = nc.alloc_sbuf_tensor("ropk_c", [32, CTX], BF16).ap()
    ropk_s = nc.alloc_sbuf_tensor("ropk_s", [32, CTX], BF16).ap()
    stat = nc.alloc_sbuf_tensor("stat", [128, 64], F32).ap()
    mv = nc.alloc_sbuf_tensor("mv", [128, 8], F32).ap()
    srt = nc.alloc_sbuf_tensor("srt", [128, NT], F32).ap()
    sm = nc.alloc_sbuf_tensor("sm", [128, 16], F32).ap()
    cst = nc.alloc_sbuf_tensor("cst", [128, 8], F32).ap()
    CONSTS = [float(LN_EPS), float(RMS_EPS), 1.0, 0.0,
              float(1.0 / QL), float(1.0 / KVL)]
    for ci, cv in enumerate(CONSTS):
        nc.const_aps.register(F32, cv, cst[:, ci:ci + 1])

    AR_BYTES = 192 * 1024
    arena_t = nc.alloc_sbuf_tensor("arena", [128, AR_BYTES], mybir.dt.uint8)
    ar_bf = arena_t.ap().bitcast(BF16)
    ar_f32 = arena_t.ap().bitcast(F32)

    def abf(off, n):
        assert off % 2 == 0 and off + 2 * n <= AR_BYTES, (off, n)
        return ar_bf[:, off // 2: off // 2 + n]

    def af32(off, n):
        assert off % 4 == 0 and off + 4 * n <= AR_BYTES, (off, n)
        return ar_f32[:, off // 4: off // 4 + n]

    def ar1(tile, t0=0, n=1024):
        return abf(2048 * tile + 2 * t0, n)

    def qlra(j, c0=0, n=1024):
        return abf(65536 + 2048 * j + 2 * c0, n)

    psA = nc.alloc_psum_tensor("psA", [128, 2048], F32).ap()
    psB = nc.alloc_psum_tensor("psB", [128, 512], F32).ap()
    psC = nc.alloc_psum_tensor("psC", [128, 512], F32).ap()
    psT = nc.alloc_psum_tensor("psT", [128, 1024], BF16).ap()
    psD = nc.alloc_psum_tensor("psD", [128, 512], F32).ap()

    class Sem:
        __slots__ = ("h", "n")

        def __init__(self, name):
            self.h = nc.alloc_semaphore(name)
            self.n = 0

    ldS = Sem("ldS")
    ldG = Sem("ldG")
    ldX = [Sem("ldX0"), Sem("ldX1")]
    ldX3 = Sem("ldX3")
    ldM = Sem("ldM")
    ldXS = Sem("ldXS")
    ldW = [Sem("ldW0"), Sem("ldW1")]
    ldH = [Sem("ldH0"), Sem("ldH1")]
    ldQp = [Sem("ldQ0"), Sem("ldQ1")]
    ldBp = [Sem("ldB0"), Sem("ldB1"), Sem("ldB2"), Sem("ldB3")]
    ldZp = [Sem("ldZ0"), Sem("ldZ1")]
    stS = Sem("stS")
    oSp = [Sem("oS0"), Sem("oS1")]
    smm = Sem("smm")
    sve = Sem("sve")
    sac = Sem("sac")
    spo = Sem("spo")

    ops = {"tensor": [], "vector": [], "scalar": [], "gpsimd": [], "sync": []}
    stopped = {"v": False}

    def emit(eng, build, waits=(), incs=()):
        if stopped["v"]:
            for s, a in incs:
                s.n += a
            return
        wl = tuple((s.h, (s.n if v is None else v)) for (s, v) in waits)
        il = []
        for s, a in incs:
            s.n += a
            il.append((s.h, a))
        il = tuple(il)

        def f(e, build=build, wl=wl, il=il):
            for h, v in wl:
                e.wait_ge(h, v)
            ins = build(e)
            for h, a in il:
                ins = ins.then_inc(h, a)
        ops[eng].append(f)

    def W(sem, val=None):
        return (sem, val)

    def ewait(eng, sem, val=None):
        if stopped["v"]:
            return
        v = sem.n if val is None else val
        ops[eng].append(lambda e, h=sem.h, v=v: e.wait_ge(h, v))

    def dmaS(dst, src, sem=ldS, waits=(), slow=False):
        if slow:
            emit("sync", lambda e, d=dst, s=src: e.dma_start(
                out=d, in_=s, allow_slow_non_contiguous=True),
                waits=waits, incs=((sem, 16),))
        else:
            emit("sync", lambda e, d=dst, s=src: e.dma_start(out=d, in_=s),
                 waits=waits, incs=((sem, 16),))

    def dmaG(dst, src, accum=None, sem=ldG, waits=()):
        if accum is None:
            emit("gpsimd", lambda e, d=dst, s=src: e.dma_start(out=d, in_=s),
                 waits=waits, incs=((sem, 16),))
        else:
            emit("gpsimd",
                 lambda e, d=dst, s=src, a=accum:
                     e.dma_start(out=d, in_=s, accum_op=a),
                 waits=waits, incs=((sem, 16),))

    def mm(out, lhsT, rhs, start, stop, waits=(), incs=()):
        emit("tensor",
             lambda e, o=out, l=lhsT, r=rhs, s1=start, s2=stop:
                 e.matmul(o, l, r, start=s1, stop=s2),
             waits=waits, incs=incs)

    def tr(out, in_, waits=(), incs=()):
        emit("tensor", lambda e, o=out, i=in_: e.transpose(o, i, ident),
             waits=waits, incs=incs)

    for ci, cv in enumerate(CONSTS):
        emit("vector", lambda e, ci=ci, cv=cv: e.memset(cst[:, ci:ci + 1], cv),
             incs=((sve, 1),))
    if dbg_phase is not None:
        emit("vector", lambda e: e.memset(ar_f32[:, :], 0.0),
             incs=((sve, 1),))
    emit("gpsimd", lambda e: e.memset(ident, 0.0), incs=((spo, 1),))
    emit("gpsimd", lambda e: e.memset(ones_col, 1.0), incs=((spo, 1),))
    ewait("gpsimd", spo)
    emit("gpsimd", lambda e: e.affine_select(
        out=ident, in_=ident, compare_op=ALU.not_equal, fill=1.0,
        base=0, pattern=[[-1, 128]], channel_multiplier=1), incs=((spo, 1),))

    dmaS(qnw_sb[:, 0:QLT], q_nw.rearrange("(j p) o -> p (j o)", p=128),
         slow=True)
    dmaS(kvnw_sb[:, 0:KVLT], kv_nw.rearrange("(j p) o -> p (j o)", p=128),
         slow=True)
    dmaS(ropq_c[:, :], rope_q[0:32, :])
    dmaS(ropq_s[:, :], rope_q[32:64, :])
    dmaS(ropk_c[:, :], rope_k[0:32, :])
    dmaS(ropk_s[:, :], rope_k[32:64, :])

    cvt_jobs = [(gate_s, sbf["gs"], KT * II), (gate_z, sbf["gz"], KT * II),
                (up_s, sbf["us"], KT * II), (up_z, sbf["uz"], KT * II),
                (down_s, sbf["ds"], IT * D), (down_z, sbf["dz"], IT * D)]
    for src, dst, tot in cvt_jobs:
        per = tot // 128
        cv_f = af32(0, per)
        cv_b = abf(4 * per, per)
        flat = src.rearrange("o (p m) -> (o p) m", p=128)
        dmaS(cv_f, flat, waits=(W(sve),))
        emit("vector", lambda e, a=cv_b, b=cv_f: e.tensor_copy(a, b),
             waits=(W(ldS),), incs=((sve, 1),))
        emit("sync", lambda e, d=dst, s=cv_b:
             e.dma_start(out=d.rearrange("o (p m) -> p (o m)", p=128), in_=s),
             waits=(W(sve),), incs=((stS, 16),))
        ewait("vector", stS)
    cvt_done_st = stS.n

    def layer_norm_apply(src, dst_bf, width, pre_waits):
        nch = width // 512
        for j in range(nch):
            emit("vector",
                 lambda e, s=src, j=j: e.bn_stats(
                     stat[:, 6 * j:6 * j + 6], s[:, 512 * j:512 * (j + 1)]),
                 waits=pre_waits if j == 0 else (), incs=((sve, 1),))
        emit("vector", lambda e, n=nch: e.bn_aggr(mv[:, 0:2], stat[:, 0:6 * n]),
             waits=(W(sve),), incs=((sve, 1),))
        emit("scalar", lambda e: e.activation(
            out=mv[:, 2:3], in_=mv[:, 1:2], func=AF.Sqrt,
            bias=float(LN_EPS), scale=1.0),
            waits=(W(sve),), incs=((sac, 1),))
        emit("vector", lambda e: e.reciprocal(mv[:, 3:4], mv[:, 2:3]),
             waits=(W(sac),), incs=((sve, 1),))
        emit("vector", lambda e, s=src, d=dst_bf: e.tensor_scalar(
            out=d, in0=s, scalar1=mv[:, 0:1], scalar2=mv[:, 3:4],
            op0=ALU.subtract, op1=ALU.mult),
            waits=(W(sve),), incs=((sve, 1),))
        return sve.n

    def chmajor_from_tok(src_bf, tt, dst_fn, src_done_sve, nk=KT):
        for k4 in range((nk + 3) // 4):
            ks = list(range(4 * k4, min(4 * k4 + 4, nk)))
            for idx, k in enumerate(ks):
                wl = ()
                if idx == 0:
                    wl = (W(sve, src_done_sve), W(sve)) if k4 == 0 else (W(sve),)
                tr(psT[:, 128 * idx:128 * (idx + 1)],
                   src_bf[:, 128 * k:128 * (k + 1)],
                   waits=wl,
                   incs=((smm, 1),) if idx == len(ks) - 1 else ())
            for idx, k in enumerate(ks):
                emit("vector",
                     lambda e, k=k, idx=idx, d=dst_fn, tt=tt: e.tensor_copy(
                         d(k)[:, 128 * tt:128 * (tt + 1)],
                         psT[:, 128 * idx:128 * (idx + 1)]),
                     waits=(W(smm),) if idx == 0 else (),
                     incs=((sve, 1),) if idx == len(ks) - 1 else ())


    def _dbg_stop(tag):
        if dbg_phase != tag or stopped["v"]:
            return False
        ewait("sync", sve)
        ewait("sync", smm)
        ewait("sync", sac)
        emit("sync", lambda e: e.dma_start(out=dbg_t, in_=arena_t.ap()),
             waits=(), incs=((stS, 16),))
        ewait("sync", stS)
        stopped["v"] = True
        return True

    ewait("tensor", spo)
    O_XA = 90112
    O_XB2 = 106496
    O_XT = 122880
    apply_done = {}
    for tt in range(NT):
        xa = af32(O_XA if tt % 2 == 0 else O_XB2, 4096)
        wl = ()
        if tt >= 2:
            wl = (W(sve, apply_done[tt - 2]),)
        if tt == 0:
            wl = (W(stS, cvt_done_st),)
        dmaS(xa, x_q[128 * tt:128 * (tt + 1), :], sem=ldX[tt % 2], waits=wl)
        xt = abf(O_XT, 4096)
        a_done = layer_norm_apply(
            xa, xt, 4096,
            (W(ldX[tt % 2]),) + ((W(smm),) if tt > 0 else ()))
        apply_done[tt] = a_done
        chmajor_from_tok(xt, tt, lambda k: ar1(k), a_done)

    if _dbg_stop('p1'):
        pass
    O_WQA = 90112
    p1_v = sve.n
    for k in range(KT):
        dmaG(abf(O_WQA + 3072 * k, 1536), wq_aT[128 * k:128 * (k + 1), :],
             waits=(W(sve, p1_v),) if k == 0 else ())
    wqa_ld = ldG.n
    for j in range(QLT):
        for tc in range(2):
            ps = psB if (2 * j + tc) % 2 == 0 else psC
            for k in range(KT):
                mm(ps[:, 0:512],
                   abf(O_WQA + 3072 * k + 256 * j, 128),
                   ar1(k, 512 * tc, 512),
                   start=(k == 0), stop=(k == KT - 1),
                   waits=((W(ldG, wqa_ld), W(sve)) if k == 0 else ()),
                   incs=((smm, 1),) if k == KT - 1 else ())
            emit("vector",
                 lambda e, j=j, tc=tc, p=ps: e.tensor_copy(
                     qlra(j, 512 * tc, 512), p[:, 0:512]),
                 waits=(W(smm),), incs=((sve, 1),))
    O_SQ = 188416
    rowv = af32(189440, 1024)
    for tc in range(2):
        psq = psD if tc == 0 else psB
        for j in range(QLT):
            sq = abf(O_SQ, 512)
            emit("vector",
                 lambda e, j=j, tc=tc, s=sq: e.tensor_mul(
                     s, qlra(j, 512 * tc, 512), qlra(j, 512 * tc, 512)),
                 waits=(W(sve), W(smm)), incs=((sve, 1),))
            mm(psq[0:1, 0:512], ones_col, sq,
               start=(j == 0), stop=(j == QLT - 1),
               waits=(W(sve),), incs=((smm, 1),))
        emit("scalar", lambda e, tc=tc, p=psq: e.activation(
            out=rowv[0:1, 512 * tc:512 * (tc + 1)], in_=p[0:1, 0:512],
            func=AF.Sqrt, bias=float(RMS_EPS), scale=float(1.0 / QL)),
            waits=(W(smm),), incs=((sac, 1),))
    emit("vector", lambda e: e.reciprocal(rowv[0:1, 0:1024],
                                          rowv[0:1, 0:1024]),
         waits=(W(sac),), incs=((sve, 1),))
    emit("sync", lambda e: e.dma_start(out=rq_sc, in_=rowv[0:1, 0:1024]),
         waits=(W(sve),), incs=((stS, 16),))
    dmaS(srt[:, 0:NT], rq_sc.rearrange("o (a p) -> p (o a)", p=128),
         waits=(W(stS),), slow=True)
    emit("vector", lambda e: e.tensor_scalar(
        out=srt[:, 0:NT], in0=srt[:, 0:NT], scalar1=float(SCALE), scalar2=None,
        op0=ALU.mult), waits=(W(ldS),), incs=((sve, 1),))

    if _dbg_stop('p2'):
        pass
    O_WKA = 90112
    O_XLC = 126976
    O_CKV = 159744
    O_KPE = 176128
    O_KPO = 180224
    O_KPB = 184320
    O_RSB = 188416
    O_TMP1 = 135168
    O_TMP2 = 139264
    rowv = af32(143360, 2048)
    p2_v, p2_m = sve.n, smm.n
    for k in range(KT):
        dmaG(abf(O_WKA + 1152 * k, 576), wkv_aT[128 * k:128 * (k + 1), :],
             waits=(W(sve, p2_v), W(smm, p2_m)) if k == 0 else ())
    wka_ld = ldG.n
    O_X3 = 0
    O_XT3 = 16384
    for cc in range(NCC):
        for st4 in range(4):
            ct = 4 * cc + st4
            xa = af32(O_X3, 4096)
            wl = (W(sve),) if ct > 0 else (W(sve, p2_v),)
            dmaS(xa, x_b[128 * ct:128 * (ct + 1), :], sem=ldX3, waits=wl)
            xt = abf(O_XT3, 4096)
            a_done = layer_norm_apply(xa, xt, 4096, (W(ldX3), W(smm)))
            chmajor_from_tok(xt, st4, lambda k: abf(O_XLC + 1024 * k, 512),
                             a_done)
        ewait("tensor", sve)
        for c in range(6):
            ps = [psA[:, 0:512], psA[:, 512:1024], psA[:, 1024:1536],
                  psA[:, 1536:2048], psB[:, 0:512], psD[:, 0:512]][c]
            mcols = 128 if c < 4 else 32
            coff = 256 * c if c < 4 else 1024 + 64 * (c - 4)
            for k in range(KT):
                wl = ()
                if k == 0:
                    wl = (W(sve),)
                    if cc == 0 and c == 0:
                        wl = (W(ldG, wka_ld), W(sve))
                mm(ps[0:mcols, 0:512],
                   abf(O_WKA + 1152 * k + coff, mcols),
                   abf(O_XLC + 1024 * k, 512),
                   start=(k == 0), stop=(k == KT - 1),
                   waits=wl,
                   incs=((smm, 1),) if k == KT - 1 else ())
            if c < 4:
                emit("vector", lambda e, c=c, cc=cc, p=ps: e.tensor_copy(
                    abf(O_CKV + 4096 * c + 1024 * cc, 512), p[:, 0:512]),
                    waits=(W(smm),), incs=((sve, 1),))
            else:
                emit("vector", lambda e, c=c, cc=cc, p=ps: e.tensor_copy(
                    abf((O_KPE if c == 4 else O_KPO) + 1024 * cc, 512)
                    [0:32, :],
                    p[0:32, 0:512]),
                    waits=(W(smm),), incs=((sve, 1),))
    kpe = abf(O_KPE, 2048)
    kpo = abf(O_KPO, 2048)
    tmp1 = abf(O_TMP1, 2048)
    tmp2 = abf(O_TMP2, 2048)
    kpb = abf(O_KPB, 2048)
    emit("vector", lambda e: e.tensor_mul(tmp1[0:32, :], kpe[0:32, :],
                                          ropk_c[:, :]),
         waits=(W(sve),), incs=((sve, 1),))
    emit("vector", lambda e: e.tensor_mul(tmp2[0:32, :], kpo[0:32, :],
                                          ropk_s[:, :]), incs=((sve, 1),))
    emit("vector", lambda e: e.tensor_sub(kpb[0:32, :], tmp1[0:32, :],
                                          tmp2[0:32, :]),
         waits=(W(sve),), incs=((sve, 1),))
    emit("vector", lambda e: e.tensor_mul(tmp1[0:32, :], kpe[0:32, :],
                                          ropk_s[:, :]),
         waits=(W(sve),), incs=((sve, 1),))
    emit("vector", lambda e: e.tensor_mul(tmp2[0:32, :], kpo[0:32, :],
                                          ropk_c[:, :]), incs=((sve, 1),))
    emit("vector", lambda e: e.tensor_add(kpb[32:64, :], tmp1[0:32, :],
                                          tmp2[0:32, :]),
         waits=(W(sve),), incs=((sve, 1),))
    for c in range(KVLT):
        sq = abf(O_XLC, 2048)
        emit("vector", lambda e, c=c, s=sq: e.tensor_mul(
            s, abf(O_CKV + 4096 * c, 2048), abf(O_CKV + 4096 * c, 2048)),
            waits=(W(sve), W(smm)), incs=((sve, 1),))
        for cc in range(NCC):
            mm(psA[0:1, 512 * cc:512 * (cc + 1)], ones_col,
               sq[:, 512 * cc:512 * (cc + 1)],
               start=(c == 0), stop=(c == KVLT - 1),
               waits=(W(sve),) if cc == 0 else (), incs=((smm, 1),))
    for cc in range(NCC):
        emit("scalar", lambda e, cc=cc: e.activation(
            out=rowv[0:1, 512 * cc:512 * (cc + 1)],
            in_=psA[0:1, 512 * cc:512 * (cc + 1)],
            func=AF.Sqrt, bias=float(RMS_EPS), scale=float(1.0 / KVL)),
            waits=(W(smm),), incs=((sac, 1),))
    emit("vector", lambda e: e.reciprocal(rowv[0:1, 0:2048],
                                          rowv[0:1, 0:2048]),
         waits=(W(sac),), incs=((sve, 1),))
    emit("sync", lambda e: e.dma_start(out=rkv_sc, in_=rowv[0:1, 0:2048]),
         waits=(W(sve),), incs=((stS, 16),))
    rsb = af32(O_RSB, 2048)
    emit("gpsimd", lambda e: e.dma_start(
        out=rsb, in_=rkv_sc.broadcast_to([128, CTX])),
        waits=(W(stS), W(sve)), incs=((ldG, 16),))
    for c in range(KVLT):
        emit("vector", lambda e, c=c: e.tensor_mul(
            abf(O_CKV + 4096 * c, 2048), abf(O_CKV + 4096 * c, 2048), rsb),
            waits=(W(ldG),) if c == 0 else (), incs=((sve, 1),))
    ckv_ready_v = sve.n

    if _dbg_stop('p3'):
        pass
    O_WKVB = 90112
    O_WQB = 94208
    O_KN = 103424
    O_VT = 111616
    O_QN = 119808
    O_QPF = 123904
    O_QPB = 128000
    O_PRB = 132096
    O_PRT = 136192
    O_MSK = 140288

    def arK(h2, n0, n):
        return abf(O_KN + 4096 * h2 + 2 * n0, n)

    def arV(ct, n0=0, n=256):
        return abf(O_VT + 512 * ct + 2 * n0, n)

    def arQN(h2, t0, n):
        return abf(O_QN + 2048 * h2 + 2 * t0, n)

    O_PRB2 = 152576
    probsT = abf(O_PRT, 2048)
    maskb = abf(O_MSK, 2048)
    O_SC = [144384, 188416]
    p4it = {"i": 0, "evac": {}, "exp": {}, "pend": None}

    def emit_trav(hh, tt, par, pnorm_v):
        prb = abf([O_PRB, O_PRB2][par], 2048)
        h2 = hh % 2
        for b4 in range(4):
            for j in range(4):
                ct = 4 * b4 + j
                wl = ()
                if j == 0:
                    wl = (W(sve, pnorm_v), W(sve)) if b4 == 0 else (W(sve),)
                tr(psT[:, 128 * j:128 * (j + 1)],
                   prb[:, 128 * ct:128 * (ct + 1)],
                   waits=wl,
                   incs=((smm, 1),) if j == 3 else ())
            emit("vector", lambda e, b4=b4: e.tensor_copy(
                probsT[:, 512 * b4:512 * (b4 + 1)], psT[:, 0:512]),
                waits=(W(smm),), incs=((sve, 1),))
        for ct in range(NCT):
            mm(psD[:, 0:128],
               arV(ct, 128 * h2, 128),
               probsT[:, 128 * ct:128 * (ct + 1)],
               start=(ct == 0), stop=(ct == NCT - 1),
               waits=(W(sve),) if ct == 0 else (),
               incs=((smm, 1),) if ct == NCT - 1 else ())
        emit("vector", lambda e, hh=hh, tt=tt: e.tensor_copy(
            ar1(hh, 128 * tt, 128), psD[:, 0:128]),
            waits=(W(smm),), incs=((sve, 1),))

    def flush_trav():
        if p4it["pend"] is not None:
            emit_trav(*p4it["pend"])
            p4it["pend"] = None

    for pr in range(NPAIR):
        flush_trav()
        pm0 = smm.n
        for c in range(KVLT):
            wl = ()
            if c == 0:
                wl = (W(sve, ckv_ready_v),) if pr == 0 else (W(smm, pm0),)
            dmaG(abf(O_WKVB + 1024 * c, 512),
                 wkv_bT[128 * c:128 * (c + 1), 512 * pr:512 * (pr + 1)],
                 waits=wl)
        for c in range(QLT):
            dmaG(abf(O_WQB + 768 * c, 384),
                 wq_bT[128 * c:128 * (c + 1), 384 * pr:384 * (pr + 1)])
        wld = ldG.n
        for c in range(KVLT):
            emit("vector", lambda e, c=c: e.tensor_scalar(
                out=abf(O_WKVB + 1024 * c, 512), in0=abf(O_WKVB + 1024 * c, 512),
                scalar1=kvnw_sb[:, c:c + 1], scalar2=None, op0=ALU.mult),
                waits=(W(ldG, wld),) if c == 0 else (), incs=((sve, 1),))
        for c in range(QLT):
            emit("vector", lambda e, c=c: e.tensor_scalar(
                out=abf(O_WQB + 768 * c, 384), in0=abf(O_WQB + 768 * c, 384),
                scalar1=qnw_sb[:, c:c + 1], scalar2=None, op0=ALU.mult),
                incs=((sve, 1),))
        wfold = sve.n

        for h2 in range(2):
            for tc in range(2):
                ps = psB if tc == 0 else psC
                for c in range(QLT):
                    mm(ps[:, 0:512],
                       abf(O_WQB + 768 * c + 384 * h2, 128),
                       qlra(c, 512 * tc, 512),
                       start=(c == 0), stop=(c == QLT - 1),
                       waits=(W(sve, wfold), W(sve)) if c == 0 else (),
                       incs=((smm, 1),) if c == QLT - 1 else ())
                emit("vector", lambda e, h2=h2, tc=tc, p=ps: e.tensor_copy(
                    arQN(h2, 512 * tc, 512), p[:, 0:512]),
                    waits=(W(smm),), incs=((sve, 1),))
            for pe_half in range(2):
                ps = psB if pe_half == 0 else psC
                for tc in range(2):
                    for c in range(QLT):
                        mm(ps[0:32, 512 * 0:512] if False else
                           ps[0:32, 0:512],
                           abf(O_WQB + 768 * c + 384 * h2 + 256
                               + 64 * pe_half, 32),
                           qlra(c, 512 * tc, 512),
                           start=(c == 0), stop=(c == QLT - 1),
                           waits=(W(sve),) if c == 0 else (),
                           incs=((smm, 1),) if c == QLT - 1 else ())
                    emit("vector", lambda e, tc=tc, p=ps, ph=pe_half:
                         e.tensor_copy(
                             abf(O_QPF + 2048 * ph + 1024 * tc, 512)
                             [0:32, :],
                             p[0:32, 0:512]),
                         waits=(W(smm),), incs=((sve, 1),))
            qpe = abf(O_QPF, 1024)
            qpo = abf(O_QPF + 2048, 1024)
            qt1 = abf(O_PRB, 1024)
            qt2 = abf(O_PRB + 2048, 1024)
            qpb = abf(O_QPB + 2048 * h2, 1024)
            emit("vector", lambda e, qpe=qpe, qt1=qt1: e.tensor_mul(
                qt1[0:32, :], qpe[0:32, :], ropq_c[:, :]),
                waits=(W(sve), W(smm)), incs=((sve, 1),))
            emit("vector", lambda e, qpo=qpo, qt2=qt2: e.tensor_mul(
                qt2[0:32, :], qpo[0:32, :], ropq_s[:, :]),
                incs=((sve, 1),))
            emit("vector", lambda e, qpb=qpb, qt1=qt1, qt2=qt2: e.tensor_sub(
                qpb[0:32, :], qt1[0:32, :], qt2[0:32, :]),
                waits=(W(sve),), incs=((sve, 1),))
            emit("vector", lambda e, qpe=qpe, qt1=qt1: e.tensor_mul(
                qt1[0:32, :], qpe[0:32, :], ropq_s[:, :]),
                waits=(W(sve),), incs=((sve, 1),))
            emit("vector", lambda e, qpo=qpo, qt2=qt2: e.tensor_mul(
                qt2[0:32, :], qpo[0:32, :], ropq_c[:, :]),
                incs=((sve, 1),))
            emit("vector", lambda e, qpb=qpb, qt1=qt1, qt2=qt2: e.tensor_add(
                qpb[32:64, :], qt1[0:32, :], qt2[0:32, :]),
                waits=(W(sve),), incs=((sve, 1),))

        for h2 in range(2):
            for cc in range(NCC):
                ps = psB if cc % 2 == 0 else psC
                for c in range(KVLT):
                    mm(ps[:, 0:512],
                       abf(O_WKVB + 1024 * c + 256 * h2, 128),
                       abf(O_CKV + 4096 * c + 1024 * cc, 512),
                       start=(c == 0), stop=(c == KVLT - 1),
                       waits=(W(sve),) if c == 0 else (),
                       incs=((smm, 1),) if c == KVLT - 1 else ())
                emit("vector", lambda e, h2=h2, cc=cc, p=ps: e.tensor_copy(
                    arK(h2, 512 * cc, 512), p[:, 0:512]),
                    waits=(W(smm),), incs=((sve, 1),))
        for ct in range(NCT):
            ps = psB if ct % 2 == 0 else psC
            for c in range(KVLT):
                mm(ps[:, 0:256],
                   abf(O_CKV + 4096 * c + 256 * ct, 128),
                   abf(O_WKVB + 1024 * c + 512, 256),
                   start=(c == 0), stop=(c == KVLT - 1),
                   waits=(W(sve),) if c == 0 else (),
                   incs=((smm, 1),) if c == KVLT - 1 else ())
            emit("vector", lambda e, ct=ct, p=ps: e.tensor_copy(
                arV(ct), p[:, 0:256]),
                waits=(W(smm),), incs=((sve, 1),))
        pair_ready_v = sve.n

        for tt in range(NT):
            dmaS(maskb, mask_in[tt, :, :], sem=ldM, waits=(W(sve),))
            mask_ld = ldM.n
            for h2 in range(2):
                hh = 2 * pr + h2
                it = p4it["i"]
                p4it["i"] = it + 1
                sc = af32(O_SC[it % 2], 2048)
                for cc in range(NCC):
                    wl = ()
                    if cc == 0:
                        wl = (W(sve, p4it["evac"].get(it - 1, 0)),
                              W(sve, pair_ready_v) if (tt == 0 and h2 == 0)
                              else (sve, 0))
                    mm(psA[:, 512 * cc:512 * (cc + 1)],
                       arQN(h2, 128 * tt, 128),
                       arK(h2, 512 * cc, 512),
                       start=True, stop=False, waits=wl)
                    mm(psA[:, 512 * cc:512 * (cc + 1)],
                       abf(O_QPB + 2048 * h2 + 256 * tt, 128)[0:64, :],
                       kpb[0:64, 512 * cc:512 * (cc + 1)],
                       start=False, stop=True,
                       incs=((smm, 1),) if cc == NCC - 1 else ())
                emit("vector", lambda e, sc=sc: e.tensor_add(
                    sc, psA[:, 0:2048], maskb),
                    waits=(W(smm), W(ldM, mask_ld),
                           W(sac, p4it["exp"].get(it - 2, 0))),
                    incs=((sve, 1),))
                p4it["evac"][it] = sve.n
                emit("vector", lambda e, sc=sc: e.reduce_max(
                    sm[:, 0:1], sc, axis=AX.X),
                    waits=(W(sve),), incs=((sve, 1),))
                emit("vector", lambda e, tt=tt: e.tensor_scalar(
                    out=sm[:, 1:2], in0=sm[:, 0:1],
                    scalar1=srt[:, tt:tt + 1], scalar2=-1.0,
                    op0=ALU.mult, op1=ALU.mult),
                    waits=(W(sve),), incs=((sve, 1),))
                prb = abf([O_PRB, O_PRB2][it % 2], 2048)
                for cc in range(NCC):
                    emit("scalar",
                         lambda e, cc=cc, tt=tt, sc=sc, prb=prb: e.activation(
                             out=prb[:, 512 * cc:512 * (cc + 1)],
                             in_=sc[:, 512 * cc:512 * (cc + 1)],
                             func=AF.Exp, bias=sm[:, 1:2],
                             scale=srt[:, tt:tt + 1],
                             accum_out=sm[:, 4 + cc:5 + cc]),
                         waits=(W(sve),) if cc == 0 else (),
                         incs=((sac, 1),))
                p4it["exp"][it] = sac.n
                emit("vector", lambda e: e.reduce_sum(
                    sm[:, 2:3], sm[:, 4:8], axis=AX.X),
                    waits=(W(sac),), incs=((sve, 1),))
                emit("vector", lambda e: e.reciprocal(sm[:, 3:4], sm[:, 2:3]),
                     waits=(W(sve),), incs=((sve, 1),))
                emit("vector", lambda e, prb=prb: e.tensor_scalar(
                    out=prb, in0=prb, scalar1=sm[:, 3:4], scalar2=None,
                    op0=ALU.mult),
                    waits=(W(sve),), incs=((sve, 1),))
                pnorm_v = sve.n
                pend = p4it["pend"]
                p4it["pend"] = (hh, tt, it % 2, pnorm_v)
                if pend is not None:
                    emit_trav(*pend)

    if _dbg_stop('p4'):
        pass
    flush_trav()

    O_OO = 65536
    O_WO = 131072
    O_XF0 = 147456
    O_XF1 = 163840
    O_X1T = 180224
    O_XT5 = 188416

    def oot(ht, t0=0, n=1024):
        return abf(O_OO + 2048 * ht + 2 * t0, n)

    p4_v = sve.n
    wo_mm_done = {}
    wosrc = woT.rearrange("(v p) h -> p v h", p=128)
    for ht in range(KT):
        pse, pso = (psB, psC) if ht % 2 == 0 else \
            (psA[:, 0:512], psA[:, 512:1024])
        par = ht % 2
        woblk = abf(O_WO + 8192 * par, 4096)
        woblk3 = woblk.rearrange("p (v h) -> p v h", h=128)
        for si in range(4):
            wl = ()
            if si == 0:
                if ht >= 2:
                    wl = (W(smm, wo_mm_done[ht - 2]),)
                elif ht == 0:
                    wl = (W(sve, p4_v),)
            emit("gpsimd", lambda e, si=si, ht=ht, woblk3=woblk3:
                 e.dma_start(
                     out=woblk3[:, 8 * si:8 * (si + 1), :],
                     in_=wosrc[:, 8 * si:8 * (si + 1),
                               128 * ht:128 * (ht + 1)]),
                 waits=wl, incs=((ldW[par], 16),))
        wo_ld = ldW[par].n
        for vc in range(KT):
            wob = abf(O_WO + 8192 * par + 256 * vc, 128)
            for tc in range(2):
                ps = pse if tc == 0 else pso
                mm(ps[:, 0:512] if ht % 2 == 0 else ps,
                   wob, ar1(vc, 512 * tc, 512),
                   start=(vc == 0), stop=(vc == KT - 1),
                   waits=((W(ldW[par], wo_ld), W(sve))
                          if vc == 0 else ()) if tc == 0 else (),
                   incs=((smm, 1),) if tc == 1 else ())
        wo_mm_done[ht] = smm.n
        for tc in range(2):
            ps = pse if tc == 0 else pso
            emit("vector", lambda e, ht=ht, tc=tc,
                 p=(pse if tc == 0 else pso), ish=(ht % 2 == 0):
                 e.tensor_copy(oot(ht, 512 * tc, 512),
                               p[:, 0:512] if ish else p),
                 waits=(W(smm),) if tc == 0 else (), incs=((sve, 1),))
    spill_prev = stS.n
    for tt in range(NT):
        xf = af32(O_XF0 if tt % 2 == 0 else O_XF1, 4096)
        dmaS(xf, x_q[128 * tt:128 * (tt + 1), :], sem=ldX[tt % 2],
             waits=(W(sve),))
        xf_ld = ldX[tt % 2].n
        x1tile = abf(O_X1T, 4096)
        for k4 in range(8):
            for j in range(4):
                ht = 4 * k4 + j
                wl = ()
                if j == 0:
                    wl = (W(sve), W(stS, spill_prev)) if k4 == 0 else (W(sve),)
                tr(psT[:, 128 * j:128 * (j + 1)],
                   oot(ht, 128 * tt, 128), waits=wl,
                   incs=((smm, 1),) if j == 3 else ())
            emit("vector", lambda e, k4=k4, x1tile=x1tile, xf=xf:
                 e.tensor_add(
                     x1tile[:, 512 * k4:512 * (k4 + 1)], psT[:, 0:512],
                     xf[:, 512 * k4:512 * (k4 + 1)]),
                 waits=(W(smm), W(ldX[tt % 2], xf_ld)), incs=((sve, 1),))
        xt5 = abf(O_XT5, 4096)
        a_done = layer_norm_apply(x1tile, xt5, 4096, (W(sve),))
        chmajor_from_tok(xt5, tt, lambda k: ar1(k), a_done)
        emit("sync", lambda e, tt=tt, x1tile=x1tile: e.dma_start(
            out=x1_sc[128 * tt:128 * (tt + 1), :], in_=x1tile),
            waits=(W(sve, a_done),), incs=((stS, 16),))
        spill_prev = stS.n
    x1_spilled = stS.n

    if _dbg_stop('p5'):
        pass
    slot_state = {"off": 0, "slot_bytes": 0}

    def dequant_unit(region_off, q8_flat, q8_off, s_flat, z_flat, s_off,
                     nk, n, par, w_memset=(), kseg=8):
        lq, lz = ldQp[par], ldZp[par]
        reg = abf(region_off, nk * n)
        emit("vector", lambda e, reg=reg: e.memset(reg, -8.0),
             waits=w_memset, incs=((sve, 1),))
        ms_v = sve.n
        emit("gpsimd", lambda e, q8=q8_flat, o=q8_off, reg=reg, m=nk * n:
             e.dma_start(out=reg, in_=q8[:, o:o + m], accum_op=ALU.add,
                         max_dma_last_dim=2048),
             waits=(W(sve, ms_v),), incs=((lq, 16),))
        q_tot = lq.n
        segs = [(k0, min(kseg, nk - k0)) for k0 in range(0, nk, kseg)]
        mu_done = []
        for si, (k0, kn) in enumerate(segs):
            lb = ldBp[2 * par + si % 2]
            slot = abf(slot_state["off"]
                       + slot_state["slot_bytes"] * (si % 2), kn * n)
            emit("gpsimd",
                 lambda e, slot=slot, sf=s_flat, o=s_off + k0 * n, m=kn * n:
                     e.dma_start(out=slot,
                                 in_=sf[0:1, o:o + m].broadcast_to([128, m])),
                 waits=(W(sve, mu_done[si - 2]),) if si >= 2 else (),
                 incs=((lb, 16),))
            s_tot = lb.n
            emit("vector",
                 lambda e, a=abf(region_off + 2 * k0 * n, kn * n), slot=slot:
                     e.tensor_mul(a, a, slot),
                 waits=(W(lq, q_tot), W(lb, s_tot)), incs=((sve, 1),))
            mu_done.append(sve.n)
        mu_all = sve.n
        emit("gpsimd", lambda e, zf=z_flat, o=s_off, reg=reg, m=nk * n:
             e.dma_start(out=reg,
                         in_=zf[0:1, o:o + m].broadcast_to([128, m]),
                         accum_op=ALU.add, max_dma_last_dim=2048),
             waits=(W(sve, mu_all),), incs=((lz, 16),))
        return lz.n

    O_WFF = 65536

    def wff(half, k, n=512):
        return abf(O_WFF + 32768 * half + 2 * n * k, n)
    O_SG = 131072
    O_HB = 139264
    O_HT = 140288
    O_SS6 = 148480
    slot_state["off"] = O_SS6
    slot_state["slot_bytes"] = 8192

    p5_v, p5_m = sve.n, smm.n
    units = []
    for j in range(NNC):
        units.append(("g", j))
        units.append(("u", j))
    unit_mm_done = {}
    spill_cnt = {"n": stS.n}
    for ui, (mat, j) in enumerate(units):
        half = ui % 2
        n0, n = 512 * j, _ncw(j)
        q8 = gate_q8 if mat == "g" else up_q8
        s_fl = sbf["gs"] if mat == "g" else sbf["us"]
        z_fl = sbf["gz"] if mat == "g" else sbf["uz"]
        if ui >= 2:
            wm = (W(smm, unit_mm_done[ui - 2]),)
        else:
            wm = (W(sve, p5_v), W(smm, p5_m), W(stS, x1_spilled))
        s_m = sbf["gs"] if mat == "g" else sbf["us"]
        z_m = sbf["gz"] if mat == "g" else sbf["uz"]
        zl = dequant_unit(O_WFF + 32768 * half, q8, KT * n0,
                          s_m, z_m, KT * n0, KT, n, half, w_memset=wm)
        ps_pool = [psA[:, 0:512], psA[:, 512:1024], psA[:, 1024:1536],
                   psA[:, 1536:2048]]
        for tt in range(NT):
            ps = ps_pool[tt % 4]
            for k in range(KT):
                wl = ()
                if k == 0:
                    wl = (W(sac), W(sve))
                    if tt == 0:
                        wl = (W(ldZp[half], zl), W(sac), W(sve))
                mm(ps[:, 0:n], ar1(k, 128 * tt, 128), wff(half, k, n),
                   start=(k == 0), stop=(k == KT - 1),
                   waits=wl, incs=((smm, 1),) if k == KT - 1 else ())
            if mat == "g":
                emit("scalar", lambda e, tt=tt, p=ps, n=n: e.activation(
                    out=abf(O_SG + 1024 * tt, n), in_=p[:, 0:n],
                    func=AF.Silu, bias=0.0, scale=1.0),
                    waits=(W(smm), W(sve)), incs=((sac, 1),))
            else:
                hb = abf(O_HB, 512)
                emit("vector", lambda e, tt=tt, p=ps, n=n, hb=hb: e.tensor_mul(
                    hb[:, 0:n], abf(O_SG + 1024 * tt, n), p[:, 0:n]),
                    waits=(W(smm), W(sac)), incs=((sve, 1),))
                hmul_v = sve.n
                nk4 = (n + 127) // 128
                for jj in range(nk4):
                    tr(psT[:, 128 * jj:128 * (jj + 1)],
                       hb[:, 128 * jj:128 * (jj + 1)],
                       waits=(W(sve, hmul_v), W(sve)) if jj == 0 else (),
                       incs=((smm, 1),) if jj == nk4 - 1 else ())
                for jj in range(nk4):
                    emit("vector", lambda e, jj=jj, tt=tt: e.tensor_copy(
                        abf(O_HT + 2048 * jj + 256 * tt, 128),
                        psT[:, 128 * jj:128 * (jj + 1)]),
                        waits=(W(smm), W(stS, spill_cnt["n"]))
                        if jj == 0 else (),
                        incs=((sve, 1),) if jj == nk4 - 1 else ())
        unit_mm_done[ui] = smm.n
        if mat == "u":
            nk4 = (n + 127) // 128
            for jj in range(nk4):
                it_g = 4 * j + jj
                emit("sync", lambda e, jj=jj, it_g=it_g: e.dma_start(
                    out=hT_sc[128 * it_g:128 * (it_g + 1), :],
                    in_=abf(O_HT + 2048 * jj, 1024)),
                    waits=(W(sve),), incs=((stS, 16),))
            spill_cnt["n"] = stS.n
    h_spilled = stS.n
    p6_m = smm.n

    O_HT7 = 0

    def ht7(buf, i, n=128):
        return abf(O_HT7 + 22528 * buf + 256 * i, n)
    O_X1S = 45056
    O_OUTB = 49152
    O_WD = 65536

    def wdd(buf, i, n=256):
        return abf(O_WD + 44032 * buf + 512 * i, n)
    O_SS7 = 155648
    slot_state["off"] = O_SS7
    slot_state["slot_bytes"] = 8192

    p6_v = sve.n
    NDCQ = 16
    dcq_mm = {}
    ht_segs = [(0, 16), (16, 16), (32, 16), (48, 16), (64, 16), (80, 6)]
    ob_store = {0: 0, 1: 0}
    tt_mm = {}
    for dq in range(NDCQ):
        wbuf = dq % 2
        if dq >= 2:
            wm = (W(smm, dcq_mm[dq - 2]),)
        elif dq == 0:
            wm = (W(sve, p6_v), W(smm, p6_m), W(stS, h_spilled))
        else:
            wm = ()
        deq_done = dequant_unit(O_WD + 44032 * wbuf,
                                down_q8, IT * 256 * dq,
                                sbf["ds"], sbf["dz"], IT * 256 * dq,
                                IT, 256, wbuf, w_memset=wm, kseg=16)
        for tt in range(NT):
            buf = tt % 2
            key = (dq, tt)
            pkey = (dq, tt - 2) if tt >= 2 else (dq - 1, tt + 6)
            if pkey in tt_mm:
                hw = (W(smm, tt_mm[pkey]),)
            else:
                hw = (W(smm, p6_m), W(stS, h_spilled))
            for si, (i0, ni) in enumerate(ht_segs):
                hsrc = hT_sc[128 * i0:128 * (i0 + ni),
                             128 * tt:128 * (tt + 1)]
                hsrc = hsrc.rearrange("(i p) c -> p i c", p=128)
                dst = abf(O_HT7 + 22528 * buf + 256 * i0, ni * 128)
                dst = dst.rearrange("p (i c) -> p i c", c=128)
                dmaS(dst, hsrc, sem=ldH[buf], waits=hw if si == 0 else ())
            dmaS(abf(O_X1S + 1024 * buf, 256),
                 x1_sc[128 * tt:128 * (tt + 1), 256 * dq:256 * (dq + 1)],
                 sem=ldH[buf],
                 waits=(W(stS, x1_spilled),) if (dq == 0 and tt < 2)
                 else (W(sve),))
            h_ld = ldH[buf].n
            ps = psB if tt % 2 == 0 else psC
            for i in range(IT):
                mm(ps[:, 0:256], ht7(buf, i), wdd(wbuf, i),
                   start=(i == 0), stop=(i == IT - 1),
                   waits=(W(ldH[buf], h_ld), W(ldZp[wbuf], deq_done),
                          W(sve))
                   if i == 0 else (),
                   incs=((smm, 1),) if i == IT - 1 else ())
            tt_mm[key] = smm.n
            ob = af32(O_OUTB + 2048 * buf, 256)
            emit("vector", lambda e, p=ps, ob=ob, buf=buf: e.tensor_add(
                ob, p[:, 0:256], abf(O_X1S + 1024 * buf, 256)),
                waits=(W(smm), W(oSp[buf], ob_store[buf])), incs=((sve, 1),))
            emit("sync", lambda e, ob=ob, tt=tt, dq=dq: e.dma_start(
                out=out_t[128 * tt:128 * (tt + 1),
                          256 * dq:256 * (dq + 1)],
                in_=ob),
                waits=(W(sve),), incs=((oSp[buf], 16),))
            ob_store[buf] = oSp[buf].n
        dcq_mm[dq] = smm.n

    ewait("sync", oSp[0])
    ewait("sync", oSp[1])

    with nc.Block() as block:
        @block.tensor
        def _(eng):
            for f in ops["tensor"]:
                f(eng)

        @block.vector
        def _(eng):
            for f in ops["vector"]:
                f(eng)

        @block.scalar
        def _(eng):
            for f in ops["scalar"]:
                f(eng)

        @block.gpsimd
        def _(eng):
            for f in ops["gpsimd"]:
                f(eng)

        @block.sync
        def _(eng):
            for f in ops["sync"]:
                f(eng)

    return nc



_CACHE = {}


def _rope_perm_cols():
    return np.concatenate([np.arange(0, 64, 2), np.arange(1, 64, 2)])


def _prep_weights(inp):
    w = {}
    w["wq_aT"] = np.ascontiguousarray(inp["L0_attn_wq_a"].T)
    w["q_nw"] = np.ascontiguousarray(inp["L0_attn_q_norm_w"].reshape(QL, 1))
    wqbT = np.ascontiguousarray(inp["L0_attn_wq_b"].T)
    perm = np.zeros(NH * 192, dtype=np.int64)
    rp = _rope_perm_cols()
    for h in range(NH):
        b = 192 * h
        perm[b:b + 128] = b + np.arange(128)
        perm[b + 128:b + 192] = b + 128 + rp
    w["wq_bT"] = np.ascontiguousarray(wqbT[:, perm])
    wkvaT = np.ascontiguousarray(inp["L0_attn_wkv_a"].T)
    perm2 = np.concatenate([np.arange(KVL), KVL + rp])
    w["wkv_aT"] = np.ascontiguousarray(wkvaT[:, perm2])
    w["kv_nw"] = np.ascontiguousarray(inp["L0_attn_kv_norm_w"].reshape(KVL, 1))
    wkvbT = np.ascontiguousarray(inp["L0_attn_wkv_b"].T)
    perm3 = np.zeros(NH * 256, dtype=np.int64)
    for p in range(NPAIR):
        b = 512 * p
        h0, h1 = 2 * p, 2 * p + 1
        perm3[b:b + 128] = 256 * h0 + np.arange(128)
        perm3[b + 128:b + 256] = 256 * h1 + np.arange(128)
        perm3[b + 256:b + 384] = 256 * h0 + 128 + np.arange(128)
        perm3[b + 384:b + 512] = 256 * h1 + 128 + np.arange(128)
    w["wkv_bT"] = np.ascontiguousarray(wkvbT[:, perm3])
    w["woT"] = np.ascontiguousarray(inp["L0_attn_wo"].T)
    def q8_cm(q, chunks):
        q8 = np.ascontiguousarray(q).view(np.int8)[:, ::4]
        G = q8.shape[0] // 128
        qt = np.ascontiguousarray(q8).reshape(G, 128, q8.shape[1])
        blocks = [qt[:, :, n0:n0 + n].transpose(1, 0, 2).reshape(128, -1)
                  for (n0, n) in chunks]
        return np.ascontiguousarray(np.concatenate(blocks, axis=1))

    def sz_cm(a, chunks):
        blocks = [np.ascontiguousarray(a[:, n0:n0 + n]).reshape(-1)
                  for (n0, n) in chunks]
        return np.ascontiguousarray(np.concatenate(blocks)).reshape(1, -1)

    gch = [(512 * j, min(512, II - 512 * j)) for j in range(NNC)]
    dch = [(256 * j, 256) for j in range(16)]
    w["gate_q8"] = q8_cm(inp["L0_ffn_gate_q"], gch)
    w["up_q8"] = q8_cm(inp["L0_ffn_up_q"], gch)
    w["down_q8"] = q8_cm(inp["L0_ffn_down_q"], dch)
    w["gate_s"] = sz_cm(inp["L0_ffn_gate_s"], gch)
    w["gate_z"] = sz_cm(inp["L0_ffn_gate_z"], gch)
    w["up_s"] = sz_cm(inp["L0_ffn_up_s"], gch)
    w["up_z"] = sz_cm(inp["L0_ffn_up_z"], gch)
    w["down_s"] = sz_cm(inp["L0_ffn_down_s"], dch)
    w["down_z"] = sz_cm(inp["L0_ffn_down_z"], dch)
    return w


def _rope_tables(p0, n):
    freqs = (1.0 / 10000.0 ** (np.arange(0, 64, 2, dtype=np.float64) / 64.0))
    t = np.arange(p0, p0 + n, dtype=np.float64)
    ang = freqs[:, None] * t[None, :]
    return np.concatenate([np.cos(ang), np.sin(ang)], axis=0).astype(BF)


def _mask(p0):
    s = np.arange(CTX)
    m = np.zeros((NT, 128, CTX), dtype=np.float32)
    for tt in range(NT):
        qpos = p0 + 128 * tt + np.arange(128)
        m[tt] = np.where(s[None, :] <= qpos[:, None], 0.0, -1e9)
    return m.astype(BF)


def kernel(L0_attn_x, L0_attn_wq_a, L0_attn_q_norm_w, L0_attn_wq_b,
           L0_attn_wkv_a, L0_attn_kv_norm_w, L0_attn_wkv_b, L0_attn_wo,
           L0_ffn_gate_q, L0_ffn_gate_s, L0_ffn_gate_z, L0_ffn_up_q,
           L0_ffn_up_s, L0_ffn_up_z, L0_ffn_down_q, L0_ffn_down_s,
           L0_ffn_down_z):
    inp = dict(L0_attn_wq_a=np.asarray(L0_attn_wq_a),
               L0_attn_q_norm_w=np.asarray(L0_attn_q_norm_w),
               L0_attn_wq_b=np.asarray(L0_attn_wq_b),
               L0_attn_wkv_a=np.asarray(L0_attn_wkv_a),
               L0_attn_kv_norm_w=np.asarray(L0_attn_kv_norm_w),
               L0_attn_wkv_b=np.asarray(L0_attn_wkv_b),
               L0_attn_wo=np.asarray(L0_attn_wo),
               L0_ffn_gate_q=np.asarray(L0_ffn_gate_q),
               L0_ffn_gate_s=np.asarray(L0_ffn_gate_s),
               L0_ffn_gate_z=np.asarray(L0_ffn_gate_z),
               L0_ffn_up_q=np.asarray(L0_ffn_up_q),
               L0_ffn_up_s=np.asarray(L0_ffn_up_s),
               L0_ffn_up_z=np.asarray(L0_ffn_up_z),
               L0_ffn_down_q=np.asarray(L0_ffn_down_q),
               L0_ffn_down_s=np.asarray(L0_ffn_down_s),
               L0_ffn_down_z=np.asarray(L0_ffn_down_z))
    x = np.asarray(L0_attn_x)
    w = _prep_weights(inp)

    if "nc" not in _CACHE:
        _CACHE["nc"] = build_nc()
    nc = _CACHE["nc"]

    in_maps = []
    for c in range(8):
        b, half = c // 2, c % 2
        p0 = 1024 * half
        m = dict(w)
        m["x_q"] = np.ascontiguousarray(x[b, p0:p0 + T, :])
        m["x_b"] = np.ascontiguousarray(x[b])
        m["rope_q"] = _rope_tables(p0, T)
        m["rope_k"] = _rope_tables(0, CTX)
        m["mask_in"] = _mask(p0)
        in_maps.append(m)

    res = run_bass_kernel_spmd(nc, in_maps, core_ids=list(range(8)))
    out = np.empty((4, CTX, D), dtype=np.float32)
    for c in range(8):
        b, half = c // 2, c % 2
        out[b, 1024 * half:1024 * half + T, :] = res.results[c]["out"]
    return out


# version 06
# speedup vs baseline: 1.2667x; 1.1912x over previous
import numpy as np
import ml_dtypes
import concourse.bass as bass
import concourse.mybir as mybir
from concourse.bass_utils import run_bass_kernel_spmd

F32 = mybir.dt.float32
BF16 = mybir.dt.bfloat16
I8 = mybir.dt.int8
AF = mybir.ActivationFunctionType
ALU = mybir.AluOpType
AX = mybir.AxisListType
BF = ml_dtypes.bfloat16

D = 4096
T = 1024
NT = 8
CTX = 2048
NCC = 4
NCT = 16
KT = 32
QL = 1536
QLT = 12
KVL = 512
KVLT = 4
NH = 32
NPAIR = 16
II = 11008
IT = 86
NNC = 22
NDC = 8
SCALE = float(1.0 / np.sqrt(192.0))
LN_EPS = 1e-5
RMS_EPS = 1e-6


def _ncw(j):
    return min(512, II - 512 * j)


def build_nc(dbg_phase=None):
    nc = bass.Bass("TRN2", target_bir_lowering=False)

    def din(name, shape, dt):
        return nc.dram_tensor(name, list(shape), dt, kind="ExternalInput").ap()

    x_q = din("x_q", (T, D), F32)
    x_b = din("x_b", (CTX, D), F32)
    wq_aT = din("wq_aT", (D, QL), F32)
    q_nw = din("q_nw", (QL, 1), F32)
    wq_bT = din("wq_bT", (QL, NH * 192), F32)
    wkv_aT = din("wkv_aT", (D, KVL + 64), F32)
    kv_nw = din("kv_nw", (KVL, 1), F32)
    wkv_bT = din("wkv_bT", (KVL, NH * 256), F32)
    woT = din("woT", (D, D), F32)
    gate_q8 = din("gate_q8", (128, KT * II), I8)
    up_q8 = din("up_q8", (128, KT * II), I8)
    down_q8 = din("down_q8", (128, IT * D), I8)
    gate_s = din("gate_s", (1, KT * II), F32)
    gate_z = din("gate_z", (1, KT * II), F32)
    up_s = din("up_s", (1, KT * II), F32)
    up_z = din("up_z", (1, KT * II), F32)
    down_s = din("down_s", (1, IT * D), F32)
    down_z = din("down_z", (1, IT * D), F32)
    rope_q = din("rope_q", (64, T), BF16)
    rope_k = din("rope_k", (64, CTX), BF16)
    mask_in = din("mask_in", (NT, 128, CTX), BF16)

    out_t = nc.dram_tensor("out", [T, D], F32, kind="ExternalOutput").ap()
    dbg_t = nc.dram_tensor("dbg_arena", [128, 192 * 1024], mybir.dt.uint8,
                           kind="ExternalOutput").ap()

    sbf = {}
    for nm, tot in [("gs", KT * II), ("gz", KT * II), ("us", KT * II),
                    ("uz", KT * II), ("ds", IT * D), ("dz", IT * D)]:
        sbf[nm] = nc.dram_tensor("sc_" + nm, [1, tot], BF16, kind="Internal").ap()
    hT_sc = nc.dram_tensor("hT_sc", [II, T], BF16, kind="Internal").ap()
    x1_sc = nc.dram_tensor("x1_sc", [T, D], BF16, kind="Internal").ap()
    rq_sc = nc.dram_tensor("rq_sc", [1, T], F32, kind="Internal").ap()
    rkv_sc = nc.dram_tensor("rkv_sc", [1, CTX], F32, kind="Internal").ap()

    ident = nc.alloc_sbuf_tensor("ident", [128, 128], BF16).ap()
    ones_col = nc.alloc_sbuf_tensor("ones_col", [128, 1], BF16).ap()
    qnw_sb = nc.alloc_sbuf_tensor("qnw_sb", [128, QLT], F32).ap()
    kvnw_sb = nc.alloc_sbuf_tensor("kvnw_sb", [128, KVLT], F32).ap()
    ropq_c = nc.alloc_sbuf_tensor("ropq_c", [32, T], BF16).ap()
    ropq_s = nc.alloc_sbuf_tensor("ropq_s", [32, T], BF16).ap()
    ropk_c = nc.alloc_sbuf_tensor("ropk_c", [32, CTX], BF16).ap()
    ropk_s = nc.alloc_sbuf_tensor("ropk_s", [32, CTX], BF16).ap()
    stat = nc.alloc_sbuf_tensor("stat", [128, 64], F32).ap()
    mv = nc.alloc_sbuf_tensor("mv", [128, 8], F32).ap()
    srt = nc.alloc_sbuf_tensor("srt", [128, NT], F32).ap()
    sm = nc.alloc_sbuf_tensor("sm", [128, 16], F32).ap()
    cst = nc.alloc_sbuf_tensor("cst", [128, 8], F32).ap()
    CONSTS = [float(LN_EPS), float(RMS_EPS), 1.0, 0.0,
              float(1.0 / QL), float(1.0 / KVL)]
    for ci, cv in enumerate(CONSTS):
        nc.const_aps.register(F32, cv, cst[:, ci:ci + 1])

    AR_BYTES = 192 * 1024
    arena_t = nc.alloc_sbuf_tensor("arena", [128, AR_BYTES], mybir.dt.uint8)
    ar_bf = arena_t.ap().bitcast(BF16)
    ar_f32 = arena_t.ap().bitcast(F32)

    def abf(off, n):
        assert off % 2 == 0 and off + 2 * n <= AR_BYTES, (off, n)
        return ar_bf[:, off // 2: off // 2 + n]

    def af32(off, n):
        assert off % 4 == 0 and off + 4 * n <= AR_BYTES, (off, n)
        return ar_f32[:, off // 4: off // 4 + n]

    def ar1(tile, t0=0, n=1024):
        return abf(2048 * tile + 2 * t0, n)

    def qlra(j, c0=0, n=1024):
        return abf(65536 + 2048 * j + 2 * c0, n)

    psA = nc.alloc_psum_tensor("psA", [128, 2048], F32).ap()
    psB = nc.alloc_psum_tensor("psB", [128, 512], F32).ap()
    psC = nc.alloc_psum_tensor("psC", [128, 512], F32).ap()
    psT = nc.alloc_psum_tensor("psT", [128, 1024], BF16).ap()
    psD = nc.alloc_psum_tensor("psD", [128, 512], F32).ap()

    class Sem:
        __slots__ = ("h", "n")

        def __init__(self, name):
            self.h = nc.alloc_semaphore(name)
            self.n = 0

    ldS = Sem("ldS")
    ldG = Sem("ldG")
    ldX = [Sem("ldX0"), Sem("ldX1")]
    ldX3 = Sem("ldX3")
    ldM = Sem("ldM")
    ldXS = Sem("ldXS")
    ldW = [Sem("ldW0"), Sem("ldW1")]
    ldH = [Sem("ldH0"), Sem("ldH1")]
    ldQp = [Sem("ldQ0"), Sem("ldQ1")]
    ldBp = [Sem("ldB0"), Sem("ldB1"), Sem("ldB2"), Sem("ldB3")]
    ldZp = [Sem("ldZ0"), Sem("ldZ1")]
    stS = Sem("stS")
    oSp = [Sem("oS0"), Sem("oS1")]
    smm = Sem("smm")
    sve = Sem("sve")
    sac = Sem("sac")
    spo = Sem("spo")

    ops = {"tensor": [], "vector": [], "scalar": [], "gpsimd": [], "sync": []}
    stopped = {"v": False}

    def emit(eng, build, waits=(), incs=()):
        if stopped["v"]:
            for s, a in incs:
                s.n += a
            return
        wl = tuple((s.h, (s.n if v is None else v)) for (s, v) in waits)
        il = []
        for s, a in incs:
            s.n += a
            il.append((s.h, a))
        il = tuple(il)

        def f(e, build=build, wl=wl, il=il):
            for h, v in wl:
                e.wait_ge(h, v)
            ins = build(e)
            for h, a in il:
                ins = ins.then_inc(h, a)
        ops[eng].append(f)

    def W(sem, val=None):
        return (sem, val)

    def ewait(eng, sem, val=None):
        if stopped["v"]:
            return
        v = sem.n if val is None else val
        ops[eng].append(lambda e, h=sem.h, v=v: e.wait_ge(h, v))

    def dmaS(dst, src, sem=ldS, waits=(), slow=False):
        if slow:
            emit("sync", lambda e, d=dst, s=src: e.dma_start(
                out=d, in_=s, allow_slow_non_contiguous=True),
                waits=waits, incs=((sem, 16),))
        else:
            emit("sync", lambda e, d=dst, s=src: e.dma_start(out=d, in_=s),
                 waits=waits, incs=((sem, 16),))

    def dmaG(dst, src, accum=None, sem=ldG, waits=()):
        if accum is None:
            emit("gpsimd", lambda e, d=dst, s=src: e.dma_start(out=d, in_=s),
                 waits=waits, incs=((sem, 16),))
        else:
            emit("gpsimd",
                 lambda e, d=dst, s=src, a=accum:
                     e.dma_start(out=d, in_=s, accum_op=a),
                 waits=waits, incs=((sem, 16),))

    def mm(out, lhsT, rhs, start, stop, waits=(), incs=()):
        emit("tensor",
             lambda e, o=out, l=lhsT, r=rhs, s1=start, s2=stop:
                 e.matmul(o, l, r, start=s1, stop=s2),
             waits=waits, incs=incs)

    def tr(out, in_, waits=(), incs=()):
        emit("tensor", lambda e, o=out, i=in_: e.transpose(o, i, ident),
             waits=waits, incs=incs)

    for ci, cv in enumerate(CONSTS):
        emit("vector", lambda e, ci=ci, cv=cv: e.memset(cst[:, ci:ci + 1], cv),
             incs=((sve, 1),))
    if dbg_phase is not None:
        emit("vector", lambda e: e.memset(ar_f32[:, :], 0.0),
             incs=((sve, 1),))
    emit("gpsimd", lambda e: e.memset(ident, 0.0), incs=((spo, 1),))
    emit("gpsimd", lambda e: e.memset(ones_col, 1.0), incs=((spo, 1),))
    ewait("gpsimd", spo)
    emit("gpsimd", lambda e: e.affine_select(
        out=ident, in_=ident, compare_op=ALU.not_equal, fill=1.0,
        base=0, pattern=[[-1, 128]], channel_multiplier=1), incs=((spo, 1),))

    dmaS(qnw_sb[:, 0:QLT], q_nw.rearrange("(j p) o -> p (j o)", p=128),
         slow=True)
    dmaS(kvnw_sb[:, 0:KVLT], kv_nw.rearrange("(j p) o -> p (j o)", p=128),
         slow=True)
    dmaS(ropq_c[:, :], rope_q[0:32, :])
    dmaS(ropq_s[:, :], rope_q[32:64, :])
    dmaS(ropk_c[:, :], rope_k[0:32, :])
    dmaS(ropk_s[:, :], rope_k[32:64, :])

    cvt_jobs = [(gate_s, sbf["gs"], KT * II), (gate_z, sbf["gz"], KT * II),
                (up_s, sbf["us"], KT * II), (up_z, sbf["uz"], KT * II),
                (down_s, sbf["ds"], IT * D), (down_z, sbf["dz"], IT * D)]
    for src, dst, tot in cvt_jobs:
        per = tot // 128
        cv_f = af32(0, per)
        cv_b = abf(4 * per, per)
        flat = src.rearrange("o (p m) -> (o p) m", p=128)
        dmaS(cv_f, flat, waits=(W(sve),))
        emit("vector", lambda e, a=cv_b, b=cv_f: e.tensor_copy(a, b),
             waits=(W(ldS),), incs=((sve, 1),))
        emit("sync", lambda e, d=dst, s=cv_b:
             e.dma_start(out=d.rearrange("o (p m) -> p (o m)", p=128), in_=s),
             waits=(W(sve),), incs=((stS, 16),))
        ewait("vector", stS)
    cvt_done_st = stS.n

    def layer_norm_apply(src, dst_bf, width, pre_waits):
        nch = width // 512
        for j in range(nch):
            emit("vector",
                 lambda e, s=src, j=j: e.bn_stats(
                     stat[:, 6 * j:6 * j + 6], s[:, 512 * j:512 * (j + 1)]),
                 waits=pre_waits if j == 0 else (), incs=((sve, 1),))
        emit("vector", lambda e, n=nch: e.bn_aggr(mv[:, 0:2], stat[:, 0:6 * n]),
             waits=(W(sve),), incs=((sve, 1),))
        emit("scalar", lambda e: e.activation(
            out=mv[:, 2:3], in_=mv[:, 1:2], func=AF.Sqrt,
            bias=float(LN_EPS), scale=1.0),
            waits=(W(sve),), incs=((sac, 1),))
        emit("vector", lambda e: e.reciprocal(mv[:, 3:4], mv[:, 2:3]),
             waits=(W(sac),), incs=((sve, 1),))
        emit("vector", lambda e, s=src, d=dst_bf: e.tensor_scalar(
            out=d, in0=s, scalar1=mv[:, 0:1], scalar2=mv[:, 3:4],
            op0=ALU.subtract, op1=ALU.mult),
            waits=(W(sve),), incs=((sve, 1),))
        return sve.n

    def chmajor_from_tok(src_bf, tt, dst_fn, src_done_sve, nk=KT):
        for k4 in range((nk + 3) // 4):
            ks = list(range(4 * k4, min(4 * k4 + 4, nk)))
            for idx, k in enumerate(ks):
                wl = ()
                if idx == 0:
                    wl = (W(sve, src_done_sve), W(sve)) if k4 == 0 else (W(sve),)
                tr(psT[:, 128 * idx:128 * (idx + 1)],
                   src_bf[:, 128 * k:128 * (k + 1)],
                   waits=wl,
                   incs=((smm, 1),) if idx == len(ks) - 1 else ())
            for idx, k in enumerate(ks):
                emit("vector",
                     lambda e, k=k, idx=idx, d=dst_fn, tt=tt: e.tensor_copy(
                         d(k)[:, 128 * tt:128 * (tt + 1)],
                         psT[:, 128 * idx:128 * (idx + 1)]),
                     waits=(W(smm),) if idx == 0 else (),
                     incs=((sve, 1),) if idx == len(ks) - 1 else ())


    def _dbg_stop(tag):
        if dbg_phase != tag or stopped["v"]:
            return False
        ewait("sync", sve)
        ewait("sync", smm)
        ewait("sync", sac)
        emit("sync", lambda e: e.dma_start(out=dbg_t, in_=arena_t.ap()),
             waits=(), incs=((stS, 16),))
        ewait("sync", stS)
        stopped["v"] = True
        return True

    ewait("tensor", spo)
    O_XA = 90112
    O_XB2 = 106496
    O_XT = 122880
    apply_done = {}
    for tt in range(NT):
        xa = af32(O_XA if tt % 2 == 0 else O_XB2, 4096)
        wl = ()
        if tt >= 2:
            wl = (W(sve, apply_done[tt - 2]),)
        if tt == 0:
            wl = (W(stS, cvt_done_st),)
        dmaS(xa, x_q[128 * tt:128 * (tt + 1), :], sem=ldX[tt % 2], waits=wl)
        xt = abf(O_XT, 4096)
        a_done = layer_norm_apply(
            xa, xt, 4096,
            (W(ldX[tt % 2]),) + ((W(smm),) if tt > 0 else ()))
        apply_done[tt] = a_done
        chmajor_from_tok(xt, tt, lambda k: ar1(k), a_done)

    if _dbg_stop('p1'):
        pass
    O_WQA = 90112
    p1_v = sve.n
    for k in range(KT):
        dmaG(abf(O_WQA + 3072 * k, 1536), wq_aT[128 * k:128 * (k + 1), :],
             waits=(W(sve, p1_v),) if k == 0 else ())
    wqa_ld = ldG.n
    for j in range(QLT):
        for tc in range(2):
            ps = psB if (2 * j + tc) % 2 == 0 else psC
            for k in range(KT):
                mm(ps[:, 0:512],
                   abf(O_WQA + 3072 * k + 256 * j, 128),
                   ar1(k, 512 * tc, 512),
                   start=(k == 0), stop=(k == KT - 1),
                   waits=((W(ldG, wqa_ld), W(sve)) if k == 0 else ()),
                   incs=((smm, 1),) if k == KT - 1 else ())
            emit("vector",
                 lambda e, j=j, tc=tc, p=ps: e.tensor_copy(
                     qlra(j, 512 * tc, 512), p[:, 0:512]),
                 waits=(W(smm),), incs=((sve, 1),))
    O_SQ = 188416
    rowv = af32(189440, 1024)
    for tc in range(2):
        psq = psD if tc == 0 else psB
        for j in range(QLT):
            sq = abf(O_SQ, 512)
            emit("vector",
                 lambda e, j=j, tc=tc, s=sq: e.tensor_mul(
                     s, qlra(j, 512 * tc, 512), qlra(j, 512 * tc, 512)),
                 waits=(W(sve), W(smm)), incs=((sve, 1),))
            mm(psq[0:1, 0:512], ones_col, sq,
               start=(j == 0), stop=(j == QLT - 1),
               waits=(W(sve),), incs=((smm, 1),))
        emit("scalar", lambda e, tc=tc, p=psq: e.activation(
            out=rowv[0:1, 512 * tc:512 * (tc + 1)], in_=p[0:1, 0:512],
            func=AF.Sqrt, bias=float(RMS_EPS), scale=float(1.0 / QL)),
            waits=(W(smm),), incs=((sac, 1),))
    emit("vector", lambda e: e.reciprocal(rowv[0:1, 0:1024],
                                          rowv[0:1, 0:1024]),
         waits=(W(sac),), incs=((sve, 1),))
    emit("sync", lambda e: e.dma_start(out=rq_sc, in_=rowv[0:1, 0:1024]),
         waits=(W(sve),), incs=((stS, 16),))
    dmaS(srt[:, 0:NT], rq_sc.rearrange("o (a p) -> p (o a)", p=128),
         waits=(W(stS),), slow=True)
    emit("vector", lambda e: e.tensor_scalar(
        out=srt[:, 0:NT], in0=srt[:, 0:NT], scalar1=float(SCALE), scalar2=None,
        op0=ALU.mult), waits=(W(ldS),), incs=((sve, 1),))

    if _dbg_stop('p2'):
        pass
    O_WKA = 90112
    O_XLC = 126976
    O_CKV = 159744
    O_KPE = 176128
    O_KPO = 180224
    O_KPB = 184320
    O_RSB = 188416
    O_TMP1 = 135168
    O_TMP2 = 139264
    rowv = af32(143360, 2048)
    p2_v, p2_m = sve.n, smm.n
    for k in range(KT):
        dmaG(abf(O_WKA + 1152 * k, 576), wkv_aT[128 * k:128 * (k + 1), :],
             waits=(W(sve, p2_v), W(smm, p2_m)) if k == 0 else ())
    wka_ld = ldG.n
    O_X3 = 0
    O_XT3 = 16384
    for cc in range(NCC):
        for st4 in range(4):
            ct = 4 * cc + st4
            xa = af32(O_X3, 4096)
            wl = (W(sve),) if ct > 0 else (W(sve, p2_v),)
            dmaS(xa, x_b[128 * ct:128 * (ct + 1), :], sem=ldX3, waits=wl)
            xt = abf(O_XT3, 4096)
            a_done = layer_norm_apply(xa, xt, 4096, (W(ldX3), W(smm)))
            chmajor_from_tok(xt, st4, lambda k: abf(O_XLC + 1024 * k, 512),
                             a_done)
        ewait("tensor", sve)
        for c in range(6):
            ps = [psA[:, 0:512], psA[:, 512:1024], psA[:, 1024:1536],
                  psA[:, 1536:2048], psB[:, 0:512], psD[:, 0:512]][c]
            mcols = 128 if c < 4 else 32
            coff = 256 * c if c < 4 else 1024 + 64 * (c - 4)
            for k in range(KT):
                wl = ()
                if k == 0:
                    wl = (W(sve),)
                    if cc == 0 and c == 0:
                        wl = (W(ldG, wka_ld), W(sve))
                mm(ps[0:mcols, 0:512],
                   abf(O_WKA + 1152 * k + coff, mcols),
                   abf(O_XLC + 1024 * k, 512),
                   start=(k == 0), stop=(k == KT - 1),
                   waits=wl,
                   incs=((smm, 1),) if k == KT - 1 else ())
            if c < 4:
                emit("vector", lambda e, c=c, cc=cc, p=ps: e.tensor_copy(
                    abf(O_CKV + 4096 * c + 1024 * cc, 512), p[:, 0:512]),
                    waits=(W(smm),), incs=((sve, 1),))
            else:
                emit("vector", lambda e, c=c, cc=cc, p=ps: e.tensor_copy(
                    abf((O_KPE if c == 4 else O_KPO) + 1024 * cc, 512)
                    [0:32, :],
                    p[0:32, 0:512]),
                    waits=(W(smm),), incs=((sve, 1),))
    kpe = abf(O_KPE, 2048)
    kpo = abf(O_KPO, 2048)
    tmp1 = abf(O_TMP1, 2048)
    tmp2 = abf(O_TMP2, 2048)
    kpb = abf(O_KPB, 2048)
    emit("vector", lambda e: e.tensor_mul(tmp1[0:32, :], kpe[0:32, :],
                                          ropk_c[:, :]),
         waits=(W(sve),), incs=((sve, 1),))
    emit("vector", lambda e: e.tensor_mul(tmp2[0:32, :], kpo[0:32, :],
                                          ropk_s[:, :]), incs=((sve, 1),))
    emit("vector", lambda e: e.tensor_sub(kpb[0:32, :], tmp1[0:32, :],
                                          tmp2[0:32, :]),
         waits=(W(sve),), incs=((sve, 1),))
    emit("vector", lambda e: e.tensor_mul(tmp1[0:32, :], kpe[0:32, :],
                                          ropk_s[:, :]),
         waits=(W(sve),), incs=((sve, 1),))
    emit("vector", lambda e: e.tensor_mul(tmp2[0:32, :], kpo[0:32, :],
                                          ropk_c[:, :]), incs=((sve, 1),))
    emit("vector", lambda e: e.tensor_add(kpb[32:64, :], tmp1[0:32, :],
                                          tmp2[0:32, :]),
         waits=(W(sve),), incs=((sve, 1),))
    for c in range(KVLT):
        sq = abf(O_XLC, 2048)
        emit("vector", lambda e, c=c, s=sq: e.tensor_mul(
            s, abf(O_CKV + 4096 * c, 2048), abf(O_CKV + 4096 * c, 2048)),
            waits=(W(sve), W(smm)), incs=((sve, 1),))
        for cc in range(NCC):
            mm(psA[0:1, 512 * cc:512 * (cc + 1)], ones_col,
               sq[:, 512 * cc:512 * (cc + 1)],
               start=(c == 0), stop=(c == KVLT - 1),
               waits=(W(sve),) if cc == 0 else (), incs=((smm, 1),))
    for cc in range(NCC):
        emit("scalar", lambda e, cc=cc: e.activation(
            out=rowv[0:1, 512 * cc:512 * (cc + 1)],
            in_=psA[0:1, 512 * cc:512 * (cc + 1)],
            func=AF.Sqrt, bias=float(RMS_EPS), scale=float(1.0 / KVL)),
            waits=(W(smm),), incs=((sac, 1),))
    emit("vector", lambda e: e.reciprocal(rowv[0:1, 0:2048],
                                          rowv[0:1, 0:2048]),
         waits=(W(sac),), incs=((sve, 1),))
    emit("sync", lambda e: e.dma_start(out=rkv_sc, in_=rowv[0:1, 0:2048]),
         waits=(W(sve),), incs=((stS, 16),))
    rsb = af32(O_RSB, 2048)
    emit("gpsimd", lambda e: e.dma_start(
        out=rsb, in_=rkv_sc.broadcast_to([128, CTX])),
        waits=(W(stS), W(sve)), incs=((ldG, 16),))
    for c in range(KVLT):
        emit("vector", lambda e, c=c: e.tensor_mul(
            abf(O_CKV + 4096 * c, 2048), abf(O_CKV + 4096 * c, 2048), rsb),
            waits=(W(ldG),) if c == 0 else (), incs=((sve, 1),))
    ckv_ready_v = sve.n

    if _dbg_stop('p3'):
        pass
    O_WKVB = 90112
    O_WQB = 94208
    O_KN = 103424
    O_VT = 111616
    O_QN = 119808
    O_QPF = 123904
    O_QPB = 128000
    O_PRB = 132096
    O_PRT = 136192
    O_MSK = 140288

    def arK(h2, n0, n):
        return abf(O_KN + 4096 * h2 + 2 * n0, n)

    def arV(ct, n0=0, n=256):
        return abf(O_VT + 512 * ct + 2 * n0, n)

    def arQN(h2, t0, n):
        return abf(O_QN + 2048 * h2 + 2 * t0, n)

    O_PRB2 = 152576
    O_PRB3 = 176128
    PRBS = [O_PRB, O_PRB2, O_PRB3]
    probsT = abf(O_PRT, 2048)
    maskb = abf(O_MSK, 2048)
    O_SC = [144384, 188416]
    p4it = {"i": 0, "evac": {}, "exp": {}, "pend": []}

    def emit_trav(hh, tt, par, pnorm_v):
        prb = abf(PRBS[par], 2048)
        h2 = hh % 2
        for b4 in range(4):
            for j in range(4):
                ct = 4 * b4 + j
                wl = ()
                if j == 0:
                    wl = (W(sve, pnorm_v), W(sac), W(sve)) if b4 == 0 \
                        else (W(sac),)
                tr(psT[:, 128 * j:128 * (j + 1)],
                   prb[:, 128 * ct:128 * (ct + 1)],
                   waits=wl,
                   incs=((smm, 1),) if j == 3 else ())
            emit("scalar", lambda e, b4=b4: e.copy(
                probsT[:, 512 * b4:512 * (b4 + 1)], psT[:, 0:512]),
                waits=(W(smm),), incs=((sac, 1),))
        for ct in range(NCT):
            mm(psD[:, 0:128],
               arV(ct, 128 * h2, 128),
               probsT[:, 128 * ct:128 * (ct + 1)],
               start=(ct == 0), stop=(ct == NCT - 1),
               waits=(W(sac),) if ct == 0 else (),
               incs=((smm, 1),) if ct == NCT - 1 else ())
        emit("vector", lambda e, hh=hh, tt=tt: e.tensor_copy(
            ar1(hh, 128 * tt, 128), psD[:, 0:128]),
            waits=(W(smm),), incs=((sve, 1),))

    def flush_trav():
        while p4it["pend"]:
            emit_trav(*p4it["pend"].pop(0))

    for pr in range(NPAIR):
        flush_trav()
        pm0 = smm.n
        for c in range(KVLT):
            wl = ()
            if c == 0:
                wl = (W(sve, ckv_ready_v),) if pr == 0 else (W(smm, pm0),)
            dmaG(abf(O_WKVB + 1024 * c, 512),
                 wkv_bT[128 * c:128 * (c + 1), 512 * pr:512 * (pr + 1)],
                 waits=wl)
        for c in range(QLT):
            dmaG(abf(O_WQB + 768 * c, 384),
                 wq_bT[128 * c:128 * (c + 1), 384 * pr:384 * (pr + 1)])
        wld = ldG.n
        for c in range(KVLT):
            emit("vector", lambda e, c=c: e.tensor_scalar(
                out=abf(O_WKVB + 1024 * c, 512), in0=abf(O_WKVB + 1024 * c, 512),
                scalar1=kvnw_sb[:, c:c + 1], scalar2=None, op0=ALU.mult),
                waits=(W(ldG, wld),) if c == 0 else (), incs=((sve, 1),))
        for c in range(QLT):
            emit("vector", lambda e, c=c: e.tensor_scalar(
                out=abf(O_WQB + 768 * c, 384), in0=abf(O_WQB + 768 * c, 384),
                scalar1=qnw_sb[:, c:c + 1], scalar2=None, op0=ALU.mult),
                incs=((sve, 1),))
        wfold = sve.n

        for h2 in range(2):
            for tc in range(2):
                ps = psB if tc == 0 else psC
                for c in range(QLT):
                    mm(ps[:, 0:512],
                       abf(O_WQB + 768 * c + 384 * h2, 128),
                       qlra(c, 512 * tc, 512),
                       start=(c == 0), stop=(c == QLT - 1),
                       waits=(W(sve, wfold), W(sve)) if c == 0 else (),
                       incs=((smm, 1),) if c == QLT - 1 else ())
                emit("vector", lambda e, h2=h2, tc=tc, p=ps: e.tensor_copy(
                    arQN(h2, 512 * tc, 512), p[:, 0:512]),
                    waits=(W(smm),), incs=((sve, 1),))
            for pe_half in range(2):
                ps = psB if pe_half == 0 else psC
                for tc in range(2):
                    for c in range(QLT):
                        mm(ps[0:32, 512 * 0:512] if False else
                           ps[0:32, 0:512],
                           abf(O_WQB + 768 * c + 384 * h2 + 256
                               + 64 * pe_half, 32),
                           qlra(c, 512 * tc, 512),
                           start=(c == 0), stop=(c == QLT - 1),
                           waits=(W(sve),) if c == 0 else (),
                           incs=((smm, 1),) if c == QLT - 1 else ())
                    emit("vector", lambda e, tc=tc, p=ps, ph=pe_half:
                         e.tensor_copy(
                             abf(O_QPF + 2048 * ph + 1024 * tc, 512)
                             [0:32, :],
                             p[0:32, 0:512]),
                         waits=(W(smm),), incs=((sve, 1),))
            qpe = abf(O_QPF, 1024)
            qpo = abf(O_QPF + 2048, 1024)
            qt1 = abf(O_PRB, 1024)
            qt2 = abf(O_PRB + 2048, 1024)
            qpb = abf(O_QPB + 2048 * h2, 1024)
            emit("vector", lambda e, qpe=qpe, qt1=qt1: e.tensor_mul(
                qt1[0:32, :], qpe[0:32, :], ropq_c[:, :]),
                waits=(W(sve), W(smm)), incs=((sve, 1),))
            emit("vector", lambda e, qpo=qpo, qt2=qt2: e.tensor_mul(
                qt2[0:32, :], qpo[0:32, :], ropq_s[:, :]),
                incs=((sve, 1),))
            emit("vector", lambda e, qpb=qpb, qt1=qt1, qt2=qt2: e.tensor_sub(
                qpb[0:32, :], qt1[0:32, :], qt2[0:32, :]),
                waits=(W(sve),), incs=((sve, 1),))
            emit("vector", lambda e, qpe=qpe, qt1=qt1: e.tensor_mul(
                qt1[0:32, :], qpe[0:32, :], ropq_s[:, :]),
                waits=(W(sve),), incs=((sve, 1),))
            emit("vector", lambda e, qpo=qpo, qt2=qt2: e.tensor_mul(
                qt2[0:32, :], qpo[0:32, :], ropq_c[:, :]),
                incs=((sve, 1),))
            emit("vector", lambda e, qpb=qpb, qt1=qt1, qt2=qt2: e.tensor_add(
                qpb[32:64, :], qt1[0:32, :], qt2[0:32, :]),
                waits=(W(sve),), incs=((sve, 1),))

        for h2 in range(2):
            for cc in range(NCC):
                ps = psB if cc % 2 == 0 else psC
                for c in range(KVLT):
                    mm(ps[:, 0:512],
                       abf(O_WKVB + 1024 * c + 256 * h2, 128),
                       abf(O_CKV + 4096 * c + 1024 * cc, 512),
                       start=(c == 0), stop=(c == KVLT - 1),
                       waits=(W(sve),) if c == 0 else (),
                       incs=((smm, 1),) if c == KVLT - 1 else ())
                emit("vector", lambda e, h2=h2, cc=cc, p=ps: e.tensor_copy(
                    arK(h2, 512 * cc, 512), p[:, 0:512]),
                    waits=(W(smm),), incs=((sve, 1),))
        for ct in range(NCT):
            ps = psB if ct % 2 == 0 else psC
            for c in range(KVLT):
                mm(ps[:, 0:256],
                   abf(O_CKV + 4096 * c + 256 * ct, 128),
                   abf(O_WKVB + 1024 * c + 512, 256),
                   start=(c == 0), stop=(c == KVLT - 1),
                   waits=(W(sve),) if c == 0 else (),
                   incs=((smm, 1),) if c == KVLT - 1 else ())
            emit("vector", lambda e, ct=ct, p=ps: e.tensor_copy(
                arV(ct), p[:, 0:256]),
                waits=(W(smm),), incs=((sve, 1),))
        pair_ready_v = sve.n

        for tt in range(NT):
            dmaS(maskb, mask_in[tt, :, :], sem=ldM, waits=(W(sve),))
            mask_ld = ldM.n
            for h2 in range(2):
                hh = 2 * pr + h2
                it = p4it["i"]
                p4it["i"] = it + 1
                sc = af32(O_SC[it % 2], 2048)
                for cc in range(NCC):
                    wl = ()
                    if cc == 0:
                        wl = (W(sve, p4it["evac"].get(it - 1, 0)),
                              W(sve, pair_ready_v) if (tt == 0 and h2 == 0)
                              else (sve, 0))
                    mm(psA[:, 512 * cc:512 * (cc + 1)],
                       arQN(h2, 128 * tt, 128),
                       arK(h2, 512 * cc, 512),
                       start=True, stop=False, waits=wl)
                    mm(psA[:, 512 * cc:512 * (cc + 1)],
                       abf(O_QPB + 2048 * h2 + 256 * tt, 128)[0:64, :],
                       kpb[0:64, 512 * cc:512 * (cc + 1)],
                       start=False, stop=True,
                       incs=((smm, 1),) if cc == NCC - 1 else ())
                if len(p4it["pend"]) >= 2:
                    emit_trav(*p4it["pend"].pop(0))
                emit("vector", lambda e, sc=sc: e.tensor_add(
                    sc, psA[:, 0:2048], maskb),
                    waits=(W(smm), W(ldM, mask_ld),
                           W(sac, p4it["exp"].get(it - 2, 0))),
                    incs=((sve, 1),))
                p4it["evac"][it] = sve.n
                emit("vector", lambda e, sc=sc: e.reduce_max(
                    sm[:, 0:1], sc, axis=AX.X),
                    waits=(W(sve),), incs=((sve, 1),))
                emit("vector", lambda e, tt=tt: e.tensor_scalar(
                    out=sm[:, 1:2], in0=sm[:, 0:1],
                    scalar1=srt[:, tt:tt + 1], scalar2=-1.0,
                    op0=ALU.mult, op1=ALU.mult),
                    waits=(W(sve),), incs=((sve, 1),))
                prb = abf(PRBS[it % 3], 2048)
                for cc in range(NCC):
                    emit("scalar",
                         lambda e, cc=cc, tt=tt, sc=sc, prb=prb: e.activation(
                             out=prb[:, 512 * cc:512 * (cc + 1)],
                             in_=sc[:, 512 * cc:512 * (cc + 1)],
                             func=AF.Exp, bias=sm[:, 1:2],
                             scale=srt[:, tt:tt + 1],
                             accum_out=sm[:, 4 + cc:5 + cc]),
                         waits=(W(sve),) if cc == 0 else (),
                         incs=((sac, 1),))
                p4it["exp"][it] = sac.n
                emit("vector", lambda e: e.reduce_sum(
                    sm[:, 2:3], sm[:, 4:8], axis=AX.X),
                    waits=(W(sac),), incs=((sve, 1),))
                emit("vector", lambda e: e.reciprocal(sm[:, 3:4], sm[:, 2:3]),
                     waits=(W(sve),), incs=((sve, 1),))
                emit("vector", lambda e, prb=prb: e.tensor_scalar(
                    out=prb, in0=prb, scalar1=sm[:, 3:4], scalar2=None,
                    op0=ALU.mult),
                    waits=(W(sve),), incs=((sve, 1),))
                pnorm_v = sve.n
                p4it["pend"].append((hh, tt, it % 3, pnorm_v))

    if _dbg_stop('p4'):
        pass
    flush_trav()

    O_OO = 65536
    O_WO = 131072
    O_XF0 = 147456
    O_XF1 = 163840
    O_X1T = 180224
    O_XT5 = 188416

    def oot(ht, t0=0, n=1024):
        return abf(O_OO + 2048 * ht + 2 * t0, n)

    p4_v = sve.n
    wo_mm_done = {}
    wosrc = woT.rearrange("(v p) h -> p v h", p=128)
    for ht in range(KT):
        pse, pso = (psB, psC) if ht % 2 == 0 else \
            (psA[:, 0:512], psA[:, 512:1024])
        par = ht % 2
        woblk = abf(O_WO + 8192 * par, 4096)
        woblk3 = woblk.rearrange("p (v h) -> p v h", h=128)
        for si in range(4):
            wl = ()
            if si == 0:
                if ht >= 2:
                    wl = (W(smm, wo_mm_done[ht - 2]),)
                elif ht == 0:
                    wl = (W(sve, p4_v),)
            emit("gpsimd", lambda e, si=si, ht=ht, woblk3=woblk3:
                 e.dma_start(
                     out=woblk3[:, 8 * si:8 * (si + 1), :],
                     in_=wosrc[:, 8 * si:8 * (si + 1),
                               128 * ht:128 * (ht + 1)]),
                 waits=wl, incs=((ldW[par], 16),))
        wo_ld = ldW[par].n
        for vc in range(KT):
            wob = abf(O_WO + 8192 * par + 256 * vc, 128)
            for tc in range(2):
                ps = pse if tc == 0 else pso
                mm(ps[:, 0:512] if ht % 2 == 0 else ps,
                   wob, ar1(vc, 512 * tc, 512),
                   start=(vc == 0), stop=(vc == KT - 1),
                   waits=((W(ldW[par], wo_ld), W(sve))
                          if vc == 0 else ()) if tc == 0 else (),
                   incs=((smm, 1),) if tc == 1 else ())
        wo_mm_done[ht] = smm.n
        for tc in range(2):
            ps = pse if tc == 0 else pso
            emit("vector", lambda e, ht=ht, tc=tc,
                 p=(pse if tc == 0 else pso), ish=(ht % 2 == 0):
                 e.tensor_copy(oot(ht, 512 * tc, 512),
                               p[:, 0:512] if ish else p),
                 waits=(W(smm),) if tc == 0 else (), incs=((sve, 1),))
    spill_prev = stS.n
    for tt in range(NT):
        xf = af32(O_XF0 if tt % 2 == 0 else O_XF1, 4096)
        dmaS(xf, x_q[128 * tt:128 * (tt + 1), :], sem=ldX[tt % 2],
             waits=(W(sve),))
        xf_ld = ldX[tt % 2].n
        x1tile = abf(O_X1T, 4096)
        for k4 in range(8):
            for j in range(4):
                ht = 4 * k4 + j
                wl = ()
                if j == 0:
                    wl = (W(sve), W(stS, spill_prev)) if k4 == 0 else (W(sve),)
                tr(psT[:, 128 * j:128 * (j + 1)],
                   oot(ht, 128 * tt, 128), waits=wl,
                   incs=((smm, 1),) if j == 3 else ())
            emit("vector", lambda e, k4=k4, x1tile=x1tile, xf=xf:
                 e.tensor_add(
                     x1tile[:, 512 * k4:512 * (k4 + 1)], psT[:, 0:512],
                     xf[:, 512 * k4:512 * (k4 + 1)]),
                 waits=(W(smm), W(ldX[tt % 2], xf_ld)), incs=((sve, 1),))
        xt5 = abf(O_XT5, 4096)
        a_done = layer_norm_apply(x1tile, xt5, 4096, (W(sve),))
        chmajor_from_tok(xt5, tt, lambda k: ar1(k), a_done)
        emit("sync", lambda e, tt=tt, x1tile=x1tile: e.dma_start(
            out=x1_sc[128 * tt:128 * (tt + 1), :], in_=x1tile),
            waits=(W(sve, a_done),), incs=((stS, 16),))
        spill_prev = stS.n
    x1_spilled = stS.n

    if _dbg_stop('p5'):
        pass
    slot_state = {"off": 0, "slot_bytes": 0}

    def dequant_unit(region_off, q8_flat, q8_off, s_flat, z_flat, s_off,
                     nk, n, par, w_memset=(), kseg=8):
        lq, lz = ldQp[par], ldZp[par]
        reg = abf(region_off, nk * n)
        emit("vector", lambda e, reg=reg: e.memset(reg, -8.0),
             waits=w_memset, incs=((sve, 1),))
        ms_v = sve.n
        emit("gpsimd", lambda e, q8=q8_flat, o=q8_off, reg=reg, m=nk * n:
             e.dma_start(out=reg, in_=q8[:, o:o + m], accum_op=ALU.add,
                         max_dma_last_dim=2048),
             waits=(W(sve, ms_v),), incs=((lq, 16),))
        q_tot = lq.n
        segs = [(k0, min(kseg, nk - k0)) for k0 in range(0, nk, kseg)]
        mu_done = []
        for si, (k0, kn) in enumerate(segs):
            lb = ldBp[2 * par + si % 2]
            slot = abf(slot_state["off"]
                       + slot_state["slot_bytes"] * (si % 2), kn * n)
            emit("gpsimd",
                 lambda e, slot=slot, sf=s_flat, o=s_off + k0 * n, m=kn * n:
                     e.dma_start(out=slot,
                                 in_=sf[0:1, o:o + m].broadcast_to([128, m])),
                 waits=(W(sve, mu_done[si - 2]),) if si >= 2 else (),
                 incs=((lb, 16),))
            s_tot = lb.n
            emit("vector",
                 lambda e, a=abf(region_off + 2 * k0 * n, kn * n), slot=slot:
                     e.tensor_mul(a, a, slot),
                 waits=(W(lq, q_tot), W(lb, s_tot)), incs=((sve, 1),))
            mu_done.append(sve.n)
        mu_all = sve.n
        emit("gpsimd", lambda e, zf=z_flat, o=s_off, reg=reg, m=nk * n:
             e.dma_start(out=reg,
                         in_=zf[0:1, o:o + m].broadcast_to([128, m]),
                         accum_op=ALU.add, max_dma_last_dim=2048),
             waits=(W(sve, mu_all),), incs=((lz, 16),))
        return lz.n

    O_WFF = 65536

    def wff(half, k, n=512):
        return abf(O_WFF + 32768 * half + 2 * n * k, n)
    O_SG = 131072
    O_HB = 139264
    O_HT = 140288
    O_SS6 = 148480
    slot_state["off"] = O_SS6
    slot_state["slot_bytes"] = 8192

    p5_v, p5_m = sve.n, smm.n
    units = []
    for j in range(NNC):
        units.append(("g", j))
        units.append(("u", j))
    unit_mm_done = {}
    spill_cnt = {"n": stS.n}
    for ui, (mat, j) in enumerate(units):
        half = ui % 2
        n0, n = 512 * j, _ncw(j)
        q8 = gate_q8 if mat == "g" else up_q8
        s_fl = sbf["gs"] if mat == "g" else sbf["us"]
        z_fl = sbf["gz"] if mat == "g" else sbf["uz"]
        if ui >= 2:
            wm = (W(smm, unit_mm_done[ui - 2]),)
        else:
            wm = (W(sve, p5_v), W(smm, p5_m), W(stS, x1_spilled))
        s_m = sbf["gs"] if mat == "g" else sbf["us"]
        z_m = sbf["gz"] if mat == "g" else sbf["uz"]
        zl = dequant_unit(O_WFF + 32768 * half, q8, KT * n0,
                          s_m, z_m, KT * n0, KT, n, half, w_memset=wm)
        ps_pool = [psA[:, 0:512], psA[:, 512:1024], psA[:, 1024:1536],
                   psA[:, 1536:2048]]
        for tt in range(NT):
            ps = ps_pool[tt % 4]
            for k in range(KT):
                wl = ()
                if k == 0:
                    wl = (W(sac), W(sve))
                    if tt == 0:
                        wl = (W(ldZp[half], zl), W(sac), W(sve))
                mm(ps[:, 0:n], ar1(k, 128 * tt, 128), wff(half, k, n),
                   start=(k == 0), stop=(k == KT - 1),
                   waits=wl, incs=((smm, 1),) if k == KT - 1 else ())
            if mat == "g":
                emit("scalar", lambda e, tt=tt, p=ps, n=n: e.activation(
                    out=abf(O_SG + 1024 * tt, n), in_=p[:, 0:n],
                    func=AF.Silu, bias=0.0, scale=1.0),
                    waits=(W(smm), W(sve)), incs=((sac, 1),))
            else:
                hb = abf(O_HB, 512)
                emit("vector", lambda e, tt=tt, p=ps, n=n, hb=hb: e.tensor_mul(
                    hb[:, 0:n], abf(O_SG + 1024 * tt, n), p[:, 0:n]),
                    waits=(W(smm), W(sac)), incs=((sve, 1),))
                hmul_v = sve.n
                nk4 = (n + 127) // 128
                for jj in range(nk4):
                    tr(psT[:, 128 * jj:128 * (jj + 1)],
                       hb[:, 128 * jj:128 * (jj + 1)],
                       waits=(W(sve, hmul_v), W(sve)) if jj == 0 else (),
                       incs=((smm, 1),) if jj == nk4 - 1 else ())
                for jj in range(nk4):
                    emit("vector", lambda e, jj=jj, tt=tt: e.tensor_copy(
                        abf(O_HT + 2048 * jj + 256 * tt, 128),
                        psT[:, 128 * jj:128 * (jj + 1)]),
                        waits=(W(smm), W(stS, spill_cnt["n"]))
                        if jj == 0 else (),
                        incs=((sve, 1),) if jj == nk4 - 1 else ())
        unit_mm_done[ui] = smm.n
        if mat == "u":
            nk4 = (n + 127) // 128
            for jj in range(nk4):
                it_g = 4 * j + jj
                emit("sync", lambda e, jj=jj, it_g=it_g: e.dma_start(
                    out=hT_sc[128 * it_g:128 * (it_g + 1), :],
                    in_=abf(O_HT + 2048 * jj, 1024)),
                    waits=(W(sve),), incs=((stS, 16),))
            spill_cnt["n"] = stS.n
    h_spilled = stS.n
    p6_m = smm.n

    O_HT7 = 0

    def ht7(buf, i, n=128):
        return abf(O_HT7 + 22528 * buf + 256 * i, n)
    O_X1S = 45056
    O_OUTB = 49152
    O_WD = 65536

    def wdd(buf, i, n=256):
        return abf(O_WD + 44032 * buf + 512 * i, n)
    O_SS7 = 155648
    slot_state["off"] = O_SS7
    slot_state["slot_bytes"] = 8192

    p6_v = sve.n
    NDCQ = 16
    dcq_mm = {}
    ht_segs = [(0, 16), (16, 16), (32, 16), (48, 16), (64, 16), (80, 6)]
    ob_store = {0: 0, 1: 0}
    tt_mm = {}
    for dq in range(NDCQ):
        wbuf = dq % 2
        if dq >= 2:
            wm = (W(smm, dcq_mm[dq - 2]),)
        elif dq == 0:
            wm = (W(sve, p6_v), W(smm, p6_m), W(stS, h_spilled))
        else:
            wm = ()
        deq_done = dequant_unit(O_WD + 44032 * wbuf,
                                down_q8, IT * 256 * dq,
                                sbf["ds"], sbf["dz"], IT * 256 * dq,
                                IT, 256, wbuf, w_memset=wm, kseg=16)
        for tt in range(NT):
            buf = tt % 2
            key = (dq, tt)
            pkey = (dq, tt - 2) if tt >= 2 else (dq - 1, tt + 6)
            if pkey in tt_mm:
                hw = (W(smm, tt_mm[pkey]),)
            else:
                hw = (W(smm, p6_m), W(stS, h_spilled))
            for si, (i0, ni) in enumerate(ht_segs):
                hsrc = hT_sc[128 * i0:128 * (i0 + ni),
                             128 * tt:128 * (tt + 1)]
                hsrc = hsrc.rearrange("(i p) c -> p i c", p=128)
                dst = abf(O_HT7 + 22528 * buf + 256 * i0, ni * 128)
                dst = dst.rearrange("p (i c) -> p i c", c=128)
                dmaS(dst, hsrc, sem=ldH[buf], waits=hw if si == 0 else ())
            dmaS(abf(O_X1S + 1024 * buf, 256),
                 x1_sc[128 * tt:128 * (tt + 1), 256 * dq:256 * (dq + 1)],
                 sem=ldH[buf],
                 waits=(W(stS, x1_spilled),) if (dq == 0 and tt < 2)
                 else (W(sve),))
            h_ld = ldH[buf].n
            ps = psB if tt % 2 == 0 else psC
            for i in range(IT):
                mm(ps[:, 0:256], ht7(buf, i), wdd(wbuf, i),
                   start=(i == 0), stop=(i == IT - 1),
                   waits=(W(ldH[buf], h_ld), W(ldZp[wbuf], deq_done),
                          W(sve))
                   if i == 0 else (),
                   incs=((smm, 1),) if i == IT - 1 else ())
            tt_mm[key] = smm.n
            ob = af32(O_OUTB + 2048 * buf, 256)
            emit("vector", lambda e, p=ps, ob=ob, buf=buf: e.tensor_add(
                ob, p[:, 0:256], abf(O_X1S + 1024 * buf, 256)),
                waits=(W(smm), W(oSp[buf], ob_store[buf])), incs=((sve, 1),))
            emit("sync", lambda e, ob=ob, tt=tt, dq=dq: e.dma_start(
                out=out_t[128 * tt:128 * (tt + 1),
                          256 * dq:256 * (dq + 1)],
                in_=ob),
                waits=(W(sve),), incs=((oSp[buf], 16),))
            ob_store[buf] = oSp[buf].n
        dcq_mm[dq] = smm.n

    ewait("sync", oSp[0])
    ewait("sync", oSp[1])

    with nc.Block() as block:
        @block.tensor
        def _(eng):
            for f in ops["tensor"]:
                f(eng)

        @block.vector
        def _(eng):
            for f in ops["vector"]:
                f(eng)

        @block.scalar
        def _(eng):
            for f in ops["scalar"]:
                f(eng)

        @block.gpsimd
        def _(eng):
            for f in ops["gpsimd"]:
                f(eng)

        @block.sync
        def _(eng):
            for f in ops["sync"]:
                f(eng)

    return nc



_CACHE = {}


def _rope_perm_cols():
    return np.concatenate([np.arange(0, 64, 2), np.arange(1, 64, 2)])


def _prep_weights(inp):
    w = {}
    w["wq_aT"] = np.ascontiguousarray(inp["L0_attn_wq_a"].T)
    w["q_nw"] = np.ascontiguousarray(inp["L0_attn_q_norm_w"].reshape(QL, 1))
    wqbT = np.ascontiguousarray(inp["L0_attn_wq_b"].T)
    perm = np.zeros(NH * 192, dtype=np.int64)
    rp = _rope_perm_cols()
    for h in range(NH):
        b = 192 * h
        perm[b:b + 128] = b + np.arange(128)
        perm[b + 128:b + 192] = b + 128 + rp
    w["wq_bT"] = np.ascontiguousarray(wqbT[:, perm])
    wkvaT = np.ascontiguousarray(inp["L0_attn_wkv_a"].T)
    perm2 = np.concatenate([np.arange(KVL), KVL + rp])
    w["wkv_aT"] = np.ascontiguousarray(wkvaT[:, perm2])
    w["kv_nw"] = np.ascontiguousarray(inp["L0_attn_kv_norm_w"].reshape(KVL, 1))
    wkvbT = np.ascontiguousarray(inp["L0_attn_wkv_b"].T)
    perm3 = np.zeros(NH * 256, dtype=np.int64)
    for p in range(NPAIR):
        b = 512 * p
        h0, h1 = 2 * p, 2 * p + 1
        perm3[b:b + 128] = 256 * h0 + np.arange(128)
        perm3[b + 128:b + 256] = 256 * h1 + np.arange(128)
        perm3[b + 256:b + 384] = 256 * h0 + 128 + np.arange(128)
        perm3[b + 384:b + 512] = 256 * h1 + 128 + np.arange(128)
    w["wkv_bT"] = np.ascontiguousarray(wkvbT[:, perm3])
    w["woT"] = np.ascontiguousarray(inp["L0_attn_wo"].T)
    def q8_cm(q, chunks):
        q8 = np.ascontiguousarray(q).view(np.int8)[:, ::4]
        G = q8.shape[0] // 128
        qt = np.ascontiguousarray(q8).reshape(G, 128, q8.shape[1])
        blocks = [qt[:, :, n0:n0 + n].transpose(1, 0, 2).reshape(128, -1)
                  for (n0, n) in chunks]
        return np.ascontiguousarray(np.concatenate(blocks, axis=1))

    def sz_cm(a, chunks):
        blocks = [np.ascontiguousarray(a[:, n0:n0 + n]).reshape(-1)
                  for (n0, n) in chunks]
        return np.ascontiguousarray(np.concatenate(blocks)).reshape(1, -1)

    gch = [(512 * j, min(512, II - 512 * j)) for j in range(NNC)]
    dch = [(256 * j, 256) for j in range(16)]
    w["gate_q8"] = q8_cm(inp["L0_ffn_gate_q"], gch)
    w["up_q8"] = q8_cm(inp["L0_ffn_up_q"], gch)
    w["down_q8"] = q8_cm(inp["L0_ffn_down_q"], dch)
    w["gate_s"] = sz_cm(inp["L0_ffn_gate_s"], gch)
    w["gate_z"] = sz_cm(inp["L0_ffn_gate_z"], gch)
    w["up_s"] = sz_cm(inp["L0_ffn_up_s"], gch)
    w["up_z"] = sz_cm(inp["L0_ffn_up_z"], gch)
    w["down_s"] = sz_cm(inp["L0_ffn_down_s"], dch)
    w["down_z"] = sz_cm(inp["L0_ffn_down_z"], dch)
    return w


def _rope_tables(p0, n):
    freqs = (1.0 / 10000.0 ** (np.arange(0, 64, 2, dtype=np.float64) / 64.0))
    t = np.arange(p0, p0 + n, dtype=np.float64)
    ang = freqs[:, None] * t[None, :]
    return np.concatenate([np.cos(ang), np.sin(ang)], axis=0).astype(BF)


def _mask(p0):
    s = np.arange(CTX)
    m = np.zeros((NT, 128, CTX), dtype=np.float32)
    for tt in range(NT):
        qpos = p0 + 128 * tt + np.arange(128)
        m[tt] = np.where(s[None, :] <= qpos[:, None], 0.0, -1e9)
    return m.astype(BF)


def kernel(L0_attn_x, L0_attn_wq_a, L0_attn_q_norm_w, L0_attn_wq_b,
           L0_attn_wkv_a, L0_attn_kv_norm_w, L0_attn_wkv_b, L0_attn_wo,
           L0_ffn_gate_q, L0_ffn_gate_s, L0_ffn_gate_z, L0_ffn_up_q,
           L0_ffn_up_s, L0_ffn_up_z, L0_ffn_down_q, L0_ffn_down_s,
           L0_ffn_down_z):
    inp = dict(L0_attn_wq_a=np.asarray(L0_attn_wq_a),
               L0_attn_q_norm_w=np.asarray(L0_attn_q_norm_w),
               L0_attn_wq_b=np.asarray(L0_attn_wq_b),
               L0_attn_wkv_a=np.asarray(L0_attn_wkv_a),
               L0_attn_kv_norm_w=np.asarray(L0_attn_kv_norm_w),
               L0_attn_wkv_b=np.asarray(L0_attn_wkv_b),
               L0_attn_wo=np.asarray(L0_attn_wo),
               L0_ffn_gate_q=np.asarray(L0_ffn_gate_q),
               L0_ffn_gate_s=np.asarray(L0_ffn_gate_s),
               L0_ffn_gate_z=np.asarray(L0_ffn_gate_z),
               L0_ffn_up_q=np.asarray(L0_ffn_up_q),
               L0_ffn_up_s=np.asarray(L0_ffn_up_s),
               L0_ffn_up_z=np.asarray(L0_ffn_up_z),
               L0_ffn_down_q=np.asarray(L0_ffn_down_q),
               L0_ffn_down_s=np.asarray(L0_ffn_down_s),
               L0_ffn_down_z=np.asarray(L0_ffn_down_z))
    x = np.asarray(L0_attn_x)
    w = _prep_weights(inp)

    if "nc" not in _CACHE:
        _CACHE["nc"] = build_nc()
    nc = _CACHE["nc"]

    in_maps = []
    for c in range(8):
        b, half = c // 2, c % 2
        p0 = 1024 * half
        m = dict(w)
        m["x_q"] = np.ascontiguousarray(x[b, p0:p0 + T, :])
        m["x_b"] = np.ascontiguousarray(x[b])
        m["rope_q"] = _rope_tables(p0, T)
        m["rope_k"] = _rope_tables(0, CTX)
        m["mask_in"] = _mask(p0)
        in_maps.append(m)

    res = run_bass_kernel_spmd(nc, in_maps, core_ids=list(range(8)))
    out = np.empty((4, CTX, D), dtype=np.float32)
    for c in range(8):
        b, half = c // 2, c % 2
        out[b, 1024 * half:1024 * half + T, :] = res.results[c]["out"]
    return out


# version 08
# speedup vs baseline: 1.2672x; 1.0004x over previous
import numpy as np
import ml_dtypes
import concourse.bass as bass
import concourse.mybir as mybir
from concourse.bass_utils import run_bass_kernel_spmd

F32 = mybir.dt.float32
BF16 = mybir.dt.bfloat16
I8 = mybir.dt.int8
AF = mybir.ActivationFunctionType
ALU = mybir.AluOpType
AX = mybir.AxisListType
BF = ml_dtypes.bfloat16

D = 4096
T = 1024
NT = 8
CTX = 2048
NCC = 4
NCT = 16
KT = 32
QL = 1536
QLT = 12
KVL = 512
KVLT = 4
NH = 32
NPAIR = 16
II = 11008
IT = 86
NNC = 22
NDC = 8
SCALE = float(1.0 / np.sqrt(192.0))
LN_EPS = 1e-5
RMS_EPS = 1e-6


def _ncw(j):
    return min(512, II - 512 * j)


def build_nc(dbg_phase=None):
    nc = bass.Bass("TRN2", target_bir_lowering=False)

    def din(name, shape, dt):
        return nc.dram_tensor(name, list(shape), dt, kind="ExternalInput").ap()

    x_q = din("x_q", (T, D), F32)
    x_b = din("x_b", (CTX, D), F32)
    wq_aT = din("wq_aT", (D, QL), F32)
    q_nw = din("q_nw", (QL, 1), F32)
    wq_bT = din("wq_bT", (QL, NH * 192), F32)
    wkv_aT = din("wkv_aT", (D, KVL + 64), F32)
    kv_nw = din("kv_nw", (KVL, 1), F32)
    wkv_bT = din("wkv_bT", (KVL, NH * 256), F32)
    woT = din("woT", (D, D), F32)
    gate_q8 = din("gate_q8", (128, KT * II), I8)
    up_q8 = din("up_q8", (128, KT * II), I8)
    down_q8 = din("down_q8", (128, IT * D), I8)
    gate_s = din("gate_s", (1, KT * II), F32)
    gate_z = din("gate_z", (1, KT * II), F32)
    up_s = din("up_s", (1, KT * II), F32)
    up_z = din("up_z", (1, KT * II), F32)
    down_s = din("down_s", (1, IT * D), F32)
    down_z = din("down_z", (1, IT * D), F32)
    rope_q = din("rope_q", (64, T), BF16)
    rope_k = din("rope_k", (64, CTX), BF16)
    mask_in = din("mask_in", (NT, 128, CTX), BF16)

    out_t = nc.dram_tensor("out", [T, D], F32, kind="ExternalOutput").ap()
    dbg_t = nc.dram_tensor("dbg_arena", [128, 192 * 1024], mybir.dt.uint8,
                           kind="ExternalOutput").ap()

    sbf = {}
    for nm, tot in [("gs", KT * II), ("gz", KT * II), ("us", KT * II),
                    ("uz", KT * II), ("ds", IT * D), ("dz", IT * D)]:
        sbf[nm] = nc.dram_tensor("sc_" + nm, [1, tot], BF16, kind="Internal").ap()
    hT_sc = nc.dram_tensor("hT_sc", [II, T], BF16, kind="Internal").ap()
    x1_sc = nc.dram_tensor("x1_sc", [T, D], BF16, kind="Internal").ap()
    rq_sc = nc.dram_tensor("rq_sc", [1, T], F32, kind="Internal").ap()
    rkv_sc = nc.dram_tensor("rkv_sc", [1, CTX], F32, kind="Internal").ap()

    ident = nc.alloc_sbuf_tensor("ident", [128, 128], BF16).ap()
    ones_col = nc.alloc_sbuf_tensor("ones_col", [128, 1], BF16).ap()
    qnw_sb = nc.alloc_sbuf_tensor("qnw_sb", [128, QLT], F32).ap()
    kvnw_sb = nc.alloc_sbuf_tensor("kvnw_sb", [128, KVLT], F32).ap()
    ropq_c = nc.alloc_sbuf_tensor("ropq_c", [32, T], BF16).ap()
    ropq_s = nc.alloc_sbuf_tensor("ropq_s", [32, T], BF16).ap()
    ropk_c = nc.alloc_sbuf_tensor("ropk_c", [32, CTX], BF16).ap()
    ropk_s = nc.alloc_sbuf_tensor("ropk_s", [32, CTX], BF16).ap()
    stat = nc.alloc_sbuf_tensor("stat", [128, 64], F32).ap()
    mv = nc.alloc_sbuf_tensor("mv", [128, 8], F32).ap()
    srt = nc.alloc_sbuf_tensor("srt", [128, NT], F32).ap()
    sm = nc.alloc_sbuf_tensor("sm", [128, 16], F32).ap()
    cst = nc.alloc_sbuf_tensor("cst", [128, 8], F32).ap()
    CONSTS = [float(LN_EPS), float(RMS_EPS), 1.0, 0.0,
              float(1.0 / QL), float(1.0 / KVL)]
    for ci, cv in enumerate(CONSTS):
        nc.const_aps.register(F32, cv, cst[:, ci:ci + 1])

    AR_BYTES = 192 * 1024
    arena_t = nc.alloc_sbuf_tensor("arena", [128, AR_BYTES], mybir.dt.uint8)
    ar_bf = arena_t.ap().bitcast(BF16)
    ar_f32 = arena_t.ap().bitcast(F32)

    def abf(off, n):
        assert off % 2 == 0 and off + 2 * n <= AR_BYTES, (off, n)
        return ar_bf[:, off // 2: off // 2 + n]

    def af32(off, n):
        assert off % 4 == 0 and off + 4 * n <= AR_BYTES, (off, n)
        return ar_f32[:, off // 4: off // 4 + n]

    def ar1(tile, t0=0, n=1024):
        return abf(2048 * tile + 2 * t0, n)

    def qlra(j, c0=0, n=1024):
        return abf(65536 + 2048 * j + 2 * c0, n)

    psA = nc.alloc_psum_tensor("psA", [128, 2048], F32).ap()
    psB = nc.alloc_psum_tensor("psB", [128, 512], F32).ap()
    psC = nc.alloc_psum_tensor("psC", [128, 512], F32).ap()
    psT = nc.alloc_psum_tensor("psT", [128, 1024], BF16).ap()
    psD = nc.alloc_psum_tensor("psD", [128, 512], F32).ap()

    class Sem:
        __slots__ = ("h", "n")

        def __init__(self, name):
            self.h = nc.alloc_semaphore(name)
            self.n = 0

    ldS = Sem("ldS")
    ldG = Sem("ldG")
    ldX = [Sem("ldX0"), Sem("ldX1")]
    ldX3 = Sem("ldX3")
    ldM = Sem("ldM")
    ldXS = Sem("ldXS")
    ldW = [Sem("ldW0"), Sem("ldW1")]
    ldH = [Sem("ldH0"), Sem("ldH1")]
    ldQp = [Sem("ldQ0"), Sem("ldQ1")]
    ldBp = [Sem("ldB0"), Sem("ldB1"), Sem("ldB2"), Sem("ldB3")]
    ldZp = [Sem("ldZ0"), Sem("ldZ1")]
    stS = Sem("stS")
    oSp = [Sem("oS0"), Sem("oS1")]
    smm = Sem("smm")
    sve = Sem("sve")
    sac = Sem("sac")
    spo = Sem("spo")

    ops = {"tensor": [], "vector": [], "scalar": [], "gpsimd": [], "sync": []}
    stopped = {"v": False}

    def emit(eng, build, waits=(), incs=()):
        if stopped["v"]:
            for s, a in incs:
                s.n += a
            return
        wl = tuple((s.h, (s.n if v is None else v)) for (s, v) in waits)
        il = []
        for s, a in incs:
            s.n += a
            il.append((s.h, a))
        il = tuple(il)

        def f(e, build=build, wl=wl, il=il):
            for h, v in wl:
                e.wait_ge(h, v)
            ins = build(e)
            for h, a in il:
                ins = ins.then_inc(h, a)
        ops[eng].append(f)

    def W(sem, val=None):
        return (sem, val)

    def ewait(eng, sem, val=None):
        if stopped["v"]:
            return
        v = sem.n if val is None else val
        ops[eng].append(lambda e, h=sem.h, v=v: e.wait_ge(h, v))

    def dmaS(dst, src, sem=ldS, waits=(), slow=False):
        if slow:
            emit("sync", lambda e, d=dst, s=src: e.dma_start(
                out=d, in_=s, allow_slow_non_contiguous=True),
                waits=waits, incs=((sem, 16),))
        else:
            emit("sync", lambda e, d=dst, s=src: e.dma_start(out=d, in_=s),
                 waits=waits, incs=((sem, 16),))

    def dmaG(dst, src, accum=None, sem=ldG, waits=()):
        if accum is None:
            emit("gpsimd", lambda e, d=dst, s=src: e.dma_start(out=d, in_=s),
                 waits=waits, incs=((sem, 16),))
        else:
            emit("gpsimd",
                 lambda e, d=dst, s=src, a=accum:
                     e.dma_start(out=d, in_=s, accum_op=a),
                 waits=waits, incs=((sem, 16),))

    def mm(out, lhsT, rhs, start, stop, waits=(), incs=()):
        emit("tensor",
             lambda e, o=out, l=lhsT, r=rhs, s1=start, s2=stop:
                 e.matmul(o, l, r, start=s1, stop=s2),
             waits=waits, incs=incs)

    def tr(out, in_, waits=(), incs=()):
        emit("tensor", lambda e, o=out, i=in_: e.transpose(o, i, ident),
             waits=waits, incs=incs)

    for ci, cv in enumerate(CONSTS):
        emit("vector", lambda e, ci=ci, cv=cv: e.memset(cst[:, ci:ci + 1], cv),
             incs=((sve, 1),))
    if dbg_phase is not None:
        emit("vector", lambda e: e.memset(ar_f32[:, :], 0.0),
             incs=((sve, 1),))
    emit("gpsimd", lambda e: e.memset(ident, 0.0), incs=((spo, 1),))
    emit("gpsimd", lambda e: e.memset(ones_col, 1.0), incs=((spo, 1),))
    ewait("gpsimd", spo)
    emit("gpsimd", lambda e: e.affine_select(
        out=ident, in_=ident, compare_op=ALU.not_equal, fill=1.0,
        base=0, pattern=[[-1, 128]], channel_multiplier=1), incs=((spo, 1),))

    dmaS(qnw_sb[:, 0:QLT], q_nw.rearrange("(j p) o -> p (j o)", p=128),
         slow=True)
    dmaS(kvnw_sb[:, 0:KVLT], kv_nw.rearrange("(j p) o -> p (j o)", p=128),
         slow=True)
    dmaS(ropq_c[:, :], rope_q[0:32, :])
    dmaS(ropq_s[:, :], rope_q[32:64, :])
    dmaS(ropk_c[:, :], rope_k[0:32, :])
    dmaS(ropk_s[:, :], rope_k[32:64, :])

    cvt_jobs = [(gate_s, sbf["gs"], KT * II), (gate_z, sbf["gz"], KT * II),
                (up_s, sbf["us"], KT * II), (up_z, sbf["uz"], KT * II),
                (down_s, sbf["ds"], IT * D), (down_z, sbf["dz"], IT * D)]
    for src, dst, tot in cvt_jobs:
        per = tot // 128
        cv_f = af32(0, per)
        cv_b = abf(4 * per, per)
        flat = src.rearrange("o (p m) -> (o p) m", p=128)
        dmaS(cv_f, flat, waits=(W(sve),))
        emit("vector", lambda e, a=cv_b, b=cv_f: e.tensor_copy(a, b),
             waits=(W(ldS),), incs=((sve, 1),))
        emit("sync", lambda e, d=dst, s=cv_b:
             e.dma_start(out=d.rearrange("o (p m) -> p (o m)", p=128), in_=s),
             waits=(W(sve),), incs=((stS, 16),))
        ewait("vector", stS)
    cvt_done_st = stS.n

    def layer_norm_apply(src, dst_bf, width, pre_waits):
        nch = width // 512
        for j in range(nch):
            emit("vector",
                 lambda e, s=src, j=j: e.bn_stats(
                     stat[:, 6 * j:6 * j + 6], s[:, 512 * j:512 * (j + 1)]),
                 waits=pre_waits if j == 0 else (), incs=((sve, 1),))
        emit("vector", lambda e, n=nch: e.bn_aggr(mv[:, 0:2], stat[:, 0:6 * n]),
             waits=(W(sve),), incs=((sve, 1),))
        emit("scalar", lambda e: e.activation(
            out=mv[:, 2:3], in_=mv[:, 1:2], func=AF.Sqrt,
            bias=float(LN_EPS), scale=1.0),
            waits=(W(sve),), incs=((sac, 1),))
        emit("vector", lambda e: e.reciprocal(mv[:, 3:4], mv[:, 2:3]),
             waits=(W(sac),), incs=((sve, 1),))
        emit("vector", lambda e, s=src, d=dst_bf: e.tensor_scalar(
            out=d, in0=s, scalar1=mv[:, 0:1], scalar2=mv[:, 3:4],
            op0=ALU.subtract, op1=ALU.mult),
            waits=(W(sve),), incs=((sve, 1),))
        return sve.n

    def chmajor_from_tok(src_bf, tt, dst_fn, src_done_sve, nk=KT):
        for k4 in range((nk + 3) // 4):
            ks = list(range(4 * k4, min(4 * k4 + 4, nk)))
            for idx, k in enumerate(ks):
                wl = ()
                if idx == 0:
                    wl = (W(sve, src_done_sve), W(sve)) if k4 == 0 else (W(sve),)
                tr(psT[:, 128 * idx:128 * (idx + 1)],
                   src_bf[:, 128 * k:128 * (k + 1)],
                   waits=wl,
                   incs=((smm, 1),) if idx == len(ks) - 1 else ())
            for idx, k in enumerate(ks):
                emit("vector",
                     lambda e, k=k, idx=idx, d=dst_fn, tt=tt: e.tensor_copy(
                         d(k)[:, 128 * tt:128 * (tt + 1)],
                         psT[:, 128 * idx:128 * (idx + 1)]),
                     waits=(W(smm),) if idx == 0 else (),
                     incs=((sve, 1),) if idx == len(ks) - 1 else ())


    def _dbg_stop(tag):
        if dbg_phase != tag or stopped["v"]:
            return False
        ewait("sync", sve)
        ewait("sync", smm)
        ewait("sync", sac)
        emit("sync", lambda e: e.dma_start(out=dbg_t, in_=arena_t.ap()),
             waits=(), incs=((stS, 16),))
        ewait("sync", stS)
        stopped["v"] = True
        return True

    ewait("tensor", spo)
    O_XA = 90112
    O_XB2 = 106496
    O_XT = 122880
    apply_done = {}
    for tt in range(NT):
        xa = af32(O_XA if tt % 2 == 0 else O_XB2, 4096)
        wl = ()
        if tt >= 2:
            wl = (W(sve, apply_done[tt - 2]),)
        if tt == 0:
            wl = (W(stS, cvt_done_st),)
        dmaS(xa, x_q[128 * tt:128 * (tt + 1), :], sem=ldX[tt % 2], waits=wl)
        xt = abf(O_XT, 4096)
        a_done = layer_norm_apply(
            xa, xt, 4096,
            (W(ldX[tt % 2]),) + ((W(smm),) if tt > 0 else ()))
        apply_done[tt] = a_done
        chmajor_from_tok(xt, tt, lambda k: ar1(k), a_done)

    if _dbg_stop('p1'):
        pass
    O_WQA = 90112
    p1_v = sve.n
    for k in range(KT):
        dmaG(abf(O_WQA + 3072 * k, 1536), wq_aT[128 * k:128 * (k + 1), :],
             waits=(W(sve, p1_v),) if k == 0 else ())
    wqa_ld = ldG.n
    for j in range(QLT):
        for tc in range(2):
            ps = psB if (2 * j + tc) % 2 == 0 else psC
            for k in range(KT):
                mm(ps[:, 0:512],
                   abf(O_WQA + 3072 * k + 256 * j, 128),
                   ar1(k, 512 * tc, 512),
                   start=(k == 0), stop=(k == KT - 1),
                   waits=((W(ldG, wqa_ld), W(sve)) if k == 0 else ()),
                   incs=((smm, 1),) if k == KT - 1 else ())
            emit("vector",
                 lambda e, j=j, tc=tc, p=ps: e.tensor_copy(
                     qlra(j, 512 * tc, 512), p[:, 0:512]),
                 waits=(W(smm),), incs=((sve, 1),))
    O_SQ = 188416
    rowv = af32(189440, 1024)
    for tc in range(2):
        psq = psD if tc == 0 else psB
        for j in range(QLT):
            sq = abf(O_SQ, 512)
            emit("vector",
                 lambda e, j=j, tc=tc, s=sq: e.tensor_mul(
                     s, qlra(j, 512 * tc, 512), qlra(j, 512 * tc, 512)),
                 waits=(W(sve), W(smm)), incs=((sve, 1),))
            mm(psq[0:1, 0:512], ones_col, sq,
               start=(j == 0), stop=(j == QLT - 1),
               waits=(W(sve),), incs=((smm, 1),))
        emit("scalar", lambda e, tc=tc, p=psq: e.activation(
            out=rowv[0:1, 512 * tc:512 * (tc + 1)], in_=p[0:1, 0:512],
            func=AF.Sqrt, bias=float(RMS_EPS), scale=float(1.0 / QL)),
            waits=(W(smm),), incs=((sac, 1),))
    emit("vector", lambda e: e.reciprocal(rowv[0:1, 0:1024],
                                          rowv[0:1, 0:1024]),
         waits=(W(sac),), incs=((sve, 1),))
    emit("sync", lambda e: e.dma_start(out=rq_sc, in_=rowv[0:1, 0:1024]),
         waits=(W(sve),), incs=((stS, 16),))
    dmaS(srt[:, 0:NT], rq_sc.rearrange("o (a p) -> p (o a)", p=128),
         waits=(W(stS),), slow=True)
    emit("vector", lambda e: e.tensor_scalar(
        out=srt[:, 0:NT], in0=srt[:, 0:NT], scalar1=float(SCALE), scalar2=None,
        op0=ALU.mult), waits=(W(ldS),), incs=((sve, 1),))

    if _dbg_stop('p2'):
        pass
    O_WKA = 90112
    O_XLC = 126976
    O_CKV = 159744
    O_KPE = 176128
    O_KPO = 180224
    O_KPB = 184320
    O_RSB = 188416
    O_TMP1 = 135168
    O_TMP2 = 139264
    rowv = af32(143360, 2048)
    p2_v, p2_m = sve.n, smm.n
    for k in range(KT):
        dmaG(abf(O_WKA + 1152 * k, 576), wkv_aT[128 * k:128 * (k + 1), :],
             waits=(W(sve, p2_v), W(smm, p2_m)) if k == 0 else ())
    wka_ld = ldG.n
    O_X3 = 0
    O_XT3 = 16384
    for cc in range(NCC):
        for st4 in range(4):
            ct = 4 * cc + st4
            xa = af32(O_X3, 4096)
            wl = (W(sve),) if ct > 0 else (W(sve, p2_v),)
            dmaS(xa, x_b[128 * ct:128 * (ct + 1), :], sem=ldX3, waits=wl)
            xt = abf(O_XT3, 4096)
            a_done = layer_norm_apply(xa, xt, 4096, (W(ldX3), W(smm)))
            chmajor_from_tok(xt, st4, lambda k: abf(O_XLC + 1024 * k, 512),
                             a_done)
        ewait("tensor", sve)
        for c in range(6):
            ps = [psA[:, 0:512], psA[:, 512:1024], psA[:, 1024:1536],
                  psA[:, 1536:2048], psB[:, 0:512], psD[:, 0:512]][c]
            mcols = 128 if c < 4 else 32
            coff = 256 * c if c < 4 else 1024 + 64 * (c - 4)
            for k in range(KT):
                wl = ()
                if k == 0:
                    wl = (W(sve),)
                    if cc == 0 and c == 0:
                        wl = (W(ldG, wka_ld), W(sve))
                mm(ps[0:mcols, 0:512],
                   abf(O_WKA + 1152 * k + coff, mcols),
                   abf(O_XLC + 1024 * k, 512),
                   start=(k == 0), stop=(k == KT - 1),
                   waits=wl,
                   incs=((smm, 1),) if k == KT - 1 else ())
            if c < 4:
                emit("vector", lambda e, c=c, cc=cc, p=ps: e.tensor_copy(
                    abf(O_CKV + 4096 * c + 1024 * cc, 512), p[:, 0:512]),
                    waits=(W(smm),), incs=((sve, 1),))
            else:
                emit("vector", lambda e, c=c, cc=cc, p=ps: e.tensor_copy(
                    abf((O_KPE if c == 4 else O_KPO) + 1024 * cc, 512)
                    [0:32, :],
                    p[0:32, 0:512]),
                    waits=(W(smm),), incs=((sve, 1),))
    kpe = abf(O_KPE, 2048)
    kpo = abf(O_KPO, 2048)
    tmp1 = abf(O_TMP1, 2048)
    tmp2 = abf(O_TMP2, 2048)
    kpb = abf(O_KPB, 2048)
    emit("vector", lambda e: e.tensor_mul(tmp1[0:32, :], kpe[0:32, :],
                                          ropk_c[:, :]),
         waits=(W(sve),), incs=((sve, 1),))
    emit("vector", lambda e: e.tensor_mul(tmp2[0:32, :], kpo[0:32, :],
                                          ropk_s[:, :]), incs=((sve, 1),))
    emit("vector", lambda e: e.tensor_sub(kpb[0:32, :], tmp1[0:32, :],
                                          tmp2[0:32, :]),
         waits=(W(sve),), incs=((sve, 1),))
    emit("vector", lambda e: e.tensor_mul(tmp1[0:32, :], kpe[0:32, :],
                                          ropk_s[:, :]),
         waits=(W(sve),), incs=((sve, 1),))
    emit("vector", lambda e: e.tensor_mul(tmp2[0:32, :], kpo[0:32, :],
                                          ropk_c[:, :]), incs=((sve, 1),))
    emit("vector", lambda e: e.tensor_add(kpb[32:64, :], tmp1[0:32, :],
                                          tmp2[0:32, :]),
         waits=(W(sve),), incs=((sve, 1),))
    for c in range(KVLT):
        sq = abf(O_XLC, 2048)
        emit("vector", lambda e, c=c, s=sq: e.tensor_mul(
            s, abf(O_CKV + 4096 * c, 2048), abf(O_CKV + 4096 * c, 2048)),
            waits=(W(sve), W(smm)), incs=((sve, 1),))
        for cc in range(NCC):
            mm(psA[0:1, 512 * cc:512 * (cc + 1)], ones_col,
               sq[:, 512 * cc:512 * (cc + 1)],
               start=(c == 0), stop=(c == KVLT - 1),
               waits=(W(sve),) if cc == 0 else (), incs=((smm, 1),))
    for cc in range(NCC):
        emit("scalar", lambda e, cc=cc: e.activation(
            out=rowv[0:1, 512 * cc:512 * (cc + 1)],
            in_=psA[0:1, 512 * cc:512 * (cc + 1)],
            func=AF.Sqrt, bias=float(RMS_EPS), scale=float(1.0 / KVL)),
            waits=(W(smm),), incs=((sac, 1),))
    emit("vector", lambda e: e.reciprocal(rowv[0:1, 0:2048],
                                          rowv[0:1, 0:2048]),
         waits=(W(sac),), incs=((sve, 1),))
    emit("sync", lambda e: e.dma_start(out=rkv_sc, in_=rowv[0:1, 0:2048]),
         waits=(W(sve),), incs=((stS, 16),))
    rsb = af32(O_RSB, 2048)
    emit("gpsimd", lambda e: e.dma_start(
        out=rsb, in_=rkv_sc.broadcast_to([128, CTX])),
        waits=(W(stS), W(sve)), incs=((ldG, 16),))
    for c in range(KVLT):
        emit("vector", lambda e, c=c: e.tensor_mul(
            abf(O_CKV + 4096 * c, 2048), abf(O_CKV + 4096 * c, 2048), rsb),
            waits=(W(ldG),) if c == 0 else (), incs=((sve, 1),))
    ckv_ready_v = sve.n

    if _dbg_stop('p3'):
        pass
    O_WKVB = 90112
    O_WQB = 94208
    O_KN = 103424
    O_VT = 111616
    O_QN = 119808
    O_QPF = 123904
    O_QPB = 128000
    O_PRB = 132096
    O_PRT = 136192
    O_MSK = 140288

    def arK(h2, n0, n):
        return abf(O_KN + 4096 * h2 + 2 * n0, n)

    def arV(ct, n0=0, n=256):
        return abf(O_VT + 512 * ct + 2 * n0, n)

    def arQN(h2, t0, n):
        return abf(O_QN + 2048 * h2 + 2 * t0, n)

    O_PRB2 = 152576
    O_PRB3 = 176128
    PRBS = [O_PRB, O_PRB2, O_PRB3]
    probsT = abf(O_PRT, 2048)
    maskb = abf(O_MSK, 2048)
    O_SC = [144384, 188416]
    p4it = {"i": 0, "evac": {}, "exp": {}, "pend": [], "tail": None}

    def emit_tail():
        if p4it["tail"] is None:
            return
        itp, par3 = p4it["tail"]
        p4it["tail"] = None
        sb = 8 * (itp % 2)
        prb = abf(PRBS[par3], 2048)
        emit("vector", lambda e, sb=sb: e.reduce_sum(
            sm[:, sb + 2:sb + 3], sm[:, sb + 4:sb + 8], axis=AX.X),
            waits=(W(sac, p4it["exp"][itp]),), incs=((sve, 1),))
        emit("vector", lambda e, sb=sb: e.reciprocal(
            sm[:, sb + 3:sb + 4], sm[:, sb + 2:sb + 3]),
            waits=(W(sve),), incs=((sve, 1),))
        emit("vector", lambda e, prb=prb, sb=sb: e.tensor_scalar(
            out=prb, in0=prb, scalar1=sm[:, sb + 3:sb + 4], scalar2=None,
            op0=ALU.mult),
            waits=(W(sve),), incs=((sve, 1),))
        p4it["pend"].append((p4it["meta"][itp][0], p4it["meta"][itp][1],
                             par3, sve.n))

    def emit_trav(hh, tt, par, pnorm_v):
        prb = abf(PRBS[par], 2048)
        h2 = hh % 2
        for b4 in range(4):
            for j in range(4):
                ct = 4 * b4 + j
                wl = ()
                if j == 0:
                    wl = (W(sve, pnorm_v), W(sac), W(sve)) if b4 == 0 \
                        else (W(sac),)
                tr(psT[:, 128 * j:128 * (j + 1)],
                   prb[:, 128 * ct:128 * (ct + 1)],
                   waits=wl,
                   incs=((smm, 1),) if j == 3 else ())
            emit("scalar", lambda e, b4=b4: e.copy(
                probsT[:, 512 * b4:512 * (b4 + 1)], psT[:, 0:512]),
                waits=(W(smm),), incs=((sac, 1),))
        for ct in range(NCT):
            mm(psD[:, 0:128],
               arV(ct, 128 * h2, 128),
               probsT[:, 128 * ct:128 * (ct + 1)],
               start=(ct == 0), stop=(ct == NCT - 1),
               waits=(W(sac),) if ct == 0 else (),
               incs=((smm, 1),) if ct == NCT - 1 else ())
        emit("scalar", lambda e, hh=hh, tt=tt: e.copy(
            ar1(hh, 128 * tt, 128), psD[:, 0:128]),
            waits=(W(smm),), incs=((sac, 1),))

    def flush_trav():
        emit_tail()
        while p4it["pend"]:
            emit_trav(*p4it["pend"].pop(0))

    for pr in range(NPAIR):
        flush_trav()
        pm0 = smm.n
        for c in range(KVLT):
            wl = ()
            if c == 0:
                wl = (W(sve, ckv_ready_v),) if pr == 0 else (W(smm, pm0),)
            dmaG(abf(O_WKVB + 1024 * c, 512),
                 wkv_bT[128 * c:128 * (c + 1), 512 * pr:512 * (pr + 1)],
                 waits=wl)
        for c in range(QLT):
            dmaG(abf(O_WQB + 768 * c, 384),
                 wq_bT[128 * c:128 * (c + 1), 384 * pr:384 * (pr + 1)])
        wld = ldG.n
        for c in range(KVLT):
            emit("vector", lambda e, c=c: e.tensor_scalar(
                out=abf(O_WKVB + 1024 * c, 512), in0=abf(O_WKVB + 1024 * c, 512),
                scalar1=kvnw_sb[:, c:c + 1], scalar2=None, op0=ALU.mult),
                waits=(W(ldG, wld),) if c == 0 else (), incs=((sve, 1),))
        for c in range(QLT):
            emit("vector", lambda e, c=c: e.tensor_scalar(
                out=abf(O_WQB + 768 * c, 384), in0=abf(O_WQB + 768 * c, 384),
                scalar1=qnw_sb[:, c:c + 1], scalar2=None, op0=ALU.mult),
                incs=((sve, 1),))
        wfold = sve.n

        for h2 in range(2):
            for tc in range(2):
                ps = psB if tc == 0 else psC
                for c in range(QLT):
                    mm(ps[:, 0:512],
                       abf(O_WQB + 768 * c + 384 * h2, 128),
                       qlra(c, 512 * tc, 512),
                       start=(c == 0), stop=(c == QLT - 1),
                       waits=(W(sve, wfold), W(sve)) if c == 0 else (),
                       incs=((smm, 1),) if c == QLT - 1 else ())
                emit("vector", lambda e, h2=h2, tc=tc, p=ps: e.tensor_copy(
                    arQN(h2, 512 * tc, 512), p[:, 0:512]),
                    waits=(W(smm),), incs=((sve, 1),))
            for pe_half in range(2):
                ps = psB if pe_half == 0 else psC
                for tc in range(2):
                    for c in range(QLT):
                        mm(ps[0:32, 512 * 0:512] if False else
                           ps[0:32, 0:512],
                           abf(O_WQB + 768 * c + 384 * h2 + 256
                               + 64 * pe_half, 32),
                           qlra(c, 512 * tc, 512),
                           start=(c == 0), stop=(c == QLT - 1),
                           waits=(W(sve),) if c == 0 else (),
                           incs=((smm, 1),) if c == QLT - 1 else ())
                    emit("vector", lambda e, tc=tc, p=ps, ph=pe_half:
                         e.tensor_copy(
                             abf(O_QPF + 2048 * ph + 1024 * tc, 512)
                             [0:32, :],
                             p[0:32, 0:512]),
                         waits=(W(smm),), incs=((sve, 1),))
            qpe = abf(O_QPF, 1024)
            qpo = abf(O_QPF + 2048, 1024)
            qt1 = abf(O_PRB, 1024)
            qt2 = abf(O_PRB + 2048, 1024)
            qpb = abf(O_QPB + 2048 * h2, 1024)
            emit("vector", lambda e, qpe=qpe, qt1=qt1: e.tensor_mul(
                qt1[0:32, :], qpe[0:32, :], ropq_c[:, :]),
                waits=(W(sve), W(smm)), incs=((sve, 1),))
            emit("vector", lambda e, qpo=qpo, qt2=qt2: e.tensor_mul(
                qt2[0:32, :], qpo[0:32, :], ropq_s[:, :]),
                incs=((sve, 1),))
            emit("vector", lambda e, qpb=qpb, qt1=qt1, qt2=qt2: e.tensor_sub(
                qpb[0:32, :], qt1[0:32, :], qt2[0:32, :]),
                waits=(W(sve),), incs=((sve, 1),))
            emit("vector", lambda e, qpe=qpe, qt1=qt1: e.tensor_mul(
                qt1[0:32, :], qpe[0:32, :], ropq_s[:, :]),
                waits=(W(sve),), incs=((sve, 1),))
            emit("vector", lambda e, qpo=qpo, qt2=qt2: e.tensor_mul(
                qt2[0:32, :], qpo[0:32, :], ropq_c[:, :]),
                incs=((sve, 1),))
            emit("vector", lambda e, qpb=qpb, qt1=qt1, qt2=qt2: e.tensor_add(
                qpb[32:64, :], qt1[0:32, :], qt2[0:32, :]),
                waits=(W(sve),), incs=((sve, 1),))

        for h2 in range(2):
            for cc in range(NCC):
                ps = psB if cc % 2 == 0 else psC
                for c in range(KVLT):
                    mm(ps[:, 0:512],
                       abf(O_WKVB + 1024 * c + 256 * h2, 128),
                       abf(O_CKV + 4096 * c + 1024 * cc, 512),
                       start=(c == 0), stop=(c == KVLT - 1),
                       waits=(W(sve),) if c == 0 else (),
                       incs=((smm, 1),) if c == KVLT - 1 else ())
                emit("vector", lambda e, h2=h2, cc=cc, p=ps: e.tensor_copy(
                    arK(h2, 512 * cc, 512), p[:, 0:512]),
                    waits=(W(smm),), incs=((sve, 1),))
        for ct in range(NCT):
            ps = psB if ct % 2 == 0 else psC
            for c in range(KVLT):
                mm(ps[:, 0:256],
                   abf(O_CKV + 4096 * c + 256 * ct, 128),
                   abf(O_WKVB + 1024 * c + 512, 256),
                   start=(c == 0), stop=(c == KVLT - 1),
                   waits=(W(sve),) if c == 0 else (),
                   incs=((smm, 1),) if c == KVLT - 1 else ())
            emit("vector", lambda e, ct=ct, p=ps: e.tensor_copy(
                arV(ct), p[:, 0:256]),
                waits=(W(smm),), incs=((sve, 1),))
        pair_ready_v = sve.n

        for tt in range(NT):
            dmaS(maskb, mask_in[tt, :, :], sem=ldM, waits=(W(sve),))
            mask_ld = ldM.n
            for h2 in range(2):
                hh = 2 * pr + h2
                it = p4it["i"]
                p4it["i"] = it + 1
                sc = af32(O_SC[it % 2], 2048)
                for cc in range(NCC):
                    wl = ()
                    if cc == 0:
                        wl = (W(sve, p4it["evac"].get(it - 1, 0)),
                              W(sve, pair_ready_v) if (tt == 0 and h2 == 0)
                              else (sve, 0))
                    mm(psA[:, 512 * cc:512 * (cc + 1)],
                       arQN(h2, 128 * tt, 128),
                       arK(h2, 512 * cc, 512),
                       start=True, stop=False, waits=wl)
                    mm(psA[:, 512 * cc:512 * (cc + 1)],
                       abf(O_QPB + 2048 * h2 + 256 * tt, 128)[0:64, :],
                       kpb[0:64, 512 * cc:512 * (cc + 1)],
                       start=False, stop=True,
                       incs=((smm, 1),) if cc == NCC - 1 else ())
                if len(p4it["pend"]) >= 2:
                    emit_trav(*p4it["pend"].pop(0))
                emit("vector", lambda e, sc=sc: e.tensor_add(
                    sc, psA[:, 0:2048], maskb),
                    waits=(W(smm), W(ldM, mask_ld),
                           W(sac, p4it["exp"].get(it - 2, 0))),
                    incs=((sve, 1),))
                p4it["evac"][it] = sve.n
                emit_tail()
                sb = 8 * (it % 2)
                emit("vector", lambda e, sc=sc, sb=sb: e.reduce_max(
                    sm[:, sb:sb + 1], sc, axis=AX.X),
                    waits=(W(sve),), incs=((sve, 1),))
                emit("vector", lambda e, tt=tt, sb=sb: e.tensor_scalar(
                    out=sm[:, sb + 1:sb + 2], in0=sm[:, sb:sb + 1],
                    scalar1=srt[:, tt:tt + 1], scalar2=-1.0,
                    op0=ALU.mult, op1=ALU.mult),
                    waits=(W(sve),), incs=((sve, 1),))
                prb = abf(PRBS[it % 3], 2048)
                for cc in range(NCC):
                    emit("scalar",
                         lambda e, cc=cc, tt=tt, sc=sc, prb=prb, sb=sb:
                         e.activation(
                             out=prb[:, 512 * cc:512 * (cc + 1)],
                             in_=sc[:, 512 * cc:512 * (cc + 1)],
                             func=AF.Exp, bias=sm[:, sb + 1:sb + 2],
                             scale=srt[:, tt:tt + 1],
                             accum_out=sm[:, sb + 4 + cc:sb + 5 + cc]),
                         waits=(W(sve),) if cc == 0 else (),
                         incs=((sac, 1),))
                p4it["exp"][it] = sac.n
                p4it["meta"] = getattr(p4it, "meta", None) or p4it.setdefault(
                    "meta", {})
                p4it["meta"][it] = (hh, tt)
                p4it["tail"] = (it, it % 3)

    if _dbg_stop('p4'):
        pass
    flush_trav()

    O_OO = 65536
    O_WO = 131072
    O_XF0 = 147456
    O_XF1 = 163840
    O_X1T = 180224
    O_XT5 = 188416

    def oot(ht, t0=0, n=1024):
        return abf(O_OO + 2048 * ht + 2 * t0, n)

    p4_v, p4_a, p4_m = sve.n, sac.n, smm.n
    wo_mm_done = {}
    wosrc = woT.rearrange("(v p) h -> p v h", p=128)
    for ht in range(KT):
        pse, pso = (psB, psC) if ht % 2 == 0 else \
            (psA[:, 0:512], psA[:, 512:1024])
        par = ht % 2
        woblk = abf(O_WO + 8192 * par, 4096)
        woblk3 = woblk.rearrange("p (v h) -> p v h", h=128)
        for si in range(4):
            wl = ()
            if si == 0:
                if ht >= 2:
                    wl = (W(smm, wo_mm_done[ht - 2]),)
                elif ht == 0:
                    wl = (W(sve, p4_v), W(sac, p4_a), W(smm, p4_m))
            emit("gpsimd", lambda e, si=si, ht=ht, woblk3=woblk3:
                 e.dma_start(
                     out=woblk3[:, 8 * si:8 * (si + 1), :],
                     in_=wosrc[:, 8 * si:8 * (si + 1),
                               128 * ht:128 * (ht + 1)]),
                 waits=wl, incs=((ldW[par], 16),))
        wo_ld = ldW[par].n
        for vc in range(KT):
            wob = abf(O_WO + 8192 * par + 256 * vc, 128)
            for tc in range(2):
                ps = pse if tc == 0 else pso
                mm(ps[:, 0:512] if ht % 2 == 0 else ps,
                   wob, ar1(vc, 512 * tc, 512),
                   start=(vc == 0), stop=(vc == KT - 1),
                   waits=((W(ldW[par], wo_ld), W(sve))
                          if vc == 0 else ()) if tc == 0 else (),
                   incs=((smm, 1),) if tc == 1 else ())
        wo_mm_done[ht] = smm.n
        for tc in range(2):
            ps = pse if tc == 0 else pso
            emit("vector", lambda e, ht=ht, tc=tc,
                 p=(pse if tc == 0 else pso), ish=(ht % 2 == 0):
                 e.tensor_copy(oot(ht, 512 * tc, 512),
                               p[:, 0:512] if ish else p),
                 waits=(W(smm),) if tc == 0 else (), incs=((sve, 1),))
    spill_prev = stS.n
    for tt in range(NT):
        xf = af32(O_XF0 if tt % 2 == 0 else O_XF1, 4096)
        dmaS(xf, x_q[128 * tt:128 * (tt + 1), :], sem=ldX[tt % 2],
             waits=(W(sve),))
        xf_ld = ldX[tt % 2].n
        x1tile = abf(O_X1T, 4096)
        for k4 in range(8):
            for j in range(4):
                ht = 4 * k4 + j
                wl = ()
                if j == 0:
                    wl = (W(sve), W(stS, spill_prev)) if k4 == 0 else (W(sve),)
                tr(psT[:, 128 * j:128 * (j + 1)],
                   oot(ht, 128 * tt, 128), waits=wl,
                   incs=((smm, 1),) if j == 3 else ())
            emit("vector", lambda e, k4=k4, x1tile=x1tile, xf=xf:
                 e.tensor_add(
                     x1tile[:, 512 * k4:512 * (k4 + 1)], psT[:, 0:512],
                     xf[:, 512 * k4:512 * (k4 + 1)]),
                 waits=(W(smm), W(ldX[tt % 2], xf_ld)), incs=((sve, 1),))
        xt5 = abf(O_XT5, 4096)
        a_done = layer_norm_apply(x1tile, xt5, 4096, (W(sve),))
        chmajor_from_tok(xt5, tt, lambda k: ar1(k), a_done)
        emit("sync", lambda e, tt=tt, x1tile=x1tile: e.dma_start(
            out=x1_sc[128 * tt:128 * (tt + 1), :], in_=x1tile),
            waits=(W(sve, a_done),), incs=((stS, 16),))
        spill_prev = stS.n
    x1_spilled = stS.n

    if _dbg_stop('p5'):
        pass
    slot_state = {"off": 0, "slot_bytes": 0}

    def dequant_unit(region_off, q8_flat, q8_off, s_flat, z_flat, s_off,
                     nk, n, par, w_memset=(), kseg=8):
        lq, lz = ldQp[par], ldZp[par]
        reg = abf(region_off, nk * n)
        emit("vector", lambda e, reg=reg: e.memset(reg, -8.0),
             waits=w_memset, incs=((sve, 1),))
        ms_v = sve.n
        emit("gpsimd", lambda e, q8=q8_flat, o=q8_off, reg=reg, m=nk * n:
             e.dma_start(out=reg, in_=q8[:, o:o + m], accum_op=ALU.add,
                         max_dma_last_dim=2048),
             waits=(W(sve, ms_v),), incs=((lq, 16),))
        q_tot = lq.n
        segs = [(k0, min(kseg, nk - k0)) for k0 in range(0, nk, kseg)]
        mu_done = []
        for si, (k0, kn) in enumerate(segs):
            lb = ldBp[2 * par + si % 2]
            slot = abf(slot_state["off"]
                       + slot_state["slot_bytes"] * (si % 2), kn * n)
            emit("gpsimd",
                 lambda e, slot=slot, sf=s_flat, o=s_off + k0 * n, m=kn * n:
                     e.dma_start(out=slot,
                                 in_=sf[0:1, o:o + m].broadcast_to([128, m])),
                 waits=(W(sve, mu_done[si - 2]),) if si >= 2 else (),
                 incs=((lb, 16),))
            s_tot = lb.n
            emit("vector",
                 lambda e, a=abf(region_off + 2 * k0 * n, kn * n), slot=slot:
                     e.tensor_mul(a, a, slot),
                 waits=(W(lq, q_tot), W(lb, s_tot)), incs=((sve, 1),))
            mu_done.append(sve.n)
        mu_all = sve.n
        emit("gpsimd", lambda e, zf=z_flat, o=s_off, reg=reg, m=nk * n:
             e.dma_start(out=reg,
                         in_=zf[0:1, o:o + m].broadcast_to([128, m]),
                         accum_op=ALU.add, max_dma_last_dim=2048),
             waits=(W(sve, mu_all),), incs=((lz, 16),))
        return lz.n

    O_WFF = 65536

    def wff(half, k, n=512):
        return abf(O_WFF + 32768 * half + 2 * n * k, n)
    O_SG = 131072
    O_HB = 139264
    O_HT = 140288
    O_SS6 = 148480
    slot_state["off"] = O_SS6
    slot_state["slot_bytes"] = 8192

    p5_v, p5_m = sve.n, smm.n
    units = []
    for j in range(NNC):
        units.append(("g", j))
        units.append(("u", j))
    unit_mm_done = {}
    spill_cnt = {"n": stS.n}
    for ui, (mat, j) in enumerate(units):
        half = ui % 2
        n0, n = 512 * j, _ncw(j)
        q8 = gate_q8 if mat == "g" else up_q8
        s_fl = sbf["gs"] if mat == "g" else sbf["us"]
        z_fl = sbf["gz"] if mat == "g" else sbf["uz"]
        if ui >= 2:
            wm = (W(smm, unit_mm_done[ui - 2]),)
        else:
            wm = (W(sve, p5_v), W(smm, p5_m), W(stS, x1_spilled))
        s_m = sbf["gs"] if mat == "g" else sbf["us"]
        z_m = sbf["gz"] if mat == "g" else sbf["uz"]
        zl = dequant_unit(O_WFF + 32768 * half, q8, KT * n0,
                          s_m, z_m, KT * n0, KT, n, half, w_memset=wm)
        ps_pool = [psA[:, 0:512], psA[:, 512:1024], psA[:, 1024:1536],
                   psA[:, 1536:2048]]
        for tt in range(NT):
            ps = ps_pool[tt % 4]
            for k in range(KT):
                wl = ()
                if k == 0:
                    wl = (W(sac), W(sve))
                    if tt == 0:
                        wl = (W(ldZp[half], zl), W(sac), W(sve))
                mm(ps[:, 0:n], ar1(k, 128 * tt, 128), wff(half, k, n),
                   start=(k == 0), stop=(k == KT - 1),
                   waits=wl, incs=((smm, 1),) if k == KT - 1 else ())
            if mat == "g":
                emit("scalar", lambda e, tt=tt, p=ps, n=n: e.activation(
                    out=abf(O_SG + 1024 * tt, n), in_=p[:, 0:n],
                    func=AF.Silu, bias=0.0, scale=1.0),
                    waits=(W(smm), W(sve)), incs=((sac, 1),))
            else:
                hb = abf(O_HB, 512)
                emit("vector", lambda e, tt=tt, p=ps, n=n, hb=hb: e.tensor_mul(
                    hb[:, 0:n], abf(O_SG + 1024 * tt, n), p[:, 0:n]),
                    waits=(W(smm), W(sac)), incs=((sve, 1),))
                hmul_v = sve.n
                nk4 = (n + 127) // 128
                for jj in range(nk4):
                    tr(psT[:, 128 * jj:128 * (jj + 1)],
                       hb[:, 128 * jj:128 * (jj + 1)],
                       waits=(W(sve, hmul_v), W(sve)) if jj == 0 else (),
                       incs=((smm, 1),) if jj == nk4 - 1 else ())
                for jj in range(nk4):
                    emit("vector", lambda e, jj=jj, tt=tt: e.tensor_copy(
                        abf(O_HT + 2048 * jj + 256 * tt, 128),
                        psT[:, 128 * jj:128 * (jj + 1)]),
                        waits=(W(smm), W(stS, spill_cnt["n"]))
                        if jj == 0 else (),
                        incs=((sve, 1),) if jj == nk4 - 1 else ())
        unit_mm_done[ui] = smm.n
        if mat == "u":
            nk4 = (n + 127) // 128
            for jj in range(nk4):
                it_g = 4 * j + jj
                emit("sync", lambda e, jj=jj, it_g=it_g: e.dma_start(
                    out=hT_sc[128 * it_g:128 * (it_g + 1), :],
                    in_=abf(O_HT + 2048 * jj, 1024)),
                    waits=(W(sve),), incs=((stS, 16),))
            spill_cnt["n"] = stS.n
    h_spilled = stS.n
    p6_m = smm.n

    O_HT7 = 0

    def ht7(buf, i, n=128):
        return abf(O_HT7 + 22528 * buf + 256 * i, n)
    O_X1S = 45056
    O_OUTB = 49152
    O_WD = 65536

    def wdd(buf, i, n=256):
        return abf(O_WD + 44032 * buf + 512 * i, n)
    O_SS7 = 155648
    slot_state["off"] = O_SS7
    slot_state["slot_bytes"] = 8192

    p6_v = sve.n
    NDCQ = 16
    dcq_mm = {}
    ht_segs = [(0, 16), (16, 16), (32, 16), (48, 16), (64, 16), (80, 6)]
    ob_store = {0: 0, 1: 0}
    tt_mm = {}
    for dq in range(NDCQ):
        wbuf = dq % 2
        if dq >= 2:
            wm = (W(smm, dcq_mm[dq - 2]),)
        elif dq == 0:
            wm = (W(sve, p6_v), W(smm, p6_m), W(stS, h_spilled))
        else:
            wm = ()
        deq_done = dequant_unit(O_WD + 44032 * wbuf,
                                down_q8, IT * 256 * dq,
                                sbf["ds"], sbf["dz"], IT * 256 * dq,
                                IT, 256, wbuf, w_memset=wm, kseg=16)
        for tt in range(NT):
            buf = tt % 2
            key = (dq, tt)
            pkey = (dq, tt - 2) if tt >= 2 else (dq - 1, tt + 6)
            if pkey in tt_mm:
                hw = (W(smm, tt_mm[pkey]),)
            else:
                hw = (W(smm, p6_m), W(stS, h_spilled))
            for si, (i0, ni) in enumerate(ht_segs):
                hsrc = hT_sc[128 * i0:128 * (i0 + ni),
                             128 * tt:128 * (tt + 1)]
                hsrc = hsrc.rearrange("(i p) c -> p i c", p=128)
                dst = abf(O_HT7 + 22528 * buf + 256 * i0, ni * 128)
                dst = dst.rearrange("p (i c) -> p i c", c=128)
                dmaS(dst, hsrc, sem=ldH[buf], waits=hw if si == 0 else ())
            dmaS(abf(O_X1S + 1024 * buf, 256),
                 x1_sc[128 * tt:128 * (tt + 1), 256 * dq:256 * (dq + 1)],
                 sem=ldH[buf],
                 waits=(W(stS, x1_spilled),) if (dq == 0 and tt < 2)
                 else (W(sve),))
            h_ld = ldH[buf].n
            ps = psB if tt % 2 == 0 else psC
            for i in range(IT):
                mm(ps[:, 0:256], ht7(buf, i), wdd(wbuf, i),
                   start=(i == 0), stop=(i == IT - 1),
                   waits=(W(ldH[buf], h_ld), W(ldZp[wbuf], deq_done),
                          W(sve))
                   if i == 0 else (),
                   incs=((smm, 1),) if i == IT - 1 else ())
            tt_mm[key] = smm.n
            ob = af32(O_OUTB + 2048 * buf, 256)
            emit("vector", lambda e, p=ps, ob=ob, buf=buf: e.tensor_add(
                ob, p[:, 0:256], abf(O_X1S + 1024 * buf, 256)),
                waits=(W(smm), W(oSp[buf], ob_store[buf])), incs=((sve, 1),))
            emit("sync", lambda e, ob=ob, tt=tt, dq=dq: e.dma_start(
                out=out_t[128 * tt:128 * (tt + 1),
                          256 * dq:256 * (dq + 1)],
                in_=ob),
                waits=(W(sve),), incs=((oSp[buf], 16),))
            ob_store[buf] = oSp[buf].n
        dcq_mm[dq] = smm.n

    ewait("sync", oSp[0])
    ewait("sync", oSp[1])

    with nc.Block() as block:
        @block.tensor
        def _(eng):
            for f in ops["tensor"]:
                f(eng)

        @block.vector
        def _(eng):
            for f in ops["vector"]:
                f(eng)

        @block.scalar
        def _(eng):
            for f in ops["scalar"]:
                f(eng)

        @block.gpsimd
        def _(eng):
            for f in ops["gpsimd"]:
                f(eng)

        @block.sync
        def _(eng):
            for f in ops["sync"]:
                f(eng)

    return nc



_CACHE = {}


def _rope_perm_cols():
    return np.concatenate([np.arange(0, 64, 2), np.arange(1, 64, 2)])


def _prep_weights(inp):
    w = {}
    w["wq_aT"] = np.ascontiguousarray(inp["L0_attn_wq_a"].T)
    w["q_nw"] = np.ascontiguousarray(inp["L0_attn_q_norm_w"].reshape(QL, 1))
    wqbT = np.ascontiguousarray(inp["L0_attn_wq_b"].T)
    perm = np.zeros(NH * 192, dtype=np.int64)
    rp = _rope_perm_cols()
    for h in range(NH):
        b = 192 * h
        perm[b:b + 128] = b + np.arange(128)
        perm[b + 128:b + 192] = b + 128 + rp
    w["wq_bT"] = np.ascontiguousarray(wqbT[:, perm])
    wkvaT = np.ascontiguousarray(inp["L0_attn_wkv_a"].T)
    perm2 = np.concatenate([np.arange(KVL), KVL + rp])
    w["wkv_aT"] = np.ascontiguousarray(wkvaT[:, perm2])
    w["kv_nw"] = np.ascontiguousarray(inp["L0_attn_kv_norm_w"].reshape(KVL, 1))
    wkvbT = np.ascontiguousarray(inp["L0_attn_wkv_b"].T)
    perm3 = np.zeros(NH * 256, dtype=np.int64)
    for p in range(NPAIR):
        b = 512 * p
        h0, h1 = 2 * p, 2 * p + 1
        perm3[b:b + 128] = 256 * h0 + np.arange(128)
        perm3[b + 128:b + 256] = 256 * h1 + np.arange(128)
        perm3[b + 256:b + 384] = 256 * h0 + 128 + np.arange(128)
        perm3[b + 384:b + 512] = 256 * h1 + 128 + np.arange(128)
    w["wkv_bT"] = np.ascontiguousarray(wkvbT[:, perm3])
    w["woT"] = np.ascontiguousarray(inp["L0_attn_wo"].T)
    def q8_cm(q, chunks):
        q8 = np.ascontiguousarray(q).view(np.int8)[:, ::4]
        G = q8.shape[0] // 128
        qt = np.ascontiguousarray(q8).reshape(G, 128, q8.shape[1])
        blocks = [qt[:, :, n0:n0 + n].transpose(1, 0, 2).reshape(128, -1)
                  for (n0, n) in chunks]
        return np.ascontiguousarray(np.concatenate(blocks, axis=1))

    def sz_cm(a, chunks):
        blocks = [np.ascontiguousarray(a[:, n0:n0 + n]).reshape(-1)
                  for (n0, n) in chunks]
        return np.ascontiguousarray(np.concatenate(blocks)).reshape(1, -1)

    gch = [(512 * j, min(512, II - 512 * j)) for j in range(NNC)]
    dch = [(256 * j, 256) for j in range(16)]
    w["gate_q8"] = q8_cm(inp["L0_ffn_gate_q"], gch)
    w["up_q8"] = q8_cm(inp["L0_ffn_up_q"], gch)
    w["down_q8"] = q8_cm(inp["L0_ffn_down_q"], dch)
    w["gate_s"] = sz_cm(inp["L0_ffn_gate_s"], gch)
    w["gate_z"] = sz_cm(inp["L0_ffn_gate_z"], gch)
    w["up_s"] = sz_cm(inp["L0_ffn_up_s"], gch)
    w["up_z"] = sz_cm(inp["L0_ffn_up_z"], gch)
    w["down_s"] = sz_cm(inp["L0_ffn_down_s"], dch)
    w["down_z"] = sz_cm(inp["L0_ffn_down_z"], dch)
    return w


def _rope_tables(p0, n):
    freqs = (1.0 / 10000.0 ** (np.arange(0, 64, 2, dtype=np.float64) / 64.0))
    t = np.arange(p0, p0 + n, dtype=np.float64)
    ang = freqs[:, None] * t[None, :]
    return np.concatenate([np.cos(ang), np.sin(ang)], axis=0).astype(BF)


def _mask(p0):
    s = np.arange(CTX)
    m = np.zeros((NT, 128, CTX), dtype=np.float32)
    for tt in range(NT):
        qpos = p0 + 128 * tt + np.arange(128)
        m[tt] = np.where(s[None, :] <= qpos[:, None], 0.0, -1e9)
    return m.astype(BF)


def kernel(L0_attn_x, L0_attn_wq_a, L0_attn_q_norm_w, L0_attn_wq_b,
           L0_attn_wkv_a, L0_attn_kv_norm_w, L0_attn_wkv_b, L0_attn_wo,
           L0_ffn_gate_q, L0_ffn_gate_s, L0_ffn_gate_z, L0_ffn_up_q,
           L0_ffn_up_s, L0_ffn_up_z, L0_ffn_down_q, L0_ffn_down_s,
           L0_ffn_down_z):
    inp = dict(L0_attn_wq_a=np.asarray(L0_attn_wq_a),
               L0_attn_q_norm_w=np.asarray(L0_attn_q_norm_w),
               L0_attn_wq_b=np.asarray(L0_attn_wq_b),
               L0_attn_wkv_a=np.asarray(L0_attn_wkv_a),
               L0_attn_kv_norm_w=np.asarray(L0_attn_kv_norm_w),
               L0_attn_wkv_b=np.asarray(L0_attn_wkv_b),
               L0_attn_wo=np.asarray(L0_attn_wo),
               L0_ffn_gate_q=np.asarray(L0_ffn_gate_q),
               L0_ffn_gate_s=np.asarray(L0_ffn_gate_s),
               L0_ffn_gate_z=np.asarray(L0_ffn_gate_z),
               L0_ffn_up_q=np.asarray(L0_ffn_up_q),
               L0_ffn_up_s=np.asarray(L0_ffn_up_s),
               L0_ffn_up_z=np.asarray(L0_ffn_up_z),
               L0_ffn_down_q=np.asarray(L0_ffn_down_q),
               L0_ffn_down_s=np.asarray(L0_ffn_down_s),
               L0_ffn_down_z=np.asarray(L0_ffn_down_z))
    x = np.asarray(L0_attn_x)
    w = _prep_weights(inp)

    if "nc" not in _CACHE:
        _CACHE["nc"] = build_nc()
    nc = _CACHE["nc"]

    in_maps = []
    for c in range(8):
        b, half = c // 2, c % 2
        p0 = 1024 * half
        m = dict(w)
        m["x_q"] = np.ascontiguousarray(x[b, p0:p0 + T, :])
        m["x_b"] = np.ascontiguousarray(x[b])
        m["rope_q"] = _rope_tables(p0, T)
        m["rope_k"] = _rope_tables(0, CTX)
        m["mask_in"] = _mask(p0)
        in_maps.append(m)

    res = run_bass_kernel_spmd(nc, in_maps, core_ids=list(range(8)))
    out = np.empty((4, CTX, D), dtype=np.float32)
    for c in range(8):
        b, half = c // 2, c % 2
        out[b, 1024 * half:1024 * half + T, :] = res.results[c]["out"]
    return out


# version 09
# speedup vs baseline: 1.3494x; 1.0649x over previous
import numpy as np
import ml_dtypes
import concourse.bass as bass
import concourse.mybir as mybir
from concourse.bass_utils import run_bass_kernel_spmd

F32 = mybir.dt.float32
BF16 = mybir.dt.bfloat16
I8 = mybir.dt.int8
AF = mybir.ActivationFunctionType
ALU = mybir.AluOpType
AX = mybir.AxisListType
BF = ml_dtypes.bfloat16

D = 4096
T = 1024
NT = 8
CTX = 2048
NCC = 4
NCT = 16
KT = 32
QL = 1536
QLT = 12
KVL = 512
KVLT = 4
NH = 32
NPAIR = 16
II = 11008
IT = 86
NNC = 22
NDC = 8
SCALE = float(1.0 / np.sqrt(192.0))
LN_EPS = 1e-5
RMS_EPS = 1e-6


def _ncw(j):
    return min(512, II - 512 * j)


def build_nc(dbg_phase=None):
    nc = bass.Bass("TRN2", target_bir_lowering=False)

    def din(name, shape, dt):
        return nc.dram_tensor(name, list(shape), dt, kind="ExternalInput").ap()

    x_q = din("x_q", (T, D), F32)
    x_b = din("x_b", (CTX, D), F32)
    wq_aT = din("wq_aT", (D, QL), F32)
    q_nw = din("q_nw", (QL, 1), F32)
    wq_bT = din("wq_bT", (QL, NH * 192), F32)
    wkv_aT = din("wkv_aT", (D, KVL + 64), F32)
    kv_nw = din("kv_nw", (KVL, 1), F32)
    wkv_bT = din("wkv_bT", (KVL, NH * 256), F32)
    woT = din("woT", (D, D), F32)
    gate_q8 = din("gate_q8", (128, KT * II), I8)
    up_q8 = din("up_q8", (128, KT * II), I8)
    down_q8 = din("down_q8", (128, IT * D), I8)
    gate_s = din("gate_s", (1, KT * II), F32)
    gate_z = din("gate_z", (1, KT * II), F32)
    up_s = din("up_s", (1, KT * II), F32)
    up_z = din("up_z", (1, KT * II), F32)
    down_s = din("down_s", (1, IT * D), F32)
    down_z = din("down_z", (1, IT * D), F32)
    rope_q = din("rope_q", (64, T), BF16)
    rope_k = din("rope_k", (64, CTX), BF16)
    mask_in = din("mask_in", (NT, 128, CTX), BF16)

    out_t = nc.dram_tensor("out", [T, D], F32, kind="ExternalOutput").ap()
    dbg_t = nc.dram_tensor("dbg_arena", [128, 192 * 1024], mybir.dt.uint8,
                           kind="ExternalOutput").ap()

    sbf = {}
    for nm, tot in [("gs", KT * II), ("gz", KT * II), ("us", KT * II),
                    ("uz", KT * II), ("ds", IT * D), ("dz", IT * D)]:
        sbf[nm] = nc.dram_tensor("sc_" + nm, [1, tot], BF16, kind="Internal").ap()
    hT_sc = nc.dram_tensor("hT_sc", [II, T], BF16, kind="Internal").ap()
    x1_sc = nc.dram_tensor("x1_sc", [T, D], BF16, kind="Internal").ap()
    rq_sc = nc.dram_tensor("rq_sc", [1, T], F32, kind="Internal").ap()
    rkv_sc = nc.dram_tensor("rkv_sc", [1, CTX], F32, kind="Internal").ap()

    ident = nc.alloc_sbuf_tensor("ident", [128, 128], BF16).ap()
    ones_col = nc.alloc_sbuf_tensor("ones_col", [128, 1], BF16).ap()
    qnw_sb = nc.alloc_sbuf_tensor("qnw_sb", [128, QLT], F32).ap()
    kvnw_sb = nc.alloc_sbuf_tensor("kvnw_sb", [128, KVLT], F32).ap()
    ropq_c = nc.alloc_sbuf_tensor("ropq_c", [32, T], BF16).ap()
    ropq_s = nc.alloc_sbuf_tensor("ropq_s", [32, T], BF16).ap()
    ropk_c = nc.alloc_sbuf_tensor("ropk_c", [32, CTX], BF16).ap()
    ropk_s = nc.alloc_sbuf_tensor("ropk_s", [32, CTX], BF16).ap()
    stat = nc.alloc_sbuf_tensor("stat", [128, 64], F32).ap()
    mv = nc.alloc_sbuf_tensor("mv", [128, 8], F32).ap()
    srt = nc.alloc_sbuf_tensor("srt", [128, NT], F32).ap()
    sm = nc.alloc_sbuf_tensor("sm", [128, 16], F32).ap()
    cst = nc.alloc_sbuf_tensor("cst", [128, 8], F32).ap()
    CONSTS = [float(LN_EPS), float(RMS_EPS), 1.0, 0.0,
              float(1.0 / QL), float(1.0 / KVL)]
    for ci, cv in enumerate(CONSTS):
        nc.const_aps.register(F32, cv, cst[:, ci:ci + 1])

    AR_BYTES = 192 * 1024
    arena_t = nc.alloc_sbuf_tensor("arena", [128, AR_BYTES], mybir.dt.uint8)
    ar_bf = arena_t.ap().bitcast(BF16)
    ar_f32 = arena_t.ap().bitcast(F32)

    def abf(off, n):
        assert off % 2 == 0 and off + 2 * n <= AR_BYTES, (off, n)
        return ar_bf[:, off // 2: off // 2 + n]

    def af32(off, n):
        assert off % 4 == 0 and off + 4 * n <= AR_BYTES, (off, n)
        return ar_f32[:, off // 4: off // 4 + n]

    def ar1(tile, t0=0, n=1024):
        return abf(2048 * tile + 2 * t0, n)

    def qlra(j, c0=0, n=1024):
        return abf(65536 + 2048 * j + 2 * c0, n)

    psA = nc.alloc_psum_tensor("psA", [128, 2048], F32).ap()
    psB = nc.alloc_psum_tensor("psB", [128, 512], F32).ap()
    psC = nc.alloc_psum_tensor("psC", [128, 512], F32).ap()
    psT = nc.alloc_psum_tensor("psT", [128, 1024], BF16).ap()
    psD = nc.alloc_psum_tensor("psD", [128, 512], F32).ap()

    class Sem:
        __slots__ = ("h", "n")

        def __init__(self, name):
            self.h = nc.alloc_semaphore(name)
            self.n = 0

    ldS = Sem("ldS")
    ldG = Sem("ldG")
    ldX = [Sem("ldX0"), Sem("ldX1")]
    ldX3 = Sem("ldX3")
    ldM = Sem("ldM")
    ldXS = Sem("ldXS")
    ldW = [Sem("ldW0"), Sem("ldW1")]
    ldH = [Sem("ldH0"), Sem("ldH1")]
    ldQp = [Sem("ldQ0"), Sem("ldQ1")]
    ldBp = [Sem("ldB0"), Sem("ldB1"), Sem("ldB2"), Sem("ldB3")]
    ldZp = [Sem("ldZ0"), Sem("ldZ1")]
    stS = Sem("stS")
    oSp = [Sem("oS0"), Sem("oS1")]
    smm = Sem("smm")
    sve = Sem("sve")
    sac = Sem("sac")
    spo = Sem("spo")

    ops = {"tensor": [], "vector": [], "scalar": [], "gpsimd": [], "sync": []}
    stopped = {"v": False}

    def emit(eng, build, waits=(), incs=()):
        if stopped["v"]:
            for s, a in incs:
                s.n += a
            return
        wl = tuple((s.h, (s.n if v is None else v)) for (s, v) in waits)
        il = []
        for s, a in incs:
            s.n += a
            il.append((s.h, a))
        il = tuple(il)

        def f(e, build=build, wl=wl, il=il):
            for h, v in wl:
                e.wait_ge(h, v)
            ins = build(e)
            for h, a in il:
                ins = ins.then_inc(h, a)
        ops[eng].append(f)

    def W(sem, val=None):
        return (sem, val)

    def ewait(eng, sem, val=None):
        if stopped["v"]:
            return
        v = sem.n if val is None else val
        ops[eng].append(lambda e, h=sem.h, v=v: e.wait_ge(h, v))

    def dmaS(dst, src, sem=ldS, waits=(), slow=False):
        if slow:
            emit("sync", lambda e, d=dst, s=src: e.dma_start(
                out=d, in_=s, allow_slow_non_contiguous=True),
                waits=waits, incs=((sem, 16),))
        else:
            emit("sync", lambda e, d=dst, s=src: e.dma_start(out=d, in_=s),
                 waits=waits, incs=((sem, 16),))

    def dmaG(dst, src, accum=None, sem=ldG, waits=()):
        if accum is None:
            emit("gpsimd", lambda e, d=dst, s=src: e.dma_start(out=d, in_=s),
                 waits=waits, incs=((sem, 16),))
        else:
            emit("gpsimd",
                 lambda e, d=dst, s=src, a=accum:
                     e.dma_start(out=d, in_=s, accum_op=a),
                 waits=waits, incs=((sem, 16),))

    def mm(out, lhsT, rhs, start, stop, waits=(), incs=()):
        emit("tensor",
             lambda e, o=out, l=lhsT, r=rhs, s1=start, s2=stop:
                 e.matmul(o, l, r, start=s1, stop=s2),
             waits=waits, incs=incs)

    def tr(out, in_, waits=(), incs=()):
        emit("tensor", lambda e, o=out, i=in_: e.transpose(o, i, ident),
             waits=waits, incs=incs)

    for ci, cv in enumerate(CONSTS):
        emit("vector", lambda e, ci=ci, cv=cv: e.memset(cst[:, ci:ci + 1], cv),
             incs=((sve, 1),))
    if dbg_phase is not None:
        emit("vector", lambda e: e.memset(ar_f32[:, :], 0.0),
             incs=((sve, 1),))
    emit("gpsimd", lambda e: e.memset(ident, 0.0), incs=((spo, 1),))
    emit("gpsimd", lambda e: e.memset(ones_col, 1.0), incs=((spo, 1),))
    ewait("gpsimd", spo)
    emit("gpsimd", lambda e: e.affine_select(
        out=ident, in_=ident, compare_op=ALU.not_equal, fill=1.0,
        base=0, pattern=[[-1, 128]], channel_multiplier=1), incs=((spo, 1),))

    dmaS(qnw_sb[:, 0:QLT], q_nw.rearrange("(j p) o -> p (j o)", p=128),
         slow=True)
    dmaS(kvnw_sb[:, 0:KVLT], kv_nw.rearrange("(j p) o -> p (j o)", p=128),
         slow=True)
    dmaS(ropq_c[:, :], rope_q[0:32, :])
    dmaS(ropq_s[:, :], rope_q[32:64, :])
    dmaS(ropk_c[:, :], rope_k[0:32, :])
    dmaS(ropk_s[:, :], rope_k[32:64, :])

    cvt_jobs = [(gate_s, sbf["gs"], KT * II), (gate_z, sbf["gz"], KT * II),
                (up_s, sbf["us"], KT * II), (up_z, sbf["uz"], KT * II),
                (down_s, sbf["ds"], IT * D), (down_z, sbf["dz"], IT * D)]
    for src, dst, tot in cvt_jobs:
        per = tot // 128
        cv_f = af32(0, per)
        cv_b = abf(4 * per, per)
        flat = src.rearrange("o (p m) -> (o p) m", p=128)
        dmaS(cv_f, flat, waits=(W(sve),))
        emit("vector", lambda e, a=cv_b, b=cv_f: e.tensor_copy(a, b),
             waits=(W(ldS),), incs=((sve, 1),))
        emit("sync", lambda e, d=dst, s=cv_b:
             e.dma_start(out=d.rearrange("o (p m) -> p (o m)", p=128), in_=s),
             waits=(W(sve),), incs=((stS, 16),))
        ewait("vector", stS)
    cvt_done_st = stS.n

    def layer_norm_apply(src, dst_bf, width, pre_waits):
        nch = width // 512
        for j in range(nch):
            emit("vector",
                 lambda e, s=src, j=j: e.bn_stats(
                     stat[:, 6 * j:6 * j + 6], s[:, 512 * j:512 * (j + 1)]),
                 waits=pre_waits if j == 0 else (), incs=((sve, 1),))
        emit("vector", lambda e, n=nch: e.bn_aggr(mv[:, 0:2], stat[:, 0:6 * n]),
             waits=(W(sve),), incs=((sve, 1),))
        emit("scalar", lambda e: e.activation(
            out=mv[:, 2:3], in_=mv[:, 1:2], func=AF.Sqrt,
            bias=float(LN_EPS), scale=1.0),
            waits=(W(sve),), incs=((sac, 1),))
        emit("vector", lambda e: e.reciprocal(mv[:, 3:4], mv[:, 2:3]),
             waits=(W(sac),), incs=((sve, 1),))
        emit("vector", lambda e, s=src, d=dst_bf: e.tensor_scalar(
            out=d, in0=s, scalar1=mv[:, 0:1], scalar2=mv[:, 3:4],
            op0=ALU.subtract, op1=ALU.mult),
            waits=(W(sve),), incs=((sve, 1),))
        return sve.n

    def chmajor_from_tok(src_bf, tt, dst_fn, src_done_sve, nk=KT):
        for k4 in range((nk + 3) // 4):
            ks = list(range(4 * k4, min(4 * k4 + 4, nk)))
            for idx, k in enumerate(ks):
                wl = ()
                if idx == 0:
                    wl = (W(sve, src_done_sve), W(sve)) if k4 == 0 else (W(sve),)
                tr(psT[:, 128 * idx:128 * (idx + 1)],
                   src_bf[:, 128 * k:128 * (k + 1)],
                   waits=wl,
                   incs=((smm, 1),) if idx == len(ks) - 1 else ())
            for idx, k in enumerate(ks):
                emit("vector",
                     lambda e, k=k, idx=idx, d=dst_fn, tt=tt: e.tensor_copy(
                         d(k)[:, 128 * tt:128 * (tt + 1)],
                         psT[:, 128 * idx:128 * (idx + 1)]),
                     waits=(W(smm),) if idx == 0 else (),
                     incs=((sve, 1),) if idx == len(ks) - 1 else ())


    def _dbg_stop(tag):
        if dbg_phase != tag or stopped["v"]:
            return False
        ewait("sync", sve)
        ewait("sync", smm)
        ewait("sync", sac)
        emit("sync", lambda e: e.dma_start(out=dbg_t, in_=arena_t.ap()),
             waits=(), incs=((stS, 16),))
        ewait("sync", stS)
        stopped["v"] = True
        return True

    ewait("tensor", spo)
    O_XA = 90112
    O_XB2 = 106496
    O_XT = 122880
    apply_done = {}
    for tt in range(NT):
        xa = af32(O_XA if tt % 2 == 0 else O_XB2, 4096)
        wl = ()
        if tt >= 2:
            wl = (W(sve, apply_done[tt - 2]),)
        if tt == 0:
            wl = (W(stS, cvt_done_st),)
        dmaS(xa, x_q[128 * tt:128 * (tt + 1), :], sem=ldX[tt % 2], waits=wl)
        xt = abf(O_XT, 4096)
        a_done = layer_norm_apply(
            xa, xt, 4096,
            (W(ldX[tt % 2]),) + ((W(smm),) if tt > 0 else ()))
        apply_done[tt] = a_done
        chmajor_from_tok(xt, tt, lambda k: ar1(k), a_done)

    if _dbg_stop('p1'):
        pass
    O_WQA = 90112
    p1_v = sve.n
    for k in range(KT):
        dmaG(abf(O_WQA + 3072 * k, 1536), wq_aT[128 * k:128 * (k + 1), :],
             waits=(W(sve, p1_v),) if k == 0 else ())
    wqa_ld = ldG.n
    for j in range(QLT):
        for tc in range(2):
            ps = psB if (2 * j + tc) % 2 == 0 else psC
            for k in range(KT):
                mm(ps[:, 0:512],
                   abf(O_WQA + 3072 * k + 256 * j, 128),
                   ar1(k, 512 * tc, 512),
                   start=(k == 0), stop=(k == KT - 1),
                   waits=((W(ldG, wqa_ld), W(sve)) if k == 0 else ()),
                   incs=((smm, 1),) if k == KT - 1 else ())
            emit("vector",
                 lambda e, j=j, tc=tc, p=ps: e.tensor_copy(
                     qlra(j, 512 * tc, 512), p[:, 0:512]),
                 waits=(W(smm),), incs=((sve, 1),))
    O_SQ = 188416
    rowv = af32(189440, 1024)
    for tc in range(2):
        psq = psD if tc == 0 else psB
        for j in range(QLT):
            sq = abf(O_SQ, 512)
            emit("vector",
                 lambda e, j=j, tc=tc, s=sq: e.tensor_mul(
                     s, qlra(j, 512 * tc, 512), qlra(j, 512 * tc, 512)),
                 waits=(W(sve), W(smm)), incs=((sve, 1),))
            mm(psq[0:1, 0:512], ones_col, sq,
               start=(j == 0), stop=(j == QLT - 1),
               waits=(W(sve),), incs=((smm, 1),))
        emit("scalar", lambda e, tc=tc, p=psq: e.activation(
            out=rowv[0:1, 512 * tc:512 * (tc + 1)], in_=p[0:1, 0:512],
            func=AF.Sqrt, bias=float(RMS_EPS), scale=float(1.0 / QL)),
            waits=(W(smm),), incs=((sac, 1),))
    emit("vector", lambda e: e.reciprocal(rowv[0:1, 0:1024],
                                          rowv[0:1, 0:1024]),
         waits=(W(sac),), incs=((sve, 1),))
    emit("sync", lambda e: e.dma_start(out=rq_sc, in_=rowv[0:1, 0:1024]),
         waits=(W(sve),), incs=((stS, 16),))
    dmaS(srt[:, 0:NT], rq_sc.rearrange("o (a p) -> p (o a)", p=128),
         waits=(W(stS),), slow=True)
    emit("vector", lambda e: e.tensor_scalar(
        out=srt[:, 0:NT], in0=srt[:, 0:NT], scalar1=float(SCALE), scalar2=None,
        op0=ALU.mult), waits=(W(ldS),), incs=((sve, 1),))

    if _dbg_stop('p2'):
        pass
    O_WKA = 90112
    O_XLC = 126976
    O_CKV = 159744
    O_KPE = 176128
    O_KPO = 180224
    O_KPB = 184320
    O_RSB = 188416
    O_TMP1 = 135168
    O_TMP2 = 139264
    rowv = af32(143360, 2048)
    p2_v, p2_m = sve.n, smm.n
    for k in range(KT):
        dmaG(abf(O_WKA + 1152 * k, 576), wkv_aT[128 * k:128 * (k + 1), :],
             waits=(W(sve, p2_v), W(smm, p2_m)) if k == 0 else ())
    wka_ld = ldG.n
    O_X3 = 0
    O_XT3 = 16384
    for cc in range(NCC):
        for st4 in range(4):
            ct = 4 * cc + st4
            xa = af32(O_X3, 4096)
            wl = (W(sve),) if ct > 0 else (W(sve, p2_v),)
            dmaS(xa, x_b[128 * ct:128 * (ct + 1), :], sem=ldX3, waits=wl)
            xt = abf(O_XT3, 4096)
            a_done = layer_norm_apply(xa, xt, 4096, (W(ldX3), W(smm)))
            chmajor_from_tok(xt, st4, lambda k: abf(O_XLC + 1024 * k, 512),
                             a_done)
        ewait("tensor", sve)
        for c in range(6):
            ps = [psA[:, 0:512], psA[:, 512:1024], psA[:, 1024:1536],
                  psA[:, 1536:2048], psB[:, 0:512], psD[:, 0:512]][c]
            mcols = 128 if c < 4 else 32
            coff = 256 * c if c < 4 else 1024 + 64 * (c - 4)
            for k in range(KT):
                wl = ()
                if k == 0:
                    wl = (W(sve),)
                    if cc == 0 and c == 0:
                        wl = (W(ldG, wka_ld), W(sve))
                mm(ps[0:mcols, 0:512],
                   abf(O_WKA + 1152 * k + coff, mcols),
                   abf(O_XLC + 1024 * k, 512),
                   start=(k == 0), stop=(k == KT - 1),
                   waits=wl,
                   incs=((smm, 1),) if k == KT - 1 else ())
            if c < 4:
                emit("vector", lambda e, c=c, cc=cc, p=ps: e.tensor_copy(
                    abf(O_CKV + 4096 * c + 1024 * cc, 512), p[:, 0:512]),
                    waits=(W(smm),), incs=((sve, 1),))
            else:
                emit("vector", lambda e, c=c, cc=cc, p=ps: e.tensor_copy(
                    abf((O_KPE if c == 4 else O_KPO) + 1024 * cc, 512)
                    [0:32, :],
                    p[0:32, 0:512]),
                    waits=(W(smm),), incs=((sve, 1),))
    kpe = abf(O_KPE, 2048)
    kpo = abf(O_KPO, 2048)
    tmp1 = abf(O_TMP1, 2048)
    tmp2 = abf(O_TMP2, 2048)
    kpb = abf(O_KPB, 2048)
    emit("vector", lambda e: e.tensor_mul(tmp1[0:32, :], kpe[0:32, :],
                                          ropk_c[:, :]),
         waits=(W(sve),), incs=((sve, 1),))
    emit("vector", lambda e: e.tensor_mul(tmp2[0:32, :], kpo[0:32, :],
                                          ropk_s[:, :]), incs=((sve, 1),))
    emit("vector", lambda e: e.tensor_sub(kpb[0:32, :], tmp1[0:32, :],
                                          tmp2[0:32, :]),
         waits=(W(sve),), incs=((sve, 1),))
    emit("vector", lambda e: e.tensor_mul(tmp1[0:32, :], kpe[0:32, :],
                                          ropk_s[:, :]),
         waits=(W(sve),), incs=((sve, 1),))
    emit("vector", lambda e: e.tensor_mul(tmp2[0:32, :], kpo[0:32, :],
                                          ropk_c[:, :]), incs=((sve, 1),))
    emit("vector", lambda e: e.tensor_add(kpb[32:64, :], tmp1[0:32, :],
                                          tmp2[0:32, :]),
         waits=(W(sve),), incs=((sve, 1),))
    for c in range(KVLT):
        sq = abf(O_XLC, 2048)
        emit("vector", lambda e, c=c, s=sq: e.tensor_mul(
            s, abf(O_CKV + 4096 * c, 2048), abf(O_CKV + 4096 * c, 2048)),
            waits=(W(sve), W(smm)), incs=((sve, 1),))
        for cc in range(NCC):
            mm(psA[0:1, 512 * cc:512 * (cc + 1)], ones_col,
               sq[:, 512 * cc:512 * (cc + 1)],
               start=(c == 0), stop=(c == KVLT - 1),
               waits=(W(sve),) if cc == 0 else (), incs=((smm, 1),))
    for cc in range(NCC):
        emit("scalar", lambda e, cc=cc: e.activation(
            out=rowv[0:1, 512 * cc:512 * (cc + 1)],
            in_=psA[0:1, 512 * cc:512 * (cc + 1)],
            func=AF.Sqrt, bias=float(RMS_EPS), scale=float(1.0 / KVL)),
            waits=(W(smm),), incs=((sac, 1),))
    emit("vector", lambda e: e.reciprocal(rowv[0:1, 0:2048],
                                          rowv[0:1, 0:2048]),
         waits=(W(sac),), incs=((sve, 1),))
    emit("sync", lambda e: e.dma_start(out=rkv_sc, in_=rowv[0:1, 0:2048]),
         waits=(W(sve),), incs=((stS, 16),))
    rsb = af32(O_RSB, 2048)
    emit("gpsimd", lambda e: e.dma_start(
        out=rsb, in_=rkv_sc.broadcast_to([128, CTX])),
        waits=(W(stS), W(sve)), incs=((ldG, 16),))
    for c in range(KVLT):
        emit("vector", lambda e, c=c: e.tensor_mul(
            abf(O_CKV + 4096 * c, 2048), abf(O_CKV + 4096 * c, 2048), rsb),
            waits=(W(ldG),) if c == 0 else (), incs=((sve, 1),))
    ckv_ready_v = sve.n

    if _dbg_stop('p3'):
        pass
    O_WKVB = 90112
    O_WQB = 94208
    O_KN = 103424
    O_VT = 111616
    O_QN = 119808
    O_QPF = 123904
    O_QPB = 128000
    O_PRB = 132096
    O_PRT = 136192
    O_MSK = 140288

    def arK(h2, n0, n):
        return abf(O_KN + 4096 * h2 + 2 * n0, n)

    def arV(ct, n0=0, n=256):
        return abf(O_VT + 512 * ct + 2 * n0, n)

    def arQN(h2, t0, n):
        return abf(O_QN + 2048 * h2 + 2 * t0, n)

    O_PRB2 = 152576
    O_PRB3 = 176128
    PRBS = [O_PRB, O_PRB2, O_PRB3]
    probsT = abf(O_PRT, 2048)
    maskb = abf(O_MSK, 2048)
    O_SC = [144384, 188416]
    p4it = {"i": 0, "evac": {}, "exp": {}, "pend": [], "tail": None}

    def emit_tail():
        if p4it["tail"] is None:
            return
        itp, par3 = p4it["tail"]
        p4it["tail"] = None
        sb = 8 * (itp % 2)
        prb = abf(PRBS[par3], 2048)
        emit("vector", lambda e, sb=sb: e.reduce_sum(
            sm[:, sb + 2:sb + 3], sm[:, sb + 4:sb + 8], axis=AX.X),
            waits=(W(sac, p4it["exp"][itp]),), incs=((sve, 1),))
        emit("vector", lambda e, sb=sb: e.reciprocal(
            sm[:, sb + 3:sb + 4], sm[:, sb + 2:sb + 3]),
            waits=(W(sve),), incs=((sve, 1),))
        emit("vector", lambda e, prb=prb, sb=sb: e.tensor_scalar(
            out=prb, in0=prb, scalar1=sm[:, sb + 3:sb + 4], scalar2=None,
            op0=ALU.mult),
            waits=(W(sve),), incs=((sve, 1),))
        p4it["pend"].append((p4it["meta"][itp][0], p4it["meta"][itp][1],
                             par3, sve.n))

    def emit_trav(hh, tt, par, pnorm_v):
        prb = abf(PRBS[par], 2048)
        h2 = hh % 2
        for b4 in range(4):
            for j in range(4):
                ct = 4 * b4 + j
                wl = ()
                if j == 0:
                    wl = (W(sve, pnorm_v), W(sac), W(sve)) if b4 == 0 \
                        else (W(sac),)
                tr(psT[:, 128 * j:128 * (j + 1)],
                   prb[:, 128 * ct:128 * (ct + 1)],
                   waits=wl,
                   incs=((smm, 1),) if j == 3 else ())
            emit("scalar", lambda e, b4=b4: e.copy(
                probsT[:, 512 * b4:512 * (b4 + 1)], psT[:, 0:512]),
                waits=(W(smm),), incs=((sac, 1),))
        for ct in range(NCT):
            mm(psD[:, 0:128],
               arV(ct, 128 * h2, 128),
               probsT[:, 128 * ct:128 * (ct + 1)],
               start=(ct == 0), stop=(ct == NCT - 1),
               waits=(W(sac),) if ct == 0 else (),
               incs=((smm, 1),) if ct == NCT - 1 else ())
        emit("scalar", lambda e, hh=hh, tt=tt: e.copy(
            ar1(hh, 128 * tt, 128), psD[:, 0:128]),
            waits=(W(smm),), incs=((sac, 1),))

    def flush_trav():
        emit_tail()
        while p4it["pend"]:
            emit_trav(*p4it["pend"].pop(0))

    for pr in range(NPAIR):
        flush_trav()
        pm0 = smm.n
        for c in range(KVLT):
            wl = ()
            if c == 0:
                wl = (W(sve, ckv_ready_v),) if pr == 0 else (W(smm, pm0),)
            dmaG(abf(O_WKVB + 1024 * c, 512),
                 wkv_bT[128 * c:128 * (c + 1), 512 * pr:512 * (pr + 1)],
                 waits=wl)
        for c in range(QLT):
            dmaG(abf(O_WQB + 768 * c, 384),
                 wq_bT[128 * c:128 * (c + 1), 384 * pr:384 * (pr + 1)])
        wld = ldG.n
        for c in range(KVLT):
            emit("vector", lambda e, c=c: e.tensor_scalar(
                out=abf(O_WKVB + 1024 * c, 512), in0=abf(O_WKVB + 1024 * c, 512),
                scalar1=kvnw_sb[:, c:c + 1], scalar2=None, op0=ALU.mult),
                waits=(W(ldG, wld),) if c == 0 else (), incs=((sve, 1),))
        for c in range(QLT):
            emit("vector", lambda e, c=c: e.tensor_scalar(
                out=abf(O_WQB + 768 * c, 384), in0=abf(O_WQB + 768 * c, 384),
                scalar1=qnw_sb[:, c:c + 1], scalar2=None, op0=ALU.mult),
                incs=((sve, 1),))
        wfold = sve.n

        for h2 in range(2):
            for tc in range(2):
                ps = psB if tc == 0 else psC
                for c in range(QLT):
                    mm(ps[:, 0:512],
                       abf(O_WQB + 768 * c + 384 * h2, 128),
                       qlra(c, 512 * tc, 512),
                       start=(c == 0), stop=(c == QLT - 1),
                       waits=(W(sve, wfold), W(sve)) if c == 0 else (),
                       incs=((smm, 1),) if c == QLT - 1 else ())
                emit("vector", lambda e, h2=h2, tc=tc, p=ps: e.tensor_copy(
                    arQN(h2, 512 * tc, 512), p[:, 0:512]),
                    waits=(W(smm),), incs=((sve, 1),))
            for pe_half in range(2):
                ps = psB if pe_half == 0 else psC
                for tc in range(2):
                    for c in range(QLT):
                        mm(ps[0:32, 512 * 0:512] if False else
                           ps[0:32, 0:512],
                           abf(O_WQB + 768 * c + 384 * h2 + 256
                               + 64 * pe_half, 32),
                           qlra(c, 512 * tc, 512),
                           start=(c == 0), stop=(c == QLT - 1),
                           waits=(W(sve),) if c == 0 else (),
                           incs=((smm, 1),) if c == QLT - 1 else ())
                    emit("vector", lambda e, tc=tc, p=ps, ph=pe_half:
                         e.tensor_copy(
                             abf(O_QPF + 2048 * ph + 1024 * tc, 512)
                             [0:32, :],
                             p[0:32, 0:512]),
                         waits=(W(smm),), incs=((sve, 1),))
            qpe = abf(O_QPF, 1024)
            qpo = abf(O_QPF + 2048, 1024)
            qt1 = abf(O_PRB, 1024)
            qt2 = abf(O_PRB + 2048, 1024)
            qpb = abf(O_QPB + 2048 * h2, 1024)
            emit("vector", lambda e, qpe=qpe, qt1=qt1: e.tensor_mul(
                qt1[0:32, :], qpe[0:32, :], ropq_c[:, :]),
                waits=(W(sve), W(smm)), incs=((sve, 1),))
            emit("vector", lambda e, qpo=qpo, qt2=qt2: e.tensor_mul(
                qt2[0:32, :], qpo[0:32, :], ropq_s[:, :]),
                incs=((sve, 1),))
            emit("vector", lambda e, qpb=qpb, qt1=qt1, qt2=qt2: e.tensor_sub(
                qpb[0:32, :], qt1[0:32, :], qt2[0:32, :]),
                waits=(W(sve),), incs=((sve, 1),))
            emit("vector", lambda e, qpe=qpe, qt1=qt1: e.tensor_mul(
                qt1[0:32, :], qpe[0:32, :], ropq_s[:, :]),
                waits=(W(sve),), incs=((sve, 1),))
            emit("vector", lambda e, qpo=qpo, qt2=qt2: e.tensor_mul(
                qt2[0:32, :], qpo[0:32, :], ropq_c[:, :]),
                incs=((sve, 1),))
            emit("vector", lambda e, qpb=qpb, qt1=qt1, qt2=qt2: e.tensor_add(
                qpb[32:64, :], qt1[0:32, :], qt2[0:32, :]),
                waits=(W(sve),), incs=((sve, 1),))

        for h2 in range(2):
            for cc in range(NCC):
                ps = psB if cc % 2 == 0 else psC
                for c in range(KVLT):
                    mm(ps[:, 0:512],
                       abf(O_WKVB + 1024 * c + 256 * h2, 128),
                       abf(O_CKV + 4096 * c + 1024 * cc, 512),
                       start=(c == 0), stop=(c == KVLT - 1),
                       waits=(W(sve),) if c == 0 else (),
                       incs=((smm, 1),) if c == KVLT - 1 else ())
                emit("vector", lambda e, h2=h2, cc=cc, p=ps: e.tensor_copy(
                    arK(h2, 512 * cc, 512), p[:, 0:512]),
                    waits=(W(smm),), incs=((sve, 1),))
        for ct in range(NCT):
            ps = psB if ct % 2 == 0 else psC
            for c in range(KVLT):
                mm(ps[:, 0:256],
                   abf(O_CKV + 4096 * c + 256 * ct, 128),
                   abf(O_WKVB + 1024 * c + 512, 256),
                   start=(c == 0), stop=(c == KVLT - 1),
                   waits=(W(sve),) if c == 0 else (),
                   incs=((smm, 1),) if c == KVLT - 1 else ())
            emit("vector", lambda e, ct=ct, p=ps: e.tensor_copy(
                arV(ct), p[:, 0:256]),
                waits=(W(smm),), incs=((sve, 1),))
        pair_ready_v = sve.n

        for tt in range(NT):
            dmaS(maskb, mask_in[tt, :, :], sem=ldM, waits=(W(sve),))
            mask_ld = ldM.n
            for h2 in range(2):
                hh = 2 * pr + h2
                it = p4it["i"]
                p4it["i"] = it + 1
                sc = af32(O_SC[it % 2], 2048)
                for cc in range(NCC):
                    wl = ()
                    if cc == 0:
                        wl = (W(sve, p4it["evac"].get(it - 1, 0)),
                              W(sve, pair_ready_v) if (tt == 0 and h2 == 0)
                              else (sve, 0))
                    mm(psA[:, 512 * cc:512 * (cc + 1)],
                       arQN(h2, 128 * tt, 128),
                       arK(h2, 512 * cc, 512),
                       start=True, stop=False, waits=wl)
                    mm(psA[:, 512 * cc:512 * (cc + 1)],
                       abf(O_QPB + 2048 * h2 + 256 * tt, 128)[0:64, :],
                       kpb[0:64, 512 * cc:512 * (cc + 1)],
                       start=False, stop=True,
                       incs=((smm, 1),) if cc == NCC - 1 else ())
                scores_mm = smm.n
                if len(p4it["pend"]) >= 2:
                    emit_trav(*p4it["pend"].pop(0))
                emit("vector", lambda e, sc=sc: e.tensor_add(
                    sc, psA[:, 0:2048], maskb),
                    waits=(W(smm, scores_mm), W(ldM, mask_ld),
                           W(sac, p4it["exp"].get(it - 2, 0))),
                    incs=((sve, 1),))
                p4it["evac"][it] = sve.n
                emit_tail()
                sb = 8 * (it % 2)
                emit("vector", lambda e, sc=sc, sb=sb: e.reduce_max(
                    sm[:, sb:sb + 1], sc, axis=AX.X),
                    waits=(W(sve),), incs=((sve, 1),))
                emit("vector", lambda e, tt=tt, sb=sb: e.tensor_scalar(
                    out=sm[:, sb + 1:sb + 2], in0=sm[:, sb:sb + 1],
                    scalar1=srt[:, tt:tt + 1], scalar2=-1.0,
                    op0=ALU.mult, op1=ALU.mult),
                    waits=(W(sve),), incs=((sve, 1),))
                prb = abf(PRBS[it % 3], 2048)
                for cc in range(NCC):
                    emit("scalar",
                         lambda e, cc=cc, tt=tt, sc=sc, prb=prb, sb=sb:
                         e.activation(
                             out=prb[:, 512 * cc:512 * (cc + 1)],
                             in_=sc[:, 512 * cc:512 * (cc + 1)],
                             func=AF.Exp, bias=sm[:, sb + 1:sb + 2],
                             scale=srt[:, tt:tt + 1],
                             accum_out=sm[:, sb + 4 + cc:sb + 5 + cc]),
                         waits=(W(sve),) if cc == 0 else (),
                         incs=((sac, 1),))
                p4it["exp"][it] = sac.n
                p4it["meta"] = getattr(p4it, "meta", None) or p4it.setdefault(
                    "meta", {})
                p4it["meta"][it] = (hh, tt)
                p4it["tail"] = (it, it % 3)

    if _dbg_stop('p4'):
        pass
    flush_trav()

    O_OO = 65536
    O_WO = 131072
    O_XF0 = 147456
    O_XF1 = 163840
    O_X1T = 180224
    O_XT5 = 188416

    def oot(ht, t0=0, n=1024):
        return abf(O_OO + 2048 * ht + 2 * t0, n)

    p4_v, p4_a, p4_m = sve.n, sac.n, smm.n
    wo_mm_done = {}
    wosrc = woT.rearrange("(v p) h -> p v h", p=128)
    for ht in range(KT):
        pse, pso = (psB, psC) if ht % 2 == 0 else \
            (psA[:, 0:512], psA[:, 512:1024])
        par = ht % 2
        woblk = abf(O_WO + 8192 * par, 4096)
        woblk3 = woblk.rearrange("p (v h) -> p v h", h=128)
        for si in range(4):
            wl = ()
            if si == 0:
                if ht >= 2:
                    wl = (W(smm, wo_mm_done[ht - 2]),)
                elif ht == 0:
                    wl = (W(sve, p4_v), W(sac, p4_a), W(smm, p4_m))
            emit("gpsimd", lambda e, si=si, ht=ht, woblk3=woblk3:
                 e.dma_start(
                     out=woblk3[:, 8 * si:8 * (si + 1), :],
                     in_=wosrc[:, 8 * si:8 * (si + 1),
                               128 * ht:128 * (ht + 1)]),
                 waits=wl, incs=((ldW[par], 16),))
        wo_ld = ldW[par].n
        for vc in range(KT):
            wob = abf(O_WO + 8192 * par + 256 * vc, 128)
            for tc in range(2):
                ps = pse if tc == 0 else pso
                mm(ps[:, 0:512] if ht % 2 == 0 else ps,
                   wob, ar1(vc, 512 * tc, 512),
                   start=(vc == 0), stop=(vc == KT - 1),
                   waits=((W(ldW[par], wo_ld), W(sve))
                          if vc == 0 else ()) if tc == 0 else (),
                   incs=((smm, 1),) if tc == 1 else ())
        wo_mm_done[ht] = smm.n
        for tc in range(2):
            ps = pse if tc == 0 else pso
            emit("vector", lambda e, ht=ht, tc=tc,
                 p=(pse if tc == 0 else pso), ish=(ht % 2 == 0):
                 e.tensor_copy(oot(ht, 512 * tc, 512),
                               p[:, 0:512] if ish else p),
                 waits=(W(smm),) if tc == 0 else (), incs=((sve, 1),))
    spill_prev = stS.n
    for tt in range(NT):
        xf = af32(O_XF0 if tt % 2 == 0 else O_XF1, 4096)
        dmaS(xf, x_q[128 * tt:128 * (tt + 1), :], sem=ldX[tt % 2],
             waits=(W(sve),))
        xf_ld = ldX[tt % 2].n
        x1tile = abf(O_X1T, 4096)
        for k4 in range(8):
            for j in range(4):
                ht = 4 * k4 + j
                wl = ()
                if j == 0:
                    wl = (W(sve), W(stS, spill_prev)) if k4 == 0 else (W(sve),)
                tr(psT[:, 128 * j:128 * (j + 1)],
                   oot(ht, 128 * tt, 128), waits=wl,
                   incs=((smm, 1),) if j == 3 else ())
            emit("vector", lambda e, k4=k4, x1tile=x1tile, xf=xf:
                 e.tensor_add(
                     x1tile[:, 512 * k4:512 * (k4 + 1)], psT[:, 0:512],
                     xf[:, 512 * k4:512 * (k4 + 1)]),
                 waits=(W(smm), W(ldX[tt % 2], xf_ld)), incs=((sve, 1),))
        xt5 = abf(O_XT5, 4096)
        a_done = layer_norm_apply(x1tile, xt5, 4096, (W(sve),))
        chmajor_from_tok(xt5, tt, lambda k: ar1(k), a_done)
        emit("sync", lambda e, tt=tt, x1tile=x1tile: e.dma_start(
            out=x1_sc[128 * tt:128 * (tt + 1), :], in_=x1tile),
            waits=(W(sve, a_done),), incs=((stS, 16),))
        spill_prev = stS.n
    x1_spilled = stS.n

    if _dbg_stop('p5'):
        pass
    slot_state = {"off": 0, "slot_bytes": 0}

    def dequant_unit(region_off, q8_flat, q8_off, s_flat, z_flat, s_off,
                     nk, n, par, w_memset=(), kseg=8):
        lq, lz = ldQp[par], ldZp[par]
        reg = abf(region_off, nk * n)
        emit("vector", lambda e, reg=reg: e.memset(reg, -8.0),
             waits=w_memset, incs=((sve, 1),))
        ms_v = sve.n
        emit("gpsimd", lambda e, q8=q8_flat, o=q8_off, reg=reg, m=nk * n:
             e.dma_start(out=reg, in_=q8[:, o:o + m], accum_op=ALU.add,
                         max_dma_last_dim=2048),
             waits=(W(sve, ms_v),), incs=((lq, 16),))
        q_tot = lq.n
        segs = [(k0, min(kseg, nk - k0)) for k0 in range(0, nk, kseg)]
        mu_done = []
        for si, (k0, kn) in enumerate(segs):
            lb = ldBp[2 * par + si % 2]
            slot = abf(slot_state["off"]
                       + slot_state["slot_bytes"] * (si % 2), kn * n)
            emit("gpsimd",
                 lambda e, slot=slot, sf=s_flat, o=s_off + k0 * n, m=kn * n:
                     e.dma_start(out=slot,
                                 in_=sf[0:1, o:o + m].broadcast_to([128, m])),
                 waits=(W(sve, mu_done[si - 2]),) if si >= 2 else (),
                 incs=((lb, 16),))
            s_tot = lb.n
            emit("vector",
                 lambda e, a=abf(region_off + 2 * k0 * n, kn * n), slot=slot:
                     e.tensor_mul(a, a, slot),
                 waits=(W(lq, q_tot), W(lb, s_tot)), incs=((sve, 1),))
            mu_done.append(sve.n)
        mu_all = sve.n
        emit("gpsimd", lambda e, zf=z_flat, o=s_off, reg=reg, m=nk * n:
             e.dma_start(out=reg,
                         in_=zf[0:1, o:o + m].broadcast_to([128, m]),
                         accum_op=ALU.add, max_dma_last_dim=2048),
             waits=(W(sve, mu_all),), incs=((lz, 16),))
        return lz.n

    O_WFF = 65536

    def wff(half, k, n=512):
        return abf(O_WFF + 32768 * half + 2 * n * k, n)
    O_SG = 131072
    O_HB = 139264
    O_HT = 140288
    O_SS6 = 148480
    slot_state["off"] = O_SS6
    slot_state["slot_bytes"] = 8192

    p5_v, p5_m = sve.n, smm.n
    units = []
    for j in range(NNC):
        units.append(("g", j))
        units.append(("u", j))
    unit_mm_done = {}
    spill_cnt = {"n": stS.n}
    for ui, (mat, j) in enumerate(units):
        half = ui % 2
        n0, n = 512 * j, _ncw(j)
        q8 = gate_q8 if mat == "g" else up_q8
        s_fl = sbf["gs"] if mat == "g" else sbf["us"]
        z_fl = sbf["gz"] if mat == "g" else sbf["uz"]
        if ui >= 2:
            wm = (W(smm, unit_mm_done[ui - 2]),)
        else:
            wm = (W(sve, p5_v), W(smm, p5_m), W(stS, x1_spilled))
        s_m = sbf["gs"] if mat == "g" else sbf["us"]
        z_m = sbf["gz"] if mat == "g" else sbf["uz"]
        zl = dequant_unit(O_WFF + 32768 * half, q8, KT * n0,
                          s_m, z_m, KT * n0, KT, n, half, w_memset=wm)
        ps_pool = [psA[:, 0:512], psA[:, 512:1024], psA[:, 1024:1536],
                   psA[:, 1536:2048]]
        for tt in range(NT):
            ps = ps_pool[tt % 4]
            for k in range(KT):
                wl = ()
                if k == 0:
                    wl = (W(sac), W(sve))
                    if tt == 0:
                        wl = (W(ldZp[half], zl), W(sac), W(sve))
                mm(ps[:, 0:n], ar1(k, 128 * tt, 128), wff(half, k, n),
                   start=(k == 0), stop=(k == KT - 1),
                   waits=wl, incs=((smm, 1),) if k == KT - 1 else ())
            if mat == "g":
                emit("scalar", lambda e, tt=tt, p=ps, n=n: e.activation(
                    out=abf(O_SG + 1024 * tt, n), in_=p[:, 0:n],
                    func=AF.Silu, bias=0.0, scale=1.0),
                    waits=(W(smm), W(sve)), incs=((sac, 1),))
            else:
                hb = abf(O_HB, 512)
                emit("vector", lambda e, tt=tt, p=ps, n=n, hb=hb: e.tensor_mul(
                    hb[:, 0:n], abf(O_SG + 1024 * tt, n), p[:, 0:n]),
                    waits=(W(smm), W(sac)), incs=((sve, 1),))
                hmul_v = sve.n
                nk4 = (n + 127) // 128
                for jj in range(nk4):
                    tr(psT[:, 128 * jj:128 * (jj + 1)],
                       hb[:, 128 * jj:128 * (jj + 1)],
                       waits=(W(sve, hmul_v), W(sve)) if jj == 0 else (),
                       incs=((smm, 1),) if jj == nk4 - 1 else ())
                for jj in range(nk4):
                    emit("vector", lambda e, jj=jj, tt=tt: e.tensor_copy(
                        abf(O_HT + 2048 * jj + 256 * tt, 128),
                        psT[:, 128 * jj:128 * (jj + 1)]),
                        waits=(W(smm), W(stS, spill_cnt["n"]))
                        if jj == 0 else (),
                        incs=((sve, 1),) if jj == nk4 - 1 else ())
        unit_mm_done[ui] = smm.n
        if mat == "u":
            nk4 = (n + 127) // 128
            for jj in range(nk4):
                it_g = 4 * j + jj
                emit("sync", lambda e, jj=jj, it_g=it_g: e.dma_start(
                    out=hT_sc[128 * it_g:128 * (it_g + 1), :],
                    in_=abf(O_HT + 2048 * jj, 1024)),
                    waits=(W(sve),), incs=((stS, 16),))
            spill_cnt["n"] = stS.n
    h_spilled = stS.n
    p6_m = smm.n

    O_HT7 = 0

    def ht7(buf, i, n=128):
        return abf(O_HT7 + 22528 * buf + 256 * i, n)
    O_X1S = 45056
    O_OUTB = 49152
    O_WD = 65536

    def wdd(buf, i, n=256):
        return abf(O_WD + 44032 * buf + 512 * i, n)
    O_SS7 = 155648
    slot_state["off"] = O_SS7
    slot_state["slot_bytes"] = 8192

    p6_v = sve.n
    NDCQ = 16
    dcq_mm = {}
    ht_segs = [(0, 16), (16, 16), (32, 16), (48, 16), (64, 16), (80, 6)]
    ob_store = {0: 0, 1: 0}
    tt_mm = {}
    for dq in range(NDCQ):
        wbuf = dq % 2
        if dq >= 2:
            wm = (W(smm, dcq_mm[dq - 2]),)
        elif dq == 0:
            wm = (W(sve, p6_v), W(smm, p6_m), W(stS, h_spilled))
        else:
            wm = ()
        deq_done = dequant_unit(O_WD + 44032 * wbuf,
                                down_q8, IT * 256 * dq,
                                sbf["ds"], sbf["dz"], IT * 256 * dq,
                                IT, 256, wbuf, w_memset=wm, kseg=16)
        for tt in range(NT):
            buf = tt % 2
            key = (dq, tt)
            pkey = (dq, tt - 2) if tt >= 2 else (dq - 1, tt + 6)
            if pkey in tt_mm:
                hw = (W(smm, tt_mm[pkey]),)
            else:
                hw = (W(smm, p6_m), W(stS, h_spilled))
            for si, (i0, ni) in enumerate(ht_segs):
                hsrc = hT_sc[128 * i0:128 * (i0 + ni),
                             128 * tt:128 * (tt + 1)]
                hsrc = hsrc.rearrange("(i p) c -> p i c", p=128)
                dst = abf(O_HT7 + 22528 * buf + 256 * i0, ni * 128)
                dst = dst.rearrange("p (i c) -> p i c", c=128)
                dmaS(dst, hsrc, sem=ldH[buf], waits=hw if si == 0 else ())
            dmaS(abf(O_X1S + 1024 * buf, 256),
                 x1_sc[128 * tt:128 * (tt + 1), 256 * dq:256 * (dq + 1)],
                 sem=ldH[buf],
                 waits=(W(stS, x1_spilled),) if (dq == 0 and tt < 2)
                 else (W(sve),))
            h_ld = ldH[buf].n
            ps = psB if tt % 2 == 0 else psC
            for i in range(IT):
                mm(ps[:, 0:256], ht7(buf, i), wdd(wbuf, i),
                   start=(i == 0), stop=(i == IT - 1),
                   waits=(W(ldH[buf], h_ld), W(ldZp[wbuf], deq_done),
                          W(sve))
                   if i == 0 else (),
                   incs=((smm, 1),) if i == IT - 1 else ())
            tt_mm[key] = smm.n
            ob = af32(O_OUTB + 2048 * buf, 256)
            emit("vector", lambda e, p=ps, ob=ob, buf=buf: e.tensor_add(
                ob, p[:, 0:256], abf(O_X1S + 1024 * buf, 256)),
                waits=(W(smm), W(oSp[buf], ob_store[buf])), incs=((sve, 1),))
            emit("sync", lambda e, ob=ob, tt=tt, dq=dq: e.dma_start(
                out=out_t[128 * tt:128 * (tt + 1),
                          256 * dq:256 * (dq + 1)],
                in_=ob),
                waits=(W(sve),), incs=((oSp[buf], 16),))
            ob_store[buf] = oSp[buf].n
        dcq_mm[dq] = smm.n

    ewait("sync", oSp[0])
    ewait("sync", oSp[1])

    with nc.Block() as block:
        @block.tensor
        def _(eng):
            for f in ops["tensor"]:
                f(eng)

        @block.vector
        def _(eng):
            for f in ops["vector"]:
                f(eng)

        @block.scalar
        def _(eng):
            for f in ops["scalar"]:
                f(eng)

        @block.gpsimd
        def _(eng):
            for f in ops["gpsimd"]:
                f(eng)

        @block.sync
        def _(eng):
            for f in ops["sync"]:
                f(eng)

    return nc



_CACHE = {}


def _rope_perm_cols():
    return np.concatenate([np.arange(0, 64, 2), np.arange(1, 64, 2)])


def _prep_weights(inp):
    w = {}
    w["wq_aT"] = np.ascontiguousarray(inp["L0_attn_wq_a"].T)
    w["q_nw"] = np.ascontiguousarray(inp["L0_attn_q_norm_w"].reshape(QL, 1))
    wqbT = np.ascontiguousarray(inp["L0_attn_wq_b"].T)
    perm = np.zeros(NH * 192, dtype=np.int64)
    rp = _rope_perm_cols()
    for h in range(NH):
        b = 192 * h
        perm[b:b + 128] = b + np.arange(128)
        perm[b + 128:b + 192] = b + 128 + rp
    w["wq_bT"] = np.ascontiguousarray(wqbT[:, perm])
    wkvaT = np.ascontiguousarray(inp["L0_attn_wkv_a"].T)
    perm2 = np.concatenate([np.arange(KVL), KVL + rp])
    w["wkv_aT"] = np.ascontiguousarray(wkvaT[:, perm2])
    w["kv_nw"] = np.ascontiguousarray(inp["L0_attn_kv_norm_w"].reshape(KVL, 1))
    wkvbT = np.ascontiguousarray(inp["L0_attn_wkv_b"].T)
    perm3 = np.zeros(NH * 256, dtype=np.int64)
    for p in range(NPAIR):
        b = 512 * p
        h0, h1 = 2 * p, 2 * p + 1
        perm3[b:b + 128] = 256 * h0 + np.arange(128)
        perm3[b + 128:b + 256] = 256 * h1 + np.arange(128)
        perm3[b + 256:b + 384] = 256 * h0 + 128 + np.arange(128)
        perm3[b + 384:b + 512] = 256 * h1 + 128 + np.arange(128)
    w["wkv_bT"] = np.ascontiguousarray(wkvbT[:, perm3])
    w["woT"] = np.ascontiguousarray(inp["L0_attn_wo"].T)
    def q8_cm(q, chunks):
        q8 = np.ascontiguousarray(q).view(np.int8)[:, ::4]
        G = q8.shape[0] // 128
        qt = np.ascontiguousarray(q8).reshape(G, 128, q8.shape[1])
        blocks = [qt[:, :, n0:n0 + n].transpose(1, 0, 2).reshape(128, -1)
                  for (n0, n) in chunks]
        return np.ascontiguousarray(np.concatenate(blocks, axis=1))

    def sz_cm(a, chunks):
        blocks = [np.ascontiguousarray(a[:, n0:n0 + n]).reshape(-1)
                  for (n0, n) in chunks]
        return np.ascontiguousarray(np.concatenate(blocks)).reshape(1, -1)

    gch = [(512 * j, min(512, II - 512 * j)) for j in range(NNC)]
    dch = [(256 * j, 256) for j in range(16)]
    w["gate_q8"] = q8_cm(inp["L0_ffn_gate_q"], gch)
    w["up_q8"] = q8_cm(inp["L0_ffn_up_q"], gch)
    w["down_q8"] = q8_cm(inp["L0_ffn_down_q"], dch)
    w["gate_s"] = sz_cm(inp["L0_ffn_gate_s"], gch)
    w["gate_z"] = sz_cm(inp["L0_ffn_gate_z"], gch)
    w["up_s"] = sz_cm(inp["L0_ffn_up_s"], gch)
    w["up_z"] = sz_cm(inp["L0_ffn_up_z"], gch)
    w["down_s"] = sz_cm(inp["L0_ffn_down_s"], dch)
    w["down_z"] = sz_cm(inp["L0_ffn_down_z"], dch)
    return w


def _rope_tables(p0, n):
    freqs = (1.0 / 10000.0 ** (np.arange(0, 64, 2, dtype=np.float64) / 64.0))
    t = np.arange(p0, p0 + n, dtype=np.float64)
    ang = freqs[:, None] * t[None, :]
    return np.concatenate([np.cos(ang), np.sin(ang)], axis=0).astype(BF)


def _mask(p0):
    s = np.arange(CTX)
    m = np.zeros((NT, 128, CTX), dtype=np.float32)
    for tt in range(NT):
        qpos = p0 + 128 * tt + np.arange(128)
        m[tt] = np.where(s[None, :] <= qpos[:, None], 0.0, -1e9)
    return m.astype(BF)


def kernel(L0_attn_x, L0_attn_wq_a, L0_attn_q_norm_w, L0_attn_wq_b,
           L0_attn_wkv_a, L0_attn_kv_norm_w, L0_attn_wkv_b, L0_attn_wo,
           L0_ffn_gate_q, L0_ffn_gate_s, L0_ffn_gate_z, L0_ffn_up_q,
           L0_ffn_up_s, L0_ffn_up_z, L0_ffn_down_q, L0_ffn_down_s,
           L0_ffn_down_z):
    inp = dict(L0_attn_wq_a=np.asarray(L0_attn_wq_a),
               L0_attn_q_norm_w=np.asarray(L0_attn_q_norm_w),
               L0_attn_wq_b=np.asarray(L0_attn_wq_b),
               L0_attn_wkv_a=np.asarray(L0_attn_wkv_a),
               L0_attn_kv_norm_w=np.asarray(L0_attn_kv_norm_w),
               L0_attn_wkv_b=np.asarray(L0_attn_wkv_b),
               L0_attn_wo=np.asarray(L0_attn_wo),
               L0_ffn_gate_q=np.asarray(L0_ffn_gate_q),
               L0_ffn_gate_s=np.asarray(L0_ffn_gate_s),
               L0_ffn_gate_z=np.asarray(L0_ffn_gate_z),
               L0_ffn_up_q=np.asarray(L0_ffn_up_q),
               L0_ffn_up_s=np.asarray(L0_ffn_up_s),
               L0_ffn_up_z=np.asarray(L0_ffn_up_z),
               L0_ffn_down_q=np.asarray(L0_ffn_down_q),
               L0_ffn_down_s=np.asarray(L0_ffn_down_s),
               L0_ffn_down_z=np.asarray(L0_ffn_down_z))
    x = np.asarray(L0_attn_x)
    w = _prep_weights(inp)

    if "nc" not in _CACHE:
        _CACHE["nc"] = build_nc()
    nc = _CACHE["nc"]

    in_maps = []
    for c in range(8):
        b, half = c // 2, c % 2
        p0 = 1024 * half
        m = dict(w)
        m["x_q"] = np.ascontiguousarray(x[b, p0:p0 + T, :])
        m["x_b"] = np.ascontiguousarray(x[b])
        m["rope_q"] = _rope_tables(p0, T)
        m["rope_k"] = _rope_tables(0, CTX)
        m["mask_in"] = _mask(p0)
        in_maps.append(m)

    res = run_bass_kernel_spmd(nc, in_maps, core_ids=list(range(8)))
    out = np.empty((4, CTX, D), dtype=np.float32)
    for c in range(8):
        b, half = c // 2, c % 2
        out[b, 1024 * half:1024 * half + T, :] = res.results[c]["out"]
    return out


# version 10
# speedup vs baseline: 1.4312x; 1.0606x over previous
import numpy as np
import ml_dtypes
import concourse.bass as bass
import concourse.mybir as mybir
from concourse.bass_utils import run_bass_kernel_spmd

F32 = mybir.dt.float32
BF16 = mybir.dt.bfloat16
I8 = mybir.dt.int8
AF = mybir.ActivationFunctionType
ALU = mybir.AluOpType
AX = mybir.AxisListType
BF = ml_dtypes.bfloat16

D = 4096
T = 1024
NT = 8
CTX = 2048
NCC = 4
NCT = 16
KT = 32
QL = 1536
QLT = 12
KVL = 512
KVLT = 4
NH = 32
NPAIR = 16
II = 11008
IT = 86
NNC = 22
NDC = 8
SCALE = float(1.0 / np.sqrt(192.0))
LN_EPS = 1e-5
RMS_EPS = 1e-6


def _ncw(j):
    return min(512, II - 512 * j)


def build_nc(dbg_phase=None):
    nc = bass.Bass("TRN2", target_bir_lowering=False)

    def din(name, shape, dt):
        return nc.dram_tensor(name, list(shape), dt, kind="ExternalInput").ap()

    x_q = din("x_q", (T, D), F32)
    x_b = din("x_b", (CTX, D), F32)
    wq_aT = din("wq_aT", (D, QL), F32)
    q_nw = din("q_nw", (QL, 1), F32)
    wq_bT = din("wq_bT", (QL, NH * 192), F32)
    wkv_aT = din("wkv_aT", (D, KVL + 64), F32)
    kv_nw = din("kv_nw", (KVL, 1), F32)
    wkv_bT = din("wkv_bT", (KVL, NH * 256), F32)
    woT = din("woT", (D, D), F32)
    gate_q8 = din("gate_q8", (128, KT * II), I8)
    up_q8 = din("up_q8", (128, KT * II), I8)
    down_q8 = din("down_q8", (128, IT * D), I8)
    gate_s = din("gate_s", (1, KT * II), F32)
    gate_z = din("gate_z", (1, KT * II), F32)
    up_s = din("up_s", (1, KT * II), F32)
    up_z = din("up_z", (1, KT * II), F32)
    down_s = din("down_s", (1, IT * D), F32)
    down_z = din("down_z", (1, IT * D), F32)
    rope_q = din("rope_q", (64, T), BF16)
    rope_k = din("rope_k", (64, CTX), BF16)
    mask_in = din("mask_in", (NT, 128, CTX), BF16)

    out_t = nc.dram_tensor("out", [T, D], F32, kind="ExternalOutput").ap()
    dbg_t = nc.dram_tensor("dbg_arena", [128, 192 * 1024], mybir.dt.uint8,
                           kind="ExternalOutput").ap()

    sbf = {}
    for nm, tot in [("gs", KT * II), ("gz", KT * II), ("us", KT * II),
                    ("uz", KT * II), ("ds", IT * D), ("dz", IT * D)]:
        sbf[nm] = nc.dram_tensor("sc_" + nm, [1, tot], BF16, kind="Internal").ap()
    hT_sc = nc.dram_tensor("hT_sc", [T, II], BF16, kind="Internal").ap()
    x1_sc = nc.dram_tensor("x1_sc", [T, D], BF16, kind="Internal").ap()
    rq_sc = nc.dram_tensor("rq_sc", [1, T], F32, kind="Internal").ap()
    rkv_sc = nc.dram_tensor("rkv_sc", [1, CTX], F32, kind="Internal").ap()

    ident = nc.alloc_sbuf_tensor("ident", [128, 128], BF16).ap()
    ones_col = nc.alloc_sbuf_tensor("ones_col", [128, 1], BF16).ap()
    qnw_sb = nc.alloc_sbuf_tensor("qnw_sb", [128, QLT], F32).ap()
    kvnw_sb = nc.alloc_sbuf_tensor("kvnw_sb", [128, KVLT], F32).ap()
    ropq_c = nc.alloc_sbuf_tensor("ropq_c", [32, T], BF16).ap()
    ropq_s = nc.alloc_sbuf_tensor("ropq_s", [32, T], BF16).ap()
    ropk_c = nc.alloc_sbuf_tensor("ropk_c", [32, CTX], BF16).ap()
    ropk_s = nc.alloc_sbuf_tensor("ropk_s", [32, CTX], BF16).ap()
    stat = nc.alloc_sbuf_tensor("stat", [128, 64], F32).ap()
    mv = nc.alloc_sbuf_tensor("mv", [128, 8], F32).ap()
    srt = nc.alloc_sbuf_tensor("srt", [128, NT], F32).ap()
    sm = nc.alloc_sbuf_tensor("sm", [128, 16], F32).ap()
    cst = nc.alloc_sbuf_tensor("cst", [128, 8], F32).ap()
    CONSTS = [float(LN_EPS), float(RMS_EPS), 1.0, 0.0,
              float(1.0 / QL), float(1.0 / KVL)]
    for ci, cv in enumerate(CONSTS):
        nc.const_aps.register(F32, cv, cst[:, ci:ci + 1])

    AR_BYTES = 192 * 1024
    arena_t = nc.alloc_sbuf_tensor("arena", [128, AR_BYTES], mybir.dt.uint8)
    ar_bf = arena_t.ap().bitcast(BF16)
    ar_f32 = arena_t.ap().bitcast(F32)

    def abf(off, n):
        assert off % 2 == 0 and off + 2 * n <= AR_BYTES, (off, n)
        return ar_bf[:, off // 2: off // 2 + n]

    def af32(off, n):
        assert off % 4 == 0 and off + 4 * n <= AR_BYTES, (off, n)
        return ar_f32[:, off // 4: off // 4 + n]

    def ar1(tile, t0=0, n=1024):
        return abf(2048 * tile + 2 * t0, n)

    def qlra(j, c0=0, n=1024):
        return abf(65536 + 2048 * j + 2 * c0, n)

    psA = nc.alloc_psum_tensor("psA", [128, 2048], F32).ap()
    psB = nc.alloc_psum_tensor("psB", [128, 512], F32).ap()
    psC = nc.alloc_psum_tensor("psC", [128, 512], F32).ap()
    psT = nc.alloc_psum_tensor("psT", [128, 1024], BF16).ap()
    psD = nc.alloc_psum_tensor("psD", [128, 512], F32).ap()

    class Sem:
        __slots__ = ("h", "n")

        def __init__(self, name):
            self.h = nc.alloc_semaphore(name)
            self.n = 0

    ldS = Sem("ldS")
    ldG = Sem("ldG")
    ldX = [Sem("ldX0"), Sem("ldX1")]
    ldX3 = Sem("ldX3")
    ldM = Sem("ldM")
    ldXS = Sem("ldXS")
    ldW = [Sem("ldW0"), Sem("ldW1")]
    ldH = [Sem("ldH0"), Sem("ldH1")]
    ldHg = Sem("ldHg")
    ldQp = [Sem("ldQ0"), Sem("ldQ1")]
    ldBp = [Sem("ldB0"), Sem("ldB1"), Sem("ldB2"), Sem("ldB3")]
    ldZp = [Sem("ldZ0"), Sem("ldZ1")]
    stS = Sem("stS")
    oSp = [Sem("oS0"), Sem("oS1")]
    smm = Sem("smm")
    sve = Sem("sve")
    sac = Sem("sac")
    spo = Sem("spo")

    ops = {"tensor": [], "vector": [], "scalar": [], "gpsimd": [], "sync": []}
    stopped = {"v": False}

    def emit(eng, build, waits=(), incs=()):
        if stopped["v"]:
            for s, a in incs:
                s.n += a
            return
        wl = tuple((s.h, (s.n if v is None else v)) for (s, v) in waits)
        il = []
        for s, a in incs:
            s.n += a
            il.append((s.h, a))
        il = tuple(il)

        def f(e, build=build, wl=wl, il=il):
            for h, v in wl:
                e.wait_ge(h, v)
            ins = build(e)
            for h, a in il:
                ins = ins.then_inc(h, a)
        ops[eng].append(f)

    def W(sem, val=None):
        return (sem, val)

    def ewait(eng, sem, val=None):
        if stopped["v"]:
            return
        v = sem.n if val is None else val
        ops[eng].append(lambda e, h=sem.h, v=v: e.wait_ge(h, v))

    def dmaS(dst, src, sem=ldS, waits=(), slow=False):
        if slow:
            emit("sync", lambda e, d=dst, s=src: e.dma_start(
                out=d, in_=s, allow_slow_non_contiguous=True),
                waits=waits, incs=((sem, 16),))
        else:
            emit("sync", lambda e, d=dst, s=src: e.dma_start(out=d, in_=s),
                 waits=waits, incs=((sem, 16),))

    def dmaG(dst, src, accum=None, sem=ldG, waits=()):
        if accum is None:
            emit("gpsimd", lambda e, d=dst, s=src: e.dma_start(out=d, in_=s),
                 waits=waits, incs=((sem, 16),))
        else:
            emit("gpsimd",
                 lambda e, d=dst, s=src, a=accum:
                     e.dma_start(out=d, in_=s, accum_op=a),
                 waits=waits, incs=((sem, 16),))

    def mm(out, lhsT, rhs, start, stop, waits=(), incs=()):
        emit("tensor",
             lambda e, o=out, l=lhsT, r=rhs, s1=start, s2=stop:
                 e.matmul(o, l, r, start=s1, stop=s2),
             waits=waits, incs=incs)

    def tr(out, in_, waits=(), incs=()):
        emit("tensor", lambda e, o=out, i=in_: e.transpose(o, i, ident),
             waits=waits, incs=incs)

    for ci, cv in enumerate(CONSTS):
        emit("vector", lambda e, ci=ci, cv=cv: e.memset(cst[:, ci:ci + 1], cv),
             incs=((sve, 1),))
    if dbg_phase is not None:
        emit("vector", lambda e: e.memset(ar_f32[:, :], 0.0),
             incs=((sve, 1),))
    emit("gpsimd", lambda e: e.memset(ident, 0.0), incs=((spo, 1),))
    emit("gpsimd", lambda e: e.memset(ones_col, 1.0), incs=((spo, 1),))
    ewait("gpsimd", spo)
    emit("gpsimd", lambda e: e.affine_select(
        out=ident, in_=ident, compare_op=ALU.not_equal, fill=1.0,
        base=0, pattern=[[-1, 128]], channel_multiplier=1), incs=((spo, 1),))

    dmaS(qnw_sb[:, 0:QLT], q_nw.rearrange("(j p) o -> p (j o)", p=128),
         slow=True)
    dmaS(kvnw_sb[:, 0:KVLT], kv_nw.rearrange("(j p) o -> p (j o)", p=128),
         slow=True)
    dmaS(ropq_c[:, :], rope_q[0:32, :])
    dmaS(ropq_s[:, :], rope_q[32:64, :])
    dmaS(ropk_c[:, :], rope_k[0:32, :])
    dmaS(ropk_s[:, :], rope_k[32:64, :])

    cvt_jobs = [(gate_s, sbf["gs"], KT * II), (gate_z, sbf["gz"], KT * II),
                (up_s, sbf["us"], KT * II), (up_z, sbf["uz"], KT * II),
                (down_s, sbf["ds"], IT * D), (down_z, sbf["dz"], IT * D)]
    for src, dst, tot in cvt_jobs:
        per = tot // 128
        cv_f = af32(0, per)
        cv_b = abf(4 * per, per)
        flat = src.rearrange("o (p m) -> (o p) m", p=128)
        dmaS(cv_f, flat, waits=(W(sve),))
        emit("vector", lambda e, a=cv_b, b=cv_f: e.tensor_copy(a, b),
             waits=(W(ldS),), incs=((sve, 1),))
        emit("sync", lambda e, d=dst, s=cv_b:
             e.dma_start(out=d.rearrange("o (p m) -> p (o m)", p=128), in_=s),
             waits=(W(sve),), incs=((stS, 16),))
        ewait("vector", stS)
    cvt_done_st = stS.n

    def layer_norm_apply(src, dst_bf, width, pre_waits):
        nch = width // 512
        for j in range(nch):
            emit("vector",
                 lambda e, s=src, j=j: e.bn_stats(
                     stat[:, 6 * j:6 * j + 6], s[:, 512 * j:512 * (j + 1)]),
                 waits=pre_waits if j == 0 else (), incs=((sve, 1),))
        emit("vector", lambda e, n=nch: e.bn_aggr(mv[:, 0:2], stat[:, 0:6 * n]),
             waits=(W(sve),), incs=((sve, 1),))
        emit("scalar", lambda e: e.activation(
            out=mv[:, 2:3], in_=mv[:, 1:2], func=AF.Sqrt,
            bias=float(LN_EPS), scale=1.0),
            waits=(W(sve),), incs=((sac, 1),))
        emit("vector", lambda e: e.reciprocal(mv[:, 3:4], mv[:, 2:3]),
             waits=(W(sac),), incs=((sve, 1),))
        emit("vector", lambda e, s=src, d=dst_bf: e.tensor_scalar(
            out=d, in0=s, scalar1=mv[:, 0:1], scalar2=mv[:, 3:4],
            op0=ALU.subtract, op1=ALU.mult),
            waits=(W(sve),), incs=((sve, 1),))
        return sve.n

    def chmajor_from_tok(src_bf, tt, dst_fn, src_done_sve, nk=KT):
        for k4 in range((nk + 3) // 4):
            ks = list(range(4 * k4, min(4 * k4 + 4, nk)))
            for idx, k in enumerate(ks):
                wl = ()
                if idx == 0:
                    wl = (W(sve, src_done_sve), W(sve)) if k4 == 0 else (W(sve),)
                tr(psT[:, 128 * idx:128 * (idx + 1)],
                   src_bf[:, 128 * k:128 * (k + 1)],
                   waits=wl,
                   incs=((smm, 1),) if idx == len(ks) - 1 else ())
            for idx, k in enumerate(ks):
                emit("vector",
                     lambda e, k=k, idx=idx, d=dst_fn, tt=tt: e.tensor_copy(
                         d(k)[:, 128 * tt:128 * (tt + 1)],
                         psT[:, 128 * idx:128 * (idx + 1)]),
                     waits=(W(smm),) if idx == 0 else (),
                     incs=((sve, 1),) if idx == len(ks) - 1 else ())


    def _dbg_stop(tag):
        if dbg_phase != tag or stopped["v"]:
            return False
        ewait("sync", sve)
        ewait("sync", smm)
        ewait("sync", sac)
        emit("sync", lambda e: e.dma_start(out=dbg_t, in_=arena_t.ap()),
             waits=(), incs=((stS, 16),))
        ewait("sync", stS)
        stopped["v"] = True
        return True

    ewait("tensor", spo)
    O_XA = 90112
    O_XB2 = 106496
    O_XT = 122880
    apply_done = {}
    for tt in range(NT):
        xa = af32(O_XA if tt % 2 == 0 else O_XB2, 4096)
        wl = ()
        if tt >= 2:
            wl = (W(sve, apply_done[tt - 2]),)
        if tt == 0:
            wl = (W(stS, cvt_done_st),)
        dmaS(xa, x_q[128 * tt:128 * (tt + 1), :], sem=ldX[tt % 2], waits=wl)
        xt = abf(O_XT, 4096)
        a_done = layer_norm_apply(
            xa, xt, 4096,
            (W(ldX[tt % 2]),) + ((W(smm),) if tt > 0 else ()))
        apply_done[tt] = a_done
        chmajor_from_tok(xt, tt, lambda k: ar1(k), a_done)

    if _dbg_stop('p1'):
        pass
    O_WQA = 90112
    p1_v = sve.n
    for k in range(KT):
        dmaG(abf(O_WQA + 3072 * k, 1536), wq_aT[128 * k:128 * (k + 1), :],
             waits=(W(sve, p1_v),) if k == 0 else ())
    wqa_ld = ldG.n
    for j in range(QLT):
        for tc in range(2):
            ps = psB if (2 * j + tc) % 2 == 0 else psC
            for k in range(KT):
                mm(ps[:, 0:512],
                   abf(O_WQA + 3072 * k + 256 * j, 128),
                   ar1(k, 512 * tc, 512),
                   start=(k == 0), stop=(k == KT - 1),
                   waits=((W(ldG, wqa_ld), W(sve)) if k == 0 else ()),
                   incs=((smm, 1),) if k == KT - 1 else ())
            emit("vector",
                 lambda e, j=j, tc=tc, p=ps: e.tensor_copy(
                     qlra(j, 512 * tc, 512), p[:, 0:512]),
                 waits=(W(smm),), incs=((sve, 1),))
    O_SQ = 188416
    rowv = af32(189440, 1024)
    for tc in range(2):
        psq = psD if tc == 0 else psB
        for j in range(QLT):
            sq = abf(O_SQ, 512)
            emit("vector",
                 lambda e, j=j, tc=tc, s=sq: e.tensor_mul(
                     s, qlra(j, 512 * tc, 512), qlra(j, 512 * tc, 512)),
                 waits=(W(sve), W(smm)), incs=((sve, 1),))
            mm(psq[0:1, 0:512], ones_col, sq,
               start=(j == 0), stop=(j == QLT - 1),
               waits=(W(sve),), incs=((smm, 1),))
        emit("scalar", lambda e, tc=tc, p=psq: e.activation(
            out=rowv[0:1, 512 * tc:512 * (tc + 1)], in_=p[0:1, 0:512],
            func=AF.Sqrt, bias=float(RMS_EPS), scale=float(1.0 / QL)),
            waits=(W(smm),), incs=((sac, 1),))
    emit("vector", lambda e: e.reciprocal(rowv[0:1, 0:1024],
                                          rowv[0:1, 0:1024]),
         waits=(W(sac),), incs=((sve, 1),))
    emit("sync", lambda e: e.dma_start(out=rq_sc, in_=rowv[0:1, 0:1024]),
         waits=(W(sve),), incs=((stS, 16),))
    dmaS(srt[:, 0:NT], rq_sc.rearrange("o (a p) -> p (o a)", p=128),
         waits=(W(stS),), slow=True)
    emit("vector", lambda e: e.tensor_scalar(
        out=srt[:, 0:NT], in0=srt[:, 0:NT], scalar1=float(SCALE), scalar2=None,
        op0=ALU.mult), waits=(W(ldS),), incs=((sve, 1),))

    if _dbg_stop('p2'):
        pass
    O_WKA = 90112
    O_XLC = 126976
    O_CKV = 159744
    O_KPE = 176128
    O_KPO = 180224
    O_KPB = 184320
    O_RSB = 188416
    O_TMP1 = 135168
    O_TMP2 = 139264
    rowv = af32(143360, 2048)
    p2_v, p2_m = sve.n, smm.n
    for k in range(KT):
        dmaG(abf(O_WKA + 1152 * k, 576), wkv_aT[128 * k:128 * (k + 1), :],
             waits=(W(sve, p2_v), W(smm, p2_m)) if k == 0 else ())
    wka_ld = ldG.n
    O_X3 = 0
    O_XT3 = 16384
    for cc in range(NCC):
        for st4 in range(4):
            ct = 4 * cc + st4
            xa = af32(O_X3, 4096)
            wl = (W(sve),) if ct > 0 else (W(sve, p2_v),)
            dmaS(xa, x_b[128 * ct:128 * (ct + 1), :], sem=ldX3, waits=wl)
            xt = abf(O_XT3, 4096)
            a_done = layer_norm_apply(xa, xt, 4096, (W(ldX3), W(smm)))
            chmajor_from_tok(xt, st4, lambda k: abf(O_XLC + 1024 * k, 512),
                             a_done)
        ewait("tensor", sve)
        for c in range(6):
            ps = [psA[:, 0:512], psA[:, 512:1024], psA[:, 1024:1536],
                  psA[:, 1536:2048], psB[:, 0:512], psD[:, 0:512]][c]
            mcols = 128 if c < 4 else 32
            coff = 256 * c if c < 4 else 1024 + 64 * (c - 4)
            for k in range(KT):
                wl = ()
                if k == 0:
                    wl = (W(sve),)
                    if cc == 0 and c == 0:
                        wl = (W(ldG, wka_ld), W(sve))
                mm(ps[0:mcols, 0:512],
                   abf(O_WKA + 1152 * k + coff, mcols),
                   abf(O_XLC + 1024 * k, 512),
                   start=(k == 0), stop=(k == KT - 1),
                   waits=wl,
                   incs=((smm, 1),) if k == KT - 1 else ())
            if c < 4:
                emit("vector", lambda e, c=c, cc=cc, p=ps: e.tensor_copy(
                    abf(O_CKV + 4096 * c + 1024 * cc, 512), p[:, 0:512]),
                    waits=(W(smm),), incs=((sve, 1),))
            else:
                emit("vector", lambda e, c=c, cc=cc, p=ps: e.tensor_copy(
                    abf((O_KPE if c == 4 else O_KPO) + 1024 * cc, 512)
                    [0:32, :],
                    p[0:32, 0:512]),
                    waits=(W(smm),), incs=((sve, 1),))
    kpe = abf(O_KPE, 2048)
    kpo = abf(O_KPO, 2048)
    tmp1 = abf(O_TMP1, 2048)
    tmp2 = abf(O_TMP2, 2048)
    kpb = abf(O_KPB, 2048)
    emit("vector", lambda e: e.tensor_mul(tmp1[0:32, :], kpe[0:32, :],
                                          ropk_c[:, :]),
         waits=(W(sve),), incs=((sve, 1),))
    emit("vector", lambda e: e.tensor_mul(tmp2[0:32, :], kpo[0:32, :],
                                          ropk_s[:, :]), incs=((sve, 1),))
    emit("vector", lambda e: e.tensor_sub(kpb[0:32, :], tmp1[0:32, :],
                                          tmp2[0:32, :]),
         waits=(W(sve),), incs=((sve, 1),))
    emit("vector", lambda e: e.tensor_mul(tmp1[0:32, :], kpe[0:32, :],
                                          ropk_s[:, :]),
         waits=(W(sve),), incs=((sve, 1),))
    emit("vector", lambda e: e.tensor_mul(tmp2[0:32, :], kpo[0:32, :],
                                          ropk_c[:, :]), incs=((sve, 1),))
    emit("vector", lambda e: e.tensor_add(kpb[32:64, :], tmp1[0:32, :],
                                          tmp2[0:32, :]),
         waits=(W(sve),), incs=((sve, 1),))
    for c in range(KVLT):
        sq = abf(O_XLC, 2048)
        emit("vector", lambda e, c=c, s=sq: e.tensor_mul(
            s, abf(O_CKV + 4096 * c, 2048), abf(O_CKV + 4096 * c, 2048)),
            waits=(W(sve), W(smm)), incs=((sve, 1),))
        for cc in range(NCC):
            mm(psA[0:1, 512 * cc:512 * (cc + 1)], ones_col,
               sq[:, 512 * cc:512 * (cc + 1)],
               start=(c == 0), stop=(c == KVLT - 1),
               waits=(W(sve),) if cc == 0 else (), incs=((smm, 1),))
    for cc in range(NCC):
        emit("scalar", lambda e, cc=cc: e.activation(
            out=rowv[0:1, 512 * cc:512 * (cc + 1)],
            in_=psA[0:1, 512 * cc:512 * (cc + 1)],
            func=AF.Sqrt, bias=float(RMS_EPS), scale=float(1.0 / KVL)),
            waits=(W(smm),), incs=((sac, 1),))
    emit("vector", lambda e: e.reciprocal(rowv[0:1, 0:2048],
                                          rowv[0:1, 0:2048]),
         waits=(W(sac),), incs=((sve, 1),))
    emit("sync", lambda e: e.dma_start(out=rkv_sc, in_=rowv[0:1, 0:2048]),
         waits=(W(sve),), incs=((stS, 16),))
    rsb = af32(O_RSB, 2048)
    emit("gpsimd", lambda e: e.dma_start(
        out=rsb, in_=rkv_sc.broadcast_to([128, CTX])),
        waits=(W(stS), W(sve)), incs=((ldG, 16),))
    for c in range(KVLT):
        emit("vector", lambda e, c=c: e.tensor_mul(
            abf(O_CKV + 4096 * c, 2048), abf(O_CKV + 4096 * c, 2048), rsb),
            waits=(W(ldG),) if c == 0 else (), incs=((sve, 1),))
    ckv_ready_v = sve.n

    if _dbg_stop('p3'):
        pass
    O_WKVB = 90112
    O_WQB = 94208
    O_KN = 103424
    O_VT = 111616
    O_QN = 119808
    O_QPF = 123904
    O_QPB = 128000
    O_PRB = 132096
    O_PRT = 136192
    O_MSK = 140288

    def arK(h2, n0, n):
        return abf(O_KN + 4096 * h2 + 2 * n0, n)

    def arV(ct, n0=0, n=256):
        return abf(O_VT + 512 * ct + 2 * n0, n)

    def arQN(h2, t0, n):
        return abf(O_QN + 2048 * h2 + 2 * t0, n)

    O_PRB2 = 152576
    O_PRB3 = 176128
    PRBS = [O_PRB, O_PRB2, O_PRB3]
    probsT = abf(O_PRT, 2048)
    maskb = abf(O_MSK, 2048)
    O_SC = [144384, 188416]
    p4it = {"i": 0, "evac": {}, "exp": {}, "pend": [], "tail": None}

    def emit_tail():
        if p4it["tail"] is None:
            return
        itp, par3 = p4it["tail"]
        p4it["tail"] = None
        sb = 8 * (itp % 2)
        prb = abf(PRBS[par3], 2048)
        emit("vector", lambda e, sb=sb: e.reduce_sum(
            sm[:, sb + 2:sb + 3], sm[:, sb + 4:sb + 8], axis=AX.X),
            waits=(W(sac, p4it["exp"][itp]),), incs=((sve, 1),))
        emit("vector", lambda e, sb=sb: e.reciprocal(
            sm[:, sb + 3:sb + 4], sm[:, sb + 2:sb + 3]),
            waits=(W(sve),), incs=((sve, 1),))
        emit("vector", lambda e, prb=prb, sb=sb: e.tensor_scalar(
            out=prb, in0=prb, scalar1=sm[:, sb + 3:sb + 4], scalar2=None,
            op0=ALU.mult),
            waits=(W(sve),), incs=((sve, 1),))
        p4it["pend"].append((p4it["meta"][itp][0], p4it["meta"][itp][1],
                             par3, sve.n))

    def emit_trav(hh, tt, par, pnorm_v):
        prb = abf(PRBS[par], 2048)
        h2 = hh % 2
        for b4 in range(4):
            for j in range(4):
                ct = 4 * b4 + j
                wl = ()
                if j == 0:
                    wl = (W(sve, pnorm_v), W(sac), W(sve)) if b4 == 0 \
                        else (W(sac),)
                tr(psT[:, 128 * j:128 * (j + 1)],
                   prb[:, 128 * ct:128 * (ct + 1)],
                   waits=wl,
                   incs=((smm, 1),) if j == 3 else ())
            emit("scalar", lambda e, b4=b4: e.copy(
                probsT[:, 512 * b4:512 * (b4 + 1)], psT[:, 0:512]),
                waits=(W(smm),), incs=((sac, 1),))
        for ct in range(NCT):
            mm(psD[:, 0:128],
               arV(ct, 128 * h2, 128),
               probsT[:, 128 * ct:128 * (ct + 1)],
               start=(ct == 0), stop=(ct == NCT - 1),
               waits=(W(sac),) if ct == 0 else (),
               incs=((smm, 1),) if ct == NCT - 1 else ())
        emit("scalar", lambda e, hh=hh, tt=tt: e.copy(
            ar1(hh, 128 * tt, 128), psD[:, 0:128]),
            waits=(W(smm),), incs=((sac, 1),))

    def flush_trav():
        emit_tail()
        while p4it["pend"]:
            emit_trav(*p4it["pend"].pop(0))

    for pr in range(NPAIR):
        flush_trav()
        pm0 = smm.n
        for c in range(KVLT):
            wl = ()
            if c == 0:
                wl = (W(sve, ckv_ready_v),) if pr == 0 else (W(smm, pm0),)
            dmaG(abf(O_WKVB + 1024 * c, 512),
                 wkv_bT[128 * c:128 * (c + 1), 512 * pr:512 * (pr + 1)],
                 waits=wl)
        for c in range(QLT):
            dmaG(abf(O_WQB + 768 * c, 384),
                 wq_bT[128 * c:128 * (c + 1), 384 * pr:384 * (pr + 1)])
        wld = ldG.n
        for c in range(KVLT):
            emit("vector", lambda e, c=c: e.tensor_scalar(
                out=abf(O_WKVB + 1024 * c, 512), in0=abf(O_WKVB + 1024 * c, 512),
                scalar1=kvnw_sb[:, c:c + 1], scalar2=None, op0=ALU.mult),
                waits=(W(ldG, wld),) if c == 0 else (), incs=((sve, 1),))
        for c in range(QLT):
            emit("vector", lambda e, c=c: e.tensor_scalar(
                out=abf(O_WQB + 768 * c, 384), in0=abf(O_WQB + 768 * c, 384),
                scalar1=qnw_sb[:, c:c + 1], scalar2=None, op0=ALU.mult),
                incs=((sve, 1),))
        wfold = sve.n

        for h2 in range(2):
            for tc in range(2):
                ps = psB if tc == 0 else psC
                for c in range(QLT):
                    mm(ps[:, 0:512],
                       abf(O_WQB + 768 * c + 384 * h2, 128),
                       qlra(c, 512 * tc, 512),
                       start=(c == 0), stop=(c == QLT - 1),
                       waits=(W(sve, wfold), W(sve)) if c == 0 else (),
                       incs=((smm, 1),) if c == QLT - 1 else ())
                emit("vector", lambda e, h2=h2, tc=tc, p=ps: e.tensor_copy(
                    arQN(h2, 512 * tc, 512), p[:, 0:512]),
                    waits=(W(smm),), incs=((sve, 1),))
            for pe_half in range(2):
                ps = psB if pe_half == 0 else psC
                for tc in range(2):
                    for c in range(QLT):
                        mm(ps[0:32, 512 * 0:512] if False else
                           ps[0:32, 0:512],
                           abf(O_WQB + 768 * c + 384 * h2 + 256
                               + 64 * pe_half, 32),
                           qlra(c, 512 * tc, 512),
                           start=(c == 0), stop=(c == QLT - 1),
                           waits=(W(sve),) if c == 0 else (),
                           incs=((smm, 1),) if c == QLT - 1 else ())
                    emit("vector", lambda e, tc=tc, p=ps, ph=pe_half:
                         e.tensor_copy(
                             abf(O_QPF + 2048 * ph + 1024 * tc, 512)
                             [0:32, :],
                             p[0:32, 0:512]),
                         waits=(W(smm),), incs=((sve, 1),))
            qpe = abf(O_QPF, 1024)
            qpo = abf(O_QPF + 2048, 1024)
            qt1 = abf(O_PRB, 1024)
            qt2 = abf(O_PRB + 2048, 1024)
            qpb = abf(O_QPB + 2048 * h2, 1024)
            emit("vector", lambda e, qpe=qpe, qt1=qt1: e.tensor_mul(
                qt1[0:32, :], qpe[0:32, :], ropq_c[:, :]),
                waits=(W(sve), W(smm)), incs=((sve, 1),))
            emit("vector", lambda e, qpo=qpo, qt2=qt2: e.tensor_mul(
                qt2[0:32, :], qpo[0:32, :], ropq_s[:, :]),
                incs=((sve, 1),))
            emit("vector", lambda e, qpb=qpb, qt1=qt1, qt2=qt2: e.tensor_sub(
                qpb[0:32, :], qt1[0:32, :], qt2[0:32, :]),
                waits=(W(sve),), incs=((sve, 1),))
            emit("vector", lambda e, qpe=qpe, qt1=qt1: e.tensor_mul(
                qt1[0:32, :], qpe[0:32, :], ropq_s[:, :]),
                waits=(W(sve),), incs=((sve, 1),))
            emit("vector", lambda e, qpo=qpo, qt2=qt2: e.tensor_mul(
                qt2[0:32, :], qpo[0:32, :], ropq_c[:, :]),
                incs=((sve, 1),))
            emit("vector", lambda e, qpb=qpb, qt1=qt1, qt2=qt2: e.tensor_add(
                qpb[32:64, :], qt1[0:32, :], qt2[0:32, :]),
                waits=(W(sve),), incs=((sve, 1),))

        for h2 in range(2):
            for cc in range(NCC):
                ps = psB if cc % 2 == 0 else psC
                for c in range(KVLT):
                    mm(ps[:, 0:512],
                       abf(O_WKVB + 1024 * c + 256 * h2, 128),
                       abf(O_CKV + 4096 * c + 1024 * cc, 512),
                       start=(c == 0), stop=(c == KVLT - 1),
                       waits=(W(sve),) if c == 0 else (),
                       incs=((smm, 1),) if c == KVLT - 1 else ())
                emit("vector", lambda e, h2=h2, cc=cc, p=ps: e.tensor_copy(
                    arK(h2, 512 * cc, 512), p[:, 0:512]),
                    waits=(W(smm),), incs=((sve, 1),))
        for ct in range(NCT):
            ps = psB if ct % 2 == 0 else psC
            for c in range(KVLT):
                mm(ps[:, 0:256],
                   abf(O_CKV + 4096 * c + 256 * ct, 128),
                   abf(O_WKVB + 1024 * c + 512, 256),
                   start=(c == 0), stop=(c == KVLT - 1),
                   waits=(W(sve),) if c == 0 else (),
                   incs=((smm, 1),) if c == KVLT - 1 else ())
            emit("vector", lambda e, ct=ct, p=ps: e.tensor_copy(
                arV(ct), p[:, 0:256]),
                waits=(W(smm),), incs=((sve, 1),))
        pair_ready_v = sve.n

        for tt in range(NT):
            dmaS(maskb, mask_in[tt, :, :], sem=ldM, waits=(W(sve),))
            mask_ld = ldM.n
            for h2 in range(2):
                hh = 2 * pr + h2
                it = p4it["i"]
                p4it["i"] = it + 1
                sc = af32(O_SC[it % 2], 2048)
                for cc in range(NCC):
                    wl = ()
                    if cc == 0:
                        wl = (W(sve, p4it["evac"].get(it - 1, 0)),
                              W(sve, pair_ready_v) if (tt == 0 and h2 == 0)
                              else (sve, 0))
                    mm(psA[:, 512 * cc:512 * (cc + 1)],
                       arQN(h2, 128 * tt, 128),
                       arK(h2, 512 * cc, 512),
                       start=True, stop=False, waits=wl)
                    mm(psA[:, 512 * cc:512 * (cc + 1)],
                       abf(O_QPB + 2048 * h2 + 256 * tt, 128)[0:64, :],
                       kpb[0:64, 512 * cc:512 * (cc + 1)],
                       start=False, stop=True,
                       incs=((smm, 1),) if cc == NCC - 1 else ())
                scores_mm = smm.n
                if len(p4it["pend"]) >= 2:
                    emit_trav(*p4it["pend"].pop(0))
                emit("vector", lambda e, sc=sc: e.tensor_add(
                    sc, psA[:, 0:2048], maskb),
                    waits=(W(smm, scores_mm), W(ldM, mask_ld),
                           W(sac, p4it["exp"].get(it - 2, 0))),
                    incs=((sve, 1),))
                p4it["evac"][it] = sve.n
                emit_tail()
                sb = 8 * (it % 2)
                emit("vector", lambda e, sc=sc, sb=sb: e.reduce_max(
                    sm[:, sb:sb + 1], sc, axis=AX.X),
                    waits=(W(sve),), incs=((sve, 1),))
                emit("vector", lambda e, tt=tt, sb=sb: e.tensor_scalar(
                    out=sm[:, sb + 1:sb + 2], in0=sm[:, sb:sb + 1],
                    scalar1=srt[:, tt:tt + 1], scalar2=-1.0,
                    op0=ALU.mult, op1=ALU.mult),
                    waits=(W(sve),), incs=((sve, 1),))
                prb = abf(PRBS[it % 3], 2048)
                for cc in range(NCC):
                    emit("scalar",
                         lambda e, cc=cc, tt=tt, sc=sc, prb=prb, sb=sb:
                         e.activation(
                             out=prb[:, 512 * cc:512 * (cc + 1)],
                             in_=sc[:, 512 * cc:512 * (cc + 1)],
                             func=AF.Exp, bias=sm[:, sb + 1:sb + 2],
                             scale=srt[:, tt:tt + 1],
                             accum_out=sm[:, sb + 4 + cc:sb + 5 + cc]),
                         waits=(W(sve),) if cc == 0 else (),
                         incs=((sac, 1),))
                p4it["exp"][it] = sac.n
                p4it["meta"] = getattr(p4it, "meta", None) or p4it.setdefault(
                    "meta", {})
                p4it["meta"][it] = (hh, tt)
                p4it["tail"] = (it, it % 3)

    if _dbg_stop('p4'):
        pass
    flush_trav()

    O_OO = 65536
    O_WO = 131072
    O_XF0 = 147456
    O_XF1 = 163840
    O_X1T = 180224
    O_XT5 = 188416

    def oot(ht, t0=0, n=1024):
        return abf(O_OO + 2048 * ht + 2 * t0, n)

    p4_v, p4_a, p4_m = sve.n, sac.n, smm.n
    wo_mm_done = {}
    wosrc = woT.rearrange("(v p) h -> p v h", p=128)
    for ht in range(KT):
        pse, pso = (psB, psC) if ht % 2 == 0 else \
            (psA[:, 0:512], psA[:, 512:1024])
        par = ht % 2
        woblk = abf(O_WO + 8192 * par, 4096)
        woblk3 = woblk.rearrange("p (v h) -> p v h", h=128)
        for si in range(4):
            wl = ()
            if si == 0:
                if ht >= 2:
                    wl = (W(smm, wo_mm_done[ht - 2]),)
                elif ht == 0:
                    wl = (W(sve, p4_v), W(sac, p4_a), W(smm, p4_m))
            emit("gpsimd", lambda e, si=si, ht=ht, woblk3=woblk3:
                 e.dma_start(
                     out=woblk3[:, 8 * si:8 * (si + 1), :],
                     in_=wosrc[:, 8 * si:8 * (si + 1),
                               128 * ht:128 * (ht + 1)]),
                 waits=wl, incs=((ldW[par], 16),))
        wo_ld = ldW[par].n
        for vc in range(KT):
            wob = abf(O_WO + 8192 * par + 256 * vc, 128)
            for tc in range(2):
                ps = pse if tc == 0 else pso
                mm(ps[:, 0:512] if ht % 2 == 0 else ps,
                   wob, ar1(vc, 512 * tc, 512),
                   start=(vc == 0), stop=(vc == KT - 1),
                   waits=((W(ldW[par], wo_ld), W(sve))
                          if vc == 0 else ()) if tc == 0 else (),
                   incs=((smm, 1),) if tc == 1 else ())
        wo_mm_done[ht] = smm.n
        for tc in range(2):
            ps = pse if tc == 0 else pso
            emit("vector", lambda e, ht=ht, tc=tc,
                 p=(pse if tc == 0 else pso), ish=(ht % 2 == 0):
                 e.tensor_copy(oot(ht, 512 * tc, 512),
                               p[:, 0:512] if ish else p),
                 waits=(W(smm),) if tc == 0 else (), incs=((sve, 1),))
    spill_prev = stS.n
    for tt in range(NT):
        xf = af32(O_XF0 if tt % 2 == 0 else O_XF1, 4096)
        dmaS(xf, x_q[128 * tt:128 * (tt + 1), :], sem=ldX[tt % 2],
             waits=(W(sve),))
        xf_ld = ldX[tt % 2].n
        x1tile = abf(O_X1T, 4096)
        for k4 in range(8):
            for j in range(4):
                ht = 4 * k4 + j
                wl = ()
                if j == 0:
                    wl = (W(sve), W(stS, spill_prev)) if k4 == 0 else (W(sve),)
                tr(psT[:, 128 * j:128 * (j + 1)],
                   oot(ht, 128 * tt, 128), waits=wl,
                   incs=((smm, 1),) if j == 3 else ())
            emit("vector", lambda e, k4=k4, x1tile=x1tile, xf=xf:
                 e.tensor_add(
                     x1tile[:, 512 * k4:512 * (k4 + 1)], psT[:, 0:512],
                     xf[:, 512 * k4:512 * (k4 + 1)]),
                 waits=(W(smm), W(ldX[tt % 2], xf_ld)), incs=((sve, 1),))
        xt5 = abf(O_XT5, 4096)
        a_done = layer_norm_apply(x1tile, xt5, 4096, (W(sve),))
        chmajor_from_tok(xt5, tt, lambda k: ar1(k), a_done)
        emit("sync", lambda e, tt=tt, x1tile=x1tile: e.dma_start(
            out=x1_sc[128 * tt:128 * (tt + 1), :], in_=x1tile),
            waits=(W(sve, a_done),), incs=((stS, 16),))
        spill_prev = stS.n
    x1_spilled = stS.n

    if _dbg_stop('p5'):
        pass
    slot_state = {"off": 0, "slot_bytes": 0}

    def dequant_unit(region_off, q8_flat, q8_off, s_flat, z_flat, s_off,
                     nk, n, par, w_memset=(), kseg=8):
        lq, lz = ldQp[par], ldZp[par]
        reg = abf(region_off, nk * n)
        emit("vector", lambda e, reg=reg: e.memset(reg, -8.0),
             waits=w_memset, incs=((sve, 1),))
        ms_v = sve.n
        emit("gpsimd", lambda e, q8=q8_flat, o=q8_off, reg=reg, m=nk * n:
             e.dma_start(out=reg, in_=q8[:, o:o + m], accum_op=ALU.add,
                         max_dma_last_dim=2048),
             waits=(W(sve, ms_v),), incs=((lq, 16),))
        q_tot = lq.n
        segs = [(k0, min(kseg, nk - k0)) for k0 in range(0, nk, kseg)]
        mu_done = []
        for si, (k0, kn) in enumerate(segs):
            lb = ldBp[2 * par + si % 2]
            slot = abf(slot_state["off"]
                       + slot_state["slot_bytes"] * (si % 2), kn * n)
            emit("gpsimd",
                 lambda e, slot=slot, sf=s_flat, o=s_off + k0 * n, m=kn * n:
                     e.dma_start(out=slot,
                                 in_=sf[0:1, o:o + m].broadcast_to([128, m])),
                 waits=(W(sve, mu_done[si - 2]),) if si >= 2 else (),
                 incs=((lb, 16),))
            s_tot = lb.n
            emit("vector",
                 lambda e, a=abf(region_off + 2 * k0 * n, kn * n), slot=slot:
                     e.tensor_mul(a, a, slot),
                 waits=(W(lq, q_tot), W(lb, s_tot)), incs=((sve, 1),))
            mu_done.append(sve.n)
        mu_all = sve.n
        emit("gpsimd", lambda e, zf=z_flat, o=s_off, reg=reg, m=nk * n:
             e.dma_start(out=reg,
                         in_=zf[0:1, o:o + m].broadcast_to([128, m]),
                         accum_op=ALU.add, max_dma_last_dim=2048),
             waits=(W(sve, mu_all),), incs=((lz, 16),))
        return lz.n

    O_WFF = 65536

    def wff(half, k, n=512):
        return abf(O_WFF + 32768 * half + 2 * n * k, n)
    O_SG = 131072
    O_HB = 139264
    O_HT = 140288
    O_SS6 = 148480
    slot_state["off"] = O_SS6
    slot_state["slot_bytes"] = 8192

    p5_v, p5_m = sve.n, smm.n
    units = []
    for j in range(NNC):
        units.append(("g", j))
        units.append(("u", j))
    unit_mm_done = {}
    spill_cnt = {"n": stS.n}
    for ui, (mat, j) in enumerate(units):
        half = ui % 2
        n0, n = 512 * j, _ncw(j)
        q8 = gate_q8 if mat == "g" else up_q8
        s_fl = sbf["gs"] if mat == "g" else sbf["us"]
        z_fl = sbf["gz"] if mat == "g" else sbf["uz"]
        if ui >= 2:
            wm = (W(smm, unit_mm_done[ui - 2]),)
        else:
            wm = (W(sve, p5_v), W(smm, p5_m), W(stS, x1_spilled))
        s_m = sbf["gs"] if mat == "g" else sbf["us"]
        z_m = sbf["gz"] if mat == "g" else sbf["uz"]
        zl = dequant_unit(O_WFF + 32768 * half, q8, KT * n0,
                          s_m, z_m, KT * n0, KT, n, half, w_memset=wm)
        ps_pool = [psA[:, 0:512], psA[:, 512:1024], psA[:, 1024:1536],
                   psA[:, 1536:2048]]
        for tt in range(NT):
            ps = ps_pool[tt % 4]
            for k in range(KT):
                wl = ()
                if k == 0:
                    wl = (W(sac), W(sve))
                    if tt == 0:
                        wl = (W(ldZp[half], zl), W(sac), W(sve))
                mm(ps[:, 0:n], ar1(k, 128 * tt, 128), wff(half, k, n),
                   start=(k == 0), stop=(k == KT - 1),
                   waits=wl, incs=((smm, 1),) if k == KT - 1 else ())
            if mat == "g":
                emit("scalar", lambda e, tt=tt, p=ps, n=n: e.activation(
                    out=abf(O_SG + 1024 * tt, n), in_=p[:, 0:n],
                    func=AF.Silu, bias=0.0, scale=1.0),
                    waits=(W(smm), W(sve)), incs=((sac, 1),))
            else:
                hb = abf(O_HB, 512)
                emit("vector", lambda e, tt=tt, p=ps, n=n, hb=hb: e.tensor_mul(
                    hb[:, 0:n], abf(O_SG + 1024 * tt, n), p[:, 0:n]),
                    waits=(W(smm), W(sac)), incs=((sve, 1),))
                hmul_v = sve.n
                nk4 = (n + 127) // 128
                for jj in range(nk4):
                    tr(psT[:, 128 * jj:128 * (jj + 1)],
                       hb[:, 128 * jj:128 * (jj + 1)],
                       waits=(W(sve, hmul_v), W(sve)) if jj == 0 else (),
                       incs=((smm, 1),) if jj == nk4 - 1 else ())
                for jj in range(nk4):
                    emit("vector", lambda e, jj=jj, tt=tt: e.tensor_copy(
                        abf(O_HT + 1024 * tt + 256 * jj, 128),
                        psT[:, 128 * jj:128 * (jj + 1)]),
                        waits=(W(smm), W(stS, spill_cnt["n"]))
                        if jj == 0 else (),
                        incs=((sve, 1),) if jj == nk4 - 1 else ())
        unit_mm_done[ui] = smm.n
        if mat == "u":
            nk4 = (n + 127) // 128
            for tt8 in range(NT):
                emit("sync", lambda e, tt8=tt8, j=j, nk4=nk4: e.dma_start(
                    out=hT_sc[128 * tt8:128 * (tt8 + 1),
                              512 * j:512 * j + 128 * nk4],
                    in_=abf(O_HT + 1024 * tt8, 128 * nk4)),
                    waits=(W(sve),), incs=((stS, 16),))
            spill_cnt["n"] = stS.n
    h_spilled = stS.n
    p6_m = smm.n

    O_HT7 = 0

    def ht7(buf, i, n=128):
        return abf(O_HT7 + 22528 * buf + 256 * i, n)
    O_X1S = 45056
    O_OUTB = 49152
    O_WD = 65536

    def wdd(buf, i, n=256):
        return abf(O_WD + 44032 * buf + 512 * i, n)
    O_SS7 = 155648
    slot_state["off"] = O_SS7
    slot_state["slot_bytes"] = 8192

    p6_v = sve.n
    NDCQ = 16
    dcq_mm = {}
    ht_segs = [(0, 16), (16, 16), (32, 16), (48, 16), (64, 16), (80, 6)]
    ob_store = {0: 0, 1: 0}
    tt_mm = {}
    for dq in range(NDCQ):
        wbuf = dq % 2
        if dq >= 2:
            wm = (W(smm, dcq_mm[dq - 2]),)
        elif dq == 0:
            wm = (W(sve, p6_v), W(smm, p6_m), W(stS, h_spilled))
        else:
            wm = ()
        deq_done = dequant_unit(O_WD + 44032 * wbuf,
                                down_q8, IT * 256 * dq,
                                sbf["ds"], sbf["dz"], IT * 256 * dq,
                                IT, 256, wbuf, w_memset=wm, kseg=16)
        for tt in range(NT):
            buf = tt % 2
            key = (dq, tt)
            pkey = (dq, tt - 2) if tt >= 2 else (dq - 1, tt + 6)
            if pkey in tt_mm:
                hw = (W(smm, tt_mm[pkey]),)
            else:
                hw = (W(smm, p6_m), W(stS, h_spilled))
            hdst = abf(O_HT7 + 22528 * buf, IT * 128)
            hsrc = hT_sc[128 * tt:128 * (tt + 1), :]
            if buf == 0:
                dmaS(hdst, hsrc, sem=ldH[0], waits=hw)
            else:
                dmaG(hdst, hsrc, sem=ldHg, waits=hw)
            dmaS(abf(O_X1S + 1024 * buf, 256),
                 x1_sc[128 * tt:128 * (tt + 1), 256 * dq:256 * (dq + 1)],
                 sem=ldH[buf],
                 waits=(W(stS, x1_spilled),) if (dq == 0 and tt < 2)
                 else (W(sve),))
            if buf == 0:
                mmw = (W(ldH[0]),)
            else:
                mmw = (W(ldHg), W(ldH[1]))
            ps = psB if tt % 2 == 0 else psC
            for i in range(IT):
                mm(ps[:, 0:256], ht7(buf, i), wdd(wbuf, i),
                   start=(i == 0), stop=(i == IT - 1),
                   waits=mmw + (W(ldZp[wbuf], deq_done), W(sve))
                   if i == 0 else (),
                   incs=((smm, 1),) if i == IT - 1 else ())
            tt_mm[key] = smm.n
            ob = af32(O_OUTB + 2048 * buf, 256)
            emit("vector", lambda e, p=ps, ob=ob, buf=buf: e.tensor_add(
                ob, p[:, 0:256], abf(O_X1S + 1024 * buf, 256)),
                waits=(W(smm), W(oSp[buf], ob_store[buf])), incs=((sve, 1),))
            emit("sync", lambda e, ob=ob, tt=tt, dq=dq: e.dma_start(
                out=out_t[128 * tt:128 * (tt + 1),
                          256 * dq:256 * (dq + 1)],
                in_=ob),
                waits=(W(sve),), incs=((oSp[buf], 16),))
            ob_store[buf] = oSp[buf].n
        dcq_mm[dq] = smm.n

    ewait("sync", oSp[0])
    ewait("sync", oSp[1])

    with nc.Block() as block:
        @block.tensor
        def _(eng):
            for f in ops["tensor"]:
                f(eng)

        @block.vector
        def _(eng):
            for f in ops["vector"]:
                f(eng)

        @block.scalar
        def _(eng):
            for f in ops["scalar"]:
                f(eng)

        @block.gpsimd
        def _(eng):
            for f in ops["gpsimd"]:
                f(eng)

        @block.sync
        def _(eng):
            for f in ops["sync"]:
                f(eng)

    return nc



_CACHE = {}


def _rope_perm_cols():
    return np.concatenate([np.arange(0, 64, 2), np.arange(1, 64, 2)])


def _prep_weights(inp):
    w = {}
    w["wq_aT"] = np.ascontiguousarray(inp["L0_attn_wq_a"].T)
    w["q_nw"] = np.ascontiguousarray(inp["L0_attn_q_norm_w"].reshape(QL, 1))
    wqbT = np.ascontiguousarray(inp["L0_attn_wq_b"].T)
    perm = np.zeros(NH * 192, dtype=np.int64)
    rp = _rope_perm_cols()
    for h in range(NH):
        b = 192 * h
        perm[b:b + 128] = b + np.arange(128)
        perm[b + 128:b + 192] = b + 128 + rp
    w["wq_bT"] = np.ascontiguousarray(wqbT[:, perm])
    wkvaT = np.ascontiguousarray(inp["L0_attn_wkv_a"].T)
    perm2 = np.concatenate([np.arange(KVL), KVL + rp])
    w["wkv_aT"] = np.ascontiguousarray(wkvaT[:, perm2])
    w["kv_nw"] = np.ascontiguousarray(inp["L0_attn_kv_norm_w"].reshape(KVL, 1))
    wkvbT = np.ascontiguousarray(inp["L0_attn_wkv_b"].T)
    perm3 = np.zeros(NH * 256, dtype=np.int64)
    for p in range(NPAIR):
        b = 512 * p
        h0, h1 = 2 * p, 2 * p + 1
        perm3[b:b + 128] = 256 * h0 + np.arange(128)
        perm3[b + 128:b + 256] = 256 * h1 + np.arange(128)
        perm3[b + 256:b + 384] = 256 * h0 + 128 + np.arange(128)
        perm3[b + 384:b + 512] = 256 * h1 + 128 + np.arange(128)
    w["wkv_bT"] = np.ascontiguousarray(wkvbT[:, perm3])
    w["woT"] = np.ascontiguousarray(inp["L0_attn_wo"].T)
    def q8_cm(q, chunks):
        q8 = np.ascontiguousarray(q).view(np.int8)[:, ::4]
        G = q8.shape[0] // 128
        qt = np.ascontiguousarray(q8).reshape(G, 128, q8.shape[1])
        blocks = [qt[:, :, n0:n0 + n].transpose(1, 0, 2).reshape(128, -1)
                  for (n0, n) in chunks]
        return np.ascontiguousarray(np.concatenate(blocks, axis=1))

    def sz_cm(a, chunks):
        blocks = [np.ascontiguousarray(a[:, n0:n0 + n]).reshape(-1)
                  for (n0, n) in chunks]
        return np.ascontiguousarray(np.concatenate(blocks)).reshape(1, -1)

    gch = [(512 * j, min(512, II - 512 * j)) for j in range(NNC)]
    dch = [(256 * j, 256) for j in range(16)]
    w["gate_q8"] = q8_cm(inp["L0_ffn_gate_q"], gch)
    w["up_q8"] = q8_cm(inp["L0_ffn_up_q"], gch)
    w["down_q8"] = q8_cm(inp["L0_ffn_down_q"], dch)
    w["gate_s"] = sz_cm(inp["L0_ffn_gate_s"], gch)
    w["gate_z"] = sz_cm(inp["L0_ffn_gate_z"], gch)
    w["up_s"] = sz_cm(inp["L0_ffn_up_s"], gch)
    w["up_z"] = sz_cm(inp["L0_ffn_up_z"], gch)
    w["down_s"] = sz_cm(inp["L0_ffn_down_s"], dch)
    w["down_z"] = sz_cm(inp["L0_ffn_down_z"], dch)
    return w


def _rope_tables(p0, n):
    freqs = (1.0 / 10000.0 ** (np.arange(0, 64, 2, dtype=np.float64) / 64.0))
    t = np.arange(p0, p0 + n, dtype=np.float64)
    ang = freqs[:, None] * t[None, :]
    return np.concatenate([np.cos(ang), np.sin(ang)], axis=0).astype(BF)


def _mask(p0):
    s = np.arange(CTX)
    m = np.zeros((NT, 128, CTX), dtype=np.float32)
    for tt in range(NT):
        qpos = p0 + 128 * tt + np.arange(128)
        m[tt] = np.where(s[None, :] <= qpos[:, None], 0.0, -1e9)
    return m.astype(BF)


def kernel(L0_attn_x, L0_attn_wq_a, L0_attn_q_norm_w, L0_attn_wq_b,
           L0_attn_wkv_a, L0_attn_kv_norm_w, L0_attn_wkv_b, L0_attn_wo,
           L0_ffn_gate_q, L0_ffn_gate_s, L0_ffn_gate_z, L0_ffn_up_q,
           L0_ffn_up_s, L0_ffn_up_z, L0_ffn_down_q, L0_ffn_down_s,
           L0_ffn_down_z):
    inp = dict(L0_attn_wq_a=np.asarray(L0_attn_wq_a),
               L0_attn_q_norm_w=np.asarray(L0_attn_q_norm_w),
               L0_attn_wq_b=np.asarray(L0_attn_wq_b),
               L0_attn_wkv_a=np.asarray(L0_attn_wkv_a),
               L0_attn_kv_norm_w=np.asarray(L0_attn_kv_norm_w),
               L0_attn_wkv_b=np.asarray(L0_attn_wkv_b),
               L0_attn_wo=np.asarray(L0_attn_wo),
               L0_ffn_gate_q=np.asarray(L0_ffn_gate_q),
               L0_ffn_gate_s=np.asarray(L0_ffn_gate_s),
               L0_ffn_gate_z=np.asarray(L0_ffn_gate_z),
               L0_ffn_up_q=np.asarray(L0_ffn_up_q),
               L0_ffn_up_s=np.asarray(L0_ffn_up_s),
               L0_ffn_up_z=np.asarray(L0_ffn_up_z),
               L0_ffn_down_q=np.asarray(L0_ffn_down_q),
               L0_ffn_down_s=np.asarray(L0_ffn_down_s),
               L0_ffn_down_z=np.asarray(L0_ffn_down_z))
    x = np.asarray(L0_attn_x)
    w = _prep_weights(inp)

    if "nc" not in _CACHE:
        _CACHE["nc"] = build_nc()
    nc = _CACHE["nc"]

    in_maps = []
    for c in range(8):
        b, half = c // 2, c % 2
        p0 = 1024 * half
        m = dict(w)
        m["x_q"] = np.ascontiguousarray(x[b, p0:p0 + T, :])
        m["x_b"] = np.ascontiguousarray(x[b])
        m["rope_q"] = _rope_tables(p0, T)
        m["rope_k"] = _rope_tables(0, CTX)
        m["mask_in"] = _mask(p0)
        in_maps.append(m)

    res = run_bass_kernel_spmd(nc, in_maps, core_ids=list(range(8)))
    out = np.empty((4, CTX, D), dtype=np.float32)
    for c in range(8):
        b, half = c // 2, c % 2
        out[b, 1024 * half:1024 * half + T, :] = res.results[c]["out"]
    return out


# version 12 (final)
# speedup vs baseline: 1.4478x; 1.0117x over previous
import numpy as np
import ml_dtypes
import concourse.bass as bass
import concourse.mybir as mybir
from concourse.bass_utils import run_bass_kernel_spmd

F32 = mybir.dt.float32
BF16 = mybir.dt.bfloat16
I8 = mybir.dt.int8
AF = mybir.ActivationFunctionType
ALU = mybir.AluOpType
AX = mybir.AxisListType
BF = ml_dtypes.bfloat16

D = 4096
T = 1024
NT = 8
CTX = 2048
NCC = 4
NCT = 16
KT = 32
QL = 1536
QLT = 12
KVL = 512
KVLT = 4
NH = 32
NPAIR = 16
II = 11008
IT = 86
NNC = 22
NDC = 8
SCALE = float(1.0 / np.sqrt(192.0))
LN_EPS = 1e-5
RMS_EPS = 1e-6


def _ncw(j):
    return min(512, II - 512 * j)


def build_nc(dbg_phase=None):
    nc = bass.Bass("TRN2", target_bir_lowering=False)

    def din(name, shape, dt):
        return nc.dram_tensor(name, list(shape), dt, kind="ExternalInput").ap()

    x_q = din("x_q", (T, D), F32)
    x_b = din("x_b", (CTX, D), F32)
    wq_aT = din("wq_aT", (D, QL), F32)
    q_nw = din("q_nw", (QL, 1), F32)
    wq_bT = din("wq_bT", (QL, NH * 192), F32)
    wkv_aT = din("wkv_aT", (D, KVL + 64), F32)
    kv_nw = din("kv_nw", (KVL, 1), F32)
    wkv_bT = din("wkv_bT", (KVL, NH * 256), F32)
    woT = din("woT", (D, D), F32)
    gate_q8 = din("gate_q8", (128, KT * II), I8)
    up_q8 = din("up_q8", (128, KT * II), I8)
    down_q8 = din("down_q8", (128, IT * D), I8)
    gate_s = din("gate_s", (1, KT * II), F32)
    gate_z = din("gate_z", (1, KT * II), F32)
    up_s = din("up_s", (1, KT * II), F32)
    up_z = din("up_z", (1, KT * II), F32)
    down_s = din("down_s", (1, IT * D), F32)
    down_z = din("down_z", (1, IT * D), F32)
    rope_q = din("rope_q", (64, T), BF16)
    rope_k = din("rope_k", (64, CTX), BF16)
    mask_in = din("mask_in", (NT, 128, CTX), BF16)

    out_t = nc.dram_tensor("out", [T, D], F32, kind="ExternalOutput").ap()
    dbg_t = nc.dram_tensor("dbg_arena", [128, 192 * 1024], mybir.dt.uint8,
                           kind="ExternalOutput").ap()

    sbf = {}
    for nm, tot in [("gs", KT * II), ("gz", KT * II), ("us", KT * II),
                    ("uz", KT * II), ("ds", IT * D), ("dz", IT * D)]:
        sbf[nm] = nc.dram_tensor("sc_" + nm, [1, tot], BF16, kind="Internal").ap()
    hT_sc = nc.dram_tensor("hT_sc", [T, II], BF16, kind="Internal").ap()
    x1_sc = nc.dram_tensor("x1_sc", [T, D], BF16, kind="Internal").ap()
    rq_sc = nc.dram_tensor("rq_sc", [1, T], F32, kind="Internal").ap()
    rkv_sc = nc.dram_tensor("rkv_sc", [1, CTX], F32, kind="Internal").ap()

    ident = nc.alloc_sbuf_tensor("ident", [128, 128], BF16).ap()
    ones_col = nc.alloc_sbuf_tensor("ones_col", [128, 1], BF16).ap()
    qnw_sb = nc.alloc_sbuf_tensor("qnw_sb", [128, QLT], F32).ap()
    kvnw_sb = nc.alloc_sbuf_tensor("kvnw_sb", [128, KVLT], F32).ap()
    ropq_c = nc.alloc_sbuf_tensor("ropq_c", [32, T], BF16).ap()
    ropq_s = nc.alloc_sbuf_tensor("ropq_s", [32, T], BF16).ap()
    ropk_c = nc.alloc_sbuf_tensor("ropk_c", [32, CTX], BF16).ap()
    ropk_s = nc.alloc_sbuf_tensor("ropk_s", [32, CTX], BF16).ap()
    stat = nc.alloc_sbuf_tensor("stat", [128, 64], F32).ap()
    mv = nc.alloc_sbuf_tensor("mv", [128, 8], F32).ap()
    srt = nc.alloc_sbuf_tensor("srt", [128, NT], F32).ap()
    sm = nc.alloc_sbuf_tensor("sm", [128, 16], F32).ap()
    cst = nc.alloc_sbuf_tensor("cst", [128, 8], F32).ap()
    CONSTS = [float(LN_EPS), float(RMS_EPS), 1.0, 0.0,
              float(1.0 / QL), float(1.0 / KVL)]
    for ci, cv in enumerate(CONSTS):
        nc.const_aps.register(F32, cv, cst[:, ci:ci + 1])

    AR_BYTES = 192 * 1024
    arena_t = nc.alloc_sbuf_tensor("arena", [128, AR_BYTES], mybir.dt.uint8)
    ar_bf = arena_t.ap().bitcast(BF16)
    ar_f32 = arena_t.ap().bitcast(F32)

    def abf(off, n):
        assert off % 2 == 0 and off + 2 * n <= AR_BYTES, (off, n)
        return ar_bf[:, off // 2: off // 2 + n]

    def af32(off, n):
        assert off % 4 == 0 and off + 4 * n <= AR_BYTES, (off, n)
        return ar_f32[:, off // 4: off // 4 + n]

    def ar1(tile, t0=0, n=1024):
        return abf(2048 * tile + 2 * t0, n)

    def qlra(j, c0=0, n=1024):
        return abf(65536 + 2048 * j + 2 * c0, n)

    psA = nc.alloc_psum_tensor("psA", [128, 2048], F32).ap()
    psB = nc.alloc_psum_tensor("psB", [128, 512], F32).ap()
    psC = nc.alloc_psum_tensor("psC", [128, 512], F32).ap()
    psT = nc.alloc_psum_tensor("psT", [128, 1024], BF16).ap()
    psD = nc.alloc_psum_tensor("psD", [128, 512], F32).ap()

    class Sem:
        __slots__ = ("h", "n")

        def __init__(self, name):
            self.h = nc.alloc_semaphore(name)
            self.n = 0

    ldS = Sem("ldS")
    ldG = Sem("ldG")
    ldX = [Sem("ldX0"), Sem("ldX1")]
    ldX3 = Sem("ldX3")
    ldM = Sem("ldM")
    ldXS = Sem("ldXS")
    ldW = [Sem("ldW0"), Sem("ldW1")]
    ldH = [Sem("ldH0"), Sem("ldH1")]
    ldHg = Sem("ldHg")
    ldQp = [Sem("ldQ0"), Sem("ldQ1")]
    ldBp = [Sem("ldB0"), Sem("ldB1"), Sem("ldB2"), Sem("ldB3")]
    ldZp = [Sem("ldZ0"), Sem("ldZ1")]
    stS = Sem("stS")
    oSp = [Sem("oS0"), Sem("oS1")]
    smm = Sem("smm")
    sve = Sem("sve")
    sac = Sem("sac")
    spo = Sem("spo")

    ops = {"tensor": [], "vector": [], "scalar": [], "gpsimd": [], "sync": []}
    stopped = {"v": False}

    def emit(eng, build, waits=(), incs=()):
        if stopped["v"]:
            for s, a in incs:
                s.n += a
            return
        wl = tuple((s.h, (s.n if v is None else v)) for (s, v) in waits)
        il = []
        for s, a in incs:
            s.n += a
            il.append((s.h, a))
        il = tuple(il)

        def f(e, build=build, wl=wl, il=il):
            for h, v in wl:
                e.wait_ge(h, v)
            ins = build(e)
            for h, a in il:
                ins = ins.then_inc(h, a)
        ops[eng].append(f)

    def W(sem, val=None):
        return (sem, val)

    def ewait(eng, sem, val=None):
        if stopped["v"]:
            return
        v = sem.n if val is None else val
        ops[eng].append(lambda e, h=sem.h, v=v: e.wait_ge(h, v))

    def dmaS(dst, src, sem=ldS, waits=(), slow=False):
        if slow:
            emit("sync", lambda e, d=dst, s=src: e.dma_start(
                out=d, in_=s, allow_slow_non_contiguous=True),
                waits=waits, incs=((sem, 16),))
        else:
            emit("sync", lambda e, d=dst, s=src: e.dma_start(out=d, in_=s),
                 waits=waits, incs=((sem, 16),))

    def dmaG(dst, src, accum=None, sem=ldG, waits=()):
        if accum is None:
            emit("gpsimd", lambda e, d=dst, s=src: e.dma_start(out=d, in_=s),
                 waits=waits, incs=((sem, 16),))
        else:
            emit("gpsimd",
                 lambda e, d=dst, s=src, a=accum:
                     e.dma_start(out=d, in_=s, accum_op=a),
                 waits=waits, incs=((sem, 16),))

    def mm(out, lhsT, rhs, start, stop, waits=(), incs=()):
        emit("tensor",
             lambda e, o=out, l=lhsT, r=rhs, s1=start, s2=stop:
                 e.matmul(o, l, r, start=s1, stop=s2),
             waits=waits, incs=incs)

    def tr(out, in_, waits=(), incs=()):
        emit("tensor", lambda e, o=out, i=in_: e.transpose(o, i, ident),
             waits=waits, incs=incs)

    for ci, cv in enumerate(CONSTS):
        emit("vector", lambda e, ci=ci, cv=cv: e.memset(cst[:, ci:ci + 1], cv),
             incs=((sve, 1),))
    if dbg_phase is not None:
        emit("vector", lambda e: e.memset(ar_f32[:, :], 0.0),
             incs=((sve, 1),))
    emit("gpsimd", lambda e: e.memset(ident, 0.0), incs=((spo, 1),))
    emit("gpsimd", lambda e: e.memset(ones_col, 1.0), incs=((spo, 1),))
    ewait("gpsimd", spo)
    emit("gpsimd", lambda e: e.affine_select(
        out=ident, in_=ident, compare_op=ALU.not_equal, fill=1.0,
        base=0, pattern=[[-1, 128]], channel_multiplier=1), incs=((spo, 1),))

    dmaS(qnw_sb[:, 0:QLT], q_nw.rearrange("(j p) o -> p (j o)", p=128),
         slow=True)
    dmaS(kvnw_sb[:, 0:KVLT], kv_nw.rearrange("(j p) o -> p (j o)", p=128),
         slow=True)
    dmaS(ropq_c[:, :], rope_q[0:32, :])
    dmaS(ropq_s[:, :], rope_q[32:64, :])
    dmaS(ropk_c[:, :], rope_k[0:32, :])
    dmaS(ropk_s[:, :], rope_k[32:64, :])

    cvt_jobs = [(gate_s, sbf["gs"], KT * II), (gate_z, sbf["gz"], KT * II),
                (up_s, sbf["us"], KT * II), (up_z, sbf["uz"], KT * II),
                (down_s, sbf["ds"], IT * D), (down_z, sbf["dz"], IT * D)]
    for src, dst, tot in cvt_jobs:
        per = tot // 128
        cv_f = af32(0, per)
        cv_b = abf(4 * per, per)
        flat = src.rearrange("o (p m) -> (o p) m", p=128)
        dmaS(cv_f, flat, waits=(W(sve),))
        emit("vector", lambda e, a=cv_b, b=cv_f: e.tensor_copy(a, b),
             waits=(W(ldS),), incs=((sve, 1),))
        emit("sync", lambda e, d=dst, s=cv_b:
             e.dma_start(out=d.rearrange("o (p m) -> p (o m)", p=128), in_=s),
             waits=(W(sve),), incs=((stS, 16),))
        ewait("vector", stS)
    cvt_done_st = stS.n

    def layer_norm_apply(src, dst_bf, width, pre_waits):
        nch = width // 512
        for j in range(nch):
            emit("vector",
                 lambda e, s=src, j=j: e.bn_stats(
                     stat[:, 6 * j:6 * j + 6], s[:, 512 * j:512 * (j + 1)]),
                 waits=pre_waits if j == 0 else (), incs=((sve, 1),))
        emit("vector", lambda e, n=nch: e.bn_aggr(mv[:, 0:2], stat[:, 0:6 * n]),
             waits=(W(sve),), incs=((sve, 1),))
        emit("scalar", lambda e: e.activation(
            out=mv[:, 2:3], in_=mv[:, 1:2], func=AF.Sqrt,
            bias=float(LN_EPS), scale=1.0),
            waits=(W(sve),), incs=((sac, 1),))
        emit("vector", lambda e: e.reciprocal(mv[:, 3:4], mv[:, 2:3]),
             waits=(W(sac),), incs=((sve, 1),))
        emit("vector", lambda e, s=src, d=dst_bf: e.tensor_scalar(
            out=d, in0=s, scalar1=mv[:, 0:1], scalar2=mv[:, 3:4],
            op0=ALU.subtract, op1=ALU.mult),
            waits=(W(sve),), incs=((sve, 1),))
        return sve.n

    def chmajor_from_tok(src_bf, tt, dst_fn, src_done_sve, nk=KT):
        for k4 in range((nk + 3) // 4):
            ks = list(range(4 * k4, min(4 * k4 + 4, nk)))
            for idx, k in enumerate(ks):
                wl = ()
                if idx == 0:
                    wl = (W(sve, src_done_sve), W(sve)) if k4 == 0 else (W(sve),)
                tr(psT[:, 128 * idx:128 * (idx + 1)],
                   src_bf[:, 128 * k:128 * (k + 1)],
                   waits=wl,
                   incs=((smm, 1),) if idx == len(ks) - 1 else ())
            for idx, k in enumerate(ks):
                emit("vector",
                     lambda e, k=k, idx=idx, d=dst_fn, tt=tt: e.tensor_copy(
                         d(k)[:, 128 * tt:128 * (tt + 1)],
                         psT[:, 128 * idx:128 * (idx + 1)]),
                     waits=(W(smm),) if idx == 0 else (),
                     incs=((sve, 1),) if idx == len(ks) - 1 else ())


    def _dbg_stop(tag):
        if dbg_phase != tag or stopped["v"]:
            return False
        ewait("sync", sve)
        ewait("sync", smm)
        ewait("sync", sac)
        emit("sync", lambda e: e.dma_start(out=dbg_t, in_=arena_t.ap()),
             waits=(), incs=((stS, 16),))
        ewait("sync", stS)
        stopped["v"] = True
        return True

    ewait("tensor", spo)
    O_XA = 90112
    O_XB2 = 106496
    O_XT = 122880
    apply_done = {}
    for tt in range(NT):
        xa = af32(O_XA if tt % 2 == 0 else O_XB2, 4096)
        wl = ()
        if tt >= 2:
            wl = (W(sve, apply_done[tt - 2]),)
        if tt == 0:
            wl = (W(stS, cvt_done_st),)
        dmaS(xa, x_q[128 * tt:128 * (tt + 1), :], sem=ldX[tt % 2], waits=wl)
        xt = abf(O_XT, 4096)
        a_done = layer_norm_apply(
            xa, xt, 4096,
            (W(ldX[tt % 2]),) + ((W(smm),) if tt > 0 else ()))
        apply_done[tt] = a_done
        chmajor_from_tok(xt, tt, lambda k: ar1(k), a_done)

    if _dbg_stop('p1'):
        pass
    O_WQA = 90112
    p1_v = sve.n
    for k in range(KT):
        dmaG(abf(O_WQA + 3072 * k, 1536), wq_aT[128 * k:128 * (k + 1), :],
             waits=(W(sve, p1_v),) if k == 0 else ())
    wqa_ld = ldG.n
    for j in range(QLT):
        for tc in range(2):
            ps = psB if (2 * j + tc) % 2 == 0 else psC
            for k in range(KT):
                mm(ps[:, 0:512],
                   abf(O_WQA + 3072 * k + 256 * j, 128),
                   ar1(k, 512 * tc, 512),
                   start=(k == 0), stop=(k == KT - 1),
                   waits=((W(ldG, wqa_ld), W(sve)) if k == 0 else ()),
                   incs=((smm, 1),) if k == KT - 1 else ())
            emit("vector",
                 lambda e, j=j, tc=tc, p=ps: e.tensor_copy(
                     qlra(j, 512 * tc, 512), p[:, 0:512]),
                 waits=(W(smm),), incs=((sve, 1),))
    O_SQ = 188416
    rowv = af32(189440, 1024)
    for tc in range(2):
        psq = psD if tc == 0 else psB
        for j in range(QLT):
            sq = abf(O_SQ, 512)
            emit("vector",
                 lambda e, j=j, tc=tc, s=sq: e.tensor_mul(
                     s, qlra(j, 512 * tc, 512), qlra(j, 512 * tc, 512)),
                 waits=(W(sve), W(smm)), incs=((sve, 1),))
            mm(psq[0:1, 0:512], ones_col, sq,
               start=(j == 0), stop=(j == QLT - 1),
               waits=(W(sve),), incs=((smm, 1),))
        emit("scalar", lambda e, tc=tc, p=psq: e.activation(
            out=rowv[0:1, 512 * tc:512 * (tc + 1)], in_=p[0:1, 0:512],
            func=AF.Sqrt, bias=float(RMS_EPS), scale=float(1.0 / QL)),
            waits=(W(smm),), incs=((sac, 1),))
    emit("vector", lambda e: e.reciprocal(rowv[0:1, 0:1024],
                                          rowv[0:1, 0:1024]),
         waits=(W(sac),), incs=((sve, 1),))
    emit("sync", lambda e: e.dma_start(out=rq_sc, in_=rowv[0:1, 0:1024]),
         waits=(W(sve),), incs=((stS, 16),))
    dmaS(srt[:, 0:NT], rq_sc.rearrange("o (a p) -> p (o a)", p=128),
         waits=(W(stS),), slow=True)
    emit("vector", lambda e: e.tensor_scalar(
        out=srt[:, 0:NT], in0=srt[:, 0:NT], scalar1=float(SCALE), scalar2=None,
        op0=ALU.mult), waits=(W(ldS),), incs=((sve, 1),))

    if _dbg_stop('p2'):
        pass
    O_WKA = 90112
    O_XLC = 126976
    O_CKV = 159744
    O_KPE = 176128
    O_KPO = 180224
    O_KPB = 184320
    O_RSB = 188416
    O_TMP1 = 135168
    O_TMP2 = 139264
    rowv = af32(143360, 2048)
    p2_v, p2_m = sve.n, smm.n
    for k in range(KT):
        dmaG(abf(O_WKA + 1152 * k, 576), wkv_aT[128 * k:128 * (k + 1), :],
             waits=(W(sve, p2_v), W(smm, p2_m)) if k == 0 else ())
    wka_ld = ldG.n
    O_X3 = 0
    O_XT3 = 16384
    for cc in range(NCC):
        for st4 in range(4):
            ct = 4 * cc + st4
            xa = af32(O_X3, 4096)
            wl = (W(sve),) if ct > 0 else (W(sve, p2_v),)
            dmaS(xa, x_b[128 * ct:128 * (ct + 1), :], sem=ldX3, waits=wl)
            xt = abf(O_XT3, 4096)
            a_done = layer_norm_apply(xa, xt, 4096, (W(ldX3), W(smm)))
            chmajor_from_tok(xt, st4, lambda k: abf(O_XLC + 1024 * k, 512),
                             a_done)
        ewait("tensor", sve)
        for c in range(6):
            ps = [psA[:, 0:512], psA[:, 512:1024], psA[:, 1024:1536],
                  psA[:, 1536:2048], psB[:, 0:512], psD[:, 0:512]][c]
            mcols = 128 if c < 4 else 32
            coff = 256 * c if c < 4 else 1024 + 64 * (c - 4)
            for k in range(KT):
                wl = ()
                if k == 0:
                    wl = (W(sve),)
                    if cc == 0 and c == 0:
                        wl = (W(ldG, wka_ld), W(sve))
                mm(ps[0:mcols, 0:512],
                   abf(O_WKA + 1152 * k + coff, mcols),
                   abf(O_XLC + 1024 * k, 512),
                   start=(k == 0), stop=(k == KT - 1),
                   waits=wl,
                   incs=((smm, 1),) if k == KT - 1 else ())
            if c < 4:
                emit("vector", lambda e, c=c, cc=cc, p=ps: e.tensor_copy(
                    abf(O_CKV + 4096 * c + 1024 * cc, 512), p[:, 0:512]),
                    waits=(W(smm),), incs=((sve, 1),))
            else:
                emit("vector", lambda e, c=c, cc=cc, p=ps: e.tensor_copy(
                    abf((O_KPE if c == 4 else O_KPO) + 1024 * cc, 512)
                    [0:32, :],
                    p[0:32, 0:512]),
                    waits=(W(smm),), incs=((sve, 1),))
    kpe = abf(O_KPE, 2048)
    kpo = abf(O_KPO, 2048)
    tmp1 = abf(O_TMP1, 2048)
    tmp2 = abf(O_TMP2, 2048)
    kpb = abf(O_KPB, 2048)
    emit("vector", lambda e: e.tensor_mul(tmp1[0:32, :], kpe[0:32, :],
                                          ropk_c[:, :]),
         waits=(W(sve),), incs=((sve, 1),))
    emit("vector", lambda e: e.tensor_mul(tmp2[0:32, :], kpo[0:32, :],
                                          ropk_s[:, :]), incs=((sve, 1),))
    emit("vector", lambda e: e.tensor_sub(kpb[0:32, :], tmp1[0:32, :],
                                          tmp2[0:32, :]),
         waits=(W(sve),), incs=((sve, 1),))
    emit("vector", lambda e: e.tensor_mul(tmp1[0:32, :], kpe[0:32, :],
                                          ropk_s[:, :]),
         waits=(W(sve),), incs=((sve, 1),))
    emit("vector", lambda e: e.tensor_mul(tmp2[0:32, :], kpo[0:32, :],
                                          ropk_c[:, :]), incs=((sve, 1),))
    emit("vector", lambda e: e.tensor_add(kpb[32:64, :], tmp1[0:32, :],
                                          tmp2[0:32, :]),
         waits=(W(sve),), incs=((sve, 1),))
    for c in range(KVLT):
        sq = abf(O_XLC, 2048)
        emit("vector", lambda e, c=c, s=sq: e.tensor_mul(
            s, abf(O_CKV + 4096 * c, 2048), abf(O_CKV + 4096 * c, 2048)),
            waits=(W(sve), W(smm)), incs=((sve, 1),))
        for cc in range(NCC):
            mm(psA[0:1, 512 * cc:512 * (cc + 1)], ones_col,
               sq[:, 512 * cc:512 * (cc + 1)],
               start=(c == 0), stop=(c == KVLT - 1),
               waits=(W(sve),) if cc == 0 else (), incs=((smm, 1),))
    for cc in range(NCC):
        emit("scalar", lambda e, cc=cc: e.activation(
            out=rowv[0:1, 512 * cc:512 * (cc + 1)],
            in_=psA[0:1, 512 * cc:512 * (cc + 1)],
            func=AF.Sqrt, bias=float(RMS_EPS), scale=float(1.0 / KVL)),
            waits=(W(smm),), incs=((sac, 1),))
    emit("vector", lambda e: e.reciprocal(rowv[0:1, 0:2048],
                                          rowv[0:1, 0:2048]),
         waits=(W(sac),), incs=((sve, 1),))
    emit("sync", lambda e: e.dma_start(out=rkv_sc, in_=rowv[0:1, 0:2048]),
         waits=(W(sve),), incs=((stS, 16),))
    rsb = af32(O_RSB, 2048)
    emit("gpsimd", lambda e: e.dma_start(
        out=rsb, in_=rkv_sc.broadcast_to([128, CTX])),
        waits=(W(stS), W(sve)), incs=((ldG, 16),))
    for c in range(KVLT):
        emit("vector", lambda e, c=c: e.tensor_mul(
            abf(O_CKV + 4096 * c, 2048), abf(O_CKV + 4096 * c, 2048), rsb),
            waits=(W(ldG),) if c == 0 else (), incs=((sve, 1),))
    ckv_ready_v = sve.n

    if _dbg_stop('p3'):
        pass
    O_WKVB = 90112
    O_WQB = 94208
    O_KN = 103424
    O_VT = 111616
    O_QN = 119808
    O_QPF = 123904
    O_QPB = 128000
    O_PRB = 132096
    O_PRT = 136192
    O_MSK = 140288

    def arK(h2, n0, n):
        return abf(O_KN + 4096 * h2 + 2 * n0, n)

    def arV(ct, n0=0, n=256):
        return abf(O_VT + 512 * ct + 2 * n0, n)

    def arQN(h2, t0, n):
        return abf(O_QN + 2048 * h2 + 2 * t0, n)

    O_PRB2 = 152576
    O_PRB3 = 176128
    PRBS = [O_PRB, O_PRB2, O_PRB3]
    probsT = abf(O_PRT, 2048)
    maskb = abf(O_MSK, 2048)
    O_SC = [144384, 188416]
    p4it = {"i": 0, "evac": {}, "exp": {}, "pend": [], "tail": None}

    def emit_tail():
        if p4it["tail"] is None:
            return
        itp, par3 = p4it["tail"]
        p4it["tail"] = None
        sb = 8 * (itp % 2)
        prb = abf(PRBS[par3], 2048)
        emit("vector", lambda e, sb=sb: e.reduce_sum(
            sm[:, sb + 2:sb + 3], sm[:, sb + 4:sb + 8], axis=AX.X),
            waits=(W(sac, p4it["exp"][itp]),), incs=((sve, 1),))
        emit("vector", lambda e, sb=sb: e.reciprocal(
            sm[:, sb + 3:sb + 4], sm[:, sb + 2:sb + 3]),
            waits=(W(sve),), incs=((sve, 1),))
        emit("vector", lambda e, prb=prb, sb=sb: e.tensor_scalar(
            out=prb, in0=prb, scalar1=sm[:, sb + 3:sb + 4], scalar2=None,
            op0=ALU.mult),
            waits=(W(sve),), incs=((sve, 1),))
        p4it["pend"].append((p4it["meta"][itp][0], p4it["meta"][itp][1],
                             par3, sve.n))

    def emit_trav(hh, tt, par, pnorm_v):
        prb = abf(PRBS[par], 2048)
        h2 = hh % 2
        for b4 in range(4):
            for j in range(4):
                ct = 4 * b4 + j
                wl = ()
                if j == 0:
                    wl = (W(sve, pnorm_v), W(sac), W(sve)) if b4 == 0 \
                        else (W(sac),)
                tr(psT[:, 128 * j:128 * (j + 1)],
                   prb[:, 128 * ct:128 * (ct + 1)],
                   waits=wl,
                   incs=((smm, 1),) if j == 3 else ())
            emit("scalar", lambda e, b4=b4: e.copy(
                probsT[:, 512 * b4:512 * (b4 + 1)], psT[:, 0:512]),
                waits=(W(smm),), incs=((sac, 1),))
        for ct in range(NCT):
            mm(psD[:, 0:128],
               arV(ct, 128 * h2, 128),
               probsT[:, 128 * ct:128 * (ct + 1)],
               start=(ct == 0), stop=(ct == NCT - 1),
               waits=(W(sac),) if ct == 0 else (),
               incs=((smm, 1),) if ct == NCT - 1 else ())
        emit("scalar", lambda e, hh=hh, tt=tt: e.copy(
            ar1(hh, 128 * tt, 128), psD[:, 0:128]),
            waits=(W(smm),), incs=((sac, 1),))

    def flush_trav():
        emit_tail()
        while p4it["pend"]:
            emit_trav(*p4it["pend"].pop(0))

    for pr in range(NPAIR):
        flush_trav()
        pm0 = smm.n
        for c in range(KVLT):
            wl = ()
            if c == 0:
                wl = (W(sve, ckv_ready_v),) if pr == 0 else (W(smm, pm0),)
            dmaG(abf(O_WKVB + 1024 * c, 512),
                 wkv_bT[128 * c:128 * (c + 1), 512 * pr:512 * (pr + 1)],
                 waits=wl)
        for c in range(QLT):
            dmaG(abf(O_WQB + 768 * c, 384),
                 wq_bT[128 * c:128 * (c + 1), 384 * pr:384 * (pr + 1)])
        wld = ldG.n
        for c in range(KVLT):
            emit("vector", lambda e, c=c: e.tensor_scalar(
                out=abf(O_WKVB + 1024 * c, 512), in0=abf(O_WKVB + 1024 * c, 512),
                scalar1=kvnw_sb[:, c:c + 1], scalar2=None, op0=ALU.mult),
                waits=(W(ldG, wld),) if c == 0 else (), incs=((sve, 1),))
        for c in range(QLT):
            emit("vector", lambda e, c=c: e.tensor_scalar(
                out=abf(O_WQB + 768 * c, 384), in0=abf(O_WQB + 768 * c, 384),
                scalar1=qnw_sb[:, c:c + 1], scalar2=None, op0=ALU.mult),
                incs=((sve, 1),))
        wfold = sve.n

        for h2 in range(2):
            for tc in range(2):
                ps = psB if tc == 0 else psC
                for c in range(QLT):
                    mm(ps[:, 0:512],
                       abf(O_WQB + 768 * c + 384 * h2, 128),
                       qlra(c, 512 * tc, 512),
                       start=(c == 0), stop=(c == QLT - 1),
                       waits=(W(sve, wfold), W(sve)) if c == 0 else (),
                       incs=((smm, 1),) if c == QLT - 1 else ())
                emit("vector", lambda e, h2=h2, tc=tc, p=ps: e.tensor_copy(
                    arQN(h2, 512 * tc, 512), p[:, 0:512]),
                    waits=(W(smm),), incs=((sve, 1),))
            for pe_half in range(2):
                ps = psB if pe_half == 0 else psC
                for tc in range(2):
                    for c in range(QLT):
                        mm(ps[0:32, 512 * 0:512] if False else
                           ps[0:32, 0:512],
                           abf(O_WQB + 768 * c + 384 * h2 + 256
                               + 64 * pe_half, 32),
                           qlra(c, 512 * tc, 512),
                           start=(c == 0), stop=(c == QLT - 1),
                           waits=(W(sve),) if c == 0 else (),
                           incs=((smm, 1),) if c == QLT - 1 else ())
                    emit("vector", lambda e, tc=tc, p=ps, ph=pe_half:
                         e.tensor_copy(
                             abf(O_QPF + 2048 * ph + 1024 * tc, 512)
                             [0:32, :],
                             p[0:32, 0:512]),
                         waits=(W(smm),), incs=((sve, 1),))
            qpe = abf(O_QPF, 1024)
            qpo = abf(O_QPF + 2048, 1024)
            qt1 = abf(O_PRB, 1024)
            qt2 = abf(O_PRB + 2048, 1024)
            qpb = abf(O_QPB + 2048 * h2, 1024)
            emit("vector", lambda e, qpe=qpe, qt1=qt1: e.tensor_mul(
                qt1[0:32, :], qpe[0:32, :], ropq_c[:, :]),
                waits=(W(sve), W(smm)), incs=((sve, 1),))
            emit("vector", lambda e, qpo=qpo, qt2=qt2: e.tensor_mul(
                qt2[0:32, :], qpo[0:32, :], ropq_s[:, :]),
                incs=((sve, 1),))
            emit("vector", lambda e, qpb=qpb, qt1=qt1, qt2=qt2: e.tensor_sub(
                qpb[0:32, :], qt1[0:32, :], qt2[0:32, :]),
                waits=(W(sve),), incs=((sve, 1),))
            emit("vector", lambda e, qpe=qpe, qt1=qt1: e.tensor_mul(
                qt1[0:32, :], qpe[0:32, :], ropq_s[:, :]),
                waits=(W(sve),), incs=((sve, 1),))
            emit("vector", lambda e, qpo=qpo, qt2=qt2: e.tensor_mul(
                qt2[0:32, :], qpo[0:32, :], ropq_c[:, :]),
                incs=((sve, 1),))
            emit("vector", lambda e, qpb=qpb, qt1=qt1, qt2=qt2: e.tensor_add(
                qpb[32:64, :], qt1[0:32, :], qt2[0:32, :]),
                waits=(W(sve),), incs=((sve, 1),))

        for h2 in range(2):
            for cc in range(NCC):
                ps = psB if cc % 2 == 0 else psC
                for c in range(KVLT):
                    mm(ps[:, 0:512],
                       abf(O_WKVB + 1024 * c + 256 * h2, 128),
                       abf(O_CKV + 4096 * c + 1024 * cc, 512),
                       start=(c == 0), stop=(c == KVLT - 1),
                       waits=(W(sve),) if c == 0 else (),
                       incs=((smm, 1),) if c == KVLT - 1 else ())
                emit("vector", lambda e, h2=h2, cc=cc, p=ps: e.tensor_copy(
                    arK(h2, 512 * cc, 512), p[:, 0:512]),
                    waits=(W(smm),), incs=((sve, 1),))
        for ct in range(NCT):
            ps = psB if ct % 2 == 0 else psC
            for c in range(KVLT):
                mm(ps[:, 0:256],
                   abf(O_CKV + 4096 * c + 256 * ct, 128),
                   abf(O_WKVB + 1024 * c + 512, 256),
                   start=(c == 0), stop=(c == KVLT - 1),
                   waits=(W(sve),) if c == 0 else (),
                   incs=((smm, 1),) if c == KVLT - 1 else ())
            emit("vector", lambda e, ct=ct, p=ps: e.tensor_copy(
                arV(ct), p[:, 0:256]),
                waits=(W(smm),), incs=((sve, 1),))
        pair_ready_v = sve.n

        for tt in range(NT):
            dmaS(maskb, mask_in[tt, :, :], sem=ldM, waits=(W(sve),))
            mask_ld = ldM.n
            for h2 in range(2):
                hh = 2 * pr + h2
                it = p4it["i"]
                p4it["i"] = it + 1
                sc = af32(O_SC[it % 2], 2048)
                for cc in range(NCC):
                    wl = ()
                    if cc == 0:
                        wl = (W(sve, p4it["evac"].get(it - 1, 0)),
                              W(sve, pair_ready_v) if (tt == 0 and h2 == 0)
                              else (sve, 0))
                    mm(psA[:, 512 * cc:512 * (cc + 1)],
                       arQN(h2, 128 * tt, 128),
                       arK(h2, 512 * cc, 512),
                       start=True, stop=False, waits=wl)
                    mm(psA[:, 512 * cc:512 * (cc + 1)],
                       abf(O_QPB + 2048 * h2 + 256 * tt, 128)[0:64, :],
                       kpb[0:64, 512 * cc:512 * (cc + 1)],
                       start=False, stop=True,
                       incs=((smm, 1),) if cc == NCC - 1 else ())
                scores_mm = smm.n
                if len(p4it["pend"]) >= 2:
                    emit_trav(*p4it["pend"].pop(0))
                emit("vector", lambda e, sc=sc: e.tensor_add(
                    sc, psA[:, 0:2048], maskb),
                    waits=(W(smm, scores_mm), W(ldM, mask_ld),
                           W(sac, p4it["exp"].get(it - 2, 0))),
                    incs=((sve, 1),))
                p4it["evac"][it] = sve.n
                emit_tail()
                sb = 8 * (it % 2)
                emit("vector", lambda e, sc=sc, sb=sb: e.reduce_max(
                    sm[:, sb:sb + 1], sc, axis=AX.X),
                    waits=(W(sve),), incs=((sve, 1),))
                emit("vector", lambda e, tt=tt, sb=sb: e.tensor_scalar(
                    out=sm[:, sb + 1:sb + 2], in0=sm[:, sb:sb + 1],
                    scalar1=srt[:, tt:tt + 1], scalar2=-1.0,
                    op0=ALU.mult, op1=ALU.mult),
                    waits=(W(sve),), incs=((sve, 1),))
                prb = abf(PRBS[it % 3], 2048)
                for cc in range(NCC):
                    emit("scalar",
                         lambda e, cc=cc, tt=tt, sc=sc, prb=prb, sb=sb:
                         e.activation(
                             out=prb[:, 512 * cc:512 * (cc + 1)],
                             in_=sc[:, 512 * cc:512 * (cc + 1)],
                             func=AF.Exp, bias=sm[:, sb + 1:sb + 2],
                             scale=srt[:, tt:tt + 1],
                             accum_out=sm[:, sb + 4 + cc:sb + 5 + cc]),
                         waits=(W(sve),) if cc == 0 else (),
                         incs=((sac, 1),))
                p4it["exp"][it] = sac.n
                p4it["meta"] = getattr(p4it, "meta", None) or p4it.setdefault(
                    "meta", {})
                p4it["meta"][it] = (hh, tt)
                p4it["tail"] = (it, it % 3)

    if _dbg_stop('p4'):
        pass
    flush_trav()

    O_OO = 65536
    O_WO = 131072
    O_XF0 = 147456
    O_XF1 = 163840
    O_X1T = 180224
    O_XT5 = 188416

    def oot(ht, t0=0, n=1024):
        return abf(O_OO + 2048 * ht + 2 * t0, n)

    p4_v, p4_a, p4_m = sve.n, sac.n, smm.n
    wo_mm_done = {}
    wosrc = woT.rearrange("(v p) h -> p v h", p=128)
    for ht in range(KT):
        pse, pso = (psB, psC) if ht % 2 == 0 else \
            (psA[:, 0:512], psA[:, 512:1024])
        par = ht % 2
        woblk = abf(O_WO + 8192 * par, 4096)
        woblk3 = woblk.rearrange("p (v h) -> p v h", h=128)
        for si in range(4):
            wl = ()
            if si == 0:
                if ht >= 2:
                    wl = (W(smm, wo_mm_done[ht - 2]),)
                elif ht == 0:
                    wl = (W(sve, p4_v), W(sac, p4_a), W(smm, p4_m))
            emit("gpsimd", lambda e, si=si, ht=ht, woblk3=woblk3:
                 e.dma_start(
                     out=woblk3[:, 8 * si:8 * (si + 1), :],
                     in_=wosrc[:, 8 * si:8 * (si + 1),
                               128 * ht:128 * (ht + 1)]),
                 waits=wl, incs=((ldW[par], 16),))
        wo_ld = ldW[par].n
        for vc in range(KT):
            wob = abf(O_WO + 8192 * par + 256 * vc, 128)
            for tc in range(2):
                ps = pse if tc == 0 else pso
                mm(ps[:, 0:512] if ht % 2 == 0 else ps,
                   wob, ar1(vc, 512 * tc, 512),
                   start=(vc == 0), stop=(vc == KT - 1),
                   waits=((W(ldW[par], wo_ld), W(sve))
                          if vc == 0 else ()) if tc == 0 else (),
                   incs=((smm, 1),) if tc == 1 else ())
        wo_mm_done[ht] = smm.n
        for tc in range(2):
            ps = pse if tc == 0 else pso
            emit("vector", lambda e, ht=ht, tc=tc,
                 p=(pse if tc == 0 else pso), ish=(ht % 2 == 0):
                 e.tensor_copy(oot(ht, 512 * tc, 512),
                               p[:, 0:512] if ish else p),
                 waits=(W(smm),) if tc == 0 else (), incs=((sve, 1),))
    spill_prev = stS.n
    for tt in range(NT):
        xf = af32(O_XF0 if tt % 2 == 0 else O_XF1, 4096)
        dmaS(xf, x_q[128 * tt:128 * (tt + 1), :], sem=ldX[tt % 2],
             waits=(W(sve),))
        xf_ld = ldX[tt % 2].n
        x1tile = abf(O_X1T, 4096)
        for k4 in range(8):
            for j in range(4):
                ht = 4 * k4 + j
                wl = ()
                if j == 0:
                    wl = (W(sve), W(stS, spill_prev)) if k4 == 0 else (W(sve),)
                tr(psT[:, 128 * j:128 * (j + 1)],
                   oot(ht, 128 * tt, 128), waits=wl,
                   incs=((smm, 1),) if j == 3 else ())
            emit("vector", lambda e, k4=k4, x1tile=x1tile, xf=xf:
                 e.tensor_add(
                     x1tile[:, 512 * k4:512 * (k4 + 1)], psT[:, 0:512],
                     xf[:, 512 * k4:512 * (k4 + 1)]),
                 waits=(W(smm), W(ldX[tt % 2], xf_ld)), incs=((sve, 1),))
        xt5 = abf(O_XT5, 4096)
        a_done = layer_norm_apply(x1tile, xt5, 4096, (W(sve),))
        chmajor_from_tok(xt5, tt, lambda k: ar1(k), a_done)
        emit("sync", lambda e, tt=tt, x1tile=x1tile: e.dma_start(
            out=x1_sc[128 * tt:128 * (tt + 1), :], in_=x1tile),
            waits=(W(sve, a_done),), incs=((stS, 16),))
        spill_prev = stS.n
    x1_spilled = stS.n

    if _dbg_stop('p5'):
        pass
    slot_state = {"off": 0, "slot_bytes": 0}

    def dequant_unit(region_off, q8_flat, q8_off, s_flat, z_flat, s_off,
                     nk, n, par, w_memset=(), kseg=8, do_z=True):
        lq, lz = ldQp[par], ldZp[par]
        reg = abf(region_off, nk * n)
        emit("vector", lambda e, reg=reg: e.memset(reg, -8.0),
             waits=w_memset, incs=((sve, 1),))
        ms_v = sve.n
        emit("gpsimd", lambda e, q8=q8_flat, o=q8_off, reg=reg, m=nk * n:
             e.dma_start(out=reg, in_=q8[:, o:o + m], accum_op=ALU.add,
                         max_dma_last_dim=2048),
             waits=(W(sve, ms_v),), incs=((lq, 16),))
        q_tot = lq.n
        segs = [(k0, min(kseg, nk - k0)) for k0 in range(0, nk, kseg)]
        mu_done = []
        for si, (k0, kn) in enumerate(segs):
            lb = ldBp[2 * par + si % 2]
            slot = abf(slot_state["off"]
                       + slot_state["slot_bytes"] * (si % 2), kn * n)
            emit("gpsimd",
                 lambda e, slot=slot, sf=s_flat, o=s_off + k0 * n, m=kn * n:
                     e.dma_start(out=slot,
                                 in_=sf[0:1, o:o + m].broadcast_to([128, m])),
                 waits=(W(sve, mu_done[si - 2]),) if si >= 2 else (),
                 incs=((lb, 16),))
            s_tot = lb.n
            emit("vector",
                 lambda e, a=abf(region_off + 2 * k0 * n, kn * n), slot=slot:
                     e.tensor_mul(a, a, slot),
                 waits=(W(lq, q_tot), W(lb, s_tot)), incs=((sve, 1),))
            mu_done.append(sve.n)
        mu_all = sve.n
        if not do_z:
            return mu_all
        emit("gpsimd", lambda e, zf=z_flat, o=s_off, reg=reg, m=nk * n:
             e.dma_start(out=reg,
                         in_=zf[0:1, o:o + m].broadcast_to([128, m]),
                         accum_op=ALU.add, max_dma_last_dim=2048),
             waits=(W(sve, mu_all),), incs=((lz, 16),))
        return lz.n

    O_WFF = 65536

    def wff(half, k, n=512):
        return abf(O_WFF + 32768 * half + 2 * n * k, n)
    O_SG = 131072
    O_HB = 139264
    O_HT = 140288
    O_SS6 = 148480
    slot_state["off"] = O_SS6
    slot_state["slot_bytes"] = 8192

    p5_v, p5_m = sve.n, smm.n

    O_XGS = 164864
    O_OH = 166912
    O_ZSB = 168960
    zsb = [Sem("zsb0"), Sem("zsb1")]
    emit("vector", lambda e: e.memset(abf(O_OH, 1024), 0.0),
         waits=(W(sve, p5_v), W(smm, p5_m), W(sac)), incs=((sve, 1),))
    oh_ms = sve.n
    for g in range(KT):
        emit("vector", lambda e, g=g: e.tensor_copy(
            abf(O_OH + 64 * g + 2 * g, 1), ones_col),
            waits=(W(sve, oh_ms), W(spo)) if g == 0 else (),
            incs=((sve, 1),) if g == KT - 1 else ())
    oh_v = sve.n
    for g in range(KT):
        for th in range(2):
            mm((psB if th == 0 else psC)[0:32, 0:512],
               abf(O_OH + 64 * g, 32), ar1(g, 512 * th, 512),
               start=(g == 0), stop=(g == KT - 1),
               waits=(W(sve, oh_v),) if g == 0 and th == 0 else (),
               incs=((smm, 1),) if g == KT - 1 and th == 1 else ())
    xgs_m = smm.n
    for th in range(2):
        emit("vector", lambda e, th=th: e.tensor_copy(
            abf(O_XGS + 1024 * th, 512)[0:32, :],
            (psB if th == 0 else psC)[0:32, 0:512]),
            waits=(W(smm, xgs_m),) if th == 0 else (),
            incs=((sve, 1),) if th == 1 else ())
    xgz_v = sve.n

    units = []
    for j in range(NNC):
        units.append(("g", j))
        units.append(("u", j))
    unit_mm_done = {}
    spill_cnt = {"n": stS.n}
    for ui, (mat, j) in enumerate(units):
        half = ui % 2
        n0, n = 512 * j, _ncw(j)
        q8 = gate_q8 if mat == "g" else up_q8
        s_fl = sbf["gs"] if mat == "g" else sbf["us"]
        z_fl = sbf["gz"] if mat == "g" else sbf["uz"]
        if ui >= 2:
            wm = (W(smm, unit_mm_done[ui - 2]),)
        else:
            wm = (W(sve, p5_v), W(smm, p5_m), W(stS, x1_spilled))
        s_m = sbf["gs"] if mat == "g" else sbf["us"]
        z_m = sbf["gz"] if mat == "g" else sbf["uz"]
        dequant_unit(O_WFF + 32768 * half, q8, KT * n0,
                     s_m, z_m, KT * n0, KT, n, half, w_memset=wm,
                     do_z=False)
        dmaS(abf(O_ZSB + 1024 * half, n)[0:32, :],
             z_m[0:1, KT * n0:KT * n0 + KT * n].rearrange(
                 "o (g c) -> (o g) c", c=n),
             sem=zsb[half], waits=wm if ui >= 2 else (W(smm, p5_m),))
        zs_ld = zsb[half].n
        ps_pool = [psA[:, 0:512], psA[:, 512:1024], psA[:, 1024:1536],
                   psA[:, 1536:2048]]
        for tt in range(NT):
            ps = ps_pool[tt % 4]
            for k in range(KT):
                wl = ()
                if k == 0:
                    wl = (W(sac), W(sve))
                mm(ps[:, 0:n], ar1(k, 128 * tt, 128), wff(half, k, n),
                   start=(k == 0), stop=False, waits=wl)
            zw = (W(zsb[half], zs_ld),)
            if ui < 2:
                zw = zw + (W(sve, xgz_v),)
            mm(ps[:, 0:n], abf(O_XGS + 256 * tt, 128)[0:32, :],
               abf(O_ZSB + 1024 * half, n)[0:32, :],
               start=False, stop=True, waits=zw if tt == 0 else (),
               incs=((smm, 1),))
            if mat == "g":
                emit("scalar", lambda e, tt=tt, p=ps, n=n: e.activation(
                    out=abf(O_SG + 1024 * tt, n), in_=p[:, 0:n],
                    func=AF.Silu, bias=0.0, scale=1.0),
                    waits=(W(smm), W(sve)), incs=((sac, 1),))
            else:
                hb = abf(O_HB, 512)
                emit("vector", lambda e, tt=tt, p=ps, n=n, hb=hb: e.tensor_mul(
                    hb[:, 0:n], abf(O_SG + 1024 * tt, n), p[:, 0:n]),
                    waits=(W(smm), W(sac)), incs=((sve, 1),))
                hmul_v = sve.n
                nk4 = (n + 127) // 128
                for jj in range(nk4):
                    tr(psT[:, 128 * jj:128 * (jj + 1)],
                       hb[:, 128 * jj:128 * (jj + 1)],
                       waits=(W(sve, hmul_v), W(sve)) if jj == 0 else (),
                       incs=((smm, 1),) if jj == nk4 - 1 else ())
                for jj in range(nk4):
                    emit("vector", lambda e, jj=jj, tt=tt: e.tensor_copy(
                        abf(O_HT + 1024 * tt + 256 * jj, 128),
                        psT[:, 128 * jj:128 * (jj + 1)]),
                        waits=(W(smm), W(stS, spill_cnt["n"]))
                        if jj == 0 else (),
                        incs=((sve, 1),) if jj == nk4 - 1 else ())
        unit_mm_done[ui] = smm.n
        if mat == "u":
            nk4 = (n + 127) // 128
            for tt8 in range(NT):
                emit("sync", lambda e, tt8=tt8, j=j, nk4=nk4: e.dma_start(
                    out=hT_sc[128 * tt8:128 * (tt8 + 1),
                              512 * j:512 * j + 128 * nk4],
                    in_=abf(O_HT + 1024 * tt8, 128 * nk4)),
                    waits=(W(sve),), incs=((stS, 16),))
            spill_cnt["n"] = stS.n
    h_spilled = stS.n
    p6_m = smm.n

    O_HT7 = 0

    def ht7(buf, i, n=128):
        return abf(O_HT7 + 22528 * buf + 256 * i, n)
    O_X1S = 45056
    O_OUTB = 49152
    O_WD = 65536

    def wdd(buf, i, n=256):
        return abf(O_WD + 44032 * buf + 512 * i, n)
    O_SS7 = 155648
    slot_state["off"] = O_SS7
    slot_state["slot_bytes"] = 8192

    p6_v = sve.n
    NDCQ = 16
    dcq_mm = {}
    ht_segs = [(0, 16), (16, 16), (32, 16), (48, 16), (64, 16), (80, 6)]
    ob_store = {0: 0, 1: 0}
    tt_mm = {}
    for dq in range(NDCQ):
        wbuf = dq % 2
        if dq >= 2:
            wm = (W(smm, dcq_mm[dq - 2]),)
        elif dq == 0:
            wm = (W(sve, p6_v), W(smm, p6_m), W(stS, h_spilled))
        else:
            wm = ()
        deq_done = dequant_unit(O_WD + 44032 * wbuf,
                                down_q8, IT * 256 * dq,
                                sbf["ds"], sbf["dz"], IT * 256 * dq,
                                IT, 256, wbuf, w_memset=wm, kseg=16)
        for tt in range(NT):
            buf = tt % 2
            key = (dq, tt)
            pkey = (dq, tt - 2) if tt >= 2 else (dq - 1, tt + 6)
            if pkey in tt_mm:
                hw = (W(smm, tt_mm[pkey]),)
            else:
                hw = (W(smm, p6_m), W(stS, h_spilled))
            hdst = abf(O_HT7 + 22528 * buf, IT * 128)
            hsrc = hT_sc[128 * tt:128 * (tt + 1), :]
            if buf == 0:
                dmaS(hdst, hsrc, sem=ldH[0], waits=hw)
            else:
                dmaG(hdst, hsrc, sem=ldHg, waits=hw)
            dmaS(abf(O_X1S + 1024 * buf, 256),
                 x1_sc[128 * tt:128 * (tt + 1), 256 * dq:256 * (dq + 1)],
                 sem=ldH[buf],
                 waits=(W(stS, x1_spilled),) if (dq == 0 and tt < 2)
                 else (W(sve),))
            if buf == 0:
                mmw = (W(ldH[0]),)
            else:
                mmw = (W(ldHg), W(ldH[1]))
            ps = psB if tt % 2 == 0 else psC
            for i in range(IT):
                mm(ps[:, 0:256], ht7(buf, i), wdd(wbuf, i),
                   start=(i == 0), stop=(i == IT - 1),
                   waits=mmw + (W(ldZp[wbuf], deq_done), W(sve))
                   if i == 0 else (),
                   incs=((smm, 1),) if i == IT - 1 else ())
            tt_mm[key] = smm.n
            ob = af32(O_OUTB + 2048 * buf, 256)
            emit("vector", lambda e, p=ps, ob=ob, buf=buf: e.tensor_add(
                ob, p[:, 0:256], abf(O_X1S + 1024 * buf, 256)),
                waits=(W(smm), W(oSp[buf], ob_store[buf])), incs=((sve, 1),))
            emit("sync", lambda e, ob=ob, tt=tt, dq=dq: e.dma_start(
                out=out_t[128 * tt:128 * (tt + 1),
                          256 * dq:256 * (dq + 1)],
                in_=ob),
                waits=(W(sve),), incs=((oSp[buf], 16),))
            ob_store[buf] = oSp[buf].n
        dcq_mm[dq] = smm.n

    ewait("sync", oSp[0])
    ewait("sync", oSp[1])

    with nc.Block() as block:
        @block.tensor
        def _(eng):
            for f in ops["tensor"]:
                f(eng)

        @block.vector
        def _(eng):
            for f in ops["vector"]:
                f(eng)

        @block.scalar
        def _(eng):
            for f in ops["scalar"]:
                f(eng)

        @block.gpsimd
        def _(eng):
            for f in ops["gpsimd"]:
                f(eng)

        @block.sync
        def _(eng):
            for f in ops["sync"]:
                f(eng)

    return nc



_CACHE = {}


def _rope_perm_cols():
    return np.concatenate([np.arange(0, 64, 2), np.arange(1, 64, 2)])


def _prep_weights(inp):
    w = {}
    w["wq_aT"] = np.ascontiguousarray(inp["L0_attn_wq_a"].T)
    w["q_nw"] = np.ascontiguousarray(inp["L0_attn_q_norm_w"].reshape(QL, 1))
    wqbT = np.ascontiguousarray(inp["L0_attn_wq_b"].T)
    perm = np.zeros(NH * 192, dtype=np.int64)
    rp = _rope_perm_cols()
    for h in range(NH):
        b = 192 * h
        perm[b:b + 128] = b + np.arange(128)
        perm[b + 128:b + 192] = b + 128 + rp
    w["wq_bT"] = np.ascontiguousarray(wqbT[:, perm])
    wkvaT = np.ascontiguousarray(inp["L0_attn_wkv_a"].T)
    perm2 = np.concatenate([np.arange(KVL), KVL + rp])
    w["wkv_aT"] = np.ascontiguousarray(wkvaT[:, perm2])
    w["kv_nw"] = np.ascontiguousarray(inp["L0_attn_kv_norm_w"].reshape(KVL, 1))
    wkvbT = np.ascontiguousarray(inp["L0_attn_wkv_b"].T)
    perm3 = np.zeros(NH * 256, dtype=np.int64)
    for p in range(NPAIR):
        b = 512 * p
        h0, h1 = 2 * p, 2 * p + 1
        perm3[b:b + 128] = 256 * h0 + np.arange(128)
        perm3[b + 128:b + 256] = 256 * h1 + np.arange(128)
        perm3[b + 256:b + 384] = 256 * h0 + 128 + np.arange(128)
        perm3[b + 384:b + 512] = 256 * h1 + 128 + np.arange(128)
    w["wkv_bT"] = np.ascontiguousarray(wkvbT[:, perm3])
    w["woT"] = np.ascontiguousarray(inp["L0_attn_wo"].T)
    def q8_cm(q, chunks):
        q8 = np.ascontiguousarray(q).view(np.int8)[:, ::4]
        G = q8.shape[0] // 128
        qt = np.ascontiguousarray(q8).reshape(G, 128, q8.shape[1])
        blocks = [qt[:, :, n0:n0 + n].transpose(1, 0, 2).reshape(128, -1)
                  for (n0, n) in chunks]
        return np.ascontiguousarray(np.concatenate(blocks, axis=1))

    def sz_cm(a, chunks):
        blocks = [np.ascontiguousarray(a[:, n0:n0 + n]).reshape(-1)
                  for (n0, n) in chunks]
        return np.ascontiguousarray(np.concatenate(blocks)).reshape(1, -1)

    gch = [(512 * j, min(512, II - 512 * j)) for j in range(NNC)]
    dch = [(256 * j, 256) for j in range(16)]
    w["gate_q8"] = q8_cm(inp["L0_ffn_gate_q"], gch)
    w["up_q8"] = q8_cm(inp["L0_ffn_up_q"], gch)
    w["down_q8"] = q8_cm(inp["L0_ffn_down_q"], dch)
    w["gate_s"] = sz_cm(inp["L0_ffn_gate_s"], gch)
    w["gate_z"] = sz_cm(inp["L0_ffn_gate_z"], gch)
    w["up_s"] = sz_cm(inp["L0_ffn_up_s"], gch)
    w["up_z"] = sz_cm(inp["L0_ffn_up_z"], gch)
    w["down_s"] = sz_cm(inp["L0_ffn_down_s"], dch)
    w["down_z"] = sz_cm(inp["L0_ffn_down_z"], dch)
    return w


def _rope_tables(p0, n):
    freqs = (1.0 / 10000.0 ** (np.arange(0, 64, 2, dtype=np.float64) / 64.0))
    t = np.arange(p0, p0 + n, dtype=np.float64)
    ang = freqs[:, None] * t[None, :]
    return np.concatenate([np.cos(ang), np.sin(ang)], axis=0).astype(BF)


def _mask(p0):
    s = np.arange(CTX)
    m = np.zeros((NT, 128, CTX), dtype=np.float32)
    for tt in range(NT):
        qpos = p0 + 128 * tt + np.arange(128)
        m[tt] = np.where(s[None, :] <= qpos[:, None], 0.0, -1e9)
    return m.astype(BF)


def kernel(L0_attn_x, L0_attn_wq_a, L0_attn_q_norm_w, L0_attn_wq_b,
           L0_attn_wkv_a, L0_attn_kv_norm_w, L0_attn_wkv_b, L0_attn_wo,
           L0_ffn_gate_q, L0_ffn_gate_s, L0_ffn_gate_z, L0_ffn_up_q,
           L0_ffn_up_s, L0_ffn_up_z, L0_ffn_down_q, L0_ffn_down_s,
           L0_ffn_down_z):
    inp = dict(L0_attn_wq_a=np.asarray(L0_attn_wq_a),
               L0_attn_q_norm_w=np.asarray(L0_attn_q_norm_w),
               L0_attn_wq_b=np.asarray(L0_attn_wq_b),
               L0_attn_wkv_a=np.asarray(L0_attn_wkv_a),
               L0_attn_kv_norm_w=np.asarray(L0_attn_kv_norm_w),
               L0_attn_wkv_b=np.asarray(L0_attn_wkv_b),
               L0_attn_wo=np.asarray(L0_attn_wo),
               L0_ffn_gate_q=np.asarray(L0_ffn_gate_q),
               L0_ffn_gate_s=np.asarray(L0_ffn_gate_s),
               L0_ffn_gate_z=np.asarray(L0_ffn_gate_z),
               L0_ffn_up_q=np.asarray(L0_ffn_up_q),
               L0_ffn_up_s=np.asarray(L0_ffn_up_s),
               L0_ffn_up_z=np.asarray(L0_ffn_up_z),
               L0_ffn_down_q=np.asarray(L0_ffn_down_q),
               L0_ffn_down_s=np.asarray(L0_ffn_down_s),
               L0_ffn_down_z=np.asarray(L0_ffn_down_z))
    x = np.asarray(L0_attn_x)
    w = _prep_weights(inp)

    if "nc" not in _CACHE:
        _CACHE["nc"] = build_nc()
    nc = _CACHE["nc"]

    in_maps = []
    for c in range(8):
        b, half = c // 2, c % 2
        p0 = 1024 * half
        m = dict(w)
        m["x_q"] = np.ascontiguousarray(x[b, p0:p0 + T, :])
        m["x_b"] = np.ascontiguousarray(x[b])
        m["rope_q"] = _rope_tables(p0, T)
        m["rope_k"] = _rope_tables(0, CTX)
        m["mask_in"] = _mask(p0)
        in_maps.append(m)

    res = run_bass_kernel_spmd(nc, in_maps, core_ids=list(range(8)))
    out = np.empty((4, CTX, D), dtype=np.float32)
    for c in range(8):
        b, half = c // 2, c % 2
        out[b, 1024 * half:1024 * half + T, :] = res.results[c]["out"]
    return out
